# version 08
# baseline (speedup 1.0000x reference)
import numpy as np
from contextlib import ExitStack

import concourse.bass as bass
import concourse.mybir as mybir
from concourse.bass_utils import run_bass_kernel_spmd

F32 = mybir.dt.float32
BF16 = mybir.dt.bfloat16
I32 = mybir.dt.int32
U32 = mybir.dt.uint32
AF = mybir.ActivationFunctionType
ALU = mybir.AluOpType
AX = mybir.AxisListType

B, N, D = 4, 4096, 1024
HALF = N // 2
NT = HALF // 128
NF = D // 128
LN_EPS = 1e-5
DEBUG_L0 = False


ENGS = ("sync", "vector", "scalar", "tensor", "gpsimd")


class Op:
    __slots__ = ("idx", "eng", "fn", "kind", "deps", "need_inc", "val", "name")

    def __init__(self, idx, eng, fn, kind, deps, name):
        self.idx = idx
        self.eng = eng
        self.fn = fn
        self.kind = kind
        self.deps = deps
        self.need_inc = False
        self.val = None
        self.name = name


class Emitter:

    def __init__(self, nc):
        self.nc = nc
        self.ops = []
        self.last_writer = {}
        self.readers = {}
        self.out_dmas = []
        self.fence_deps = set()

    def fence(self):
        last = {}
        for op in self.ops:
            last[(op.eng, op.kind)] = op
        self.fence_deps = set(last.values())

    def add(self, eng, fn, reads=(), writes=(), kind="comp", name="", extra_deps=()):
        deps = set()
        for k in reads:
            w = self.last_writer.get(k)
            if w is not None:
                deps.add(w)
        for k in writes:
            w = self.last_writer.get(k)
            if w is not None:
                deps.add(w)
            for r in self.readers.get(k, ()):
                deps.add(r)
        for d in extra_deps:
            deps.add(d)
        deps |= self.fence_deps
        op = Op(len(self.ops), eng, fn, kind, deps, name)
        deps.discard(op)
        self.ops.append(op)
        for k in reads:
            self.readers.setdefault(k, []).append(op)
        for k in writes:
            self.last_writer[k] = op
            self.readers[k] = []
        return op

    def dma(self, eng, fn, reads=(), writes=(), name="", extra_deps=()):
        return self.add(eng, fn, reads, writes, kind="dma", name=name,
                        extra_deps=extra_deps)

    def cc(self, fn, reads=(), writes=(), name="", extra_deps=()):
        return self.add("gpsimd", fn, reads, writes, kind="cc", name=name,
                        extra_deps=extra_deps)

    def out_dma(self, eng, fn, reads=(), name=""):
        op = self.dma(eng, fn, reads=reads, writes=(), name=name)
        self.out_dmas.append(op)
        return op

    def _same_eng_skip(self, d, op):
        return (d.eng == op.eng and d.kind == "comp" and op.kind == "comp"
                and op.eng == "tensor")

    def prepare(self, ctx):
        nc = self.nc
        for op in self.ops:
            if op.kind != "comp":
                op.need_inc = True
            for d in op.deps:
                if self._same_eng_skip(d, op):
                    continue
                d.need_inc = True
        for op in self.out_dmas:
            op.need_inc = True
        sems = {}
        counts = {}
        for op in self.ops:
            key = (op.eng, op.kind)
            inc = 16 if op.kind == "dma" else 1
            if op.need_inc:
                counts[key] = counts.get(key, 0) + inc
                op.val = counts[key]
        for key in counts:
            sems[key] = ctx.enter_context(
                nc.semaphore(f"sem_{key[0]}_{key[1]}"))
        self.sems = sems
        self.counts = counts

    def emit(self, block):
        sems = self.sems
        per_eng = {e: [] for e in ENGS}
        for op in self.ops:
            per_eng[op.eng].append(op)
        prefix = {}
        for op in self.ops:
            if op.kind != "comp":
                key = (op.eng, op.kind)
                prefix.setdefault(key, []).append((op.idx, op.val))
        import bisect as _bis

        def async_val(key, before_idx):
            lst = prefix[key]
            i = _bis.bisect_left(lst, (before_idx, -1))
            return lst[i - 1][1]

        def emit_stream(eng_name, eng):
            waited = {}
            for op in per_eng[eng_name]:
                for d in sorted(op.deps, key=lambda o: o.idx):
                    if self._same_eng_skip(d, op):
                        continue
                    key = (d.eng, d.kind)
                    assert d.val is not None, (d.name, op.name)
                    v = d.val if d.kind == "comp" else async_val(key, op.idx)
                    if waited.get(key, 0) < v:
                        eng.wait_ge(sems[key], v)
                        waited[key] = v
                instr = op.fn(eng)
                if op.need_inc:
                    key = (op.eng, op.kind)
                    instr.then_inc(sems[key], 16 if op.kind == "dma" else 1)
            if eng_name == "sync":
                fin = {}
                for op in self.out_dmas:
                    key = (op.eng, op.kind)
                    fin[key] = max(fin.get(key, 0), op.val)
                for key, v in fin.items():
                    if waited.get(key, 0) < v:
                        eng.wait_ge(sems[key], v)

        @block.sync
        def _(sync):
            emit_stream("sync", sync)

        @block.vector
        def _(vector):
            emit_stream("vector", vector)

        @block.scalar
        def _(scalar):
            emit_stream("scalar", scalar)

        @block.tensor
        def _(tensor):
            emit_stream("tensor", tensor)

        @block.gpsimd
        def _(gpsimd):
            emit_stream("gpsimd", gpsimd)



class Ctx:
    pass


def alloc_common(cx, ctx):
    nc = cx.nc
    sb = lambda n, s, d: ctx.enter_context(nc.sbuf_tensor(n, s, d))
    cx.ident_i = sb("ident_i", [128, 128], I32)
    cx.ident = sb("ident", [128, 128], BF16)
    cx.psum = [ctx.enter_context(nc.psum_tensor(f"psA{i}", [128, 512], F32))
               for i in range(4)]
    cx.psT = [ctx.enter_context(nc.psum_tensor(f"psT{i}", [128, 1024], BF16))
              for i in range(2)]
    cx.psB = ctx.enter_context(nc.psum_tensor("psB", [128, 512], F32))
    cx.stat = sb("stat", [128, 6 * NT], F32)
    cx.sq_junk = sb("sq_junk", [128, 1024], BF16)
    cx.sgbuf = sb("sgbuf", [128, 1024], F32)


def make_identity(cx):
    em = cx.em
    em.add("gpsimd", lambda e: e.iota(cx.ident_i[:], [[1, 128]], base=0,
                                      channel_multiplier=-1),
           writes=("ident_i",), name="iota_ident")
    em.add("vector", lambda e: e.tensor_scalar(
        out=cx.ident[:], in0=cx.ident_i[:], scalar1=0.0, scalar2=None,
        op0=ALU.is_equal), reads=("ident_i",), writes=("ident",),
        name="ident_eq")


def layernorm_tm(cx, x_tile, xn_tile, n_tiles, x_key, xn_key, d=1024,
                 eps=LN_EPS, tag="ln"):
    em = cx.em
    st = cx.stat
    k = n_tiles
    mu = st[:, 0:k]
    sq = st[:, k:2 * k]
    var = st[:, 2 * k:3 * k]
    rstd = st[:, 3 * k:4 * k]
    tmp = st[:, 4 * k:5 * k]
    nmu = st[:, 5 * k:6 * k]
    for i in range(n_tiles):
        xt = x_tile(i)
        em.add("vector", lambda e, xt=xt, i=i: e.tensor_reduce(
            out=mu[:, i:i + 1], in_=xt, axis=AX.X, op=ALU.add),
            reads=(x_key,), writes=("stat",), name=f"{tag}_sum{i}")
        em.add("scalar", lambda e, xt=xt, i=i: e.activation(
            out=cx.sq_junk[:, 0:d], in_=xt, func=AF.Square,
            accum_out=sq[:, i:i + 1]),
            reads=(x_key,), writes=("stat", "sq_junk"), name=f"{tag}_sq{i}")
    s = ("stat",)
    em.add("vector", lambda e: e.tensor_scalar_mul(mu, mu, 1.0 / d),
           reads=s, writes=s, name=f"{tag}_mu")
    em.add("vector", lambda e: e.tensor_scalar_mul(sq, sq, 1.0 / d),
           reads=s, writes=s, name=f"{tag}_msq")
    em.add("vector", lambda e: e.scalar_tensor_tensor(
        out=var, in0=mu, scalar=-1.0, in1=mu, op0=ALU.mult, op1=ALU.mult),
        reads=s, writes=s, name=f"{tag}_mu2")
    em.add("vector", lambda e: e.tensor_add(var, var, sq),
           reads=s, writes=s, name=f"{tag}_var")
    em.add("vector", lambda e: e.tensor_scalar_add(var, var, eps),
           reads=s, writes=s, name=f"{tag}_vare")
    em.add("scalar", lambda e: e.activation(out=rstd, in_=var, func=AF.Sqrt),
           reads=s, writes=s, name=f"{tag}_sqrt")
    em.add("vector", lambda e: e.reciprocal(out=tmp, in_=rstd),
           reads=s, writes=s, name=f"{tag}_rc1")
    em.add("vector", lambda e: e.tensor_mul(tmp, tmp, var),
           reads=s, writes=s, name=f"{tag}_vs")
    em.add("vector", lambda e: e.tensor_add(rstd, rstd, tmp),
           reads=s, writes=s, name=f"{tag}_ad")
    em.add("vector", lambda e: e.tensor_scalar_mul(rstd, rstd, 0.5),
           reads=s, writes=s, name=f"{tag}_hf")
    em.add("vector", lambda e: e.reciprocal(out=rstd, in_=rstd),
           reads=s, writes=s, name=f"{tag}_rstd")
    em.add("vector", lambda e: e.tensor_scalar_mul(nmu, mu, -1.0),
           reads=s, writes=s, name=f"{tag}_nmu")
    for i in range(n_tiles):
        xt = x_tile(i)
        ot = xn_tile(i)
        em.add("vector", lambda e, xt=xt, ot=ot, i=i: e.tensor_scalar(
            out=ot, in0=xt, scalar1=nmu[:, i:i + 1], scalar2=rstd[:, i:i + 1],
            op0=ALU.add, op1=ALU.mult),
            reads=(x_key, "stat"), writes=(xn_key,), name=f"{tag}_nrm{i}")


def transpose_tm_to_fm(cx, tm_ap, fm_ap, tm_key, fm_key, n_tok_tiles,
                       n_feat_tiles, tag="tr", evac_eng="vector"):
    em = cx.em
    ntok = n_tok_tiles * 128
    dw = n_feat_tiles * 128
    for f in range(n_feat_tiles):
        for half in range((n_tok_tiles + 7) // 8):
            bank = cx.psT[half % 2]
            bkey = f"psT{half % 2}"
            lo = half * 8
            hi = min(lo + 8, n_tok_tiles)
            for i in range(lo, hi):
                src = tm_ap[:, dw * i + 128 * f: dw * i + 128 * (f + 1)]
                em.add("tensor",
                       lambda e, src=src, j=i - lo, bank=bank:
                       e.transpose(bank[:, 128 * j:128 * (j + 1)], src,
                                   cx.ident[:]),
                       reads=(tm_key, "ident"), writes=(bkey,),
                       name=f"{tag}_t{f}_{i}")
            dst = fm_ap[:, ntok * f + 128 * lo: ntok * f + 128 * hi]
            em.add(evac_eng,
                   lambda e, dst=dst, bank=bank, w=(hi - lo) * 128:
                   e.tensor_copy(out=dst, in_=bank[:, 0:w]),
                   reads=(bkey,), writes=(fm_key,), name=f"{tag}_e{f}_{half}")


def bcast_row(cx, dst_ap, src_row_ap, src_keys, dst_keys, width, name="bc"):
    em = cx.em
    nchunk = (width + 511) // 512
    for c in range(nchunk):
        w = min(512, width - 512 * c)
        ps = cx.psB
        em.add("tensor", lambda e, c=c, w=w: e.matmul(
            ps[:, 0:w], cx.ones1[:, :], src_row_ap[:, 512 * c:512 * c + w],
            start=True, stop=True),
            reads=tuple(src_keys) + ("ones1",), writes=("psB",),
            name=f"{name}_mm{c}")
        em.add("vector", lambda e, c=c, w=w: e.tensor_copy(
            out=dst_ap[:, 512 * c:512 * c + w], in_=ps[:, 0:w]),
            reads=("psB",), writes=tuple(dst_keys), name=f"{name}_ev{c}")


def make_ones1(cx, ctx):
    nc = cx.nc
    cx.ones1 = ctx.enter_context(nc.sbuf_tensor("ones1", [1, 128], BF16))
    cx.em.add("vector", lambda e: e.memset(cx.ones1[:], 1.0),
              writes=("ones1",), name="ones1")



def fmpack(w):
    K, F = w.shape
    KT, FT = K // 128, F // 128
    return np.ascontiguousarray(
        w.reshape(KT, 128, FT, 128).transpose(2, 1, 0, 3).reshape(FT, 128, K))


def slabpack(w):
    K, F = w.shape
    return np.ascontiguousarray(w.reshape(K // 128, 128, F))



def new_ctx(nc, ctx):
    cx = Ctx()
    cx.nc = nc
    cx.em = Emitter(nc)
    cx.uid = 0
    alloc_common(cx, ctx)
    make_ones1(cx, ctx)
    return cx


def t_ap(t, *idx):
    return t[idx] if idx else t[:]


def load_x_and_norm(cx, ctx, x_dram, with_fm=True):
    nc, em = cx.nc, cx.em
    sb = lambda n, s, d: ctx.enter_context(nc.sbuf_tensor(n, s, d))
    cx.x_tm = sb("x_tm", [128, NT * 1024], F32)
    cx.xn_tm = sb("xn_tm", [128, NT * 1024], BF16)
    if with_fm:
        cx.xn_fm = sb("xn_fm", [128, NF * HALF], BF16)
    xr = x_dram.rearrange("(i p) d -> i p d", p=128)
    for i in range(NT):
        em.dma("sync", lambda e, i=i: e.dma_start(
            out=cx.x_tm[:, 1024 * i:1024 * (i + 1)], in_=xr[i]),
            writes=("x_tm",), name=f"ldx{i}")
    layernorm_tm(cx, lambda i: cx.x_tm[:, 1024 * i:1024 * (i + 1)],
                 lambda i: cx.xn_tm[:, 1024 * i:1024 * (i + 1)],
                 NT, "x_tm", "xn_tm", tag="oln")
    if with_fm:
        transpose_tm_to_fm(cx, cx.xn_tm, cx.xn_fm, "xn_tm", "xn_fm", NT, NF,
                           tag="xtr")


def store_x_update(cx, x_dram, s_tm_ap, s_key, g_tm_ap, g_key):
    nc, em = cx.nc, cx.em
    xr = x_dram.rearrange("(i p) d -> i p d", p=128)
    for i in range(NT):
        sl = slice(1024 * i, 1024 * (i + 1))
        em.add("vector", lambda e, sl=sl: e.scalar_tensor_tensor(
            out=cx.xn_tm[:, sl], in0=s_tm_ap[:, sl], scalar=1.0,
            in1=g_tm_ap[:, sl], op0=ALU.mult, op1=ALU.mult),
            reads=(s_key, g_key), writes=("xn_tm",), name=f"sg{i}")
        em.add("vector", lambda e, sl=sl: e.tensor_add(
            cx.x_tm[:, sl], cx.x_tm[:, sl], cx.xn_tm[:, sl]),
            reads=("xn_tm", "x_tm"), writes=("x_tm",), name=f"xupd{i}")
        em.out_dma("sync", lambda e, i=i, sl=sl: e.dma_start(
            out=xr[i], in_=cx.x_tm[:, sl]),
            reads=("x_tm",), name=f"stx{i}")


def proj_fm(cx, w_dram_ft, src_fm_ap, src_key, dst_write, dst_key, wbuf,
            wbuf_key, n_ft, n_kt, ntok, tag, evac, kt_of_src=None,
            dtype_w=F32):
    em = cx.em
    nch = (ntok + 511) // 512
    for f in range(n_ft):
        wb = wbuf[f % 2]
        wk = wbuf_key[f % 2]
        if dtype_w == F32:
            em.dma("gpsimd", lambda e, f=f, wb=wb: e.dma_start(
                out=wb, in_=w_dram_ft[f]), writes=(wk,), name=f"{tag}_w{f}")
        else:
            em.dma("sync", lambda e, f=f, wb=wb: e.dma_start(
                out=wb, in_=w_dram_ft[f]), writes=(wk,), name=f"{tag}_w{f}")
        psl = []
        for c in range(nch):
            w = min(512, ntok - 512 * c)
            ps = cx.psum[c]
            psl.append((ps, w))
        for k in range(n_kt):
            lw = em.add("tensor", lambda e, wb=wb, k=k: e.ldweights(
                wb[:, 128 * k:128 * (k + 1)]),
                reads=(wk,), writes=(), name=f"{tag}_lw{f}_{k}")
            for c in range(nch):
                ps, w = psl[c]
                em.add("tensor", lambda e, wb=wb, k=k, c=c, ps=ps, w=w:
                       e.matmul(ps[:, 0:w], wb[:, 128 * k:128 * (k + 1)],
                                src_fm_ap[:, ntok * k + 512 * c:
                                          ntok * k + 512 * c + w],
                                start=(k == 0), stop=(k == n_kt - 1)),
                       reads=(wk, src_key), writes=(f"ps{c}",),
                       name=f"{tag}_mm{f}_{k}_{c}")
        evac(f, psl)


def transpose_fm_to_tm(cx, fm_ap, tm_ap, fm_key, tm_key, n_tok_tiles,
                       n_feat_tiles, tag="trb", evac_eng="vector"):
    em = cx.em
    ntok = n_tok_tiles * 128
    dw = n_feat_tiles * 128
    for i in range(n_tok_tiles):
        bank = cx.psT[i % 2]
        bkey = f"psT{i % 2}"
        for f in range(n_feat_tiles):
            src = fm_ap[:, ntok * f + 128 * i: ntok * f + 128 * (i + 1)]
            em.add("tensor", lambda e, src=src, f=f, bank=bank:
                   e.transpose(bank[:, 128 * f:128 * (f + 1)], src, cx.ident[:]),
                   reads=(fm_key, "ident"), writes=(bkey,), name=f"{tag}_t{i}_{f}")
        dst = tm_ap[:, dw * i: dw * (i + 1)]
        em.add(evac_eng, lambda e, dst=dst, bank=bank, w=dw:
               e.tensor_copy(out=dst, in_=bank[:, 0:w]),
               reads=(bkey,), writes=(tm_key,), name=f"{tag}_e{i}")


def outer_ln_streamed(cx, ctx, x_dram, tag="oln"):
    nc, em = cx.nc, cx.em
    sb = lambda n, s, d: ctx.enter_context(nc.sbuf_tensor(n, s, d))
    cx.xn_tm = sb(f"xn_tm_{tag}", [128, NT * 1024], BF16)
    cx.xn_fm = sb(f"xn_fm_{tag}", [128, NF * HALF], BF16)
    cx.xbuf = sb(f"xbuf_{tag}", [128, 2 * 1024], F32)
    xrk = getattr(cx, "x_rkey", None)
    xdep = (xrk,) if xrk else ()
    xr = x_dram.rearrange("(i p) d -> i p d", p=128)
    st = cx.stat
    k = NT
    mu, sq = st[:, 0:k], st[:, k:2 * k]
    var, rstd = st[:, 2 * k:3 * k], st[:, 3 * k:4 * k]
    tmp, nmu = st[:, 4 * k:5 * k], st[:, 5 * k:6 * k]
    for i in range(NT):
        xb = cx.xbuf[:, 1024 * (i % 2):1024 * (i % 2 + 1)]
        bk = f"xbuf{i % 2}"
        em.dma("sync", lambda e, i=i, xb=xb: e.dma_start(out=xb, in_=xr[i]),
               reads=xdep, writes=(bk,), name=f"{tag}_ld{i}")
        em.add("vector", lambda e, xb=xb, i=i: e.tensor_reduce(
            out=mu[:, i:i + 1], in_=xb, axis=AX.X, op=ALU.add),
            reads=(bk,), writes=("stat",), name=f"{tag}_sum{i}")
        em.add("scalar", lambda e, xb=xb, i=i: e.activation(
            out=cx.sq_junk[:, 0:1024], in_=xb, func=AF.Square,
            accum_out=sq[:, i:i + 1]),
            reads=(bk,), writes=("stat", "sq_junk"), name=f"{tag}_sq{i}")
    s = ("stat",)
    em.add("vector", lambda e: e.tensor_scalar_mul(mu, mu, 1.0 / D),
           reads=s, writes=s, name=f"{tag}_mu")
    em.add("vector", lambda e: e.tensor_scalar_mul(sq, sq, 1.0 / D),
           reads=s, writes=s, name=f"{tag}_msq")
    em.add("vector", lambda e: e.scalar_tensor_tensor(
        out=var, in0=mu, scalar=-1.0, in1=mu, op0=ALU.mult, op1=ALU.mult),
        reads=s, writes=s, name=f"{tag}_mu2")
    em.add("vector", lambda e: e.tensor_add(var, var, sq),
           reads=s, writes=s, name=f"{tag}_var")
    em.add("vector", lambda e: e.tensor_scalar_add(var, var, LN_EPS),
           reads=s, writes=s, name=f"{tag}_vare")
    em.add("scalar", lambda e: e.activation(out=rstd, in_=var, func=AF.Sqrt),
           reads=s, writes=s, name=f"{tag}_sqrt")
    em.add("vector", lambda e: e.reciprocal(out=tmp, in_=rstd),
           reads=s, writes=s, name=f"{tag}_rc1")
    em.add("vector", lambda e: e.tensor_mul(tmp, tmp, var),
           reads=s, writes=s, name=f"{tag}_vs")
    em.add("vector", lambda e: e.tensor_add(rstd, rstd, tmp),
           reads=s, writes=s, name=f"{tag}_ad")
    em.add("vector", lambda e: e.tensor_scalar_mul(rstd, rstd, 0.5),
           reads=s, writes=s, name=f"{tag}_hf")
    em.add("vector", lambda e: e.reciprocal(out=rstd, in_=rstd),
           reads=s, writes=s, name=f"{tag}_rstd")
    em.add("vector", lambda e: e.tensor_scalar_mul(nmu, mu, -1.0),
           reads=s, writes=s, name=f"{tag}_nmu")
    for i in range(NT):
        xb = cx.xbuf[:, 1024 * (i % 2):1024 * (i % 2 + 1)]
        bk = f"xbuf{i % 2}"
        em.dma("sync", lambda e, i=i, xb=xb: e.dma_start(out=xb, in_=xr[i]),
               reads=xdep, writes=(bk,), name=f"{tag}_ld2_{i}")
        em.add("vector", lambda e, xb=xb, i=i: e.tensor_scalar(
            out=cx.xn_tm[:, 1024 * i:1024 * (i + 1)], in0=xb,
            scalar1=nmu[:, i:i + 1], scalar2=rstd[:, i:i + 1],
            op0=ALU.add, op1=ALU.mult),
            reads=(bk, "stat"), writes=("xn_tm",), name=f"{tag}_nrm{i}")
    transpose_tm_to_fm(cx, cx.xn_tm, cx.xn_fm, "xn_tm", "xn_fm", NT, NF,
                       tag=f"{tag}_tr")


def final_update_streamed(cx, ctx, x_dram, xout_dram, s_tm, s_key, g_tm, g_key):
    nc, em = cx.nc, cx.em
    xrk = getattr(cx, "x_rkey", None)
    xdep = (xrk,) if xrk else ()
    xwk = getattr(cx, "x_wkey", None)
    xr = x_dram.rearrange("(i p) d -> i p d", p=128)
    xo = xout_dram.rearrange("(i p) d -> i p d", p=128)
    for i in range(NT):
        xb = cx.xbuf[:, 1024 * (i % 2):1024 * (i % 2 + 1)]
        bk = f"xbuf{i % 2}"
        sl = slice(1024 * i, 1024 * (i + 1))
        em.dma("sync", lambda e, i=i, xb=xb: e.dma_start(out=xb, in_=xr[i]),
               reads=xdep, writes=(bk,), name=f"fu_ld{i}")
        em.add("vector", lambda e, sl=sl: e.scalar_tensor_tensor(
            out=cx.sgbuf[:], in0=s_tm[:, sl], scalar=1.0, in1=g_tm[:, sl],
            op0=ALU.mult, op1=ALU.mult),
            reads=(s_key, g_key), writes=("sgbuf",), name=f"fu_sg{i}")
        em.add("vector", lambda e, xb=xb: e.tensor_add(xb, xb, cx.sgbuf[:]),
            reads=("sgbuf", bk), writes=(bk,), name=f"fu_add{i}")
        if xwk:
            em.dma("sync", lambda e, i=i, xb=xb: e.dma_start(
                out=xo[i], in_=xb), reads=(bk,), writes=(xwk,),
                name=f"fu_st{i}")
        else:
            em.out_dma("sync", lambda e, i=i, xb=xb: e.dma_start(
                out=xo[i], in_=xb), reads=(bk,), name=f"fu_st{i}")



def _flag8(nc):
    if not hasattr(nc, "_flag8_ap"):
        nc._flag8_ap = nc.dram_tensor("flag8", [1, 8], F32,
                                      kind="ExternalInput").ap()
    return nc._flag8_ap


def prep_l2_weights(inputs, h):
    w = {}
    ln_w = np.asarray(inputs["L2_attn_ln_w"])
    ln_b = np.asarray(inputs["L2_attn_ln_b"])
    w["l2_lnw"] = np.ascontiguousarray(ln_w.reshape(2, 8, 128))
    w["l2_lnb"] = np.ascontiguousarray(ln_b.reshape(2, 8, 128))
    for nm, key in (("wk", "L2_attn_Wk"), ("wv", "L2_attn_Wv"),
                    ("wr", "L2_attn_Wr"), ("wo", "L2_attn_Wo")):
        m = np.asarray(inputs[key])
        w[f"l2_{nm}"] = np.stack([fmpack(np.ascontiguousarray(m[i].T))
                                  for i in range(2)])
    w["l2f_wg"] = fmpack(np.ascontiguousarray(np.asarray(inputs["L2_ffn_w_gate"]).T))
    w["l2f_wu"] = fmpack(np.ascontiguousarray(np.asarray(inputs["L2_ffn_w_up"]).T))
    w["l2f_wd"] = slabpack(np.ascontiguousarray(np.asarray(inputs["L2_ffn_w_down"]).T))
    w["flag8"] = np.full((1, 8), float(h), np.float32)
    return w


def declare_l2_dram(nc):
    d = {}
    d["l2_lnw"] = nc.dram_tensor("l2_lnw", [2, 8, 128], F32, kind="ExternalInput").ap()
    d["l2_lnb"] = nc.dram_tensor("l2_lnb", [2, 8, 128], F32, kind="ExternalInput").ap()
    for nm in ("wk", "wv", "wr", "wo"):
        d[f"l2_{nm}"] = nc.dram_tensor(f"l2_{nm}", [2, 8, 128, 1024], F32,
                                       kind="ExternalInput").ap()
    d["l2f_wg"] = nc.dram_tensor("l2f_wg", [24, 128, 1024], I32, kind="ExternalInput").ap()
    d["l2f_wu"] = nc.dram_tensor("l2f_wu", [24, 128, 1024], I32, kind="ExternalInput").ap()
    d["l2f_wd"] = nc.dram_tensor("l2f_wd", [24, 128, 1024], I32, kind="ExternalInput").ap()
    d["flag8"] = _flag8(nc)
    d["l2_cc_in"] = nc.dram_tensor("l2_cc_in", [128, 4], F32).ap()
    d["l2_cc_out"] = nc.dram_tensor("l2_cc_out", [256, 4], F32).ap()
    for nm in ("wg16c", "wu16c", "wd16c"):
        d[nm] = nc.dram_tensor(nm, [24, 128, 1024], BF16).ap()
    return d


def build_l2(cx, ctx, dr):
    nc, em = cx.nc, cx.em
    sb = lambda n, s, d: ctx.enter_context(nc.sbuf_tensor(n, s, d))
    zfm = sb("l2_zfm", [128, NF * HALF], BF16)
    out1 = sb("l2_out1", [128, NF * HALF], BF16)
    g_tm = out1
    tmb = cx.xn_tm
    em.fence()
    psC = ctx.enter_context(nc.psum_tensor("psC", [128, 512], F32))
    FG = 4

    from contextlib import ExitStack as _ES
    with _ES() as scr:
        sc = lambda n, s, d: scr.enter_context(nc.sbuf_tensor(n, s, d))
        rd = sc("l2_rd", [128, FG * HALF], BF16)
        kf = sc("l2_kf", [128, HALF], F32)
        rf = sc("l2_rf", [128, HALF], BF16)
        kvf = sc("l2_kvf", [128, HALF], F32)
        stf = sc("l2_stf", [128, HALF], F32)
        dpow = sc("l2_dpow", [128, HALF], BF16)
        dec = cx.xbuf[:, 0:HALF]
        zero1 = sc("l2_zero1", [128, 1], F32)
        carry = sc("l2_carry", [128, 4], F32)
        carry_in = sc("l2_carry_in", [128, 4], F32)
        flag_s = sc("l2_flag", [1, 8], F32)
        lnv = sc("l2_lnv", [128, 32], F32)
        lnstage = sc("l2_lnstage", [128, 128], F32)
        ident32 = sc("ident32", [128, 128], F32)
        wkb = sc("l2_wkb", [128, 1024], BF16)
        wvb = sc("l2_wvb", [128, 1024], BF16)
        wrb = sc("l2_wrb", [128, 1024], BF16)
        wob = sc("l2_wob", [128, 1024], BF16)
        ones1f = sc("ones1f", [1, 128], F32)

        em.add("vector", lambda e: e.tensor_scalar(
            out=ident32[:], in0=cx.ident_i[:], scalar1=0.0, scalar2=None,
            op0=ALU.is_equal), reads=("ident_i",), writes=("ident32",),
            name="ident32")
        em.add("vector", lambda e: e.memset(ones1f[:], 1.0),
               writes=("ones1f",), name="ones1f")
        kfi = kf[:].bitcast(I32)
        em.add("gpsimd", lambda e: e.iota(kfi, [[1, HALF]], base=1,
                                          channel_multiplier=0),
               writes=("kf",), name="dpow_iota")
        em.add("scalar", lambda e: e.activation(
            out=dpow[:], in_=kfi, func=AF.Exp, scale=float(np.log(0.95))),
            reads=("kf",), writes=("dpow",), name="dpow_exp")
        em.add("vector", lambda e: e.memset(dec[:], 0.95), writes=("dec",),
               name="dec")
        em.add("vector", lambda e: e.memset(zero1[:], 0.0), writes=("zero1",),
               name="zero1")
        em.dma("sync", lambda e: e.dma_start(out=flag_s[:], in_=dr["flag8"][:]),
               writes=("flag_s",), name="flagld")
        em.add("tensor", lambda e: e.matmul(
            cx.psB[:, 0:8], ones1f[:, :], flag_s[:, 0:8], start=True, stop=True),
            reads=("ones1f", "flag_s"), writes=("psB",), name="fbc_mm")
        fbc = lnstage[:, 120:128]
        em.add("vector", lambda e: e.tensor_copy(out=fbc, in_=cx.psB[:, 0:8]),
               reads=("psB",), writes=("fbc",), name="fbc_ev")

        def load_ln_cols(sl):
            for (dram, col0) in ((dr["l2_lnw"], 0), (dr["l2_lnb"], 8)):
                em.add("vector", lambda e: e.memset(lnstage[:, 0:120], 0.0),
                       writes=("lnstage",), name="lnz")
                em.dma("sync", lambda e, dram=dram, sl=sl: e.dma_start(
                    out=lnstage[0:8, 0:128], in_=dram[sl]),
                    writes=("lnstage",), name="lnld")
                em.add("tensor", lambda e: e.transpose(
                    cx.psB[:, 0:128], lnstage[:, 0:128], ident32[:]),
                    reads=("lnstage", "ident32"), writes=("psB",), name="lntr")
                dst = lnv[:, 16 * sl + col0:16 * sl + col0 + 8]
                em.add("vector", lambda e, dst=dst: e.tensor_copy(
                    out=dst, in_=cx.psB[:, 0:8]),
                    reads=("psB",), writes=("lnv",), name="lnev")

        def sublayer(sl, src_fm, src_key, dst_fm, dst_keys):
            dst_key = dst_keys[0]
            load_ln_cols(sl)
            if sl == 0:
                layernorm_tm(cx, lambda i: tmb[:, 1024 * i:1024 * (i + 1)],
                             lambda i: tmb[:, 1024 * i:1024 * (i + 1)],
                             NT, "xn_tm", "ztm", tag=f"s{sl}ln")
            else:
                em.fence()
                transpose_fm_to_tm(cx, src_fm, tmb, src_key, "ztm", NT, NF,
                                   tag=f"s{sl}a")
                layernorm_tm(cx, lambda i: tmb[:, 1024 * i:1024 * (i + 1)],
                             lambda i: tmb[:, 1024 * i:1024 * (i + 1)],
                             NT, "ztm", "ztm", tag=f"s{sl}ln")
            transpose_tm_to_fm(cx, tmb, zfm, "ztm", "zfm", NT, NF,
                               tag=f"s{sl}b")
            for f in range(NF):
                wcol = lnv[:, 16 * sl + f:16 * sl + f + 1]
                bcol = lnv[:, 16 * sl + 8 + f:16 * sl + 8 + f + 1]
                em.add("vector", lambda e, f=f, wcol=wcol, bcol=bcol:
                       e.tensor_scalar(
                           out=zfm[:, HALF * f:HALF * (f + 1)],
                           in0=zfm[:, HALF * f:HALF * (f + 1)],
                           scalar1=wcol, scalar2=bcol, op0=ALU.mult,
                           op1=ALU.add),
                       reads=("zfm", "lnv"), writes=("zfm",), name=f"s{sl}af{f}")
            em.fence()
            y0 = tmb
            for grp in range(NF // FG):
                for j in range(FG):
                    f = FG * grp + j
                    for nm, buf, bkey in (("wk", wkb, "wkb"), ("wv", wvb, "wvb"),
                                          ("wr", wrb, "wrb")):
                        em.dma("gpsimd",
                               lambda e, nm=nm, f=f, buf=buf:
                               e.dma_start(
                                   out=buf[:, 0:1024],
                                   in_=dr[f"l2_{nm}"][sl, f]),
                               writes=(bkey,), name=f"s{sl}w{nm}{f}")

                    def proj(buf, bkey, f=f):
                        for k in range(NF):
                            for c in range(4):
                                em.add("tensor",
                                       lambda e, buf=buf, k=k, c=c:
                                       e.matmul(
                                           cx.psum[c][:, 0:512],
                                           buf[:, 128 * k:128 * (k + 1)],
                                           zfm[:, HALF * k + 512 * c:
                                               HALF * k + 512 * (c + 1)],
                                           start=(k == 0), stop=(k == NF - 1)),
                                       reads=(bkey, "zfm"),
                                       writes=(f"ps{c}",),
                                       name=f"s{sl}mm{f}_{c}")

                    proj(wkb, "wkb")
                    for c in range(4):
                        em.add("vector", lambda e, c=c: e.tensor_copy(
                            out=kf[:, 512 * c:512 * (c + 1)],
                            in_=cx.psum[c][:, 0:512]),
                            reads=(f"ps{c}",), writes=("kf",),
                            name=f"s{sl}ke{f}_{c}")
                    proj(wvb, "wvb")
                    for c in range(4):
                        em.add("vector", lambda e, c=c: e.scalar_tensor_tensor(
                            out=kvf[:, 512 * c:512 * (c + 1)],
                            in0=cx.psum[c][:, 0:512], scalar=0.05,
                            in1=kf[:, 512 * c:512 * (c + 1)],
                            op0=ALU.mult, op1=ALU.mult),
                            reads=(f"ps{c}", "kf"), writes=("kvf",),
                            name=f"s{sl}kv{f}_{c}")
                    proj(wrb, "wrb")
                    for c in range(4):
                        em.add("scalar", lambda e, c=c: e.activation(
                            out=rf[:, 512 * c:512 * (c + 1)],
                            in_=cx.psum[c][:, 0:512], func=AF.Sigmoid),
                            reads=(f"ps{c}",), writes=("rf",),
                            name=f"s{sl}r{f}_{c}")
                    em.add("vector", lambda e: e.tensor_tensor_scan(
                        out=stf[:], data0=dec[:], data1=kvf[:],
                        initial=zero1[:], op0=ALU.mult, op1=ALU.add),
                        reads=("dec", "kvf", "zero1"), writes=("stf",),
                        name=f"s{sl}scan{f}")
                    em.add("vector", lambda e, j=j: e.tensor_copy(
                        out=carry[:, j:j + 1], in_=stf[:, HALF - 1:HALF]),
                        reads=("stf",), writes=("carry",), name=f"s{sl}cy{f}")
                    em.add("vector", lambda e, f=f: e.tensor_mul(
                        y0[:, HALF * f:HALF * (f + 1)], rf[:], stf[:]),
                        reads=("rf", "stf"), writes=("y0",), name=f"s{sl}y0{f}")
                    em.add("vector", lambda e, j=j: e.tensor_mul(
                        rd[:, HALF * j:HALF * (j + 1)], rf[:], dpow[:]),
                        reads=("rf", "dpow"), writes=("rd",), name=f"s{sl}rd{f}")
                em.dma("gpsimd", lambda e: e.dma_start(
                    out=dr["l2_cc_in"][:, :], in_=carry[:, :]),
                    reads=("carry",), writes=("cc_in",), name=f"s{sl}g{grp}ccst")
                em.cc(lambda e: e.collective_compute(
                    "AllGather", ALU.bypass,
                    replica_groups=[[0, 1], [2, 3], [4, 5], [6, 7]],
                    ins=[dr["l2_cc_in"].opt()],
                    outs=[dr["l2_cc_out"].opt()]),
                    reads=("cc_in",), writes=("cc_out",), name=f"s{sl}g{grp}cc")
                em.dma("gpsimd", lambda e: e.dma_start(
                    out=carry_in[:, :], in_=dr["l2_cc_out"][0:128, :]),
                    reads=("cc_out",), writes=("carry_in",),
                    name=f"s{sl}g{grp}ccld")
                em.add("vector", lambda e: e.tensor_mul(
                    carry_in[:, :], carry_in[:, :], fbc[:, 0:4]),
                    reads=("fbc", "carry_in"), writes=("carry_in",),
                    name=f"s{sl}g{grp}ce")
                for j in range(FG):
                    f = FG * grp + j
                    em.add("vector", lambda e, f=f, j=j: e.scalar_tensor_tensor(
                        out=y0[:, HALF * f:HALF * (f + 1)],
                        in0=rd[:, HALF * j:HALF * (j + 1)],
                        scalar=carry_in[:, j:j + 1],
                        in1=y0[:, HALF * f:HALF * (f + 1)],
                        op0=ALU.mult, op1=ALU.add),
                        reads=("rd", "carry_in", "y0"), writes=("y0",),
                        name=f"s{sl}fix{f}")
            for f in range(NF):
                em.dma("gpsimd", lambda e, f=f: e.dma_start(
                    out=wob[:, 0:1024],
                    in_=dr["l2_wo"][sl, f]),
                    writes=("wob",), name=f"s{sl}wo{f}")
                for k in range(NF):
                    for c in range(4):
                        em.add("tensor", lambda e, f=f, k=k, c=c:
                               e.matmul(cx.psum[c][:, 0:512],
                                        wob[:, 128 * k:128 * (k + 1)],
                                        y0[:, HALF * k + 512 * c:
                                           HALF * k + 512 * (c + 1)],
                                        start=(k == 0), stop=(k == NF - 1)),
                               reads=("wob", "y0"), writes=(f"ps{c}",),
                               name=f"s{sl}womm{f}_{c}")
                for c in range(4):
                    em.add("vector", lambda e, f=f, c=c: e.scalar_tensor_tensor(
                        out=dst_fm[:, HALF * f + 512 * c:
                                   HALF * f + 512 * (c + 1)],
                        in0=cx.psum[c][:, 0:512], scalar=1.0,
                        in1=src_fm[:, HALF * f + 512 * c:
                                   HALF * f + 512 * (c + 1)],
                        op0=ALU.mult, op1=ALU.add),
                        reads=(f"ps{c}", src_key), writes=dst_keys,
                        name=f"s{sl}oev{f}_{c}")

        sublayer(0, cx.xn_fm, "xn_fm", out1, ("out1",))
        sublayer(1, out1, "out1", zfm, ("zfm_s", "zfm"))
        em.fence()
        transpose_fm_to_tm(cx, zfm, tmb, "zfm_s", "s_tm", NT, NF, tag="str")
        cx.s_tm, cx.s_key = tmb, "s_tm"
    em.fence()

    with _ES() as scr:
        sc = lambda n, s, d: scr.enter_context(nc.sbuf_tensor(n, s, d))
        wi32 = sc("l2_wi32", [128, 2 * 1024], I32)
        wgb = sc("l2_wgb", [128, 2 * 1024], BF16)
        wub = sc("l2_wub", [128, 2 * 1024], BF16)
        wdb = sc("l2_wdb", [128, 2 * 1024], BF16)
        hrnd = sc("l2_hrnd", [128, 24 * 256], BF16)
        sgb = sc("l2_sgb", [128, 256], BF16)
        for nm_src, nm_dst, wkey in (("l2f_wg", "wg16c", "wgq"),
                                     ("l2f_wu", "wu16c", "wuq"),
                                     ("l2f_wd", "wd16c", "wdq")):
            for f in range(24):
                half = f % 2
                em.dma("sync", lambda e, f=f, half=half, nm_src=nm_src:
                       e.dma_start(out=wi32[:, 1024 * half:1024 * (half + 1)],
                                   in_=dr[nm_src][f]),
                       writes=(f"wi32{half}",), name=f"dq_ld{nm_src}{f}")
                em.add("vector", lambda e, f=f, half=half: e.tensor_scalar_mul(
                    wgb[:, 1024 * half:1024 * (half + 1)],
                    wi32[:, 1024 * half:1024 * (half + 1)], 1.0 / 127.0),
                    reads=(f"wi32{half}",), writes=(f"wgb{half}",),
                    name=f"dq_ts{nm_src}{f}")
                em.dma("sync", lambda e, f=f, half=half, nm_dst=nm_dst:
                       e.dma_start(out=dr[nm_dst][f],
                                   in_=wgb[:, 1024 * half:1024 * (half + 1)]),
                       reads=(f"wgb{half}",), writes=(wkey,),
                       name=f"dq_st{nm_src}{f}")
        em.fence()
        for r in range(8):
            t0 = 256 * r
            for f in range(24):
                half = f % 2
                em.dma("sync", lambda e, f=f, half=half: e.dma_start(
                    out=wgb[:, 1024 * half:1024 * (half + 1)],
                    in_=dr["wg16c"][f]),
                    reads=("wgq",), writes=(f"wgb{half}",), name=f"ff_g{r}_{f}")
                em.dma("gpsimd", lambda e, f=f, half=half: e.dma_start(
                    out=wub[:, 1024 * half:1024 * (half + 1)],
                    in_=dr["wu16c"][f]),
                    reads=("wuq",), writes=(f"wub{half}",), name=f"ff_u{r}_{f}")
                em.dma("scalar", lambda e, f=f, half=half: e.dma_start(
                    out=wdb[:, 1024 * half:1024 * (half + 1)],
                    in_=dr["wd16c"][f]),
                    reads=("wdq",), writes=(f"wdb{half}",), name=f"ff_d{r}_{f}")
                for k in range(NF):
                    em.add("tensor", lambda e, f=f, k=k, half=half, t0=t0:
                           e.matmul(cx.psB[:, 0:256],
                                    wgb[:, 1024 * half + 128 * k:
                                        1024 * half + 128 * (k + 1)],
                                    cx.xn_fm[:, HALF * k + t0:
                                             HALF * k + t0 + 256],
                                    start=(k == 0), stop=(k == NF - 1)),
                           reads=(f"wgb{half}", "xn_fm"), writes=("psB",),
                           name=f"ff_gmm{r}_{f}_{k}")
                em.add("scalar", lambda e: e.activation(
                    out=sgb[:], in_=cx.psB[:, 0:256], func=AF.Silu),
                    reads=("psB",), writes=("sgb",), name=f"ff_sg{r}_{f}")
                for k in range(NF):
                    em.add("tensor", lambda e, f=f, k=k, half=half, t0=t0:
                           e.matmul(psC[:, 0:256],
                                    wub[:, 1024 * half + 128 * k:
                                        1024 * half + 128 * (k + 1)],
                                    cx.xn_fm[:, HALF * k + t0:
                                             HALF * k + t0 + 256],
                                    start=(k == 0), stop=(k == NF - 1)),
                           reads=(f"wub{half}", "xn_fm"), writes=("psC",),
                           name=f"ff_umm{r}_{f}_{k}")
                em.add("vector", lambda e, f=f: e.scalar_tensor_tensor(
                    out=hrnd[:, 256 * f:256 * (f + 1)], in0=psC[:, 0:256],
                    scalar=1.0, in1=sgb[:], op0=ALU.mult, op1=ALU.mult),
                    reads=("psC", "sgb"), writes=("hrnd",), name=f"ff_h{r}_{f}")
                for t2 in range(2):
                    for c in range(2):
                        em.add("tensor", lambda e, f=f, t2=t2, c=c, half=half:
                               e.matmul(cx.psum[2 * t2 + c][:, 0:512],
                                        hrnd[:, 256 * f + 128 * t2:
                                             256 * f + 128 * (t2 + 1)],
                                        wdb[:, 1024 * half + 512 * c:
                                            1024 * half + 512 * (c + 1)],
                                        start=(f == 0), stop=(f == 23)),
                               reads=("hrnd", f"wdb{half}"),
                               writes=(f"ps{2 * t2 + c}",),
                               name=f"ff_dmm{r}_{f}_{t2}_{c}")
            for t2 in range(2):
                for c in range(2):
                    dst = g_tm[:, 1024 * (2 * r + t2) + 512 * c:
                               1024 * (2 * r + t2) + 512 * (c + 1)]
                    em.add("scalar", lambda e, dst=dst, t2=t2, c=c:
                           e.activation(out=dst,
                                        in_=cx.psum[2 * t2 + c][:, 0:512],
                                        func=AF.Silu),
                           reads=(f"ps{2 * t2 + c}",), writes=("g_tm",),
                           name=f"ff_gout{r}_{t2}_{c}")
    em.fence()
    cx.g_tm, cx.g_key = g_tm, "g_tm"




def _pt_dst(pT, lo, hi, j):
    return pT[:, :].rearrange("p (t q) -> p t q", q=512)[:, lo:hi,
                                                        128 * j:128 * (j + 1)]


def emit_gm(cx, src, src_key, dst, dst_key, n_tiles, tag):
    em = cx.em
    st = cx.stat
    mx = st[:, 0:8]
    for i in range(n_tiles):
        sl = slice(64 * i, 64 * (i + 1))
        g3 = src[:, sl].rearrange("p (h e) -> p h e", e=8)
        em.add("vector", lambda e, g3=g3: e.tensor_reduce(
            out=mx, in_=g3, axis=AX.X, op=ALU.max),
            reads=(src_key,), writes=("stat",), name=f"{tag}mx{i}")
        em.add("vector", lambda e: e.tensor_copy(
            out=cx.gmsc[:, 0:64],
            in_=mx.unsqueeze(-1).to_broadcast([128, 8, 8])),
            reads=("stat",), writes=("gmsc",), name=f"{tag}bc{i}")
        em.add("vector", lambda e, sl=sl: e.tensor_tensor(
            out=cx.gmsc[:, 64:128], in0=src[:, sl], in1=cx.gmsc[:, 0:64],
            op=ALU.is_equal),
            reads=(src_key, "gmsc"), writes=("gmsc",), name=f"{tag}eq{i}")
        em.add("vector", lambda e: e.tensor_tensor_scan(
            out=cx.gmsc[:, 128:192], data0=cx.gmpat[:, 0:64],
            data1=cx.gmsc[:, 64:128], initial=cx.gmzero[:, 0:1],
            op0=ALU.mult, op1=ALU.add),
            reads=("gmsc", "gmpat", "gmzero"), writes=("gmsc",),
            name=f"{tag}cs{i}")
        em.add("vector", lambda e: e.tensor_scalar(
            out=cx.gmsc[:, 128:192], in0=cx.gmsc[:, 128:192], scalar1=1.0,
            scalar2=None, op0=ALU.is_equal),
            reads=("gmsc",), writes=("gmsc",), name=f"{tag}f1{i}")
        em.add("vector", lambda e: e.tensor_mul(
            cx.gmsc[:, 64:128], cx.gmsc[:, 64:128], cx.gmsc[:, 128:192]),
            reads=("gmsc",), writes=("gmsc",), name=f"{tag}f2{i}")
        em.add("vector", lambda e, sl=sl: e.tensor_mul(
            dst[:, sl], cx.gmsc[:, 64:128], src[:, sl]),
            reads=("gmsc", src_key), writes=(dst_key,), name=f"{tag}gm{i}")


def alloc_gm_consts(cx, ctx, full=True, tag=""):
    nc, em = cx.nc, cx.em
    if full:
        cx.gmsc = ctx.enter_context(
            nc.sbuf_tensor(f"gmsc{tag}", [128, 192], F32))
    gw = 64 if full else 8
    cx.gmpat = ctx.enter_context(
        nc.sbuf_tensor(f"gmpat{tag}", [128, gw], F32))
    cx.gmzero = ctx.enter_context(
        nc.sbuf_tensor(f"gmzero{tag}", [128, 1], F32))
    gp, gz = cx.gmpat, cx.gmzero
    gmi = gp[:, :].bitcast(I32)
    em.add("gpsimd", lambda e, gmi=gmi, gw=gw: e.iota(
        gmi, [[0, gw // 8], [1, 8]], base=0, channel_multiplier=0),
        writes=("gmpat",), name=f"gmpi{tag}")
    em.add("vector", lambda e, gp=gp, gmi=gmi: e.tensor_scalar(
        out=gp[:, :], in0=gmi, scalar1=0.0, scalar2=None,
        op0=ALU.is_gt), reads=("gmpat",), writes=("gmpat",),
        name=f"gmpg{tag}")
    em.add("vector", lambda e, gz=gz: e.memset(gz[:], 0.0),
           writes=("gmzero",), name=f"gmz{tag}")



RMS_EPS = float(np.finfo(np.float32).eps)


def prep_l3_weights(inputs, h):
    w = {}
    nw = np.asarray(inputs["L3_attn_norm_w"])
    w["l3a_nw"] = np.ascontiguousarray(nw.reshape(8, 128))
    w["l3a_wq"] = fmpack(np.asarray(inputs["L3_attn_Wq"]))
    w["l3a_wkv"] = fmpack(np.asarray(inputs["L3_attn_Wkv"]))
    w["l3a_wout"] = slabpack(np.asarray(inputs["L3_attn_Wout"]))
    wg = np.zeros((1024, 128), np.float32)
    wg[:, :16] = np.asarray(inputs["L3_attn_Wg"])
    w["l3a_wg"] = fmpack(wg)
    w["l3f_wq"] = fmpack(np.asarray(inputs["L3_ffn_w_q"]))
    w["l3f_wk"] = fmpack(np.asarray(inputs["L3_ffn_w_k"]))
    wgf = np.zeros((1024, 128), np.float32)
    wgf[:, :64] = np.asarray(inputs["L3_ffn_w_gate"])
    w["l3f_wg"] = fmpack(wgf)
    ve = np.asarray(inputs["L3_ffn_value_experts"])
    w["l3f_ve"] = np.stack([fmpack(np.ascontiguousarray(
        ve[e].transpose(1, 0, 2).reshape(1024, 512))) for e in range(8)])
    oe = np.asarray(inputs["L3_ffn_output_experts"])
    w["l3f_oe"] = np.stack([slabpack(np.ascontiguousarray(
        oe[e].reshape(512, 1024))) for e in range(8)])
    w["flag8"] = np.full((1, 8), float(h), np.float32)
    qidx = np.empty((128, NT), np.uint32)
    sgidx = np.empty((128, NT), np.uint32)
    p = np.arange(128, dtype=np.uint32)
    for i in range(NT):
        qidx[:, i] = 128 * (2 * i + h) + p
        g = 16 * h + i
        sgidx[:, i] = 2048 * (g % 2) + 128 * ((g - g % 2) // 2) + p
    w["l3_qidx"] = qidx
    w["l3_sgidx"] = sgidx
    return w


def declare_l3_dram(nc):
    d = {}
    d["l3a_nw"] = nc.dram_tensor("l3a_nw", [8, 128], F32, kind="ExternalInput").ap()
    d["l3a_wq"] = nc.dram_tensor("l3a_wq", [8, 128, 1024], F32, kind="ExternalInput").ap()
    d["l3a_wkv"] = nc.dram_tensor("l3a_wkv", [16, 128, 1024], F32, kind="ExternalInput").ap()
    d["l3a_wout"] = nc.dram_tensor("l3a_wout", [8, 128, 1024], F32, kind="ExternalInput").ap()
    d["l3a_wg"] = nc.dram_tensor("l3a_wg", [1, 128, 1024], F32, kind="ExternalInput").ap()
    d["l3f_wq"] = nc.dram_tensor("l3f_wq", [4, 128, 1024], F32, kind="ExternalInput").ap()
    d["l3f_wk"] = nc.dram_tensor("l3f_wk", [4, 128, 1024], F32, kind="ExternalInput").ap()
    d["l3f_wg"] = nc.dram_tensor("l3f_wg", [1, 128, 1024], F32, kind="ExternalInput").ap()
    d["l3f_ve"] = nc.dram_tensor("l3f_ve", [8, 4, 128, 1024], F32, kind="ExternalInput").ap()
    d["l3f_oe"] = nc.dram_tensor("l3f_oe", [8, 4, 128, 1024], F32, kind="ExternalInput").ap()
    d["flag8"] = _flag8(nc)
    d["l3_xq_in"] = nc.dram_tensor("l3_xq_in", [2048, 1024], BF16).ap()
    d["l3_xq_out"] = nc.dram_tensor("l3_xq_out", [4096, 1024], BF16).ap()
    d["l3_qidx"] = nc.dram_tensor("l3_qidx", [128, NT], U32, kind="ExternalInput").ap()
    d["l3_sgidx"] = nc.dram_tensor("l3_sgidx", [128, NT], U32, kind="ExternalInput").ap()
    d["l3_kv_in"] = nc.dram_tensor("l3_kv_in", [2048, 2048], BF16).ap()
    d["l3_kv_out"] = nc.dram_tensor("l3_kv_out", [4096, 2048], BF16).ap()
    d["l3f_kv_in"] = nc.dram_tensor("l3f_kv_in", [1024, 2048], BF16).ap()
    d["l3f_kv_out"] = nc.dram_tensor("l3f_kv_out", [2048, 2048], BF16).ap()
    d["l3_sg_in"] = nc.dram_tensor("l3_sg_in", [2048, 1024], F32).ap()
    d["l3_sg_out"] = nc.dram_tensor("l3_sg_out", [4096, 1024], F32).ap()
    return d


def _cc_pair(em, kind, in_ap, out_ap, reads, writes, name):
    def f(e):
        from concourse.aot_env import allow_ici_all_to_all
        with allow_ici_all_to_all():
            return e.collective_compute(
                kind, ALU.bypass,
                replica_groups=[[0, 1], [2, 3], [4, 5], [6, 7]],
                ins=[in_ap.opt()], outs=[out_ap.opt()])
    em.cc(f, reads=reads, writes=writes, name=name)


def rms_stats(cx, src_tile, n_tiles, src_key, tag):
    em = cx.em
    st = cx.stat
    k = n_tiles
    sq = st[:, k:2 * k]
    var = st[:, 2 * k:3 * k]
    rstd = st[:, 3 * k:4 * k]
    tmp = st[:, 4 * k:5 * k]
    for i in range(n_tiles):
        em.add("scalar", lambda e, i=i: e.activation(
            out=cx.sq_junk[:, 0:1024], in_=src_tile(i), func=AF.Square,
            accum_out=sq[:, i:i + 1]),
            reads=(src_key,), writes=("stat", "sq_junk"), name=f"{tag}_sq{i}")
    s = ("stat",)
    em.add("vector", lambda e: e.tensor_scalar(
        out=var, in0=sq, scalar1=1.0 / D, scalar2=RMS_EPS,
        op0=ALU.mult, op1=ALU.add), reads=s, writes=s, name=f"{tag}_v")
    em.add("scalar", lambda e: e.activation(out=rstd, in_=var, func=AF.Sqrt),
           reads=s, writes=s, name=f"{tag}_sq")
    em.add("vector", lambda e: e.reciprocal(out=tmp, in_=rstd),
           reads=s, writes=s, name=f"{tag}_r1")
    em.add("vector", lambda e: e.tensor_mul(tmp, tmp, var),
           reads=s, writes=s, name=f"{tag}_vs")
    em.add("vector", lambda e: e.tensor_add(rstd, rstd, tmp),
           reads=s, writes=s, name=f"{tag}_ad")
    em.add("vector", lambda e: e.tensor_scalar_mul(rstd, rstd, 0.5),
           reads=s, writes=s, name=f"{tag}_hf")
    em.add("vector", lambda e: e.reciprocal(out=rstd, in_=rstd),
           reads=s, writes=s, name=f"{tag}_rs")
    return rstd


def proj_p2(cx, wdram, src_fm, src_key, wbuf, wkeys, n_ft, evac, tag,
            ntok=HALF, nkt=NF, chunks=4, wwidth=1024):
    em = cx.em
    for f in range(n_ft):
        wb = wbuf[f % 2]
        wk = wkeys[f % 2]
        em.dma("gpsimd", lambda e, f=f, wb=wb: e.dma_start(
            out=wb[:, 0:wwidth], in_=wdram[f]), writes=(wk,), name=f"{tag}w{f}")
        for k in range(nkt):
            for c in range(chunks):
                em.add("tensor", lambda e, wb=wb, k=k, c=c: e.matmul(
                    cx.psum[c][:, 0:512], wb[:, 128 * k:128 * (k + 1)],
                    src_fm[:, ntok * k + 512 * c:ntok * k + 512 * (c + 1)],
                    start=(k == 0), stop=(k == nkt - 1)),
                    reads=(wk, src_key), writes=(f"ps{c}",),
                    name=f"{tag}mm{f}_{k}_{c}")
        evac(f)


def dump_fm(cx, name, buf, nf, key, width=HALF):
    nc, em = cx.nc, cx.em
    d = nc.dram_tensor(name, [nf, 128, width], F32, kind="ExternalOutput").ap()
    for f in range(nf):
        for c in range((width + 511) // 512):
            w = min(512, width - 512 * c)
            em.add("vector", lambda e, f=f, c=c, w=w: e.tensor_copy(
                out=cx.dmpbuf[:, 0:w], in_=buf[:, width * f + 512 * c:
                                               width * f + 512 * c + w]),
                reads=(key,), writes=("dmpbuf", "sgbuf"), name=f"dmp{name}{f}_{c}")
            em.out_dma("sync", lambda e, f=f, c=c, w=w: e.dma_start(
                out=d[f, :, 512 * c:512 * c + w], in_=cx.dmpbuf[:, 0:w]),
                reads=("dmpbuf",), name=f"dmps{name}{f}_{c}")


def build_l3(cx, ctx, dr):
    nc, em = cx.nc, cx.em
    sb = lambda n, s, d: ctx.enter_context(nc.sbuf_tensor(n, s, d))
    em.fence()
    SCALE = 0.125

    B3 = sb("l3_B3", [128, NF * HALF], BF16)
    B4 = sb("l3_B4", [128, NF * HALF], BF16)
    B5 = sb("l3_B5", [128, NF * HALF], BF16)
    q_all = B5
    qf_all = B4[:, 0:4 * HALF]
    ao = B3
    aof = B4[:, 4 * HALF:8 * HALF]
    gat_a = sb("l3_gata", [128, NT * 16], BF16)
    cx.l3idx = sb("l3_idx", [128, 2 * NT], U32)
    gm_q = sb("l3_gmq", [128, NT * 64], BF16)
    gmT_q = sb("l3_gmTq", [64, HALF], BF16)
    gmT_own = sb("l3_gmTo", [64, HALF], BF16)
    s_j = q_all
    stg = sb("l3_stg", [128, 2048], BF16)
    wb0 = sb("l3_wb0", [128, 1024], BF16)
    wb1 = sb("l3_wb1", [128, 1024], BF16)
    rloc = sb("l3_rloc", [128, 4], F32)
    colg = sb("l3_colg", [128, 512], I32)
    masks = sb("l3_masks", [128, 2 * 512], BF16)
    flag_s = sb("l3_flag", [1, 8], F32)
    ones1f2 = sb("l3_ones1f", [1, 128], F32)
    nwcol = sb("l3_nw", [128, 8], F32)
    lnstage = cx.xbuf[:, 0:128]
    ident32 = cx.xbuf[:, 128:256]

    tmb = cx.xn_tm
    if DEBUG_L3:
        cx.dmpbuf = cx.sgbuf[:, 0:512]

    em.add("vector", lambda e: e.tensor_scalar(
        out=ident32[:], in0=cx.ident_i[:], scalar1=0.0, scalar2=None,
        op0=ALU.is_equal), reads=("ident_i",), writes=("xbufc",),
        name="l3id32")
    em.add("vector", lambda e: e.memset(ones1f2[:], 1.0), writes=("l3ones",),
           name="l3ones")
    em.dma("sync", lambda e: e.dma_start(out=flag_s[:], in_=dr["flag8"][:]),
           writes=("l3flag",), name="l3flagld")
    em.add("tensor", lambda e: e.matmul(
        cx.psB[:, 0:8], ones1f2[:, :], flag_s[:, 0:8], start=True, stop=True),
        reads=("l3ones", "l3flag"), writes=("psB",), name="l3fb")
    em.add("vector", lambda e: e.tensor_scalar_mul(
        rloc[:, 3:4], cx.psB[:, 0:1], 128.0),
        reads=("psB",), writes=("l3rloc",), name="l3hb")
    iotacol = rloc[:, 2:3].bitcast(I32)
    em.add("gpsimd", lambda e: e.iota(iotacol, [[1, 1]], base=0,
                                      channel_multiplier=1),
           writes=("l3rloc",), name="l3iotap")
    em.add("vector", lambda e: e.scalar_tensor_tensor(
        out=rloc[:, 0:1], in0=iotacol, scalar=1.0, in1=rloc[:, 3:4],
        op0=ALU.mult, op1=ALU.add),
        reads=("l3rloc",), writes=("l3rloc",), name="l3rl0")
    em.add("vector", lambda e: e.tensor_scalar_add(
        rloc[:, 1:2], rloc[:, 0:1], 256.0),
        reads=("l3rloc",), writes=("l3rloc",), name="l3rl1")
    em.add("gpsimd", lambda e: e.iota(colg[:], [[1, 512]], base=0,
                                      channel_multiplier=0),
           writes=("l3colg",), name="l3colg")
    for par in range(2):
        em.add("vector", lambda e, par=par: e.tensor_scalar(
            out=masks[:, 512 * par:512 * (par + 1)], in0=colg[:],
            scalar1=rloc[:, par:par + 1], scalar2=None, op0=ALU.is_le),
            reads=("l3colg", "l3rloc"), writes=("l3masks",), name=f"l3m{par}")
    em.add("vector", lambda e: e.memset(lnstage[:], 0.0), writes=("xbufc",),
           name="l3lz")
    em.dma("sync", lambda e: e.dma_start(out=lnstage[0:8, 0:128],
                                         in_=dr["l3a_nw"][:]),
           writes=("xbufc",), name="l3lnld")
    em.add("tensor", lambda e: e.transpose(
        cx.psB[:, 0:128], lnstage[:], ident32[:]),
        reads=("xbufc",), writes=("psB",), name="l3lntr")
    em.add("vector", lambda e: e.tensor_copy(out=nwcol[:], in_=cx.psB[:, 0:8]),
           reads=("psB",), writes=("l3nw",), name="l3lnev")

    seli = cx.xbuf[:, :].bitcast(I32)[0:64, 0:2048]

    def build_sel(selb_ap, tag):
        for half in range(2):
            em.add("gpsimd", lambda e, half=half: e.iota(
                seli, [[1, 4], [16, 4], [8, 2], [0, 64]], base=4 * half,
                channel_multiplier=-1),
                writes=("xbufc",), name=f"{tag}i{half}")
            em.add("vector", lambda e, half=half, selb_ap=selb_ap:
                   e.tensor_scalar(
                       out=selb_ap[:, 2048 * half:2048 * (half + 1)],
                       in0=seli, scalar1=0.0, scalar2=None, op0=ALU.is_equal),
                   reads=("xbufc",), writes=("l3selb",), name=f"{tag}e{half}")
    alloc_gm_consts(cx, ctx, tag="l3")
    selb = sb("l3_selb", [64, 32 * 128], BF16)
    build_sel(selb, "l3sel")

    xa = dr["l3_xq_in"].rearrange("(i p) d -> i p d", p=128)
    for i in range(NT):
        em.dma("sync", lambda e, i=i: e.dma_start(
            out=xa[i], in_=tmb[:, 1024 * i:1024 * (i + 1)]),
            reads=("xn_tm",), writes=("l3xqin",), name=f"l3xqs{i}")
    _cc_pair(em, "AllGather", dr["l3_xq_in"], dr["l3_xq_out"],
             ("l3xqin",), ("l3xqout",), "l3xqcc")
    em.dma("sync", lambda e: e.dma_start(out=cx.l3idx[:, 0:NT],
                                         in_=dr["l3_qidx"][:]),
           writes=("l3idx",), name="l3qidxld")
    em.dma("sync", lambda e: e.dma_start(out=cx.l3idx[:, NT:2 * NT],
                                         in_=dr["l3_sgidx"][:]),
           writes=("l3idx",), name="l3sgidxld")

    rstd = rms_stats(cx, lambda i: tmb[:, 1024 * i:1024 * (i + 1)], NT,
                     "xn_tm", "l3rmso")
    for i in range(NT):
        em.add("vector", lambda e, i=i: e.tensor_scalar(
            out=tmb[:, 1024 * i:1024 * (i + 1)],
            in0=tmb[:, 1024 * i:1024 * (i + 1)],
            scalar1=rstd[:, i:i + 1], scalar2=None, op0=ALU.mult),
            reads=("stat",), writes=("x3a_tm", "xn_tm"), name=f"l3nrm{i}")
    with ExitStack() as scr:
        x3afm = B3
        transpose_tm_to_fm(cx, tmb, x3afm, "x3a_tm", "x3afm", NT, NF, tag="l3tr")
        for f in range(NF):
            em.add("vector", lambda e, f=f: e.tensor_scalar(
                out=x3afm[:, HALF * f:HALF * (f + 1)],
                in0=x3afm[:, HALF * f:HALF * (f + 1)],
                scalar1=nwcol[:, f:f + 1], scalar2=None, op0=ALU.mult),
                reads=("x3afm", "l3nw"), writes=("x3afm",), name=f"l3nwm{f}")
        kvr = dr["l3_kv_in"].rearrange("(f p) t -> f p t", p=128)

        def ev_kv(f):
            for c in range(4):
                em.add("vector", lambda e, f=f, c=c: e.tensor_copy(
                    out=stg[:, 512 * c:512 * (c + 1)],
                    in_=cx.psum[c][:, 0:512]),
                    reads=(f"ps{c}",), writes=("l3stg",), name=f"l3kve{f}_{c}")
            em.dma("sync", lambda e, f=f: e.dma_start(
                out=kvr[f], in_=stg[:]),
                reads=("l3stg",), writes=("l3kvin",), name=f"l3kvs{f}")

        proj_p2(cx, dr["l3a_wkv"], x3afm, "x3afm", (wb0, wb1),
                ("l3wb0", "l3wb1"), 16, ev_kv, "l3kv")
        _cc_pair(em, "AllGather", dr["l3_kv_in"], dr["l3_kv_out"],
                 ("l3kvin",), ("l3kvout",), "l3kvcc")

        gown = gm_q
        em.dma("gpsimd", lambda e: e.dma_start(out=wb0[:], in_=dr["l3f_wg"][0]),
               writes=("l3wb0",), name="l3fgw")
        for i in range(NT):
            for k in range(NF):
                em.add("tensor", lambda e, i=i, k=k: e.matmul(
                    cx.psB[:, 0:128],
                    cx.xn_fm[:, HALF * k + 128 * i:HALF * k + 128 * (i + 1)],
                    wb0[:, 128 * k:128 * (k + 1)],
                    start=(k == 0), stop=(k == NF - 1)),
                    reads=("l3wb0", "xn_fm"), writes=("psB",),
                    name=f"l3fgo{i}_{k}")
            em.add("scalar", lambda e, i=i: e.activation(
                out=gown[:, 64 * i:64 * (i + 1)], in_=cx.psB[:, 0:64],
                func=AF.Sigmoid),
                reads=("psB",), writes=("l3gown",), name=f"l3fgoe{i}")
        gm_own = B4[:, 0:NT * 64]
        emit_gm(cx, gown, "l3gown", gm_own, "l3gmown", NT, tag="l3gmo")
        for i in range(NT):
            bank = cx.psT[i % 2]
            em.add("tensor", lambda e, i=i, bank=bank: e.transpose(
                bank[0:64, 0:128],
                gm_own[:, 64 * i:64 * (i + 1)], cx.ident[:]),
                reads=("l3gmown", "ident"), writes=(f"psT{i % 2}",),
                name=f"l3gmt{i}")
            em.add("vector", lambda e, i=i, bank=bank: e.tensor_copy(
                out=gmT_own[0:64, 128 * i:128 * (i + 1)], in_=bank[0:64, 0:128]),
                reads=(f"psT{i % 2}",), writes=("l3gmTo",), name=f"l3gmte{i}")
        vals = B4[:, 4 * HALF:8 * HALF]
        em.add("vector", lambda e: e.memset(vals[:], 0.0), writes=("l3vals",),
               name="l3vz")
        bcb = B4[:, 1024:1536]
        for e_i in range(8):
            vdr = dr["l3f_ve"][e_i]
            for ft in range(4):
                wb = (wb0, wb1)[ft % 2]
                wk = ("l3wb0", "l3wb1")[ft % 2]
                em.dma("gpsimd", lambda e, vdr=vdr, ft=ft, wb=wb: e.dma_start(
                    out=wb[:], in_=vdr[ft]), writes=(wk,), name=f"l3ve{e_i}_{ft}")
                for k in range(NF):
                    for c in range(4):
                        em.add("tensor", lambda e, wb=wb, k=k, c=c: e.matmul(
                            cx.psum[c][:, 0:512], wb[:, 128 * k:128 * (k + 1)],
                            cx.xn_fm[:, HALF * k + 512 * c:
                                     HALF * k + 512 * (c + 1)],
                            start=(k == 0), stop=(k == NF - 1)),
                            reads=(wk, "xn_fm"), writes=(f"ps{c}",),
                            name=f"l3xv{e_i}_{ft}_{k}_{c}")
                for c in range(4):
                    em.add("tensor", lambda e, e_i=e_i, ft=ft, c=c: e.matmul(
                        cx.psB[:, 0:512],
                        selb[:, 128 * (4 * e_i + ft):128 * (4 * e_i + ft + 1)],
                        gmT_own[0:64, 512 * c:512 * (c + 1)],
                        start=True, stop=True),
                        reads=("l3selb", "l3gmTo"), writes=("psB",),
                        name=f"l3bc{e_i}_{ft}_{c}")
                    em.add("vector", lambda e, c=c: e.tensor_copy(
                        out=bcb[:], in_=cx.psB[:, 0:512]),
                        reads=("psB",), writes=("l3bcb",), name=f"l3bce{e_i}_{ft}_{c}")
                    em.add("vector", lambda e, ft=ft, c=c: e.scalar_tensor_tensor(
                        out=stg[:, 0:512], in0=cx.psum[c][:, 0:512],
                        scalar=1.0, in1=bcb[:], op0=ALU.mult, op1=ALU.mult),
                        reads=(f"ps{c}", "l3bcb"), writes=("l3stg",),
                        name=f"l3msk{e_i}_{ft}_{c}")
                    em.add("vector", lambda e, ft=ft, c=c: e.tensor_add(
                        vals[:, HALF * ft + 512 * c:HALF * ft + 512 * (c + 1)],
                        vals[:, HALF * ft + 512 * c:HALF * ft + 512 * (c + 1)],
                        stg[:, 0:512]),
                        reads=("l3stg", "l3vals"), writes=("l3vals",),
                        name=f"l3va{e_i}_{ft}_{c}")
        if DEBUG_L3:
            dump_fm(cx, "dbg_vals", vals, 4, "l3vals")
            dump_fm(cx, "dbg_gmo", gm_own, 1, "l3gmown", width=NT * 64)
        kvf_r = dr["l3f_kv_in"].rearrange("(f p) t -> f p t", p=128)

        def ev_kf(f):
            for c in range(4):
                em.add("vector", lambda e, f=f, c=c: e.tensor_copy(
                    out=stg[:, 512 * c:512 * (c + 1)], in_=cx.psum[c][:, 0:512]),
                    reads=(f"ps{c}",), writes=("l3stg",), name=f"l3kfe{f}_{c}")
            em.dma("sync", lambda e, f=f: e.dma_start(
                out=kvf_r[f], in_=stg[:]),
                reads=("l3stg",), writes=("l3fkvin",), name=f"l3kfs{f}")

        proj_p2(cx, dr["l3f_wk"], cx.xn_fm, "xn_fm", (wb0, wb1),
                ("l3wb0", "l3wb1"), 4, ev_kf, "l3kf")
        for f in range(4):
            em.dma("sync", lambda e, f=f: e.dma_start(
                out=kvf_r[4 + f], in_=vals[:, HALF * f:HALF * (f + 1)]),
                reads=("l3vals",), writes=("l3fkvin",), name=f"l3vs{f}")
        _cc_pair(em, "AllGather", dr["l3f_kv_in"], dr["l3f_kv_out"],
                 ("l3fkvin",), ("l3fkvout",), "l3fkvcc")
    em.fence()

    from concourse.bass import IndirectOffsetOnAxis as _IOA
    for i in range(NT):
        em.dma("gpsimd", lambda e, i=i: e.indirect_dma_start(
            out=tmb[:, 1024 * i:1024 * (i + 1)], out_offset=None,
            in_=dr["l3_xq_out"][:],
            in_offset=_IOA(ap=cx.l3idx[:, i:i + 1], axis=0)),
            reads=("l3xqout", "l3idx"), writes=("xq_tm",), name=f"l3xql{i}")
    with ExitStack() as scr:
        xq_fm = B3
        x3aq = B4
        transpose_tm_to_fm(cx, tmb, xq_fm, "xq_tm", "xqfm", NT, NF, tag="l3qtr")
        rstdq = rms_stats(cx, lambda i: tmb[:, 1024 * i:1024 * (i + 1)], NT,
                          "xq_tm", "l3rmsq")
        x3aq_tm = tmb
        for i in range(NT):
            em.add("vector", lambda e, i=i: e.tensor_scalar(
                out=x3aq_tm[:, 1024 * i:1024 * (i + 1)],
                in0=x3aq_tm[:, 1024 * i:1024 * (i + 1)],
                scalar1=rstdq[:, i:i + 1], scalar2=None, op0=ALU.mult),
                reads=("stat",), writes=("x3aq_tm", "xq_tm"), name=f"l3qn{i}")
        transpose_tm_to_fm(cx, x3aq_tm, x3aq, "x3aq_tm", "x3aqfm", NT, NF,
                           tag="l3qtr2")
        for f in range(NF):
            em.add("vector", lambda e, f=f: e.tensor_scalar(
                out=x3aq[:, HALF * f:HALF * (f + 1)],
                in0=x3aq[:, HALF * f:HALF * (f + 1)],
                scalar1=nwcol[:, f:f + 1], scalar2=None, op0=ALU.mult),
                reads=("x3aqfm", "l3nw"), writes=("x3aqfm",), name=f"l3qnw{f}")

        def ev_q(f):
            for c in range(4):
                em.add("vector", lambda e, f=f, c=c: e.tensor_copy(
                    out=q_all[:, HALF * f + 512 * c:HALF * f + 512 * (c + 1)],
                    in_=cx.psum[c][:, 0:512]),
                    reads=(f"ps{c}",), writes=("l3qall",), name=f"l3qe{f}_{c}")

        proj_p2(cx, dr["l3a_wq"], x3aq, "x3aqfm", (wb0, wb1),
                ("l3wb0", "l3wb1"), NF, ev_q, "l3q")

        em.dma("gpsimd", lambda e: e.dma_start(out=wb0[:], in_=dr["l3a_wg"][0]),
               writes=("l3wb0",), name="l3gw")
        for c in range(4):
            for k in range(NF):
                em.add("tensor", lambda e, k=k, c=c: e.matmul(
                    cx.psum[c][:, 0:512], wb0[:, 128 * k:128 * (k + 1)],
                    x3aq[:, HALF * k + 512 * c:HALF * k + 512 * (c + 1)],
                    start=(k == 0), stop=(k == NF - 1)),
                    reads=("l3wb0", "x3aqfm"), writes=(f"ps{c}",),
                    name=f"l3gmm{k}_{c}")
            em.add("scalar", lambda e, c=c: e.activation(
                out=stg[:, 512 * c:512 * (c + 1)], in_=cx.psum[c][:, 0:512],
                func=AF.Sigmoid),
                reads=(f"ps{c}",), writes=("l3stg",), name=f"l3gsig{c}")
        for i in range(NT):
            bank = cx.psT[i % 2]
            em.add("tensor", lambda e, i=i, bank=bank: e.transpose(
                bank[:, 0:128], stg[:, 128 * i:128 * (i + 1)], cx.ident[:]),
                reads=("l3stg", "ident"), writes=(f"psT{i % 2}",),
                name=f"l3gt{i}")
            em.add("vector", lambda e, i=i, bank=bank: e.tensor_copy(
                out=gat_a[:, 16 * i:16 * (i + 1)], in_=bank[:, 0:16]),
                reads=(f"psT{i % 2}",), writes=("l3gata",), name=f"l3gte{i}")

        em.fence()

        def ev_qf(f):
            for c in range(4):
                em.add("vector", lambda e, f=f, c=c: e.tensor_copy(
                    out=qf_all[:, HALF * f + 512 * c:HALF * f + 512 * (c + 1)],
                    in_=cx.psum[c][:, 0:512]),
                    reads=(f"ps{c}",), writes=("l3qfall",), name=f"l3qfe{f}_{c}")

        proj_p2(cx, dr["l3f_wq"], xq_fm, "xqfm", (wb0, wb1),
                ("l3wb0", "l3wb1"), 4, ev_qf, "l3qf")
        em.dma("gpsimd", lambda e: e.dma_start(out=wb0[:], in_=dr["l3f_wg"][0]),
               writes=("l3wb0",), name="l3fgw2")
        for i in range(NT):
            for k in range(NF):
                em.add("tensor", lambda e, i=i, k=k: e.matmul(
                    cx.psB[:, 0:128],
                    xq_fm[:, HALF * k + 128 * i:HALF * k + 128 * (i + 1)],
                    wb0[:, 128 * k:128 * (k + 1)],
                    start=(k == 0), stop=(k == NF - 1)),
                    reads=("l3wb0", "xqfm"), writes=("psB",),
                    name=f"l3fgq{i}_{k}")
            em.add("scalar", lambda e, i=i: e.activation(
                out=stg[:, 0:64], in_=cx.psB[:, 0:64], func=AF.Sigmoid),
                reads=("psB",), writes=("l3stg",), name=f"l3fgqe{i}")
            em.add("vector", lambda e, i=i: e.tensor_copy(
                out=gm_q[:, 64 * i:64 * (i + 1)], in_=stg[:, 0:64]),
                reads=("l3stg",), writes=("l3gq",), name=f"l3fgqc{i}")
        emit_gm(cx, gm_q, "l3gq", gm_q, "l3gmq", NT, tag="l3gmq")
        if DEBUG_L3:
            dump_fm(cx, "dbg_qf", qf_all, 4, "l3qfall")
            dump_fm(cx, "dbg_gmq", gm_q, 1, "l3gmq", width=NT * 64)
        for i in range(NT):
            bank = cx.psT[i % 2]
            em.add("tensor", lambda e, i=i, bank=bank: e.transpose(
                bank[0:64, 0:128], gm_q[:, 64 * i:64 * (i + 1)], cx.ident[:]),
                reads=("l3gmq", "ident"), writes=(f"psT{i % 2}",),
                name=f"l3gqt{i}")
            em.add("vector", lambda e, i=i, bank=bank: e.tensor_copy(
                out=gmT_q[0:64, 128 * i:128 * (i + 1)], in_=bank[0:64, 0:128]),
                reads=(f"psT{i % 2}",), writes=("l3gmTq",), name=f"l3gqte{i}")
    em.fence()

    if True:
        khbuf = cx.xn_tm[:, 12288:16384]
        vh = cx.xbuf[:, :].bitcast(BF16)[0:64, 0:4096]
        vt = stg[:, :]
        simb = cx.xn_tm[:, :].bitcast(F32)[:, 0:4096]
        pbuf = cx.xn_tm[:, 8192:12288]
        pT = cx.xn_fm
        mcol = cx.stat[:, 64:80]
        scol = cx.stat[:, 80:96]
        kvo = dr["l3_kv_out"]
        fkvo = dr["l3f_kv_out"]

        def attn(nheads, src_q, src_q_key, k_row0, v_row0, rankoff, kvdram,
                 out_fm, out_key, gate_tm, tag):
            for h in range(nheads):
                p0 = 64 * (h % 2)
                kh = khbuf[p0:p0 + 64, :]
                em.dma("sync", lambda e, h=h, kh=kh: e.dma_start(
                    out=kh[:, 0:2048], in_=kvdram[k_row0 + 64 * h:
                                               k_row0 + 64 * (h + 1), :]),
                    writes=("l3kh",), name=f"{tag}k0_{h}")
                em.dma("sync", lambda e, h=h, kh=kh: e.dma_start(
                    out=kh[:, 2048:4096],
                    in_=kvdram[rankoff + k_row0 + 64 * h:
                               rankoff + k_row0 + 64 * (h + 1), :]),
                    writes=("l3kh",), name=f"{tag}k1_{h}")
                em.dma("gpsimd", lambda e, h=h: e.dma_start(
                    out=vh[:, 0:2048], in_=kvdram[v_row0 + 64 * h:
                                               v_row0 + 64 * (h + 1), :]),
                    writes=("l3vh", "xbufc"), name=f"{tag}v0_{h}")
                em.dma("gpsimd", lambda e, h=h: e.dma_start(
                    out=vh[:, 2048:4096],
                    in_=kvdram[rankoff + v_row0 + 64 * h:
                               rankoff + v_row0 + 64 * (h + 1), :]),
                    writes=("l3vh", "xbufc"), name=f"{tag}v1_{h}")
                for half2 in range(2):
                    bank = cx.psT[half2]
                    for t in range(16):
                        em.add("tensor", lambda e, half2=half2, t=t, bank=bank:
                               e.transpose(bank[:, 64 * t:64 * (t + 1)],
                                           vh[:, 2048 * half2 + 128 * t:
                                              2048 * half2 + 128 * (t + 1)],
                                           cx.ident[0:64, 0:64]),
                               reads=("l3vh", "ident"), writes=(f"psT{half2}",),
                               name=f"{tag}vt{h}_{half2}_{t}")
                    em.add("vector", lambda e, half2=half2, bank=bank:
                           e.tensor_copy(out=vt[:, 1024 * half2:
                                                1024 * (half2 + 1)],
                                         in_=bank[:, 0:1024]),
                           reads=(f"psT{half2}",), writes=("l3vt",),
                           name=f"{tag}vte{h}_{half2}")
                for g in range(4):
                    EC = 2 * g + 2
                    ET = 8 * g + 8
                    for j in range(4):
                        i = 4 * g + j
                        cd = i // 2
                        qs = src_q[64 * (h % 2):64 * (h % 2) + 64,
                                   HALF * (h // 2) + 128 * i:
                                   HALF * (h // 2) + 128 * (i + 1)]
                        for c in range(cd + 1):
                            ps = cx.psum[c % 2]
                            em.add("tensor", lambda e, qs=qs, c=c, ps=ps,
                                   kh=kh:
                                   e.matmul(ps[:, 0:512], qs,
                                            kh[:, 512 * c:512 * (c + 1)],
                                            start=True, stop=True),
                                   reads=("l3kh", src_q_key),
                                   writes=(f"ps{c % 2}",),
                                   name=f"{tag}sim{h}_{i}_{c}")
                            em.add("vector", lambda e, c=c, ps=ps:
                                   e.tensor_scalar(
                                       out=simb[:, 512 * c:512 * (c + 1)],
                                       in0=ps[:, 0:512], scalar1=-SCALE,
                                       scalar2=None, op0=ALU.mult,
                                       op1=ALU.min,
                                       accum_out=mcol[:, c:c + 1]),
                                   reads=(f"ps{c % 2}",),
                                   writes=("l3simb", "l3mcol", "xn_tm_alias"),
                                   name=f"{tag}sm{h}_{i}_{c}")
                        em.add("vector", lambda e, cd=cd: e.tensor_reduce(
                            out=mcol[:, 15:16], in_=mcol[:, 0:cd + 1],
                            axis=AX.X, op=ALU.min),
                            reads=("l3mcol",), writes=("l3mcol",),
                            name=f"{tag}mm{h}_{i}")
                        for c in range(cd + 1):
                            em.add("scalar", lambda e, c=c: e.activation(
                                out=pbuf[:, 512 * c:512 * (c + 1)],
                                in_=simb[:, 512 * c:512 * (c + 1)],
                                func=AF.Exp, scale=-1.0,
                                bias=mcol[:, 15:16],
                                accum_out=scol[:, c:c + 1]),
                                reads=("l3simb", "l3mcol"),
                                writes=("l3p", "l3scol"),
                                name=f"{tag}ex{h}_{i}_{c}")
                        em.add("vector", lambda e, cd=cd, i=i:
                               e.scalar_tensor_tensor(
                                   out=pbuf[:, 512 * cd:512 * (cd + 1)],
                                   in0=pbuf[:, 512 * cd:512 * (cd + 1)],
                                   scalar=1.0,
                                   in1=masks[:, 512 * (i % 2):
                                             512 * (i % 2 + 1)],
                                   op0=ALU.mult, op1=ALU.mult,
                                   accum_out=scol[:, cd:cd + 1]),
                               reads=("l3p", "l3masks"),
                               writes=("l3p", "l3scol"),
                               name=f"{tag}msk{h}_{i}")
                        if cd + 1 < EC:
                            em.add("vector", lambda e, cd=cd, EC=EC: e.memset(
                                pbuf[:, 512 * (cd + 1):512 * EC], 0.0),
                                writes=("l3p",), name=f"{tag}z{h}_{i}")
                        em.add("vector", lambda e, cd=cd: e.tensor_reduce(
                            out=scol[:, 15:16], in_=scol[:, 0:cd + 1],
                            axis=AX.X, op=ALU.add),
                            reads=("l3scol",), writes=("l3scol",),
                            name=f"{tag}ss{h}_{i}")
                        em.add("vector", lambda e: e.reciprocal(
                            out=scol[:, 14:15], in_=scol[:, 15:16]),
                            reads=("l3scol",), writes=("l3scol",),
                            name=f"{tag}rc{h}_{i}")
                        if gate_tm is not None:
                            em.add("vector", lambda e, i=i, h=h: e.tensor_mul(
                                scol[:, 14:15], scol[:, 14:15],
                                gate_tm[:, 16 * i + h:16 * i + h + 1]),
                                reads=("l3scol", "l3gata"),
                                writes=("l3scol",), name=f"{tag}gs{h}_{i}")
                        em.add("vector", lambda e, EC=EC: e.tensor_scalar_mul(
                            pbuf[:, 0:512 * EC], pbuf[:, 0:512 * EC],
                            scol[:, 14:15]),
                            reads=("l3p", "l3scol"), writes=("l3p",),
                            name=f"{tag}psc{h}_{i}")
                        for tb in range((ET + 7) // 8):
                            bank = cx.psT[tb % 2]
                            lo, hi = 8 * tb, min(8 * tb + 8, ET)
                            for t in range(lo, hi):
                                em.add("tensor",
                                       lambda e, t=t, bank=bank, lo=lo:
                                       e.transpose(
                                           bank[:, 128 * (t - lo):
                                                128 * (t - lo + 1)],
                                           pbuf[:, 128 * t:128 * (t + 1)],
                                           cx.ident[:]),
                                       reads=("l3p", "ident"),
                                       writes=(f"psT{tb % 2}",),
                                       name=f"{tag}pt{h}_{i}_{t}")
                            em.add("vector",
                                   lambda e, bank=bank, lo=lo, hi=hi, j=j:
                                   e.tensor_copy(
                                       out=_pt_dst(pT, lo, hi, j),
                                       in_=bank[:, 0:128 * (hi - lo)].rearrange(
                                           "p (t q) -> p t q", q=128)),
                                   reads=(f"psT{tb % 2}",), writes=("l3pT",),
                                   name=f"{tag}pte{h}_{i}_{tb}")
                    for t in range(ET):
                        em.add("tensor", lambda e, t=t, ET=ET: e.matmul(
                            cx.psum[2][0:64, 0:512], vt[:, 64 * t:64 * (t + 1)],
                            pT[:, 512 * t:512 * (t + 1)],
                            start=(t == 0), stop=(t == ET - 1)),
                            reads=("l3vt", "l3pT"), writes=("ps2",),
                            name=f"{tag}av{h}_{g}_{t}")
                    em.add("vector", lambda e, h=h, g=g: e.tensor_copy(
                        out=out_fm[64 * (h % 2):64 * (h % 2) + 64,
                                   HALF * (h // 2) + 512 * g:
                                   HALF * (h // 2) + 512 * (g + 1)],
                        in_=cx.psum[2][0:64, 0:512]),
                        reads=("ps2",), writes=(out_key,),
                        name=f"{tag}ave{h}_{g}")

        attn(16, q_all, "l3qall", 0, 1024, 2048, kvo, ao, "l3ao", gat_a, "lA")
        attn(8, qf_all, "l3qfall", 0, 512, 1024, fkvo, aof, "l3aof", None, "lF")
    em.fence()

    if DEBUG_L3:
        dump_fm(cx, "dbg_aof", aof, 4, "l3aof")
        dump_fm(cx, "dbg_ao", ao, 8, "l3ao")
    if True:
        wsl = cx.xn_tm[:, 0:NF * 1024]
        for k in range(NF):
            em.dma("gpsimd", lambda e, k=k: e.dma_start(
                out=wsl[:, 1024 * k:1024 * (k + 1)], in_=dr["l3a_wout"][k]),
                writes=("l3wsl",), name=f"l3wo{k}")
        for i in range(NT):
            for k in range(NF):
                for c in range(2):
                    em.add("tensor", lambda e, k=k, i=i, c=c: e.matmul(
                        cx.psum[2 * (i % 2) + c][:, 0:512],
                        ao[:, HALF * k + 128 * i:HALF * k + 128 * (i + 1)],
                        wsl[:, 1024 * k + 512 * c:1024 * k + 512 * (c + 1)],
                        start=(k == 0), stop=(k == NF - 1)),
                        reads=("l3wsl", "l3ao"),
                        writes=(f"ps{2 * (i % 2) + c}",),
                        name=f"l3out{k}_{i}_{c}")
            for c in range(2):
                em.add("vector", lambda e, i=i, c=c: e.tensor_copy(
                    out=s_j[:, 1024 * i + 512 * c:1024 * i + 512 * (c + 1)],
                    in_=cx.psum[2 * (i % 2) + c][:, 0:512]),
                    reads=(f"ps{2 * (i % 2) + c}",), writes=("l3sj",),
                    name=f"l3oute{i}_{c}")
    em.fence()

    with ExitStack() as scr:
        mk = stg[:, 0:4 * 256]
        bcb2 = stg[:, 4 * 256:4 * 256 + 256]
        gf = ao
        oeb = [cx.xn_fm, cx.xn_tm]
        for e_i in range(8):
            for kt in range(4):
                idx = 4 * e_i + kt
                buf = oeb[idx // 16]
                off = 1024 * (idx % 16)
                em.dma("gpsimd", lambda e, e_i=e_i, kt=kt, buf=buf, off=off:
                       e.dma_start(out=buf[:, off:off + 1024],
                                   in_=dr["l3f_oe"][e_i, kt]),
                       writes=("l3oeb",), name=f"l3oel{e_i}_{kt}")
        for qg in range(8):
            q0 = 256 * qg
            for e_i in range(8):
                for pt in range(4):
                    em.add("tensor", lambda e, e_i=e_i, pt=pt, q0=q0: e.matmul(
                        cx.psB[:, 0:256],
                        selb[:, 128 * (4 * e_i + pt):128 * (4 * e_i + pt + 1)],
                        gmT_q[0:64, q0:q0 + 256], start=True, stop=True),
                        reads=("l3selb", "l3gmTq"), writes=("psB",),
                        name=f"l3obc{qg}_{e_i}_{pt}")
                    em.add("vector", lambda e: e.tensor_copy(
                        out=bcb2[:], in_=cx.psB[:, 0:256]),
                        reads=("psB",), writes=("l3bcb2",),
                        name=f"l3obe{qg}_{e_i}_{pt}")
                    em.add("vector", lambda e, pt=pt, q0=q0: e.tensor_mul(
                        mk[:, 256 * pt:256 * (pt + 1)],
                        aof[:, HALF * pt + q0:HALF * pt + q0 + 256], bcb2[:]),
                        reads=("l3aof", "l3bcb2"), writes=("l3mk",),
                        name=f"l3omk{qg}_{e_i}_{pt}")
                for kt in range(4):
                    idx = 4 * e_i + kt
                    buf = oeb[idx // 16]
                    off = 1024 * (idx % 16)
                    for tq in range(2):
                        for c in range(2):
                            em.add("tensor",
                                   lambda e, kt=kt, tq=tq, c=c, e_i=e_i,
                                   buf=buf, off=off: e.matmul(
                                       cx.psum[2 * tq + c][:, 0:512],
                                       mk[:, 256 * kt + 128 * tq:
                                          256 * kt + 128 * (tq + 1)],
                                       buf[:, off + 512 * c:
                                           off + 512 * (c + 1)],
                                       start=(e_i == 0 and kt == 0),
                                       stop=(e_i == 7 and kt == 3)),
                                   reads=("l3mk", "l3oeb"),
                                   writes=(f"ps{2 * tq + c}",),
                                   name=f"l3omm{qg}_{e_i}_{kt}_{tq}_{c}")
            for tq in range(2):
                for c in range(2):
                    em.add("scalar", lambda e, qg=qg, tq=tq, c=c: e.activation(
                        out=gf[:, 1024 * (2 * qg + tq) + 512 * c:
                               1024 * (2 * qg + tq) + 512 * (c + 1)],
                        in_=cx.psum[2 * tq + c][:, 0:512], func=AF.Silu),
                        reads=(f"ps{2 * tq + c}",), writes=("l3gf",),
                        name=f"l3gout{qg}_{tq}_{c}")
    em.fence()

    sgr = dr["l3_sg_in"].rearrange("(j p) d -> j p d", p=128)
    for j in range(NT):
        em.add("vector", lambda e, j=j: e.scalar_tensor_tensor(
            out=cx.sgbuf[:], in0=s_j[:, 1024 * j:1024 * (j + 1)], scalar=1.0,
            in1=gf[:, 1024 * j:1024 * (j + 1)], op0=ALU.mult, op1=ALU.mult),
            reads=("l3sj", "l3gf"), writes=("sgbuf",), name=f"l3sg{j}")
        em.dma("sync", lambda e, j=j: e.dma_start(
            out=sgr[j], in_=cx.sgbuf[:]),
            reads=("sgbuf",), writes=("l3sgin",), name=f"l3sgs{j}")
    _cc_pair(em, "AllGather", dr["l3_sg_in"], dr["l3_sg_out"],
             ("l3sgin",), ("l3sgout",), "l3sgcc")
    cx.l3_sg_out = dr["l3_sg_out"]
    if DEBUG_L3:
        dbg_s = nc.dram_tensor("dbg_s", [2048, 1024], F32, kind="ExternalOutput").ap()
        dbg_g = nc.dram_tensor("dbg_g", [2048, 1024], F32, kind="ExternalOutput").ap()
        dsr = dbg_s.rearrange("(j p) d -> j p d", p=128)
        dgr = dbg_g.rearrange("(j p) d -> j p d", p=128)
        for j in range(NT):
            em.add("vector", lambda e, j=j: e.tensor_copy(
                out=cx.sgbuf[:], in_=s_j[:, 1024 * j:1024 * (j + 1)]),
                reads=("l3sj",), writes=("sgbuf",), name=f"dbgs{j}")
            em.out_dma("sync", lambda e, j=j: e.dma_start(
                out=dsr[j], in_=cx.sgbuf[:]), reads=("sgbuf",), name=f"dbgss{j}")
            em.add("vector", lambda e, j=j: e.tensor_copy(
                out=cx.sgbuf[:], in_=gf[:, 1024 * j:1024 * (j + 1)]),
                reads=("l3gf",), writes=("sgbuf",), name=f"dbgg{j}")
            em.out_dma("sync", lambda e, j=j: e.dma_start(
                out=dgr[j], in_=cx.sgbuf[:]), reads=("sgbuf",), name=f"dbggs{j}")
    em.fence()


def final_update_l3(cx, ctx, x_dram, xout_dram):
    nc, em = cx.nc, cx.em
    xrk = getattr(cx, "x_rkey", None)
    xdep = (xrk,) if xrk else ()
    xr = x_dram.rearrange("(i p) d -> i p d", p=128)
    xo = xout_dram.rearrange("(i p) d -> i p d", p=128)
    from concourse.bass import IndirectOffsetOnAxis as _IOA
    for i in range(NT):
        xb = cx.xbuf[:, 1024 * (i % 2):1024 * (i % 2 + 1)]
        bk = f"xbuf{i % 2}"
        em.dma("sync", lambda e, i=i, xb=xb: e.dma_start(out=xb, in_=xr[i]),
               reads=xdep, writes=(bk,), name=f"l3fu_ld{i}")
        em.dma("gpsimd", lambda e, i=i: e.indirect_dma_start(
            out=cx.sgbuf[:], out_offset=None, in_=cx.l3_sg_out[:],
            in_offset=_IOA(ap=cx.l3idx[:, NT + i:NT + i + 1], axis=0)),
            reads=("l3sgout", "l3idx"), writes=("sgbuf",), name=f"l3fu_sg{i}")
        em.add("vector", lambda e, xb=xb: e.tensor_add(xb, xb, cx.sgbuf[:]),
               reads=("sgbuf", bk), writes=(bk,), name=f"l3fu_add{i}")
        em.out_dma("sync", lambda e, i=i, xb=xb: e.dma_start(
            out=xo[i], in_=xb), reads=(bk,), name=f"l3fu_st{i}")





def prep_l1_weights(inputs, h):
    w = {}
    for nm, key in (("w1", "L1_attn_W1"), ("w2", "L1_attn_W2"),
                    ("w3", "L1_attn_W3")):
        w[f"l1_{nm}"] = fmpack(np.ascontiguousarray(np.asarray(inputs[key]).T))
    Wo = np.asarray(inputs["L1_attn_Wo"])
    perm = np.array([8 * (m % 128) + m // 128 for m in range(1024)])
    WoP = Wo[:, perm]
    w["l1_wo"] = fmpack(np.ascontiguousarray(WoP.T))
    qw = np.asarray(inputs["L1_ffn_q_w"])
    qsel = np.concatenate([qw[:, 64 * hh:64 * hh + 32] for hh in range(16)],
                          axis=1)
    w["l1f_qw"] = fmpack(np.ascontiguousarray(qsel))
    w["l1f_kw"] = fmpack(np.asarray(inputs["L1_ffn_k_w"]))
    w["l1f_vw"] = fmpack(np.asarray(inputs["L1_ffn_v_w"]))
    w["l1f_ow"] = fmpack(np.ascontiguousarray(
        np.asarray(inputs["L1_ffn_o_w"])[:512]))
    rw = np.zeros((1024, 128), np.float32)
    rw[:, :8] = np.asarray(inputs["L1_ffn_router_w"])
    w["l1f_rw"] = fmpack(rw)
    w["l1f_w1"] = np.stack([fmpack(np.asarray(inputs["L1_ffn_w1"])[e])
                            for e in range(8)])
    w["l1f_w2"] = np.stack([slabpack(np.asarray(inputs["L1_ffn_w2"])[e])
                            for e in range(8)])
    qbs = np.asarray(inputs["L1_ffn_q_b"])
    qb = np.concatenate([qbs[64 * hh:64 * hh + 32] for hh in range(16)])
    w["l1f_qb"] = np.ascontiguousarray(qb.reshape(4, 128))
    w["l1f_kb"] = np.ascontiguousarray(
        np.asarray(inputs["L1_ffn_k_b"]).reshape(4, 128))
    w["l1f_vb"] = np.ascontiguousarray(
        np.asarray(inputs["L1_ffn_v_b"]).reshape(4, 128))
    w["l1f_ob"] = np.ascontiguousarray(
        np.asarray(inputs["L1_ffn_o_b"]).reshape(8, 128))
    rb = np.zeros(128, np.float32)
    rb[:8] = np.asarray(inputs["L1_ffn_router_b"])
    w["l1f_rb"] = np.ascontiguousarray(rb.reshape(1, 128))
    w["l1f_b1"] = np.ascontiguousarray(
        np.asarray(inputs["L1_ffn_b1"]).reshape(8, 16, 128))
    w["l1f_b2"] = np.ascontiguousarray(
        np.asarray(inputs["L1_ffn_b2"]).reshape(8, 8, 128))
    oidx = np.empty((128, 32), np.uint32)
    p = np.arange(128, dtype=np.uint32)
    for t in range(16):
        oidx[:, t] = 2048 * h + 128 * t + p
        oidx[:, 16 + t] = 4096 + 2048 * h + 128 * t + p
    w["l1_oidx"] = oidx
    return w


def declare_l1_dram(nc):
    d = {}
    for nm, sh in (("l1_w1", [8, 128, 1024]), ("l1_w2", [8, 128, 1024]),
                   ("l1_w3", [8, 128, 1024]), ("l1_wo", [8, 128, 1024]),
                   ("l1f_qw", [4, 128, 1024]), ("l1f_kw", [4, 128, 1024]),
                   ("l1f_vw", [4, 128, 1024]), ("l1f_ow", [8, 128, 512]),
                   ("l1f_rw", [1, 128, 1024]),
                   ("l1f_w1", [8, 16, 128, 1024]),
                   ("l1f_w2", [8, 16, 128, 1024]),
                   ("l1f_qb", [4, 128]), ("l1f_kb", [4, 128]),
                   ("l1f_vb", [4, 128]), ("l1f_ob", [8, 128]),
                   ("l1f_rb", [1, 128]), ("l1f_b1", [8, 16, 128]),
                   ("l1f_b2", [8, 8, 128])):
        d[nm] = nc.dram_tensor(nm, sh, F32, kind="ExternalInput").ap()
    d["l1_oidx"] = nc.dram_tensor("l1_oidx", [128, 32], U32,
                                  kind="ExternalInput").ap()
    d["l1_ao_in"] = nc.dram_tensor("l1_ao_in", [4096, 512], BF16).ap()
    d["l1_ao_out"] = nc.dram_tensor("l1_ao_out", [8192, 512], BF16).ap()
    d["l1f_kv_in"] = nc.dram_tensor("l1f_kv_in", [1024, 2048], BF16).ap()
    d["l1f_kv_out"] = nc.dram_tensor("l1f_kv_out", [2048, 2048], BF16).ap()
    d["l1f_w1c"] = nc.dram_tensor("l1f_w1c", [8, 16, 128, 1024], BF16).ap()
    d["l1f_w2c"] = nc.dram_tensor("l1f_w2c", [8, 16, 128, 1024], BF16).ap()
    return d


def build_l1(cx, ctx, dr):
    nc, em = cx.nc, cx.em
    sb = lambda n, s, d: ctx.enter_context(nc.sbuf_tensor(n, s, d))
    em.fence()
    C1 = sb("l1_C1", [128, NF * HALF], BF16)
    C2 = sb("l1_C2", [128, NF * HALF], BF16)
    C3 = sb("l1_C3", [128, NF * HALF], BF16)
    C4 = sb("l1_C4", [128, NF * HALF], BF16)
    costab = cx.xn_tm[:, 0:4096]
    sintab = cx.xn_tm[:, 4096:8192]
    alloc_gm_consts(cx, ctx, full=False, tag="l1")
    ident32x = cx.xbuf[:, 128:256]
    em.add("vector", lambda e: e.tensor_scalar(
        out=ident32x, in0=cx.ident_i[:], scalar1=0.0, scalar2=None,
        op0=ALU.is_equal), reads=("ident_i",), writes=("xbufc",),
        name="l1id32")
    wb0 = cx.xbuf[:, 1280:1792].bitcast(BF16)
    wb1 = cx.sq_junk[:, 0:1024]
    stg = cx.xbuf[:, 256:1280].bitcast(BF16)
    bcols = sb("l1_bcols", [128, 48], F32)
    combs = cx.xbuf[:, 1792:1920]
    cx.l1idx = sb("l1_idx", [128, 32], U32)
    em.dma("sync", lambda e: e.dma_start(out=cx.l1idx[:], in_=dr["l1_oidx"][:]),
           writes=("l1idx",), name="l1idxld")

    invf = bcols[:, 44:45]
    pmod = bcols[:, 45:46]
    pidx = bcols[:, 46:47].bitcast(I32)
    em.add("gpsimd", lambda e: e.iota(pidx, [[1, 1]], base=0,
                                      channel_multiplier=1),
           writes=("l1bc",), name="l1pi")
    em.add("vector", lambda e: e.tensor_scalar(
        out=pmod.bitcast(I32), in0=pidx, scalar1=1, scalar2=None,
        op0=ALU.bitwise_and), reads=("l1bc",), writes=("l1bc",), name="l1pa")
    em.add("vector", lambda e: e.scalar_tensor_tensor(
        out=invf, in0=pmod.bitcast(I32), scalar=-1.0, in1=pidx,
        op0=ALU.mult, op1=ALU.add),
        reads=("l1bc",), writes=("l1bc",), name="l1ps")
    em.add("scalar", lambda e: e.activation(
        out=invf, in_=invf, func=AF.Exp,
        scale=float(-np.log(10000.0) / 128.0)),
        reads=("l1bc",), writes=("l1bc",), name="l1if")
    em.add("vector", lambda e: e.tensor_scalar(
        out=bcols[:, 43:44], in0=pmod.bitcast(I32), scalar1=2.0, scalar2=-1.0,
        op0=ALU.mult, op1=ALU.add),
        reads=("l1bc",), writes=("l1bc",), name="l1sg")
    angf = C4[:, :].bitcast(F32)[:, 0:1024]
    angiI = C4[:, :].bitcast(F32)[:, 1024:2048].bitcast(I32)

    def build_tab(shift, outtab, tag):
        for hf in range(4):
            em.add("gpsimd", lambda e, hf=hf: e.iota(
                angiI, [[1, 2], [8, 512]], base=2 * hf,
                channel_multiplier=0),
                writes=("l1ang",), name=f"{tag}i{hf}")
            em.add("vector", lambda e: e.tensor_scalar(
                out=angf, in0=angiI, scalar1=invf, scalar2=None,
                op0=ALU.mult), reads=("l1ang", "l1bc"), writes=("l1ang",),
                name=f"{tag}m{hf}")
            if shift:
                em.add("vector", lambda e: e.tensor_scalar_add(
                    angf, angf, float(np.pi / 2)),
                    reads=("l1ang",), writes=("l1ang",), name=f"{tag}s{hf}")
            em.add("vector", lambda e: e.tensor_scalar_mul(
                cx.sgbuf[:], angf, float(1.0 / (2 * np.pi))),
                reads=("l1ang",), writes=("sgbuf",), name=f"{tag}d{hf}")
            em.add("vector", lambda e: e.tensor_copy(
                out=cx.sgbuf[:].bitcast(I32), in_=cx.sgbuf[:]),
                reads=("sgbuf",), writes=("sgbuf",), name=f"{tag}r{hf}")
            em.add("vector", lambda e: e.scalar_tensor_tensor(
                out=angf, in0=cx.sgbuf[:].bitcast(I32),
                scalar=float(-2 * np.pi), in1=angf,
                op0=ALU.mult, op1=ALU.add),
                reads=("sgbuf", "l1ang"), writes=("l1ang",),
                name=f"{tag}f{hf}")
            em.add("scalar", lambda e, hf=hf, outtab=outtab: e.activation(
                out=outtab[:, 1024 * hf:1024 * (hf + 1)], in_=angf,
                func=AF.Sin),
                reads=("l1ang",), writes=("l1tab",), name=f"{tag}sn{hf}")

    if DEBUG_L1:
        cx.dmpbuf = cx.sgbuf[:, 0:512]
    build_tab(True, costab, "l1co")
    build_tab(False, sintab, "l1si")
    em.add("vector", lambda e: e.tensor_scalar(
        out=sintab[:], in0=sintab[:], scalar1=bcols[:, 43:44], scalar2=None,
        op0=ALU.mult), reads=("l1tab", "l1bc"), writes=("l1tab",),
        name="l1ss")
    if DEBUG_L1:
        dump_fm(cx, "dbg_cos", costab, 1, "l1tab", width=4096)
        dump_fm(cx, "dbg_sin", sintab, 1, "l1tab", width=4096)
    em.fence()

    def ev_to(dstbuf, dkey, tag):
        def ev(f):
            for c in range(4):
                em.add("vector", lambda e, f=f, c=c: e.tensor_copy(
                    out=dstbuf[:, HALF * f + 512 * c:HALF * f + 512 * (c + 1)],
                    in_=cx.psum[c][:, 0:512]),
                    reads=(f"ps{c}",), writes=(dkey,), name=f"{tag}{f}_{c}")
        return ev

    proj_p2(cx, dr["l1_w1"], cx.xn_fm, "xn_fm", (wb0, wb1),
            ("l1wb0", "l1wb1"), NF, ev_to(C1, "l1q", "l1qp"), "l1w1")
    proj_p2(cx, dr["l1_w2"], cx.xn_fm, "xn_fm", (wb0, wb1),
            ("l1wb0", "l1wb1"), NF, ev_to(C2, "l1k", "l1kp"), "l1w2")
    proj_p2(cx, dr["l1_w3"], cx.xn_fm, "xn_fm", (wb0, wb1),
            ("l1wb0", "l1wb1"), NF, ev_to(C3, "l1vf", "l1vp"), "l1w3")
    transpose_fm_to_tm(cx, C3, C4, "l1vf", "l1v", NT, NF, tag="l1vt")
    vproj_tm = C4
    em.fence()
    if DEBUG_L1:
        dump_fm(cx, "dbg_q", C1, NF, "l1q")
        dump_fm(cx, "dbg_k", C2, NF, "l1k")

    shmask = []
    for i in range(0, 32, 2):
        shmask += [i + 1, i]
    for l in range(4):
        for (srcb, skey) in ((C1, "l1q"), (C2, "l1k")):
            for c in range(NF):
                src = srcb[:, HALF * c + 512 * l:HALF * c + 512 * (l + 1)]
                tabsl = slice(512 * c, 512 * (c + 1))
                em.add("vector", lambda e, src=src: e.stream_shuffle(
                    out=stg[:, 0:512], in_=src, mask=shmask),
                    reads=(skey,), writes=("l1stg",), name=f"l1sh{l}{skey}{c}")
                em.add("vector", lambda e, tabsl=tabsl: e.tensor_tensor(
                    out=stg[:, 512:1024], in0=stg[:, 0:512],
                    in1=sintab[:, tabsl], op=ALU.mult),
                    reads=("l1stg", "l1tab"), writes=("l1stg",),
                    name=f"l1rm{l}{skey}{c}")
                em.add("vector", lambda e, src=src, tabsl=tabsl:
                       e.tensor_tensor(out=src, in0=src,
                                       in1=costab[:, tabsl], op=ALU.mult),
                       reads=(skey, "l1tab"), writes=(skey,),
                       name=f"l1rc{l}{skey}{c}")
                em.add("vector", lambda e, src=src: e.tensor_add(
                    src, src, stg[:, 512:1024]),
                    reads=("l1stg", skey), writes=(skey,),
                    name=f"l1ra{l}{skey}{c}")
    em.fence()

    simb = cx.xn_tm[:, :].bitcast(F32)[:, 0:4096]
    pbuf = cx.xn_tm[:, 8192:12288]
    outh = cx.xn_tm[:, 12288:16384]
    pT = C3
    mcol = cx.stat[:, 64:80]
    scol = cx.stat[:, 80:96]
    aor = dr["l1_ao_in"]
    for l in range(4):
        for g in range(8):
            for j in range(4):
                qt = 4 * g + j
                for c2 in range(8):
                    ps = cx.psum[c2 % 2]
                    cq, jq = qt // 4, qt % 4
                    qsl = (HALF * cq + 512 * l + 128 * jq)
                    ck, jk = c2 // 1, 0
                    em.add("tensor", lambda e, qsl=qsl, c2=c2, ps=ps, l=l:
                           e.matmul(ps[:, 0:512],
                                    C1[:, qsl:qsl + 128],
                                    C2[:, HALF * (c2 // 1 % 8) + 512 * l:
                                       HALF * (c2 % 8) + 512 * (l + 1)][
                                       :, 0:512] if False else
                                    C2[:, HALF * c2 + 512 * l:
                                       HALF * c2 + 512 * l + 512],
                                    start=True, stop=True),
                           reads=("l1q", "l1k"), writes=(f"ps{c2 % 2}",),
                           name=f"l1sim{l}_{qt}_{c2}")
                    em.add("vector", lambda e, c2=c2, ps=ps: e.tensor_scalar(
                        out=simb[:, 512 * c2:512 * (c2 + 1)], in0=ps[:, 0:512],
                        scalar1=float(-1.0 / np.sqrt(8.0)), scalar2=None,
                        op0=ALU.mult, op1=ALU.min,
                        accum_out=mcol[:, c2:c2 + 1]),
                        reads=(f"ps{c2 % 2}",), writes=("l1simb", "l1mcol"),
                        name=f"l1sm{l}_{qt}_{c2}")
                em.add("vector", lambda e: e.tensor_reduce(
                    out=mcol[:, 15:16], in_=mcol[:, 0:8], axis=AX.X,
                    op=ALU.min), reads=("l1mcol",), writes=("l1mcol",),
                    name=f"l1mm{l}_{qt}")
                for c2 in range(8):
                    em.add("scalar", lambda e, c2=c2: e.activation(
                        out=pbuf[:, 512 * c2:512 * (c2 + 1)],
                        in_=simb[:, 512 * c2:512 * (c2 + 1)], func=AF.Exp,
                        scale=-1.0, bias=mcol[:, 15:16],
                        accum_out=scol[:, c2:c2 + 1]),
                        reads=("l1simb", "l1mcol"), writes=("l1p", "l1scol"),
                        name=f"l1ex{l}_{qt}_{c2}")
                em.add("vector", lambda e: e.tensor_reduce(
                    out=scol[:, 15:16], in_=scol[:, 0:8], axis=AX.X,
                    op=ALU.add), reads=("l1scol",), writes=("l1scol",),
                    name=f"l1sr{l}_{qt}")
                em.add("vector", lambda e: e.reciprocal(
                    out=scol[:, 14:15], in_=scol[:, 15:16]),
                    reads=("l1scol",), writes=("l1scol",),
                    name=f"l1rp{l}_{qt}")
                em.add("vector", lambda e: e.tensor_scalar_mul(
                    pbuf[:, 0:4096], pbuf[:, 0:4096], scol[:, 14:15]),
                    reads=("l1p", "l1scol"), writes=("l1p",),
                    name=f"l1pc{l}_{qt}")
                for tb in range(4):
                    bank = cx.psT[tb % 2]
                    for t in range(8 * tb, 8 * tb + 8):
                        em.add("tensor", lambda e, t=t, tb=tb, bank=bank:
                               e.transpose(bank[:, 128 * (t - 8 * tb):
                                                128 * (t - 8 * tb + 1)],
                                           pbuf[:, 128 * t:128 * (t + 1)],
                                           cx.ident[:]),
                               reads=("l1p", "ident"),
                               writes=(f"psT{tb % 2}",),
                               name=f"l1pt{l}_{qt}_{t}")
                    em.add("vector", lambda e, tb=tb, bank=bank, j=j:
                           e.tensor_copy(
                               out=_pt_dst(pT, 8 * tb, 8 * tb + 8, j),
                               in_=bank[:, 0:1024].rearrange(
                                   "p (t q) -> p t q", q=128)),
                           reads=(f"psT{tb % 2}",), writes=("l1pT",),
                           name=f"l1pe{l}_{qt}_{tb}")
            for t in range(32):
                voff = 1024 * (4 * l + t % 4) + 128 * (t // 4)
                em.add("tensor", lambda e, t=t, voff=voff: e.matmul(
                    cx.psum[2][:, 0:512], vproj_tm[:, voff:voff + 128],
                    pT[:, 512 * t:512 * (t + 1)],
                    start=(t == 0), stop=(t == 31)),
                    reads=("l1v", "l1pT"), writes=("ps2",),
                    name=f"l1av{l}_{g}_{t}")
            em.add("vector", lambda e: e.tensor_copy(
                out=stg[:, 0:512], in_=cx.psum[2][:, 0:512]),
                reads=("ps2",), writes=("l1stg",), name=f"l1ae{l}_{g}")
            for t4 in range(4):
                qt = 4 * g + t4
                bank = cx.psT[t4 % 2]
                em.add("tensor", lambda e, t4=t4, bank=bank: e.transpose(
                    bank[:, 0:128], stg[:, 128 * t4:128 * (t4 + 1)],
                    cx.ident[:]),
                    reads=("l1stg", "ident"), writes=(f"psT{t4 % 2}",),
                    name=f"l1ot{l}_{g}_{t4}")
                em.add("vector", lambda e, qt=qt, bank=bank: e.tensor_copy(
                    out=outh[:, 128 * qt:128 * (qt + 1)], in_=bank[:, 0:128]),
                    reads=(f"psT{t4 % 2}",), writes=("l1outh",),
                    name=f"l1oe{l}_{g}_{t4}")
        for m in range(32):
            c, jb = m // 4, m % 4
            em.dma("sync", lambda e, m=m, c=c, jb=jb, l=l: e.dma_start(
                out=aor[1024 * jb + 128 * c:1024 * jb + 128 * (c + 1),
                        128 * l:128 * (l + 1)],
                in_=outh[:, 128 * m:128 * (m + 1)]),
                reads=("l1outh",), writes=("l1aoin",), name=f"l1shp{l}_{m}")
    _cc_pair(em, "AllGather", dr["l1_ao_in"], dr["l1_ao_out"],
             ("l1aoin",), ("l1aoout",), "l1aocc")
    em.fence()
    outfull = C1
    sg16 = cx.sgbuf[:, :].bitcast(BF16)
    l1fl = ctx.enter_context(nc.sbuf_tensor("l1_fl", [1, 8], BF16))
    l1fc = ctx.enter_context(nc.sbuf_tensor("l1_fc", [128, 2], F32))
    em.dma("gpsimd", lambda e: e.dma_start(out=l1fl[:], in_=dr["flag8"][:]),
           writes=("l1fl",), name="l1flld")
    em.add("tensor", lambda e: e.matmul(
        cx.psB[:, 0:8], cx.ones1[:, :], l1fl[:, 0:8], start=True, stop=True),
        reads=("l1fl", "ones1"), writes=("psB",), name="l1flb")
    em.add("vector", lambda e: e.tensor_copy(
        out=l1fc[:, 0:1], in_=cx.psB[:, 0:1]),
        reads=("psB",), writes=("l1fc",), name="l1fce")
    em.add("vector", lambda e: e.tensor_scalar(
        out=l1fc[:, 1:2], in0=l1fc[:, 0:1], scalar1=-1.0, scalar2=1.0,
        op0=ALU.mult, op1=ALU.add),
        reads=("l1fc",), writes=("l1fc",), name="l1fci")
    for t in range(16):
        for half in range(2):
            p2 = (2 * t + half) % 2
            sA = sg16[:, 1024 * p2:1024 * p2 + 512]
            sB = sg16[:, 1024 * p2 + 512:1024 * (p2 + 1)]
            ka, kb = f"l1gA{p2}", f"l1gB{p2}"
            base = 4096 * half + 128 * t
            em.dma("sync", lambda e, sA=sA, base=base: e.dma_start(
                out=sA, in_=dr["l1_ao_out"][base:base + 128, :]),
                reads=("l1aoout",), writes=(ka,), name=f"l1gA{t}_{half}")
            em.dma("gpsimd", lambda e, sB=sB, base=base: e.dma_start(
                out=sB, in_=dr["l1_ao_out"][base + 2048:base + 2176, :]),
                reads=("l1aoout",), writes=(kb,), name=f"l1gB{t}_{half}")
            em.add("vector", lambda e, sA=sA: e.tensor_scalar(
                out=sA, in0=sA, scalar1=l1fc[:, 1:2], scalar2=None,
                op0=ALU.mult),
                reads=(ka, "l1fc"), writes=(ka,), name=f"l1gs{t}_{half}")
            em.add("vector", lambda e, sA=sA, sB=sB, t=t, half=half:
                   e.scalar_tensor_tensor(
                out=outfull[:, 1024 * t + 512 * half:
                            1024 * t + 512 * (half + 1)],
                in0=sB, scalar=l1fc[:, 0:1], in1=sA, op0=ALU.mult,
                op1=ALU.add),
                reads=(ka, kb, "l1fc"), writes=("l1of",),
                name=f"l1gb{t}_{half}")
    s_tm = C2
    for u2 in range(2):
        for w8 in range(NF):
            em.dma("gpsimd", lambda e, w8=w8: e.dma_start(
                out=wb0[:], in_=dr["l1_wo"][w8]),
                writes=("l1wb0",), name=f"l1wol{u2}_{w8}")
            for kt in range(NF):
                for c in range(2):
                    em.add("tensor", lambda e, u2=u2, w8=w8, kt=kt, c=c:
                           e.matmul(
                               cx.psum[2 * (w8 % 2) + c][:, 0:512],
                               wb0[:, 128 * kt:128 * (kt + 1)],
                               outfull[:, 1024 * (8 * u2 + kt) + 512 * c:
                                        1024 * (8 * u2 + kt) + 512 * (c + 1)],
                               start=(kt == 0), stop=(kt == NF - 1)),
                           reads=("l1wb0", "l1of"),
                           writes=(f"ps{2 * (w8 % 2) + c}",),
                           name=f"l1wo{u2}_{w8}_{kt}_{c}")
            for c in range(2):
                em.add("vector", lambda e, u2=u2, w8=w8, c=c: e.tensor_copy(
                    out=s_tm[:, 1024 * (8 * u2 + w8) + 512 * c:
                             1024 * (8 * u2 + w8) + 512 * (c + 1)],
                    in_=cx.psum[2 * (w8 % 2) + c][:, 0:512]),
                    reads=(f"ps{2 * (w8 % 2) + c}",), writes=("l1s",),
                    name=f"l1se{u2}_{w8}_{c}")
    cx.s_tm, cx.s_key = s_tm, "l1s"
    if DEBUG_L1:
        dump_fm(cx, "dbg_of", outfull, NT, "l1of", width=1024)
        dump_fm(cx, "dbg_s", s_tm, NT, "l1s", width=1024)

    em.fence()
    for e_i in range(8):
        for m, (src, dst) in enumerate((("l1f_w1", "l1f_w1c"),
                                        ("l1f_w2", "l1f_w2c"))):
            for s_ in range(16):
                par = (s_ + 16 * m) % 2
                buf = stg[:, 1024 * par:1024 * (par + 1)]
                bk = f"l1cst{par}"
                em.dma("gpsimd", lambda e, e_i=e_i, s_=s_, src=src, buf=buf:
                       e.dma_start(out=buf, in_=dr[src][e_i, s_]),
                       writes=(bk,), name=f"l1cc{e_i}_{m}_{s_}")
                em.dma("scalar", lambda e, e_i=e_i, s_=s_, dst=dst, buf=buf:
                       e.dma_start(out=dr[dst][e_i, s_], in_=buf),
                       reads=(bk,), writes=(dst,), name=f"l1cs{e_i}_{m}_{s_}")
    em.fence()

    aof = C1[:, 0:4 * HALF]
    em.add("vector", lambda e: e.memset(C4[:], 0.0), writes=("l1moe",),
           name="l1moez")
    qf = C1[:, 4 * HALF:8 * HALF]
    mla = C3
    moe = C4
    kvship = cx.xn_tm[:, 4 * HALF:4 * HALF + 2048]
    kst = C3[0:32, 0:4096]
    vst = C3[0:32, 4096:8192]
    vt2 = C3[:, 8192:9216]
    qst = C3[0:32, 9216:11264]
    est = C3[0:32, 11264:11776]
    b2all = cx.sq_junk[0:8, 0:1024]
    lnst2 = cx.xbuf[:, 0:128]

    def load_bias(dram, n, col0, tag):
        em.add("vector", lambda e: e.memset(lnst2, 0.0), writes=("xbufc",),
               name=f"{tag}z")
        em.dma("sync", lambda e, dram=dram, n=n: e.dma_start(
            out=cx.xbuf[0:n, 0:128], in_=dram), writes=("xbufc",),
            name=f"{tag}l")
        em.add("tensor", lambda e: e.transpose(
            cx.psB[:, 0:128], lnst2, ident32x),
            reads=("xbufc",), writes=("psB",), name=f"{tag}t")
        em.add("vector", lambda e, n=n, col0=col0: e.tensor_copy(
            out=bcols[:, col0:col0 + n], in_=cx.psB[:, 0:n]),
            reads=("psB",), writes=("l1bc",), name=f"{tag}e")

    load_bias(dr["l1f_qb"][:], 4, 0, "l1bq")
    load_bias(dr["l1f_kb"][:], 4, 4, "l1bk")
    load_bias(dr["l1f_vb"][:], 4, 8, "l1bv")
    load_bias(dr["l1f_ob"][:], 8, 12, "l1bo")
    em.dma("gpsimd", lambda e: e.dma_start(out=stg[0:1, 0:128],
                                           in_=dr["l1f_rb"][:]),
           writes=("l1stg",), name="l1rbl")
    em.add("tensor", lambda e: e.matmul(
        cx.psB[:, 0:8], cx.ones1[:, :], stg[0:1, 0:8],
        start=True, stop=True),
        reads=("l1stg", "ones1"), writes=("psB",), name="l1rbb")
    em.add("vector", lambda e: e.tensor_copy(
        out=bcols[:, 24:32], in_=cx.psB[:, 0:8]),
        reads=("psB",), writes=("l1bc",), name="l1rbe")

    def ev_bias(dstbuf, dkey, bcol0, tag):
        def ev(f):
            for c in range(4):
                em.add("vector", lambda e, f=f, c=c, dstbuf=dstbuf:
                       e.tensor_scalar(
                           out=dstbuf[:, HALF * f + 512 * c:
                                      HALF * f + 512 * (c + 1)],
                           in0=cx.psum[c][:, 0:512],
                           scalar1=bcols[:, bcol0 + f:bcol0 + f + 1],
                           scalar2=None, op0=ALU.add),
                       reads=(f"ps{c}", "l1bc"), writes=(dkey,),
                       name=f"{tag}{f}_{c}")
        return ev

    proj_p2(cx, dr["l1f_qw"], cx.xn_fm, "xn_fm", (wb0, wb1),
            ("l1wb0", "l1wb1"), 4, ev_bias(qf, "l1fq", 0, "l1fqp"), "l1fq")
    kvr = dr["l1f_kv_in"].rearrange("(f p) t -> f p t", p=128)

    def ev_ship(bcol0, rowoff, tag):
        def ev(f):
            for c in range(4):
                em.add("vector", lambda e, f=f, c=c, bcol0=bcol0:
                       e.tensor_scalar(
                           out=kvship[:, 512 * c:512 * (c + 1)],
                           in0=cx.psum[c][:, 0:512],
                           scalar1=bcols[:, bcol0 + f:bcol0 + f + 1],
                           scalar2=None, op0=ALU.add),
                       reads=(f"ps{c}", "l1bc"), writes=("l1kvs",),
                       name=f"{tag}{f}_{c}")
            em.dma("sync", lambda e, f=f, rowoff=rowoff: e.dma_start(
                out=kvr[rowoff + f], in_=kvship[:]),
                reads=("l1kvs",), writes=("l1fkvin",), name=f"{tag}s{f}")
        return ev

    proj_p2(cx, dr["l1f_kw"], cx.xn_fm, "xn_fm", (wb0, wb1),
            ("l1wb0", "l1wb1"), 4, ev_ship(4, 0, "l1fkp"), "l1fk")
    proj_p2(cx, dr["l1f_vw"], cx.xn_fm, "xn_fm", (wb0, wb1),
            ("l1wb0", "l1wb1"), 4, ev_ship(8, 4, "l1fvp"), "l1fv")
    _cc_pair(em, "AllGather", dr["l1f_kv_in"], dr["l1f_kv_out"],
             ("l1fkvin",), ("l1fkvout",), "l1fkvcc")
    em.fence()

    simb2 = cx.xn_tm[:, :].bitcast(F32)[:, 0:4096]
    pbuf2 = cx.xn_tm[:, 8192:12288]
    pT2 = cx.xn_tm[:, 12288:16384]
    mcol = cx.stat[:, 64:80]
    scol = cx.stat[:, 80:96]
    fkv = dr["l1f_kv_out"]
    for h in range(16):
        em.dma("gpsimd", lambda e, h=h: e.dma_start(
            out=qst[:, :], in_=qf[32 * (h % 4):32 * (h % 4) + 32,
                                  HALF * (h // 4):HALF * (h // 4 + 1)]),
            reads=("l1fq",), writes=("l1qst",), name=f"f1q{h}")
        em.dma("sync", lambda e, h=h: e.dma_start(
            out=kst[:, 0:2048], in_=fkv[32 * h:32 * (h + 1), :]),
            writes=("l1kst",), name=f"f1k0{h}")
        em.dma("sync", lambda e, h=h: e.dma_start(
            out=kst[:, 2048:4096],
            in_=fkv[1024 + 32 * h:1024 + 32 * (h + 1), :]),
            writes=("l1kst",), name=f"f1k1{h}")
        em.dma("gpsimd", lambda e, h=h: e.dma_start(
            out=vst[:, 0:2048], in_=fkv[512 + 32 * h:512 + 32 * (h + 1), :]),
            writes=("l1vst",), name=f"f1v0{h}")
        em.dma("gpsimd", lambda e, h=h: e.dma_start(
            out=vst[:, 2048:4096],
            in_=fkv[1536 + 32 * h:1536 + 32 * (h + 1), :]),
            writes=("l1vst",), name=f"f1v1{h}")
        for half2 in range(2):
            bank = cx.psT[half2]
            for t in range(16):
                em.add("tensor", lambda e, half2=half2, t=t, bank=bank:
                       e.transpose(bank[:, 32 * t:32 * (t + 1)],
                                   vst[:, 2048 * half2 + 128 * t:
                                       2048 * half2 + 128 * (t + 1)],
                                   cx.ident[0:32, 0:32]),
                       reads=("l1vst", "ident"), writes=(f"psT{half2}",),
                       name=f"f1vt{h}_{half2}_{t}")
            em.add("vector", lambda e, half2=half2, bank=bank: e.tensor_copy(
                out=vt2[:, 512 * half2:512 * (half2 + 1)], in_=bank[:, 0:512]),
                reads=(f"psT{half2}",), writes=("l1vt2",),
                name=f"f1ve{h}_{half2}")
        for qt in range(16):
            for c2 in range(8):
                ps = cx.psum[c2 % 2]
                em.add("tensor", lambda e, qt=qt, c2=c2, ps=ps: e.matmul(
                    ps[:, 0:512], qst[:, 128 * qt:128 * (qt + 1)],
                    kst[:, 512 * c2:512 * (c2 + 1)], start=True, stop=True),
                    reads=("l1qst", "l1kst"), writes=(f"ps{c2 % 2}",),
                    name=f"f1sim{h}_{qt}_{c2}")
                em.add("vector", lambda e, c2=c2, ps=ps: e.tensor_scalar(
                    out=simb2[:, 512 * c2:512 * (c2 + 1)], in0=ps[:, 0:512],
                    scalar1=float(-1.0 / np.sqrt(32.0)), scalar2=None,
                    op0=ALU.mult, op1=ALU.min, accum_out=mcol[:, c2:c2 + 1]),
                    reads=(f"ps{c2 % 2}",), writes=("f1simb2", "f1mcol"),
                    name=f"f1sm{h}_{qt}_{c2}")
            em.add("vector", lambda e: e.tensor_reduce(
                out=mcol[:, 15:16], in_=mcol[:, 0:8], axis=AX.X, op=ALU.min),
                reads=("f1mcol",), writes=("f1mcol",), name=f"f1mm{h}_{qt}")
            for c2 in range(8):
                em.add("scalar", lambda e, c2=c2: e.activation(
                    out=pbuf2[:, 512 * c2:512 * (c2 + 1)],
                    in_=simb2[:, 512 * c2:512 * (c2 + 1)], func=AF.Exp,
                    scale=-1.0, bias=mcol[:, 15:16],
                    accum_out=scol[:, c2:c2 + 1]),
                    reads=("f1simb2", "f1mcol"), writes=("f1p", "f1scol"),
                    name=f"f1ex{h}_{qt}_{c2}")
            em.add("vector", lambda e: e.tensor_reduce(
                out=scol[:, 15:16], in_=scol[:, 0:8], axis=AX.X, op=ALU.add),
                reads=("f1scol",), writes=("f1scol",), name=f"f1sr{h}_{qt}")
            em.add("vector", lambda e: e.reciprocal(
                out=scol[:, 14:15], in_=scol[:, 15:16]),
                reads=("f1scol",), writes=("f1scol",), name=f"f1rp{h}_{qt}")
            em.add("vector", lambda e: e.tensor_scalar_mul(
                pbuf2[:, 0:4096], pbuf2[:, 0:4096], scol[:, 14:15]),
                reads=("f1p", "f1scol"), writes=("f1p",), name=f"f1pc{h}_{qt}")
            for tb in range(4):
                bank = cx.psT[tb % 2]
                for t in range(8 * tb, 8 * tb + 8):
                    em.add("tensor", lambda e, t=t, tb=tb, bank=bank:
                           e.transpose(bank[:, 128 * (t - 8 * tb):
                                            128 * (t - 8 * tb + 1)],
                                       pbuf2[:, 128 * t:128 * (t + 1)],
                                       cx.ident[:]),
                           reads=("f1p", "ident"), writes=(f"psT{tb % 2}",),
                           name=f"f1pt{h}_{qt}_{t}")
                em.add("vector", lambda e, tb=tb, bank=bank: e.tensor_copy(
                    out=pT2[:, 1024 * tb:1024 * (tb + 1)], in_=bank[:, 0:1024]),
                    reads=(f"psT{tb % 2}",), writes=("f1pT2",),
                    name=f"f1pe{h}_{qt}_{tb}")
            for t in range(32):
                em.add("tensor", lambda e, t=t: e.matmul(
                    cx.psum[2][0:32, 0:128], vt2[:, 32 * t:32 * (t + 1)],
                    pT2[:, 128 * t:128 * (t + 1)],
                    start=(t == 0), stop=(t == 31)),
                    reads=("l1vt2", "f1pT2"), writes=("ps2",),
                    name=f"f1av{h}_{qt}_{t}")
            em.add("vector", lambda e: e.tensor_copy(
                out=est[:, 0:128], in_=cx.psum[2][0:32, 0:128]),
                reads=("ps2",), writes=("l1est",), name=f"f1ae{h}_{qt}")
            em.dma("gpsimd", lambda e, h=h, qt=qt: e.dma_start(
                out=aof[32 * (h % 4):32 * (h % 4) + 32,
                        HALF * (h // 4) + 128 * qt:
                        HALF * (h // 4) + 128 * (qt + 1)],
                in_=est[:, 0:128]),
                reads=("l1est",), writes=("f1aof",), name=f"f1as{h}_{qt}")
    em.fence()

    def ev_mla(f):
        for c in range(4):
            em.add("vector", lambda e, f=f, c=c: e.tensor_scalar(
                out=mla[:, HALF * f + 512 * c:HALF * f + 512 * (c + 1)],
                in0=cx.psum[c][:, 0:512],
                scalar1=bcols[:, 12 + f:12 + f + 1], scalar2=None,
                op0=ALU.add),
                reads=(f"ps{c}", "l1bc"), writes=("l1mla",),
                name=f"l1mla{f}_{c}")

    proj_p2(cx, dr["l1f_ow"], aof, "f1aof", (wb0, wb1),
            ("l1wb0", "l1wb1"), NF, ev_mla, "l1ow", nkt=4, wwidth=512)

    if DEBUG_L1:
        dump_fm(cx, "dbg_mla", mla, NF, "l1mla")
        dump_fm(cx, "dbg_aof", aof, 4, "f1aof")
    em.dma("gpsimd", lambda e: e.dma_start(out=wb0[:], in_=dr["l1f_rw"][0]),
           writes=("l1wb0",), name="l1rw")
    pr = cx.stat[:, 56:96]
    for i in range(NT):
        for k in range(NF):
            em.add("tensor", lambda e, i=i, k=k: e.matmul(
                cx.psB[:, 0:128],
                mla[:, HALF * k + 128 * i:HALF * k + 128 * (i + 1)],
                wb0[:, 128 * k:128 * (k + 1)],
                start=(k == 0), stop=(k == NF - 1)),
                reads=("l1wb0", "l1mla"), writes=("psB",), name=f"l1ro{i}_{k}")
        P0, EQ, CU = pr[:, 0:8], pr[:, 8:16], pr[:, 16:24]
        F1, F2 = pr[:, 24:32], pr[:, 32:40]
        sc_ = lambda j: pr[:, 40 - 41 + 41 + j - 1:40 + j] if False else None
        m1 = cx.stat[:, 48:49]; m2 = cx.stat[:, 49:50]
        mx = cx.stat[:, 50:51]; nmx = cx.stat[:, 51:52]
        sm = cx.stat[:, 52:53]; rs = cx.stat[:, 53:54]
        dn = cx.stat[:, 54:55]
        em.add("vector", lambda e: e.tensor_tensor(
            out=P0, in0=cx.psB[:, 0:8], in1=bcols[:, 24:32], op=ALU.add),
            reads=("psB", "l1bc"), writes=("l1pr",), name=f"l1rl{i}")
        em.add("vector", lambda e: e.tensor_reduce(
            out=mx, in_=P0, axis=AX.X, op=ALU.max),
            reads=("l1pr",), writes=("l1pr",), name=f"l1rx{i}")
        em.add("vector", lambda e: e.tensor_scalar_mul(nmx, mx, -1.0),
               reads=("l1pr",), writes=("l1pr",), name=f"l1rn{i}")
        em.add("scalar", lambda e: e.activation(
            out=P0, in_=P0, func=AF.Exp, scale=1.0, bias=nmx,
            accum_out=sm), reads=("l1pr",), writes=("l1pr",), name=f"l1rE{i}")
        em.add("vector", lambda e: e.reciprocal(out=rs, in_=sm),
               reads=("l1pr",), writes=("l1pr",), name=f"l1rr{i}")
        em.add("vector", lambda e: e.tensor_scalar_mul(P0, P0, rs),
               reads=("l1pr",), writes=("l1pr",), name=f"l1rp{i}")
        em.add("vector", lambda e: e.tensor_reduce(
            out=m1, in_=P0, axis=AX.X, op=ALU.max),
            reads=("l1pr",), writes=("l1pr",), name=f"l1t1{i}")
        em.add("vector", lambda e: e.tensor_scalar(
            out=EQ, in0=P0, scalar1=m1, scalar2=None, op0=ALU.is_equal),
            reads=("l1pr",), writes=("l1pr",), name=f"l1e1{i}")
        em.add("vector", lambda e: e.tensor_tensor_scan(
            out=CU, data0=cx.gmpat[:, 0:8], data1=EQ,
            initial=cx.gmzero[:, 0:1], op0=ALU.mult, op1=ALU.add),
            reads=("l1pr", "gmpat", "gmzero"), writes=("l1pr",),
            name=f"l1c1{i}")
        em.add("vector", lambda e: e.tensor_scalar(
            out=CU, in0=CU, scalar1=1.0, scalar2=None, op0=ALU.is_equal),
            reads=("l1pr",), writes=("l1pr",), name=f"l1f1a{i}")
        em.add("vector", lambda e: e.tensor_mul(F1, EQ, CU),
               reads=("l1pr",), writes=("l1pr",), name=f"l1f1{i}")
        em.add("vector", lambda e: e.scalar_tensor_tensor(
            out=EQ, in0=F1, scalar=-2.0, in1=P0, op0=ALU.mult, op1=ALU.add),
            reads=("l1pr",), writes=("l1pr",), name=f"l1x{i}")
        em.add("vector", lambda e: e.tensor_reduce(
            out=m2, in_=EQ, axis=AX.X, op=ALU.max),
            reads=("l1pr",), writes=("l1pr",), name=f"l1t2{i}")
        em.add("vector", lambda e: e.tensor_scalar(
            out=F2, in0=EQ, scalar1=m2, scalar2=None, op0=ALU.is_equal),
            reads=("l1pr",), writes=("l1pr",), name=f"l1e2{i}")
        em.add("vector", lambda e: e.tensor_tensor_scan(
            out=CU, data0=cx.gmpat[:, 0:8], data1=F2,
            initial=cx.gmzero[:, 0:1], op0=ALU.mult, op1=ALU.add),
            reads=("l1pr", "gmpat", "gmzero"), writes=("l1pr",),
            name=f"l1c2{i}")
        em.add("vector", lambda e: e.tensor_scalar(
            out=CU, in0=CU, scalar1=1.0, scalar2=None, op0=ALU.is_equal),
            reads=("l1pr",), writes=("l1pr",), name=f"l1f2a{i}")
        em.add("vector", lambda e: e.tensor_mul(F2, F2, CU),
               reads=("l1pr",), writes=("l1pr",), name=f"l1f2{i}")
        em.add("vector", lambda e: e.tensor_add(dn, m1, m2),
               reads=("l1pr",), writes=("l1pr",), name=f"l1dn{i}")
        em.add("vector", lambda e: e.tensor_scalar_add(dn, dn, 1e-8),
               reads=("l1pr",), writes=("l1pr",), name=f"l1dne{i}")
        em.add("vector", lambda e: e.reciprocal(out=rs, in_=dn),
               reads=("l1pr",), writes=("l1pr",), name=f"l1dr{i}")
        em.add("vector", lambda e: e.tensor_scalar(
            out=F1, in0=F1, scalar1=m1, scalar2=rs, op0=ALU.mult,
            op1=ALU.mult), reads=("l1pr",), writes=("l1pr",),
            name=f"l1w1{i}")
        em.add("vector", lambda e: e.tensor_scalar(
            out=F2, in0=F2, scalar1=m2, scalar2=rs, op0=ALU.mult,
            op1=ALU.mult), reads=("l1pr",), writes=("l1pr",),
            name=f"l1w2{i}")
        em.add("vector", lambda e, i=i: e.tensor_tensor(
            out=combs[:, 8 * i:8 * (i + 1)], in0=F1, in1=F2, op=ALU.add),
            reads=("l1pr",), writes=("l1combs",), name=f"l1cb{i}")
    em.fence()

    em.fence()
    h1r = qf
    w2res = cx.xn_tm
    b1c = bcols[:, 32:48]
    for e_i in range(8):
        em.add("vector", lambda e: e.memset(lnst2, 0.0), writes=("xbufc",),
               name=f"l1b1z{e_i}")
        em.dma("sync", lambda e, e_i=e_i: e.dma_start(
            out=cx.xbuf[0:16, 0:128], in_=dr["l1f_b1"][e_i]),
            writes=("xbufc",), name=f"l1b1l{e_i}")
        em.add("tensor", lambda e: e.transpose(
            cx.psB[:, 0:128], lnst2, ident32x),
            reads=("xbufc",), writes=("psB",), name=f"l1b1t{e_i}")
        em.add("vector", lambda e: e.tensor_copy(
            out=b1c, in_=cx.psB[:, 0:16]),
            reads=("psB",), writes=("l1bc",), name=f"l1b1e{e_i}")
        for kt in range(16):
            em.dma("scalar", lambda e, e_i=e_i, kt=kt: e.dma_start(
                out=w2res[:, 1024 * kt:1024 * (kt + 1)],
                in_=dr["l1f_w2c"][e_i, kt]),
                reads=("l1f_w2c",), writes=("l1w2r",), name=f"l1w2l{e_i}_{kt}")
        for r in range(4):
            t0 = 512 * r
            for f in range(16):
                wb = (wb0, wb1)[f % 2]
                wk = f"l1wb{f % 2}"
                em.dma("sync", lambda e, e_i=e_i, f=f, wb=wb: e.dma_start(
                    out=wb[:], in_=dr["l1f_w1c"][e_i, f]),
                    reads=("l1f_w1c",), writes=(wk,),
                    name=f"l1h1w{e_i}_{r}_{f}")
                ps = cx.psum[f % 2]
                for k in range(NF):
                    em.add("tensor", lambda e, f=f, k=k, wb=wb, t0=t0, ps=ps:
                           e.matmul(ps[:, 0:512],
                                    wb[:, 128 * k:128 * (k + 1)],
                                    mla[:, HALF * k + t0:HALF * k + t0 + 512],
                                    start=(k == 0), stop=(k == NF - 1)),
                           reads=(wk, "l1mla"), writes=(f"ps{f % 2}",),
                           name=f"l1h1m{e_i}_{r}_{f}_{k}")
                em.add("scalar", lambda e, f=f, ps=ps: e.activation(
                    out=h1r[:, 512 * f:512 * (f + 1)], in_=ps[:, 0:512],
                    func=AF.Gelu, bias=b1c[:, f:f + 1], scale=1.0),
                    reads=(f"ps{f % 2}", "l1bc"), writes=("l1h1",),
                    name=f"l1h1g{e_i}_{r}_{f}")
            for tt in range(4):
                gtile = 4 * r + tt
                pd0 = cx.psum[2 * (tt % 2)]
                pd1 = cx.psum[2 * (tt % 2) + 1]
                for kt in range(16):
                    for c, pd in ((0, pd0), (1, pd1)):
                        em.add("tensor",
                               lambda e, kt=kt, tt=tt, c=c, pd=pd: e.matmul(
                                   pd[:, 0:512],
                                   h1r[:, 512 * kt + 128 * tt:
                                       512 * kt + 128 * (tt + 1)],
                                   w2res[:, 1024 * kt + 512 * c:
                                         1024 * kt + 512 * (c + 1)],
                                   start=(kt == 0), stop=(kt == 15)),
                               reads=("l1w2r", "l1h1"),
                               writes=(f"ps{2 * (tt % 2) + c}",),
                               name=f"l1h2m{e_i}_{r}_{kt}_{tt}_{c}")
                for c, pd in ((0, pd0), (1, pd1)):
                    em.add("vector",
                           lambda e, gtile=gtile, c=c, pd=pd, e_i=e_i, tt=tt:
                           e.scalar_tensor_tensor(
                               out=moe[:, 1024 * gtile + 512 * c:
                                       1024 * gtile + 512 * (c + 1)],
                               in0=pd[:, 0:512],
                               scalar=combs[:, 8 * gtile + e_i:
                                            8 * gtile + e_i + 1],
                               in1=moe[:, 1024 * gtile + 512 * c:
                                       1024 * gtile + 512 * (c + 1)],
                               op0=ALU.mult, op1=ALU.add),
                           reads=(f"ps{2 * (tt % 2) + c}", "l1combs",
                                  "l1moe"),
                           writes=("l1moe",),
                           name=f"l1me{e_i}_{r}_{tt}_{c}")
    em.fence()
    em.dma("gpsimd", lambda e: e.dma_start(
        out=b2all[:, :], in_=dr["l1f_b2"].rearrange("e h p -> e (h p)")),
        writes=("l1b2a",), name="l1b2l")
    for i in range(NT):
        bank = cx.psT[i % 2]
        em.add("vector", lambda e, i=i: e.tensor_copy(
            out=stg[:, 0:8], in_=combs[:, 8 * i:8 * (i + 1)]),
            reads=("l1combs",), writes=("l1stg",), name=f"l1cbc{i}")
        em.add("tensor", lambda e, i=i, bank=bank: e.transpose(
            bank[0:8, 0:128], stg[:, 0:8], cx.ident[:]),
            reads=("l1stg", "ident"), writes=(f"psT{i % 2}",),
            name=f"l1cbt{i}")
        em.add("vector", lambda e, i=i, bank=bank: e.tensor_copy(
            out=est[0:8, 0:128], in_=bank[0:8, 0:128]),
            reads=(f"psT{i % 2}",), writes=("l1est",), name=f"l1cbe{i}")
        for c in range(2):
            em.add("tensor", lambda e, i=i, c=c: e.matmul(
                cx.psB[:, 0:512], est[0:8, 0:128],
                b2all[:, 512 * c:512 * (c + 1)], start=True, stop=True),
                reads=("l1est", "l1b2a"), writes=("psB",),
                name=f"l1b2m{i}_{c}")
            em.add("vector", lambda e, i=i, c=c: e.tensor_add(
                moe[:, 1024 * i + 512 * c:1024 * i + 512 * (c + 1)],
                moe[:, 1024 * i + 512 * c:1024 * i + 512 * (c + 1)],
                cx.psB[:, 0:512]),
                reads=("psB", "l1moe"), writes=("l1moe",),
                name=f"l1b2a{i}_{c}")
    em.fence()
    if DEBUG_L1:
        dump_fm(cx, "dbg_moe", moe, NT, "l1moe", width=1024)
        dump_fm(cx, "dbg_combs", combs, 1, "l1combs", width=128)
    xn_tm2 = cx.xn_tm
    transpose_fm_to_tm(cx, cx.xn_fm, xn_tm2, "xn_fm", "l1xnt", NT, NF,
                       tag="l1xr")
    g_tm = mla
    for i in range(NT):
        sl = slice(1024 * i, 1024 * (i + 1))
        em.add("vector", lambda e, sl=sl: e.scalar_tensor_tensor(
            out=stg[:, 0:1024], in0=moe[:, sl], scalar=0.3,
            in1=xn_tm2[:, sl], op0=ALU.mult, op1=ALU.add),
            reads=("l1moe", "l1xnt"), writes=("l1stg",), name=f"l1gm{i}")
        em.add("scalar", lambda e, sl=sl: e.activation(
            out=g_tm[:, sl], in_=stg[:, 0:1024], func=AF.Silu),
            reads=("l1stg",), writes=("l1g",), name=f"l1gs{i}")
    cx.g_tm, cx.g_key = g_tm, "l1g"
    em.fence()



DEBUG_L3 = False
DEBUG_L1 = False



def prep_l0_weights(inputs, h):
    w = {}
    w["l0_gamma"] = np.ascontiguousarray(
        np.asarray(inputs["L0_attn_gamma_h"]).reshape(8, 128))
    w["l0_lng"] = np.ascontiguousarray(
        np.asarray(inputs["L0_attn_ln_g"]).reshape(8, 128))
    w["l0_lnb"] = np.ascontiguousarray(
        np.asarray(inputs["L0_attn_ln_b"]).reshape(8, 128))
    w["l0_wqkv"] = fmpack(np.asarray(inputs["L0_attn_wqkv_l"]))
    w["l0_wol"] = slabpack(np.asarray(inputs["L0_attn_wo_l"]))
    w["l0_wqh"] = fmpack(np.asarray(inputs["L0_attn_wq_h"]))
    w["l0_wkvh"] = fmpack(np.asarray(inputs["L0_attn_wkv_h"]))
    w["l0_woh"] = slabpack(np.asarray(inputs["L0_attn_wo_h"]))
    w["l0_rt"] = np.stack([np.asarray(inputs["L0_attn_rt_q"]).reshape(8, 128),
                           np.asarray(inputs["L0_attn_rt_kv"]).reshape(8, 128)])
    w["l0f_wg"] = slabpack(np.asarray(inputs["L0_ffn_w_gate"]))
    w["l0f_wu"] = slabpack(np.asarray(inputs["L0_ffn_w_up"]))
    w["l0f_wd"] = slabpack(np.asarray(inputs["L0_ffn_w_down"]))
    w["l0_flag8"] = np.full((1, 8), float(h), np.float32)
    hi = np.zeros(128, np.uint32)
    base = 2048 * h
    for p in range(64):
        hi[p] = (base - 64 + p) % 4096
        hi[64 + p] = (base + 2048 + p) % 4096
    w["l0_hidx"] = np.ascontiguousarray(hi.reshape(128, 1))
    return w


def declare_l0_dram(nc):
    d = {}
    for nm, sh, dt in (("l0_gamma", [8, 128], F32), ("l0_lng", [8, 128], F32),
                       ("l0_lnb", [8, 128], F32),
                       ("l0_wqkv", [12, 128, 1024], F32),
                       ("l0_wol", [4, 128, 1024], F32),
                       ("l0_wqh", [4, 128, 1024], F32),
                       ("l0_wkvh", [8, 128, 1024], F32),
                       ("l0_woh", [4, 128, 1024], F32),
                       ("l0_rt", [2, 8, 128], F32),
                       ("l0f_wg", [22, 128, 1024], F32),
                       ("l0f_wu", [22, 128, 1024], F32),
                       ("l0f_wd", [8, 128, 2816], F32),
                       ("l0_flag8", [1, 8], F32),
                       ("l0_hidx", [128, 1], U32)):
        d[nm] = nc.dram_tensor(nm, sh, dt, kind="ExternalInput").ap()
    d["l0_xn_in"] = nc.dram_tensor("l0_xn_in", [2048, 1024], BF16).ap()
    d["l0_xn_out"] = nc.dram_tensor("l0_xn_out", [4096, 1024], BF16).ap()
    d["l0_sc_in"] = nc.dram_tensor("l0_sc_in", [128, 32], F32).ap()
    d["l0_sc_out"] = nc.dram_tensor("l0_sc_out", [256, 32], F32).ap()
    d["l0_listq"] = nc.dram_tensor("l0_listq", [4096, 1], U32).ap()
    d["l0_listkv"] = nc.dram_tensor("l0_listkv", [4096, 1], U32).ap()
    d["l0_hqT"] = nc.dram_tensor("l0_hqT", [22, 128, 2048], BF16).ap()
    d["l0_ho_in"] = nc.dram_tensor("l0_ho_in", [512, 1024], BF16).ap()
    d["l0_ho_out"] = nc.dram_tensor("l0_ho_out", [1024, 1024], BF16).ap()
    return d


def emit_fp4_quant(cx, src_ap, dst_ap, rkeys, wkeys, width, scr, tag,
                   eng="vector"):
    em = cx.em
    nb = width // 16
    K = (scr["key"],)
    mx, sc, rsc = scr["mx"], scr["sc"], scr["rsc"]
    v, q, b, t = scr["v"], scr["q"], scr["b"], scr["t"]
    src3 = src_ap.rearrange("p (n s) -> p n s", s=16)
    E = eng
    em.add(E, lambda e: e.tensor_reduce(
        out=mx[:, 0:nb], in_=src3, axis=AX.X, op=ALU.max,
        apply_absolute_value=True),
        reads=rkeys, writes=K, name=f"{tag}mx")
    em.add(E, lambda e: e.tensor_scalar(
        out=sc[:, 0:nb], in0=mx[:, 0:nb], scalar1=1.0 / 6.0, scalar2=1e-12,
        op0=ALU.mult, op1=ALU.max),
        reads=K, writes=K, name=f"{tag}sc")
    em.add("vector", lambda e: e.reciprocal(out=rsc[:, 0:nb], in_=sc[:, 0:nb]),
           reads=K, writes=K, name=f"{tag}rs")
    em.add(E, lambda e: e.tensor_tensor(
        out=v[:, 0:width].rearrange("p (n s) -> p n s", s=16), in0=src3,
        in1=rsc[:, 0:nb].unsqueeze(-1).to_broadcast([128, nb, 16]),
        op=ALU.mult),
        reads=rkeys + K, writes=K, name=f"{tag}v")
    em.add(E, lambda e: e.tensor_scalar_mul(q[:, 0:width], v[:, 0:width], 2.0),
           reads=K, writes=K, name=f"{tag}a1")
    em.add(E, lambda e: e.tensor_copy(out=q[:, 0:width].bitcast(I32),
                                      in_=q[:, 0:width]),
           reads=K, writes=K, name=f"{tag}a2")
    em.add(E, lambda e: e.tensor_scalar(
        out=q[:, 0:width], in0=q[:, 0:width].bitcast(I32), scalar1=0.5,
        scalar2=None, op0=ALU.mult),
        reads=K, writes=K, name=f"{tag}a3")
    em.add(E, lambda e: e.tensor_copy(out=b[:, 0:width].bitcast(I32),
                                      in_=v[:, 0:width]),
           reads=K, writes=K, name=f"{tag}b1")
    em.add(E, lambda e: e.tensor_scalar(
        out=b[:, 0:width], in0=b[:, 0:width].bitcast(I32), scalar1=-4.0,
        scalar2=4.0, op0=ALU.max, op1=ALU.min),
        reads=K, writes=K, name=f"{tag}b2")
    em.add(E, lambda e: e.tensor_scalar(
        out=t[:, 0:width], in0=v[:, 0:width], scalar1=5.0, scalar2=2.0,
        op0=ALU.is_gt, op1=ALU.mult),
        reads=K, writes=K, name=f"{tag}b3")
    em.add(E, lambda e: e.tensor_add(b[:, 0:width], b[:, 0:width],
                                     t[:, 0:width]),
           reads=K, writes=K, name=f"{tag}b4")
    em.add(E, lambda e: e.tensor_scalar(
        out=t[:, 0:width], in0=v[:, 0:width], scalar1=-5.0, scalar2=-2.0,
        op0=ALU.is_lt, op1=ALU.mult),
        reads=K, writes=K, name=f"{tag}b5")
    em.add(E, lambda e: e.tensor_add(b[:, 0:width], b[:, 0:width],
                                     t[:, 0:width]),
           reads=K, writes=K, name=f"{tag}b6")
    em.add(E, lambda e: e.scalar_tensor_tensor(
        out=t[:, 0:width], in0=v[:, 0:width], scalar=-1.0, in1=v[:, 0:width],
        op0=ALU.mult, op1=ALU.max),
        reads=K, writes=K, name=f"{tag}s1")
    em.add(E, lambda e: e.tensor_scalar(
        out=t[:, 0:width], in0=t[:, 0:width], scalar1=2.0, scalar2=None,
        op0=ALU.is_le),
        reads=K, writes=K, name=f"{tag}s2")
    em.add(E, lambda e: e.tensor_sub(q[:, 0:width], q[:, 0:width],
                                     b[:, 0:width]),
           reads=K, writes=K, name=f"{tag}s3")
    em.add(E, lambda e: e.tensor_mul(q[:, 0:width], q[:, 0:width],
                                     t[:, 0:width]),
           reads=K, writes=K, name=f"{tag}s4")
    em.add(E, lambda e: e.tensor_add(q[:, 0:width], q[:, 0:width],
                                     b[:, 0:width]),
           reads=K, writes=K, name=f"{tag}s5")
    em.add(E, lambda e: e.tensor_tensor(
        out=dst_ap.rearrange("p (n s) -> p n s", s=16),
        in0=q[:, 0:width].rearrange("p (n s) -> p n s", s=16),
        in1=sc[:, 0:nb].unsqueeze(-1).to_broadcast([128, nb, 16]),
        op=ALU.mult),
        reads=K, writes=wkeys, name=f"{tag}o")


def emit_fp4_quant_chunked(cx, src_ap, dst_ap, rkeys, wkeys, width, scr, tag,
                           chunk=512, eng="vector"):
    c0 = 0
    j = 0
    while c0 < width:
        w = min(chunk, width - c0)
        emit_fp4_quant(cx, src_ap[:, c0:c0 + w], dst_ap[:, c0:c0 + w],
                       rkeys, wkeys, w, scr, f"{tag}_{j}", eng=eng)
        c0 += w
        j += 1


def build_l0(cx, ctx, dr):
    from concourse.bass import IndirectOffsetOnAxis
    nc, em = cx.nc, cx.em
    sb = lambda n, s, d: ctx.enter_context(nc.sbuf_tensor(n, s, d))
    SC = 0.125
    KW = 2304

    s_tm = sb("l0_s", [128, NT * 1024], BF16)
    xqt = sb("l0_xqt", [128, NT * 1024], BF16)
    B1 = sb("l0_B1", [128, NF * HALF], BF16)
    B2 = sb("l0_B2", [128, NF * HALF], BF16)
    bcols = sb("l0_bcols", [128, 48], F32)
    hidx_s = sb("l0_hidxs", [128, 1], U32)
    flag_s = sb("l0_flag", [1, 8], BF16)
    A1 = cx.xn_tm
    A2 = cx.xn_fm
    xb = cx.xbuf

    scfull = xb[:, 0:64]
    vcat = xb[:, 128:160]
    gidxf = xb[:, 192:224]
    selA = xb[:, 256:288]
    self_ = xb[:, 320:352]
    tie = xb[:, 384:416]
    posf = xb[:, 448:480]
    posu = xb[:, 512:544].bitcast(I32)
    work = xb[:, 576:608]
    gidu = xb[:, 640:672].bitcast(I32)
    SCL = xb[:, 704:768]
    identf = xb[:, 768:896]
    kvlu = xb[:, 896:904].bitcast(I32)
    qlu = xb[:, 904:912].bitcast(I32)
    qlf = xb[:, 912:920]
    qgf = xb[:, 920:924]
    qgl = xb[:, 924:928]
    qgu = xb[:, 928:932].bitcast(I32)
    qgl8 = xb[:, 932:940]
    crow = xb[:, 960:1344]
    tcol1 = xb[:, 1364:1365]
    tcol2 = xb[:, 1365:1366]
    pcol = xb[:, 1366:1367]
    mc_s = [xb[:, 1344 + j:1345 + j] for j in range(4)]
    sc_s = [xb[:, 1348 + j:1349 + j] for j in range(4)]
    rc_s = [xb[:, 1352 + j:1353 + j] for j in range(4)]
    hmc = [xb[:, 1356:1359], xb[:, 1368:1371]]
    hsc = [xb[:, 1359:1363], xb[:, 1371:1375]]
    ptb = [xb[:, 1400:1592].bitcast(BF16), xb[:, 1592:1784].bitcast(BF16),
           xb[:, 1784:1976].bitcast(BF16), None]
    selT = xb[0:32, 960:1088]
    Wsc = xb[0:32, 1088:1216]
    scr3 = xb[0:32, 1216:1344]
    scr4 = xb[0:32, 1472:1600]
    iotap = xb[0:32, 1600:1728]
    zcol = xb[0:32, 1728:1729]
    coloff = xb[0:32, 1729:1730]
    c128c = xb[0:32, 1730:1731]
    pcc = xb[0:32, 1731:1732]
    LTf = xb[0:32, 1760:1792]
    onesrow = xb[0:1, 1912:2040]
    onescol = xb[:, 2040:2041]
    f2048 = bcols[:, 45:46]
    fcol = bcols[:, 47:48]
    finv = bcols[:, 46:47]

    def scol(i):
        return SCL[:, i:i + 1]

    def dump_sb(name, ap, rkeys, shape=None):
        if not DEBUG_L0:
            return
        dt_ = ap.tensor.dtype if hasattr(ap, "tensor") else BF16
        dd = nc.dram_tensor(name, [128, ap.shape[-1]], dt_,
                            kind="ExternalOutput").ap()
        em.out_dma("sync", lambda e: e.dma_start(out=dd, in_=ap),
                   reads=tuple(rkeys), name=f"dmp_{name}")
    LO, HI, MID, CNT, GG, GI, T1, T2, NHI, KEEP, JLO, JHI, CNTP, TOT = range(14)
    SM = ("l0sm",)

    em.dma("gpsimd", lambda e: e.dma_start(out=flag_s[:], in_=dr["l0_flag8"][:]),
           writes=("l0flag",), name="l0flagld")
    em.add("tensor", lambda e: e.matmul(
        cx.psB[:, 0:8], cx.ones1[:, :], flag_s[:, 0:8],
        start=True, stop=True),
        reads=("l0flag", "ones1"), writes=("psB",), name="l0fb")
    em.add("vector", lambda e: e.tensor_copy(out=fcol, in_=cx.psB[:, 0:1]),
           reads=("psB",), writes=("l0bc",), name="l0fc")
    em.add("vector", lambda e: e.tensor_scalar(
        out=finv, in0=fcol, scalar1=-1.0, scalar2=1.0, op0=ALU.mult,
        op1=ALU.add), reads=("l0bc",), writes=("l0bc",), name="l0fi")
    em.add("vector", lambda e: e.tensor_scalar(
        out=f2048, in0=fcol, scalar1=2048.0, scalar2=None, op0=ALU.mult),
        reads=("l0bc",), writes=("l0bc",), name="l0f2k")
    em.add("vector", lambda e: e.tensor_scalar(
        out=identf, in0=cx.ident_i[:], scalar1=0.0, scalar2=None,
        op0=ALU.is_equal), reads=("ident_i",), writes=SM, name="l0idf0")

    lnst2 = xb[:, 0:128]

    def load_cols(dram, n, col0, tag):
        em.add("vector", lambda e: e.memset(lnst2, 0.0), reads=SM, writes=SM,
               name=f"{tag}z")
        em.dma("sync", lambda e, dram=dram, n=n: e.dma_start(
            out=lnst2[0:n, 0:128], in_=dram), reads=SM, writes=SM,
            name=f"{tag}l")
        em.add("tensor", lambda e: e.transpose(
            cx.psB[:, 0:128], lnst2, identf),
            reads=SM, writes=("psB",), name=f"{tag}t")
        em.add("vector", lambda e, n=n, col0=col0: e.tensor_copy(
            out=bcols[:, col0:col0 + n], in_=cx.psB[:, 0:n]),
            reads=("psB",), writes=("l0bc",), name=f"{tag}e")

    load_cols(dr["l0_lng"][:], 8, 0, "l0lg")
    load_cols(dr["l0_lnb"][:], 8, 8, "l0lb")
    load_cols(dr["l0_gamma"][:], 8, 16, "l0gm")
    load_cols(dr["l0_rt"][0], 8, 24, "l0rq")
    load_cols(dr["l0_rt"][1], 8, 32, "l0rk")
    rtb = B2[:, 14464:14480]
    em.add("vector", lambda e: e.tensor_copy(out=rtb, in_=bcols[:, 24:40]),
           reads=("l0bc",), writes=("l0rtb",), name="l0rtb")

    xnr = dr["l0_xn_in"].rearrange("(i p) d -> i p d", p=128)
    for i in range(NT):
        em.dma("sync", lambda e, i=i: e.dma_start(
            out=xnr[i], in_=cx.xn_tm[:, 1024 * i:1024 * (i + 1)]),
            reads=("xn_tm",), writes=("l0xnin",), name=f"l0xs{i}")
    _cc_pair(em, "AllGather", dr["l0_xn_in"], dr["l0_xn_out"],
             ("l0xnin",), ("l0xnout",), "l0xncc")
    cc1_op = em.ops[-1]
    sc_own = B2[:, 15120:15184].bitcast(F32)
    for rt in range(2):
        for i in range(NT):
            for k in range(NF):
                em.add("tensor", lambda e, rt=rt, i=i, k=k: e.matmul(
                    cx.psB[:, 0:1],
                    cx.xn_fm[:, HALF * k + 128 * i:HALF * k + 128 * (i + 1)],
                    rtb[:, 8 * rt + k:8 * rt + k + 1],
                    start=(k == 0), stop=(k == NF - 1)),
                    reads=("xn_fm", "l0rtb"), writes=("psB",),
                    name=f"l0scm{rt}_{i}_{k}")
            em.add("vector", lambda e, rt=rt, i=i: e.tensor_copy(
                out=sc_own[:, 16 * rt + i:16 * rt + i + 1],
                in_=cx.psB[:, 0:1]),
                reads=("psB",), writes=("l0sco",), name=f"l0sce{rt}_{i}")
    em.dma("sync", lambda e: e.dma_start(out=dr["l0_sc_in"][:], in_=sc_own),
           reads=("l0sco",), writes=("l0scin",), name="l0scs")
    _cc_pair(em, "AllGather", dr["l0_sc_in"], dr["l0_sc_out"],
             ("l0scin",), ("l0scout",), "l0sccc")
    em.ops[-1].deps.add(cc1_op)

    st = cx.stat
    k16 = NT
    nmu = st[:, 5 * k16:6 * k16]
    rstd = st[:, 3 * k16:4 * k16]
    xr0 = cx.cur_x.rearrange("(i p) d -> i p d", p=128)
    xdep0 = (cx.x_rkey,) if getattr(cx, "x_rkey", None) else ()
    qsc = {
        "key": "l0xqS",
        "mx": cx.sgbuf[:, 0:64], "sc": cx.sgbuf[:, 64:128],
        "rsc": cx.sgbuf[:, 128:192],
        "v": s_tm[:, 0:2048].bitcast(F32),
        "q": s_tm[:, 2048:4096].bitcast(F32),
        "b": s_tm[:, 4096:6144].bitcast(F32),
        "t": s_tm[:, 6144:8192].bitcast(F32),
    }
    xnf32 = s_tm[:, 8192:10240].bitcast(F32)
    for i in range(NT):
        xbb = xb[:, 1024 * (i % 2):1024 * (i % 2 + 1)]
        bk = f"xbuf{i % 2}"
        em.dma("sync", lambda e, i=i, xbb=xbb: e.dma_start(
            out=xbb, in_=xr0[i]), reads=xdep0, writes=(bk,),
            name=f"l0q_ld{i}")
        em.add("vector", lambda e, xbb=xbb, i=i: e.tensor_scalar(
            out=xnf32, in0=xbb, scalar1=nmu[:, i:i + 1],
            scalar2=rstd[:, i:i + 1], op0=ALU.add, op1=ALU.mult),
            reads=(bk, "stat"), writes=("l0xqS",), name=f"l0q_n{i}")
        emit_fp4_quant_chunked(
            cx, xnf32, xqt[:, 1024 * i:1024 * (i + 1)],
            ("l0xqS",), ("l0xq",), 1024, qsc, f"l0xq{i}", chunk=1024)
    em.fence()

    layernorm_tm(cx, lambda i: cx.xn_tm[:, 1024 * i:1024 * (i + 1)],
                 lambda i: cx.xn_tm[:, 1024 * i:1024 * (i + 1)],
                 NT, "xn_tm", "xn_tm", tag="l0iln")
    em.fence()
    xl_fm = A2
    transpose_tm_to_fm(cx, cx.xn_tm, xl_fm, "xn_tm", "l0xlf", NT, NF,
                       tag="l0xtr")
    for f in range(NF):
        em.add("vector", lambda e, f=f: e.tensor_scalar(
            out=xl_fm[:, HALF * f:HALF * (f + 1)],
            in0=xl_fm[:, HALF * f:HALF * (f + 1)],
            scalar1=bcols[:, f:f + 1], scalar2=bcols[:, 8 + f:8 + f + 1],
            op0=ALU.mult, op1=ALU.add),
            reads=("l0xlf", "l0bc"), writes=("l0xlf",), name=f"l0af{f}")
    halo_xn = B2[:, 9216:10240]
    em.dma("sync", lambda e: e.dma_start(out=hidx_s[:], in_=dr["l0_hidx"]),
           writes=("l0hidx",), name="l0hixl")
    em.dma("gpsimd", lambda e: e.indirect_dma_start(
        out=halo_xn, out_offset=None, in_=dr["l0_xn_out"],
        in_offset=IndirectOffsetOnAxis(ap=hidx_s[:], axis=0)),
        reads=("l0xnout", "l0hidx"), writes=("l0halo",), name="l0hld")
    layernorm_tm(cx, lambda i: halo_xn, lambda i: halo_xn, 1,
                 "l0halo", "l0halo", tag="l0hln")
    halo_fm = B2[:, 10240:11264]
    for f in range(NF):
        bank = cx.psT[f % 2]
        em.add("tensor", lambda e, f=f, bank=bank: e.transpose(
            bank[:, 0:128], halo_xn[:, 128 * f:128 * (f + 1)], cx.ident[:]),
            reads=("l0halo", "ident"), writes=(f"psT{f % 2}",),
            name=f"l0ht{f}")
        em.add("vector", lambda e, f=f, bank=bank: e.tensor_scalar(
            out=halo_fm[:, 128 * f:128 * (f + 1)], in0=bank[:, 0:128],
            scalar1=bcols[:, f:f + 1], scalar2=bcols[:, 8 + f:8 + f + 1],
            op0=ALU.mult, op1=ALU.add),
            reads=(f"psT{f % 2}", "l0bc"), writes=("l0hfm",), name=f"l0he{f}")
    em.fence()

    q_l = B1[:, 0:4 * HALF]
    ao_fm = B1[:, 4 * HALF:8 * HALF]
    k_ext = B2[:, 0:9216]
    v_ext = A1
    wba = B2[:, 11264:12288]
    wbb = B2[:, 12288:13312]
    vhstage = B2[:, 14480:14608]
    vstg = B2[:, 14608:15120]
    for r in range(4):
        em.add("vector", lambda e, r=r: e.memset(
            k_ext[:, KW * r:KW * r + 64], 0.0), writes=("l0ke",),
            name=f"l0kz{r}")
        em.add("vector", lambda e, r=r: e.memset(
            k_ext[:, KW * r + 2240:KW * r + 2304], 0.0), writes=("l0ke",),
            name=f"l0kz2{r}")
    em.add("vector", lambda e: e.memset(v_ext[:, 0:512], 0.0),
           writes=("l0ve",), name="l0vz0")
    em.add("vector", lambda e: e.memset(v_ext[:, 512 * 17:512 * 18], 0.0),
           writes=("l0ve",), name="l0vz17")

    for f in range(12):
        wb = wba if f % 2 == 0 else wbb
        wk = "l0wbA" if f % 2 == 0 else "l0wbB"
        em.dma("gpsimd", lambda e, f=f, wb=wb: e.dma_start(
            out=wb, in_=dr["l0_wqkv"][f]), writes=(wk,), name=f"l0w{f}")
        for c in range(4):
            for k in range(NF):
                em.add("tensor", lambda e, wb=wb, wk=wk, k=k, c=c: e.matmul(
                    cx.psum[c][:, 0:512], wb[:, 128 * k:128 * (k + 1)],
                    xl_fm[:, HALF * k + 512 * c:HALF * k + 512 * (c + 1)],
                    start=(k == 0), stop=(k == NF - 1)),
                    reads=(wk, "l0xlf"), writes=(f"ps{c}",),
                    name=f"l0pm{f}_{c}_{k}")
        if f >= 4:
            for k in range(NF):
                em.add("tensor", lambda e, wb=wb, wk=wk, k=k: e.matmul(
                    cx.psB[:, 0:128], wb[:, 128 * k:128 * (k + 1)],
                    halo_fm[:, 128 * k:128 * (k + 1)],
                    start=(k == 0), stop=(k == NF - 1)),
                    reads=(wk, "l0hfm"), writes=("psB",),
                    name=f"l0ph{f}_{k}")
        if f < 4:
            for c in range(4):
                em.add("vector", lambda e, f=f, c=c: e.tensor_copy(
                    out=q_l[:, HALF * f + 512 * c:HALF * f + 512 * (c + 1)],
                    in_=cx.psum[c][:, 0:512]),
                    reads=(f"ps{c}",), writes=("l0ql",), name=f"l0qe{f}_{c}")
        elif f < 8:
            r = f - 4
            for c in range(4):
                em.add("vector", lambda e, r=r, c=c: e.tensor_copy(
                    out=k_ext[:, KW * r + 128 + 512 * c:
                              KW * r + 128 + 512 * (c + 1)],
                    in_=cx.psum[c][:, 0:512]),
                    reads=(f"ps{c}",), writes=("l0ke",), name=f"l0kev{r}_{c}")
            em.add("vector", lambda e, r=r: e.tensor_copy(
                out=k_ext[:, KW * r + 64:KW * r + 128],
                in_=cx.psB[:, 0:64]),
                reads=("psB",), writes=("l0ke",), name=f"l0khl{r}")
            em.add("vector", lambda e, r=r: e.tensor_copy(
                out=k_ext[:, KW * r + 2176:KW * r + 2240],
                in_=cx.psB[:, 64:128]),
                reads=("psB",), writes=("l0ke",), name=f"l0khr{r}")
        else:
            r = f - 8
            for c in range(4):
                em.add("vector", lambda e, c=c: e.tensor_copy(
                    out=vstg, in_=cx.psum[c][:, 0:512]),
                    reads=(f"ps{c}",), writes=("l0vstg",), name=f"l0vs{r}_{c}")
                for u in range(4):
                    em.add("tensor", lambda e, u=u: e.transpose(
                        cx.psT[0][:, 128 * u:128 * (u + 1)],
                        vstg[:, 128 * u:128 * (u + 1)], cx.ident[:]),
                        reads=("l0vstg", "ident"), writes=("psT0",),
                        name=f"l0vt{r}_{c}_{u}")
                for u in range(4):
                    em.add("vector", lambda e, r=r, c=c, u=u: e.tensor_copy(
                        out=v_ext[:, 512 * (1 + 4 * c + u) + 128 * r:
                                  512 * (1 + 4 * c + u) + 128 * (r + 1)],
                        in_=cx.psT[0][:, 128 * u:128 * (u + 1)]),
                        reads=("psT0",), writes=("l0ve",),
                        name=f"l0vev{r}_{c}_{u}")
            em.add("vector", lambda e: e.tensor_copy(
                out=vhstage, in_=cx.psB[:, 0:128]),
                reads=("psB",), writes=("l0vhs",), name=f"l0vhs{r}")
            em.add("tensor", lambda e: e.transpose(
                cx.psT[1][:, 0:128], vhstage, cx.ident[:]),
                reads=("l0vhs", "ident"), writes=("psT1",), name=f"l0vht{r}")
            em.add("vector", lambda e, r=r: e.tensor_copy(
                out=v_ext[64:128, 128 * r:128 * (r + 1)],
                in_=cx.psT[1][0:64, 0:128]),
                reads=("psT1",), writes=("l0ve",), name=f"l0vhl{r}")
            em.add("vector", lambda e, r=r: e.tensor_copy(
                out=v_ext[0:64, 512 * 17 + 128 * r:512 * 17 + 128 * (r + 1)],
                in_=cx.psT[1][64:128, 0:128]),
                reads=("psT1",), writes=("l0ve",), name=f"l0vhr{r}")
    em.fence()

    masks = B2[:, 13312:14464]

    def maskap(m):
        return masks[:, 384 * m:384 * (m + 1)]
    em.add("gpsimd", lambda e: e.iota(
        crow.bitcast(I32), [[1, 384]], base=0, channel_multiplier=0),
        writes=("l0crw",), name="l0crw")
    em.add("gpsimd", lambda e: e.iota(
        pcol.bitcast(I32), [[0, 1]], base=0, channel_multiplier=1),
        writes=("l0tc",), name="l0pc")
    em.add("vector", lambda e: e.tensor_scalar(
        out=tcol1, in0=pcol.bitcast(I32), scalar1=64.0, scalar2=64.0,
        op0=ALU.is_ge, op1=ALU.mult),
        reads=("l0tc",), writes=("l0tc",), name="l0t1a")
    em.add("vector", lambda e: e.tensor_scalar(
        out=tcol1, in0=tcol1, scalar1=64.0, scalar2=None, op0=ALU.add),
        reads=("l0tc",), writes=("l0tc",), name="l0t1b")
    em.add("vector", lambda e: e.tensor_scalar(
        out=tcol2, in0=tcol1, scalar1=192.0, scalar2=None, op0=ALU.add),
        reads=("l0tc",), writes=("l0tc",), name="l0t2")
    em.add("vector", lambda e: e.tensor_scalar(
        out=maskap(0), in0=crow.bitcast(I32), scalar1=tcol1, scalar2=None,
        op0=ALU.is_ge), reads=("l0crw", "l0tc"), writes=("l0msk",),
        name="l0mA1")
    em.add("vector", lambda e: e.tensor_scalar(
        out=maskap(1), in0=crow.bitcast(I32), scalar1=tcol2, scalar2=None,
        op0=ALU.is_lt), reads=("l0crw", "l0tc"), writes=("l0msk",),
        name="l0mA2")
    em.add("vector", lambda e: e.tensor_mul(maskap(0), maskap(0), maskap(1)),
           reads=("l0msk",), writes=("l0msk",), name="l0mA3")
    em.add("vector", lambda e: e.tensor_copy(out=maskap(1), in_=maskap(0)),
           reads=("l0msk",), writes=("l0msk",), name="l0m0c")
    em.add("vector", lambda e: e.tensor_scalar(
        out=maskap(1)[0:64, 64:128], in0=maskap(1)[0:64, 64:128],
        scalar1=fcol[0:64], scalar2=None, op0=ALU.mult),
        reads=("l0msk", "l0bc"), writes=("l0msk",), name="l0m0f")
    em.add("vector", lambda e: e.tensor_copy(out=maskap(2), in_=maskap(0)),
           reads=("l0msk",), writes=("l0msk",), name="l0m15c")
    em.add("vector", lambda e: e.tensor_scalar(
        out=maskap(2)[64:128, 256:320], in0=maskap(2)[64:128, 256:320],
        scalar1=finv[64:128], scalar2=None, op0=ALU.mult),
        reads=("l0msk", "l0bc"), writes=("l0msk",), name="l0m15f")

    sgb16 = cx.sgbuf[:, :].bitcast(BF16)
    pstage = [cx.sq_junk[:, 0:384], cx.sq_junk[:, 384:768],
              sgb16[:, 0:384], sgb16[:, 384:768]]
    ptb[3] = B2[:, 15248:15632]
    for i in range(NT):
        mi = 1 if i == 0 else (2 if i == NT - 1 else 0)
        for hh in range(8):
            sidx = (i * 8 + hh) % 4
            ps = cx.psum[sidx]
            pk = f"ps{sidx}"
            stg = pstage[sidx]
            sk = f"l0pg{sidx}"
            ptbuf = ptb[sidx]
            ptk = f"l0pt{sidx}"
            rb = 64 * (hh % 2)
            pt4 = hh // 2
            mcol = mc_s[sidx]
            scc = sc_s[sidx]
            rcc = rc_s[sidx]
            mk = f"l0mc{sidx}"
            em.add("tensor", lambda e, ps=ps, rb=rb, pt4=pt4, i=i: e.matmul(
                ps[:, 0:384],
                q_l[rb:rb + 64, HALF * pt4 + 128 * i:HALF * pt4 + 128 * (i + 1)],
                k_ext[rb:rb + 64, KW * pt4 + 128 * i:KW * pt4 + 128 * i + 384],
                start=True, stop=True),
                reads=("l0ql", "l0ke"), writes=(pk,),
                name=f"l0sim{i}_{hh}")
            em.add("vector", lambda e, ps=ps, stg=stg, mcol=mcol:
                   e.tensor_scalar(
                out=stg, in0=ps[:, 0:384], scalar1=-SC, scalar2=3.0e38,
                op0=ALU.mult, op1=ALU.min, accum_out=mcol),
                reads=(pk,), writes=(sk, mk), name=f"l0ng{i}_{hh}")
            em.add("scalar", lambda e, stg=stg, mcol=mcol: e.activation(
                out=stg, in_=stg, func=AF.Exp, bias=mcol, scale=-1.0),
                reads=(sk, mk), writes=(sk,), name=f"l0ex{i}_{hh}")
            em.add("vector", lambda e, stg=stg, mi=mi: e.tensor_mul(
                stg, stg, maskap(mi)),
                reads=(sk, "l0msk"), writes=(sk,), name=f"l0pmm{i}_{hh}")
            em.add("vector", lambda e, stg=stg, scc=scc: e.tensor_reduce(
                out=scc, in_=stg, axis=AX.X, op=ALU.add),
                reads=(sk,), writes=(mk,), name=f"l0psm{i}_{hh}")
            em.add("vector", lambda e, scc=scc, rcc=rcc: e.reciprocal(
                out=rcc, in_=scc),
                reads=(mk,), writes=(mk,), name=f"l0prc{i}_{hh}")
            em.add("vector", lambda e, stg=stg, rcc=rcc: e.tensor_scalar(
                out=stg, in0=stg, scalar1=rcc, scalar2=None, op0=ALU.mult),
                reads=(sk, mk), writes=(sk,), name=f"l0pnm{i}_{hh}")
            bankk = f"psT{sidx % 2}"
            bank = cx.psT[sidx % 2]
            for j in range(3):
                em.add("tensor", lambda e, stg=stg, j=j, bank=bank:
                       e.transpose(
                    bank[:, 128 * j:128 * (j + 1)],
                    stg[:, 128 * j:128 * (j + 1)], cx.ident[:]),
                    reads=(sk, "ident"), writes=(bankk,),
                    name=f"l0ptT{i}_{hh}_{j}")
            em.add("vector", lambda e, bank=bank, ptbuf=ptbuf: e.tensor_copy(
                out=ptbuf, in_=bank[:, 0:384]),
                reads=(bankk,), writes=(ptk,), name=f"l0pev{i}_{hh}")
            avb = ps
            avk = pk
            for j in range(3):
                em.add("tensor", lambda e, j=j, i=i, hh=hh, ptbuf=ptbuf,
                       avb=avb, ptk=ptk: e.matmul(
                    avb[0:64, 384:512],
                    v_ext[:, 512 * (i + j) + 64 * hh:
                          512 * (i + j) + 64 * (hh + 1)],
                    ptbuf[:, 128 * j:128 * (j + 1)],
                    start=(j == 0), stop=(j == 2)),
                    reads=("l0ve", ptk), writes=(avk,),
                    name=f"l0av{i}_{hh}_{j}")
            em.add("vector", lambda e, avb=avb, rb=rb, pt4=pt4, i=i:
                   e.tensor_copy(
                out=ao_fm[rb:rb + 64, HALF * pt4 + 128 * i:
                          HALF * pt4 + 128 * (i + 1)],
                in_=avb[0:64, 384:512]),
                reads=(avk,), writes=("l0ao",), name=f"l0ae{i}_{hh}")
    wolb = B2[:, 9216:13312]
    for kt in range(4):
        em.dma("gpsimd", lambda e, kt=kt: e.dma_start(
            out=wolb[:, 1024 * kt:1024 * (kt + 1)], in_=dr["l0_wol"][kt]),
            writes=("l0wol", "l0halo", "l0hfm", "l0wbA", "l0wbB"),
            name=f"l0wol{kt}")
    for i in range(NT):
        for half in range(2):
            ps = cx.psum[half]
            pk = f"ps{half}"
            for kt in range(4):
                em.add("tensor", lambda e, i=i, half=half, kt=kt, ps=ps:
                       e.matmul(
                    ps[:, 0:512],
                    ao_fm[:, HALF * kt + 128 * i:HALF * kt + 128 * (i + 1)],
                    wolb[:, 1024 * kt + 512 * half:
                         1024 * kt + 512 * (half + 1)],
                    start=(kt == 0), stop=(kt == 3)),
                    reads=("l0ao", "l0wol"), writes=(pk,),
                    name=f"l0wo{i}_{half}_{kt}")
            em.add("vector", lambda e, i=i, half=half, ps=ps: e.tensor_copy(
                out=s_tm[:, 1024 * i + 512 * half:1024 * i + 512 * (half + 1)],
                in_=ps[:, 0:512]),
                reads=(pk,), writes=("l0s",), name=f"l0se{i}_{half}")
    em.fence()
    dump_sb("dbg_slight", s_tm[:, 0:16384], ("l0s",))
    dump_sb("dbg_ao", ao_fm, ("l0ao",))
    dump_sb("dbg_ql", q_l, ("l0ql",))
    dump_sb("dbg_ke", k_ext, ("l0ke",))
    dump_sb("dbg_ve", v_ext[:, 0:9216], ("l0ve",))
    dump_sb("dbg_xlf", xl_fm[:, 0:16384], ("l0xlf",))

    em.dma("sync", lambda e: e.dma_start(
        out=scfull[:, 0:32], in_=dr["l0_sc_out"][0:128, :]),
        reads=("l0scout",), writes=SM, name="l0sf0")
    em.dma("sync", lambda e: e.dma_start(
        out=scfull[:, 32:64], in_=dr["l0_sc_out"][128:256, :]),
        reads=("l0scout",), writes=SM, name="l0sf1")
    em.add("vector", lambda e: e.tensor_scalar(
        out=identf, in0=cx.ident_i[:], scalar1=0.0, scalar2=None,
        op0=ALU.is_equal), reads=("ident_i",), writes=SM, name="l0idf")
    em.add("vector", lambda e: e.tensor_scalar(
        out=LTf, in0=cx.ident_i[0:32, 0:32], scalar1=0.0, scalar2=None,
        op0=ALU.is_gt), reads=("ident_i",), writes=SM, name="l0lt")
    em.add("vector", lambda e: e.memset(onesrow, 1.0), reads=SM, writes=SM,
           name="l0or")
    em.add("vector", lambda e: e.memset(onescol, 1.0), reads=SM, writes=SM,
           name="l0oc")
    em.add("gpsimd", lambda e: e.iota(
        gidxf[:, 0:16].bitcast(I32), [[128, 16]], base=0,
        channel_multiplier=1), reads=SM, writes=SM, name="l0gi0")
    em.add("gpsimd", lambda e: e.iota(
        gidxf[:, 16:32].bitcast(I32), [[128, 16]], base=2048,
        channel_multiplier=1), reads=SM, writes=SM, name="l0gi1")
    em.add("vector", lambda e: e.tensor_copy(
        out=gidxf, in_=gidxf.bitcast(I32)),
        reads=SM, writes=SM, name="l0gif")
    em.add("gpsimd", lambda e: e.iota(
        iotap.bitcast(I32), [[1, 128]], base=0, channel_multiplier=0),
        reads=SM, writes=SM, name="l0ip")
    em.add("vector", lambda e: e.tensor_copy(
        out=iotap, in_=iotap.bitcast(I32)),
        reads=SM, writes=SM, name="l0ipf")
    em.add("gpsimd", lambda e: e.iota(
        c128c.bitcast(I32), [[0, 1]], base=0, channel_multiplier=128),
        reads=SM, writes=SM, name="l0c128")
    em.add("vector", lambda e: e.tensor_copy(
        out=c128c, in_=c128c.bitcast(I32)),
        reads=SM, writes=SM, name="l0c128f")
    em.add("vector", lambda e: e.memset(zcol, 0.0), reads=SM, writes=SM,
           name="l0zc")
    jrow = A2.bitcast(F32)[:, 2048:3072]
    em.add("gpsimd", lambda e: e.iota(
        jrow.bitcast(I32), [[1, 1024]], base=0, channel_multiplier=0),
        writes=("l0jr",), name="l0jri")
    em.add("vector", lambda e: e.tensor_copy(
        out=jrow, in_=jrow.bitcast(I32)),
        reads=("l0jr",), writes=("l0jr",), name="l0jrf")

    def vop(fn, name):
        em.add("vector", fn, reads=SM, writes=SM, name=name)

    def pop(fn, name):
        em.add("vector", fn, reads=SM + ("psB",), writes=SM, name=name)

    def count_into(src_ap, dst_idx, tag):
        em.add("tensor", lambda e: e.matmul(
            cx.psB[0:1, 0:1], src_ap, onescol, start=True, stop=True),
            reads=SM, writes=("psB",), name=f"{tag}r")
        pop(lambda e: e.tensor_copy(
            out=scol(TOT)[0:1], in_=cx.psB[0:1, 0:1]), f"{tag}c")
        em.add("tensor", lambda e: e.matmul(
            cx.psB[:, 1:2], onesrow, scol(TOT)[0:1], start=True, stop=True),
            reads=SM, writes=("psB",), name=f"{tag}b")
        pop(lambda e: e.tensor_copy(
            out=scol(dst_idx), in_=cx.psB[:, 1:2]), f"{tag}e")

    def bisect_route(r, listu, tag):
        vop(lambda e: e.tensor_copy(
            out=vcat[:, 0:16], in_=scfull[:, 16 * r:16 * r + 16]),
            f"{tag}v0")
        vop(lambda e: e.tensor_copy(
            out=vcat[:, 16:32], in_=scfull[:, 32 + 16 * r:48 + 16 * r]),
            f"{tag}v1")
        vop(lambda e: e.memset(scol(LO), -16384.0), f"{tag}lo")
        vop(lambda e: e.memset(scol(HI), 16384.0), f"{tag}hi")
        for it in range(42):
            vop(lambda e: e.tensor_add(scol(MID), scol(LO), scol(HI)),
                f"{tag}m{it}")
            vop(lambda e: e.tensor_scalar_mul(scol(MID), scol(MID), 0.5),
                f"{tag}mh{it}")
            vop(lambda e: e.tensor_scalar(
                out=work, in0=vcat, scalar1=scol(MID), scalar2=0.0,
                op0=ALU.is_gt, op1=ALU.add, accum_out=scol(CNTP)),
                f"{tag}ct{it}")
            count_into(scol(CNTP), CNT, f"{tag}cc{it}")
            vop(lambda e: e.tensor_scalar(
                out=scol(GG), in0=scol(CNT), scalar1=1024.0, scalar2=None,
                op0=ALU.is_ge), f"{tag}g{it}")
            vop(lambda e: e.tensor_scalar(
                out=scol(GI), in0=scol(GG), scalar1=-1.0, scalar2=1.0,
                op0=ALU.mult, op1=ALU.add), f"{tag}gi{it}")
            vop(lambda e: e.tensor_sub(scol(T1), scol(MID), scol(LO)),
                f"{tag}t1{it}")
            vop(lambda e: e.tensor_mul(scol(T1), scol(T1), scol(GG)),
                f"{tag}t1m{it}")
            vop(lambda e: e.tensor_add(scol(LO), scol(LO), scol(T1)),
                f"{tag}lou{it}")
            vop(lambda e: e.tensor_sub(scol(T2), scol(MID), scol(HI)),
                f"{tag}t2{it}")
            vop(lambda e: e.tensor_mul(scol(T2), scol(T2), scol(GI)),
                f"{tag}t2m{it}")
            vop(lambda e: e.tensor_add(scol(HI), scol(HI), scol(T2)),
                f"{tag}hiu{it}")
        vop(lambda e: e.tensor_scalar(
            out=selA, in0=vcat, scalar1=scol(HI), scalar2=0.0,
            op0=ALU.is_gt, op1=ALU.add, accum_out=scol(CNTP)), f"{tag}sa")
        count_into(scol(CNTP), NHI, f"{tag}nh")
        vop(lambda e: e.tensor_scalar(
            out=scol(KEEP), in0=scol(NHI), scalar1=-1.0, scalar2=1024.0,
            op0=ALU.mult, op1=ALU.add), f"{tag}kp")
        vop(lambda e: e.tensor_scalar(
            out=self_, in0=vcat, scalar1=scol(LO), scalar2=None,
            op0=ALU.is_gt), f"{tag}sb")
        vop(lambda e: e.tensor_sub(tie, self_, selA), f"{tag}tie")
        vop(lambda e: e.memset(scol(JLO), 0.0), f"{tag}jl")
        vop(lambda e: e.memset(scol(JHI), 4097.0), f"{tag}jh")
        for it in range(14):
            vop(lambda e: e.tensor_add(scol(MID), scol(JLO), scol(JHI)),
                f"{tag}jm{it}")
            vop(lambda e: e.tensor_scalar_mul(scol(MID), scol(MID), 0.5),
                f"{tag}jmh{it}")
            vop(lambda e: e.tensor_scalar(
                out=work, in0=gidxf, scalar1=scol(MID), scalar2=None,
                op0=ALU.is_ge), f"{tag}jw{it}")
            vop(lambda e: e.tensor_mul(work, work, tie), f"{tag}jt{it}")
            vop(lambda e: e.tensor_reduce(
                out=scol(CNTP), in_=work, axis=AX.X, op=ALU.add),
                f"{tag}jr{it}")
            count_into(scol(CNTP), CNT, f"{tag}jc{it}")
            vop(lambda e: e.tensor_scalar(
                out=scol(GG), in0=scol(CNT), scalar1=scol(KEEP), scalar2=None,
                op0=ALU.is_ge), f"{tag}jg{it}")
            vop(lambda e: e.tensor_scalar(
                out=scol(GI), in0=scol(GG), scalar1=-1.0, scalar2=1.0,
                op0=ALU.mult, op1=ALU.add), f"{tag}jgi{it}")
            vop(lambda e: e.tensor_sub(scol(T1), scol(MID), scol(JLO)),
                f"{tag}jt1{it}")
            vop(lambda e: e.tensor_mul(scol(T1), scol(T1), scol(GG)),
                f"{tag}jt1m{it}")
            vop(lambda e: e.tensor_add(scol(JLO), scol(JLO), scol(T1)),
                f"{tag}jlou{it}")
            vop(lambda e: e.tensor_sub(scol(T2), scol(MID), scol(JHI)),
                f"{tag}jt2{it}")
            vop(lambda e: e.tensor_mul(scol(T2), scol(T2), scol(GI)),
                f"{tag}jt2m{it}")
            vop(lambda e: e.tensor_add(scol(JHI), scol(JHI), scol(T2)),
                f"{tag}jhiu{it}")
        vop(lambda e: e.tensor_scalar(
            out=work, in0=gidxf, scalar1=scol(JLO), scalar2=None,
            op0=ALU.is_ge), f"{tag}fw")
        vop(lambda e: e.tensor_mul(work, work, tie), f"{tag}fm")
        vop(lambda e: e.tensor_add(self_, selA, work), f"{tag}fs")
        em.add("tensor", lambda e: e.transpose(
            cx.psB[0:32, 0:128], self_, identf),
            reads=SM, writes=("psB",), name=f"{tag}tT")
        pop(lambda e: e.tensor_copy(out=selT, in_=cx.psB[0:32, 0:128]),
            f"{tag}tTe")
        vop(lambda e: e.memset(scr3, 0.0), f"{tag}z3")
        vop(lambda e: e.tensor_tensor_scan(
            out=Wsc, data0=selT, data1=scr3, initial=zcol, op0=ALU.add,
            op1=ALU.add), f"{tag}scan")
        em.add("tensor", lambda e: e.matmul(
            cx.psB[0:32, 2:3], LTf, Wsc[:, 127:128], start=True, stop=True),
            reads=SM, writes=("psB",), name=f"{tag}co")
        pop(lambda e: e.tensor_copy(out=coloff, in_=cx.psB[0:32, 2:3]),
            f"{tag}coe")
        vop(lambda e: e.tensor_scalar(
            out=pcc, in0=coloff, scalar1=-1.0, scalar2=1024.0, op0=ALU.mult,
            op1=ALU.add), f"{tag}pc1")
        vop(lambda e: e.tensor_add(pcc, pcc, c128c), f"{tag}pc2")
        vop(lambda e: e.tensor_sub(scr3, Wsc, selT), f"{tag}p1")
        vop(lambda e: e.tensor_scalar(
            out=scr3, in0=scr3, scalar1=coloff, scalar2=None, op0=ALU.add),
            f"{tag}p2")
        vop(lambda e: e.tensor_sub(scr4, iotap, Wsc), f"{tag}k1")
        vop(lambda e: e.tensor_add(scr4, scr4, selT), f"{tag}k2")
        vop(lambda e: e.tensor_scalar(
            out=scr4, in0=scr4, scalar1=pcc, scalar2=None, op0=ALU.add),
            f"{tag}k3")
        vop(lambda e: e.tensor_sub(scr3, scr3, scr4), f"{tag}c1")
        vop(lambda e: e.tensor_mul(scr3, scr3, selT), f"{tag}c2")
        vop(lambda e: e.tensor_add(scr3, scr3, scr4), f"{tag}c3")
        em.add("tensor", lambda e: e.transpose(
            cx.psB[:, 128:160], scr3, identf[0:32, 0:32]),
            reads=SM, writes=("psB",), name=f"{tag}pT")
        pop(lambda e: e.tensor_copy(out=posf, in_=cx.psB[:, 128:160]),
            f"{tag}pTe")
        oh = [A2.bitcast(F32)[:, 0:128], A2.bitcast(F32)[:, 128:256]]
        ohk = ("l0ohA", "l0ohB")
        bank = cx.psum[3]
        for jc in range(8):
            for g in range(32):
                em.add("vector", lambda e, g=g, jc=jc: e.tensor_scalar(
                    out=oh[g % 2], in0=jrow[:, 128 * jc:128 * (jc + 1)],
                    scalar1=posf[:, g:g + 1],
                    scalar2=None, op0=ALU.is_equal),
                    reads=SM + ("l0jr",), writes=(ohk[g % 2],),
                    name=f"{tag}oh{jc}_{g}")
                em.add("tensor", lambda e, g=g, jc=jc: e.matmul(
                    bank[:, jc:jc + 1], oh[g % 2],
                    gidxf[:, g:g + 1], start=(g == 0), stop=(g == 31)),
                    reads=(ohk[g % 2],) + SM, writes=("ps3",),
                    name=f"{tag}lm{jc}_{g}")
        em.add("vector", lambda e, listu=listu: e.tensor_copy(
            out=listu, in_=bank[:, 0:8]),
            reads=("ps3",) + SM, writes=SM, name=f"{tag}le")


    bisect_route(0, qlu, "l0bq")
    bisect_route(1, kvlu, "l0bk")
    em.add("vector", lambda e: e.tensor_copy(out=qlf, in_=qlu.bitcast(I32)),
           reads=SM, writes=SM, name="l0qlf")
    em.add("vector", lambda e: e.tensor_scalar(
        out=qgf, in0=qlf[:, 0:4], scalar1=finv, scalar2=None, op0=ALU.mult),
        reads=SM + ("l0bc",), writes=SM, name="l0qg1")
    em.add("vector", lambda e: e.tensor_scalar(
        out=qgl, in0=qlf[:, 4:8], scalar1=fcol, scalar2=None, op0=ALU.mult),
        reads=SM + ("l0bc",), writes=SM, name="l0qg2")
    em.add("vector", lambda e: e.tensor_add(qgf, qgf, qgl),
           reads=SM, writes=SM, name="l0qg3")
    em.add("vector", lambda e: e.tensor_copy(out=qgu, in_=qgf),
           reads=SM, writes=SM, name="l0qgu")
    em.add("vector", lambda e: e.tensor_scalar(
        out=qgl8, in0=qlf, scalar1=f2048, scalar2=None, op0=ALU.subtract),
        reads=SM + ("l0bc",), writes=SM, name="l0qgl8")
    em.fence()
    dump_sb("dbg_sm", xb[:, 0:2048], SM)

    rkv_tm = B1[:, 0:8192]
    rkv_fm = B1[:, 8192:16384]
    rq_tm = A1[:, 0:4096]
    rq_fm = A1[:, 4096:8192]
    iota_own = A1[:, 9216:13312].bitcast(I32)
    wbh = [A1[:, 13312:14336], A1[:, 14336:15360]]
    wosl = A1[:, 15360:16384]
    onehot = A2[:, 0:16384]
    hq_fm = A2[:, 8192:10240]
    hstages = [A2[:, 10240:11264], A2[:, 14336:15360]]
    hpTs = [A2[:, 11264:12288], A2[:, 15360:16384]]
    ao_h = A2[:, 12288:14336]
    hk_fm = B2[:, 0:4096]
    hv_fm = B2[:, 4096:8192]
    hv_tm = B2[:, 8192:12288]
    ho_tm = B2[:, 12288:16384]

    for c in range(8):
        em.dma("gpsimd",
               lambda e, c=c: e.indirect_dma_start(
            out=rkv_tm[:, 1024 * c:1024 * (c + 1)], out_offset=None,
            in_=dr["l0_xn_out"],
            in_offset=IndirectOffsetOnAxis(
                ap=kvlu.bitcast(U32)[:, c:c + 1], axis=0)),
            reads=SM + ("l0xnout",), writes=("l0rkv",), name=f"l0gkv{c}")
    for c in range(4):
        em.dma("gpsimd",
               lambda e, c=c: e.indirect_dma_start(
            out=rq_tm[:, 1024 * c:1024 * (c + 1)], out_offset=None,
            in_=dr["l0_xn_out"],
            in_offset=IndirectOffsetOnAxis(
                ap=qgu.bitcast(U32)[:, c:c + 1], axis=0)),
            reads=SM + ("l0xnout",), writes=("l0rq",), name=f"l0gq{c}")
    rstd8 = rms_stats(cx, lambda i: rkv_tm[:, 1024 * i:1024 * (i + 1)], 8,
                      "l0rkv", "l0rmk")
    for i in range(8):
        em.add("vector", lambda e, i=i: e.tensor_scalar(
            out=rkv_tm[:, 1024 * i:1024 * (i + 1)],
            in0=rkv_tm[:, 1024 * i:1024 * (i + 1)],
            scalar1=rstd8[:, i:i + 1], scalar2=None, op0=ALU.mult),
            reads=("l0rkv", "stat"), writes=("l0rkv",), name=f"l0rka{i}")
    transpose_tm_to_fm(cx, rkv_tm, rkv_fm, "l0rkv", "l0rkf", 8, 8,
                       tag="l0rkt")
    for f in range(NF):
        em.add("vector", lambda e, f=f: e.tensor_scalar(
            out=rkv_fm[:, 1024 * f:1024 * (f + 1)],
            in0=rkv_fm[:, 1024 * f:1024 * (f + 1)],
            scalar1=bcols[:, 16 + f:17 + f], scalar2=None, op0=ALU.mult),
            reads=("l0rkf", "l0bc"), writes=("l0rkf",), name=f"l0rkg{f}")
    rstd4 = rms_stats(cx, lambda i: rq_tm[:, 1024 * i:1024 * (i + 1)], 4,
                      "l0rq", "l0rmq")
    for i in range(4):
        em.add("vector", lambda e, i=i: e.tensor_scalar(
            out=rq_tm[:, 1024 * i:1024 * (i + 1)],
            in0=rq_tm[:, 1024 * i:1024 * (i + 1)],
            scalar1=rstd4[:, i:i + 1], scalar2=None, op0=ALU.mult),
            reads=("l0rq", "stat"), writes=("l0rq",), name=f"l0rqa{i}")
    transpose_tm_to_fm(cx, rq_tm, rq_fm, "l0rq", "l0rqf", 4, 8, tag="l0rqt")
    for f in range(NF):
        em.add("vector", lambda e, f=f: e.tensor_scalar(
            out=rq_fm[:, 512 * f:512 * (f + 1)],
            in0=rq_fm[:, 512 * f:512 * (f + 1)],
            scalar1=bcols[:, 16 + f:17 + f], scalar2=None, op0=ALU.mult),
            reads=("l0rqf", "l0bc"), writes=("l0rqf",), name=f"l0rqg{f}")

    def ev_hq(f):
        em.add("vector", lambda e, f=f: e.tensor_copy(
            out=hq_fm[:, 512 * f:512 * (f + 1)], in_=cx.psum[0][:, 0:512]),
            reads=("ps0",), writes=("l0hq",), name=f"l0hqe{f}")

    proj_p2(cx, dr["l0_wqh"], rq_fm, "l0rqf", wbh, ("l0whA", "l0whB"),
            4, ev_hq, "l0hqp", ntok=512, nkt=8, chunks=1)

    def ev_hkv(f):
        dst = hk_fm if f < 4 else hv_fm
        dk = "l0hk" if f < 4 else "l0hv"
        fo = f if f < 4 else f - 4
        for c in range(2):
            em.add("vector", lambda e, dst=dst, fo=fo, c=c, dk=dk:
                   e.tensor_copy(
                out=dst[:, 1024 * fo + 512 * c:1024 * fo + 512 * (c + 1)],
                in_=cx.psum[c][:, 0:512]),
                reads=(f"ps{c}",), writes=(dk,), name=f"l0kve{f}_{c}")

    proj_p2(cx, dr["l0_wkvh"], rkv_fm, "l0rkf", wbh, ("l0whA", "l0whB"),
            8, ev_hkv, "l0kvp", ntok=1024, nkt=8, chunks=2)
    for g in range(8):
        for fp in range(4):
            em.add("tensor", lambda e, g=g, fp=fp: e.transpose(
                cx.psT[0][:, 128 * fp:128 * (fp + 1)],
                hv_fm[:, 1024 * fp + 128 * g:1024 * fp + 128 * (g + 1)],
                cx.ident[:]),
                reads=("l0hv", "ident"), writes=("psT0",),
                name=f"l0vtt{g}_{fp}")
        em.add("vector", lambda e, g=g: e.tensor_copy(
            out=hv_tm[:, 512 * g:512 * (g + 1)], in_=cx.psT[0][:, 0:512]),
            reads=("psT0",), writes=("l0hvt",), name=f"l0vte{g}")
    for hh in range(8):
        rb = 64 * (hh % 2)
        pt4 = hh // 2
        for qt in range(4):
            st = (hh * 4 + qt) % 2
            hst = hstages[st]
            hstk = f"l0hst{st}"
            hpT = hpTs[st]
            hpk = f"l0hpT{st}"
            hm = hmc[st]
            hs2 = hsc[st]
            hmk = f"l0hmc{st}"
            for c in range(2):
                ps = cx.psum[2 * st + c]
                psk = f"ps{2 * st + c}"
                em.add("tensor", lambda e, ps=ps, rb=rb, pt4=pt4, qt=qt, c=c:
                       e.matmul(
                    ps[:, 0:512],
                    hq_fm[rb:rb + 64, 512 * pt4 + 128 * qt:
                          512 * pt4 + 128 * (qt + 1)],
                    hk_fm[rb:rb + 64, 1024 * pt4 + 512 * c:
                          1024 * pt4 + 512 * (c + 1)],
                    start=True, stop=True),
                    reads=("l0hq", "l0hk"), writes=(psk,),
                    name=f"l0hsim{hh}_{qt}_{c}")
                em.add("vector", lambda e, ps=ps, c=c, hst=hst, hm=hm:
                       e.tensor_scalar(
                    out=hst[:, 512 * c:512 * (c + 1)], in0=ps[:, 0:512],
                    scalar1=-SC, scalar2=3.0e38, op0=ALU.mult, op1=ALU.min,
                    accum_out=hm[:, c:c + 1]),
                    reads=(psk,), writes=(hstk, hmk),
                    name=f"l0hng{hh}_{qt}_{c}")
            em.add("vector", lambda e, hm=hm: e.tensor_reduce(
                out=hm[:, 2:3], in_=hm[:, 0:2], axis=AX.X, op=ALU.min),
                reads=(hmk,), writes=(hmk,), name=f"l0hmr{hh}_{qt}")
            for c in range(2):
                em.add("scalar", lambda e, c=c, hst=hst, hm=hm, hs2=hs2:
                       e.activation(
                    out=hst[:, 512 * c:512 * (c + 1)],
                    in_=hst[:, 512 * c:512 * (c + 1)], func=AF.Exp,
                    bias=hm[:, 2:3], scale=-1.0,
                    accum_out=hs2[:, c:c + 1]),
                    reads=(hstk, hmk), writes=(hstk, hmk),
                    name=f"l0hex{hh}_{qt}_{c}")
            em.add("vector", lambda e, hs2=hs2: e.tensor_add(
                hs2[:, 2:3], hs2[:, 0:1], hs2[:, 1:2]),
                reads=(hmk,), writes=(hmk,), name=f"l0hsm{hh}_{qt}")
            em.add("vector", lambda e, hs2=hs2: e.reciprocal(
                out=hs2[:, 3:4], in_=hs2[:, 2:3]),
                reads=(hmk,), writes=(hmk,), name=f"l0hrc{hh}_{qt}")
            em.add("vector", lambda e, hst=hst, hs2=hs2: e.tensor_scalar(
                out=hst, in0=hst, scalar1=hs2[:, 3:4], scalar2=None,
                op0=ALU.mult),
                reads=(hstk, hmk), writes=(hstk,),
                name=f"l0hnm{hh}_{qt}")
            for j in range(8):
                em.add("tensor", lambda e, j=j, hst=hst, st=st: e.transpose(
                    cx.psT[st][:, 128 * j:128 * (j + 1)],
                    hst[:, 128 * j:128 * (j + 1)], cx.ident[:]),
                    reads=(hstk, "ident"), writes=(f"psT{st}",),
                    name=f"l0hpt{hh}_{qt}_{j}")
            em.add("vector", lambda e, hpT=hpT, st=st: e.tensor_copy(
                out=hpT, in_=cx.psT[st][:, 0:1024]),
                reads=(f"psT{st}",), writes=(hpk,), name=f"l0hpe{hh}_{qt}")
            for g in range(8):
                em.add("tensor", lambda e, g=g, hh=hh, hpT=hpT, st=st:
                       e.matmul(
                    cx.psB[0:64, 128 * st:128 * (st + 1)],
                    hv_tm[:, 512 * g + 64 * hh:512 * g + 64 * (hh + 1)],
                    hpT[:, 128 * g:128 * (g + 1)],
                    start=(g == 0), stop=(g == 7)),
                    reads=("l0hvt", hpk), writes=("psB",),
                    name=f"l0hav{hh}_{qt}_{g}")
            em.add("vector", lambda e, rb=rb, pt4=pt4, qt=qt, st=st:
                   e.tensor_copy(
                out=ao_h[rb:rb + 64, 512 * pt4 + 128 * qt:
                         512 * pt4 + 128 * (qt + 1)],
                in_=cx.psB[0:64, 128 * st:128 * (st + 1)]),
                reads=("psB",), writes=("l0aoh",), name=f"l0hae{hh}_{qt}")
    for qt in range(4):
        for kt in range(4):
            em.dma("gpsimd", lambda e, kt=kt, qt=qt: e.dma_start(
                out=wosl, in_=dr["l0_woh"][kt]),
                writes=("l0wos",), name=f"l0whl{qt}_{kt}")
            for half in range(2):
                em.add("tensor", lambda e, qt=qt, kt=kt, half=half:
                       e.matmul(
                    cx.psum[half][:, 0:512],
                    ao_h[:, 512 * kt + 128 * qt:512 * kt + 128 * (qt + 1)],
                    wosl[:, 512 * half:512 * (half + 1)],
                    start=(kt == 0), stop=(kt == 3)),
                    reads=("l0aoh", "l0wos"), writes=(f"ps{half}",),
                    name=f"l0woh{qt}_{half}_{kt}")
        for half in range(2):
            em.add("vector", lambda e, qt=qt, half=half: e.tensor_copy(
                out=ho_tm[:, 1024 * qt + 512 * half:
                          1024 * qt + 512 * (half + 1)],
                in_=cx.psum[half][:, 0:512]),
                reads=(f"ps{half}",), writes=("l0ho",),
                name=f"l0whe{qt}_{half}")
    em.fence()
    dump_sb("dbg_ho", ho_tm, ("l0ho",))
    dump_sb("dbg_aoh", ao_h, ("l0aoh",))
    dump_sb("dbg_hq", hq_fm, ("l0hq",))
    dump_sb("dbg_hk", hk_fm, ("l0hk",))
    dump_sb("dbg_rkv", rkv_tm, ("l0rkv",))
    hor = dr["l0_ho_in"].rearrange("(i p) d -> i p d", p=128)
    for i in range(4):
        em.dma("sync", lambda e, i=i: e.dma_start(
            out=hor[i], in_=ho_tm[:, 1024 * i:1024 * (i + 1)]),
            reads=("l0ho",), writes=("l0hoin",), name=f"l0hos{i}")
    _cc_pair(em, "AllGather", dr["l0_ho_in"], dr["l0_ho_out"],
             ("l0hoin",), ("l0hoout",), "l0hocc")
    ho_agg = B2[:, 8192:16384]
    em.dma("sync", lambda e: e.dma_start(
        out=ho_agg.rearrange("p (c d) -> p c d", d=1024),
        in_=dr["l0_ho_out"].rearrange("(c p) d -> p c d", p=128)),
        reads=("l0hoout",), writes=("l0hoA", "l0hvt", "l0ho"),
        name="l0hol")
    em.add("gpsimd", lambda e: e.iota(
        iota_own[:, 0:2048], [[1, 2048]], base=0, channel_multiplier=0),
        writes=("l0io",), name="l0io")
    for jt in range(8):
        em.add("vector", lambda e, jt=jt: e.tensor_scalar(
            out=onehot[:, 2048 * jt:2048 * (jt + 1)],
            in0=iota_own[:, 0:2048], scalar1=qgl8[:, jt:jt + 1],
            scalar2=None, op0=ALU.is_equal),
            reads=("l0io",) + SM, writes=("l0oh",), name=f"l0oh{jt}")
    for i in range(NT):
        for half in range(2):
            ps = cx.psum[half]
            for jt in range(8):
                em.add("tensor", lambda e, ps=ps, i=i, half=half, jt=jt:
                       e.matmul(
                    ps[:, 0:512],
                    onehot[:, 2048 * jt + 128 * i:2048 * jt + 128 * (i + 1)],
                    ho_agg[:, 1024 * jt + 512 * half:
                           1024 * jt + 512 * (half + 1)],
                    start=(jt == 0), stop=(jt == 7)),
                    reads=("l0oh", "l0hoA"), writes=(f"ps{half}",),
                    name=f"l0sct{i}_{half}_{jt}")
            em.add("vector", lambda e, ps=ps, i=i, half=half:
                   e.scalar_tensor_tensor(
                out=s_tm[:, 1024 * i + 512 * half:1024 * i + 512 * (half + 1)],
                in0=ps[:, 0:512], scalar=1.0,
                in1=s_tm[:, 1024 * i + 512 * half:1024 * i + 512 * (half + 1)],
                op0=ALU.mult, op1=ALU.add),
                reads=(f"ps{half}", "l0s"), writes=("l0s",),
                name=f"l0sadd{i}_{half}")
    em.fence()
    dump_sb("dbg_s", s_tm[:, 0:16384], ("l0s",))

    xq_fm = A1
    transpose_tm_to_fm(cx, xqt, xq_fm, "l0xq", "l0xqf", NT, NF, tag="l0qtr")
    dump_sb("dbg_xqf", xq_fm[:, 0:16384], ("l0xqf",))
    em.fence()

    def gslab(si):
        if si < 5:
            return A2[:, 2816 * si:2816 * (si + 1)], "l0wgA2"
        if si < 10:
            return B1[:, 2816 * (si - 5):2816 * (si - 4)], "l0wgB1"
        if si < 15:
            return B2[:, 2816 * (si - 10):2816 * (si - 9)], "l0wgB2"
        return xqt[:, 0:2816], "l0wgXQ"

    wldf = B2[:, 14080:16128].bitcast(F32)
    wnat = xqt[:, 2816:3840]
    hqs = xqt[:, 3840:6656]
    h_tm = xqt[:, 6656:9472]
    gsil = xqt[:, 9472:10496].bitcast(F32)
    qv = xqt[:, 10496:11520].bitcast(F32)
    qq = xqt[:, 11520:12544].bitcast(F32)
    qb = xqt[:, 12544:13568].bitcast(F32)
    qt2 = xqt[:, 13568:14592].bitcast(F32)
    qmx = B2[:, 16128:16192].bitcast(F32)
    qsc2 = B2[:, 16192:16256].bitcast(F32)
    qrs2 = B2[:, 16256:16320].bitcast(F32)
    fsc = {"key": "l0fscr", "mx": qmx, "sc": qsc2, "rsc": qrs2, "v": qv,
           "q": qq, "b": qb, "t": qt2}
    for wi, wdram in enumerate((dr["l0f_wg"], dr["l0f_wu"])):
        for s5 in range(22):
            em.dma("sync", lambda e, wdram=wdram, s5=s5: e.dma_start(
                out=wldf, in_=wdram[s5]), reads=("l0fwS",),
                writes=("l0fwS",), name=f"l0fwl{wi}_{s5}")
            emit_fp4_quant_chunked(
                cx, wldf, wnat, ("l0fwS",), ("l0fwS",), 1024,
                fsc, f"l0fq{wi}_{s5}", chunk=512)
            for kt in range(8):
                em.add("tensor", lambda e, kt=kt: e.transpose(
                    cx.psT[0][:, 128 * kt:128 * (kt + 1)],
                    wnat[:, 128 * kt:128 * (kt + 1)], cx.ident[:]),
                    reads=("l0fwS", "ident"), writes=("psT0",),
                    name=f"l0ftt{wi}_{s5}_{kt}")
            for kt in range(8):
                sl, slk = gslab(wi * 8 + kt)
                em.add("vector", lambda e, kt=kt, sl=sl, s5=s5: e.tensor_copy(
                    out=sl[:, 128 * s5:128 * (s5 + 1)],
                    in_=cx.psT[0][:, 128 * kt:128 * (kt + 1)]),
                    reads=("psT0",), writes=(slk,),
                    name=f"l0fte{wi}_{s5}_{kt}")
    hq_dram = dr["l0_hqT"]
    for i in range(NT):
        for c6 in range(6):
            wd = 512 if c6 < 5 else 256
            for which in range(2):
                ps = cx.psum[which]
                for kt in range(8):
                    sl, slk = gslab(which * 8 + kt)
                    em.add("tensor", lambda e, ps=ps, kt=kt, sl=sl, slk=slk,
                           i=i, c6=c6, wd=wd: e.matmul(
                        ps[:, 0:wd],
                        xq_fm[:, HALF * kt + 128 * i:HALF * kt + 128 * (i + 1)],
                        sl[:, 512 * c6:512 * c6 + wd],
                        start=(kt == 0), stop=(kt == 7)),
                        reads=("l0xqf", slk), writes=(f"ps{which}",),
                        name=f"l0gm{i}_{c6}_{which}_{kt}")
            em.add("scalar", lambda e, wd=wd: e.activation(
                out=gsil[:, 0:wd], in_=cx.psum[0][:, 0:wd], func=AF.Silu),
                reads=("ps0",), writes=("l0gs",), name=f"l0sl{i}_{c6}")
            em.add("vector", lambda e, i=i, c6=c6, wd=wd:
                   e.scalar_tensor_tensor(
                out=h_tm[:, 512 * c6:512 * c6 + wd], in0=cx.psum[1][:, 0:wd],
                scalar=1.0, in1=gsil[:, 0:wd], op0=ALU.mult, op1=ALU.mult),
                reads=("ps1", "l0gs"), writes=("l0fwS",),
                name=f"l0hh{i}_{c6}")
        emit_fp4_quant_chunked(
            cx, h_tm, h_tm, ("l0fwS",), ("l0fwS",), 2816, fsc,
            f"l0hq{i}", chunk=512)
        for b4 in range(3):
            lo = b4 * 8
            hi2 = min(lo + 8, 22)
            for kt in range(lo, hi2):
                em.add("tensor", lambda e, kt=kt, j=kt - lo: e.transpose(
                    cx.psT[0][:, 128 * j:128 * (j + 1)],
                    h_tm[:, 128 * kt:128 * (kt + 1)], cx.ident[:]),
                    reads=("l0fwS", "ident"), writes=("psT0",),
                    name=f"l0htt{i}_{kt}")
            em.add("vector", lambda e, lo=lo, hi2=hi2: e.tensor_copy(
                out=hqs[:, 128 * lo:128 * hi2],
                in_=cx.psT[0][:, 0:128 * (hi2 - lo)]),
                reads=("psT0",), writes=("l0hqs",), name=f"l0hte{i}_{b4}")
        em.dma("sync", lambda e, i=i: e.dma_start(
            out=hq_dram[:, :, 128 * i:128 * (i + 1)].rearrange(
                "k p t -> p k t"),
            in_=hqs.rearrange("p (k t) -> p k t", t=128)),
            reads=("l0hqs",), writes=("l0hqd",), name=f"l0hqw{i}")
    em.fence()
    g_tm = A1
    cx.s_tm, cx.s_key = s_tm, "l0s"
    cx.g_tm, cx.g_key = g_tm, "l0g"

    def dslab(kt):
        if kt < 16:
            return A2[:, 1024 * kt:1024 * (kt + 1)], "l0wdA2"
        return B1[:, 1024 * (kt - 16):1024 * (kt - 15)], "l0wdB1"

    wdf = B2[:, 0:5632].bitcast(F32)
    wdnat = B2[:, 5632:8448]
    hqin = [B2[:, 8448:11264], B2[:, 11264:14080]]
    dv = B2[:, 14080:15104].bitcast(F32)
    dq = B2[:, 15104:16128].bitcast(F32)
    dmx = B2[:, 16128:16192].bitcast(F32)
    dsc = B2[:, 16192:16256].bitcast(F32)
    drs = B2[:, 16256:16320].bitcast(F32)
    dsc2 = {"key": "l0fscr", "mx": dmx, "sc": dsc, "rsc": drs, "v": dv,
            "q": dq, "b": qb, "t": qt2}
    for s5 in range(8):
        em.dma("sync", lambda e, s5=s5: e.dma_start(
            out=wdf, in_=dr["l0f_wd"][s5]), reads=("l0fwS",),
            writes=("l0fwS",), name=f"l0dwl{s5}")
        emit_fp4_quant_chunked(
            cx, wdf, wdnat, ("l0fwS",), ("l0fwS",), 2816, dsc2,
            f"l0dq{s5}", chunk=512)
        for b4 in range(3):
            lo = b4 * 8
            hi2 = min(lo + 8, 22)
            for kt in range(lo, hi2):
                em.add("tensor", lambda e, kt=kt, j=kt - lo: e.transpose(
                    cx.psT[0][:, 128 * j:128 * (j + 1)],
                    wdnat[:, 128 * kt:128 * (kt + 1)], cx.ident[:]),
                    reads=("l0fwS", "ident"), writes=("psT0",),
                    name=f"l0dtt{s5}_{kt}")
            for kt in range(lo, hi2):
                sl, slk = dslab(kt)
                em.add("vector", lambda e, kt=kt, j=kt - lo, sl=sl, s5=s5:
                       e.tensor_copy(
                    out=sl[:, 128 * s5:128 * (s5 + 1)],
                    in_=cx.psT[0][:, 128 * j:128 * (j + 1)]),
                    reads=("psT0",), writes=(slk,), name=f"l0dte{s5}_{kt}")
    for i in range(NT):
        hb = hqin[i % 2]
        hk2 = f"l0hqi{i % 2}"
        em.dma("sync", lambda e, i=i, hb=hb: e.dma_start(
            out=hb.rearrange("p (k t) -> p k t", t=128),
            in_=hq_dram[:, :, 128 * i:128 * (i + 1)].rearrange(
                "k p t -> p k t")),
            reads=("l0hqd",), writes=(hk2,), name=f"l0hqr{i}")
        for half in range(2):
            ps = cx.psum[half]
            for kt in range(22):
                sl, slk = dslab(kt)
                em.add("tensor", lambda e, ps=ps, kt=kt, sl=sl, slk=slk,
                       hb=hb, hk2=hk2, half=half: e.matmul(
                    ps[:, 0:512], hb[:, 128 * kt:128 * (kt + 1)],
                    sl[:, 512 * half:512 * (half + 1)],
                    start=(kt == 0), stop=(kt == 21)),
                    reads=(hk2, slk), writes=(f"ps{half}",),
                    name=f"l0dm{i}_{half}_{kt}")
            em.add("scalar", lambda e, ps=ps, i=i, half=half: e.activation(
                out=g_tm[:, 1024 * i + 512 * half:
                         1024 * i + 512 * (half + 1)],
                in_=ps[:, 0:512], func=AF.Silu),
                reads=(f"ps{half}",), writes=("l0g",),
                name=f"l0ge{i}_{half}")
    em.fence()
    dump_sb("dbg_g", g_tm[:, 0:16384], ("l0g",))


def build_program(layers, trunc=None):
    nc = bass.Bass("TRN2", target_bir_lowering=False)
    x_in = nc.dram_tensor("x_in", [HALF, D], F32, kind="ExternalInput").ap()
    x_out = nc.dram_tensor("x_out", [HALF, D], F32, kind="ExternalOutput").ap()
    xmids = [nc.dram_tensor(f"x_mid{j}", [HALF, D], F32).ap()
             for j in range(max(0, len(layers) - 1))]
    decl = {0: declare_l0_dram, 1: declare_l1_dram, 2: declare_l2_dram, 3: declare_l3_dram}
    build = {0: build_l0, 1: build_l1, 2: build_l2, 3: build_l3}
    dr = {}
    for li in layers:
        dr.update(decl[li](nc))
    with ExitStack() as ctx:
        cx = new_ctx(nc, ctx)
        make_identity(cx)
        cur_in = x_in
        for j, li in enumerate(layers):
            last = j == len(layers) - 1
            cur_out = x_out if last else xmids[j]
            cx.cur_x = cur_in
            cx.x_rkey = None if j == 0 else f"xm{j - 1}"
            cx.x_wkey = None if last else f"xm{j}"
            with ExitStack() as lctx:
                outer_ln_streamed(cx, lctx, cur_in, tag=f"oln{li}")
                build[li](cx, lctx, dr)
                if li == 3:
                    final_update_l3(cx, lctx, cur_in, cur_out)
                else:
                    final_update_streamed(cx, lctx, cur_in, cur_out, cx.s_tm,
                                          cx.s_key, cx.g_tm, cx.g_key)
            cx.em.fence()
            cur_in = cur_out
        if trunc is not None:
            cx.em.ops = cx.em.ops[:trunc]
            keep = set(id(o) for o in cx.em.ops)
            cx.em.out_dmas = [o for o in cx.em.out_dmas if id(o) in keep]
        cx.em.prepare(ctx)
        block = ctx.enter_context(nc.Block())
        cx.em.emit(block)
    return nc


PREP = {0: prep_l0_weights, 1: prep_l1_weights, 2: prep_l2_weights, 3: prep_l3_weights}


def run_layers(layers, x, inputs, trunc=None):
    nc = build_program(layers, trunc=trunc)
    in_maps = []
    for c in range(8):
        b, h = c // 2, c % 2
        m = {"x_in": np.ascontiguousarray(x[b, HALF * h:HALF * (h + 1)])}
        for li in layers:
            m.update(PREP[li](inputs, h))
        in_maps.append(m)
    res = run_bass_kernel_spmd(nc, in_maps, core_ids=list(range(8)))
    out = np.empty_like(x)
    for c in range(8):
        b, h = c // 2, c % 2
        out[b, HALF * h:HALF * (h + 1)] = res.results[c]["x_out"]
    if "dbg_s" in res.results[0]:
        out = (out, [dict(r) for r in res.results])
    return out


def kernel(**inputs):
    x = np.asarray(inputs["L0_attn_x"], np.float32)
    return run_layers([0, 1, 2, 3], x, inputs)


# version 09
# speedup vs baseline: 1.0623x; 1.0623x over previous
import numpy as np
from contextlib import ExitStack

import concourse.bass as bass
import concourse.mybir as mybir
from concourse.bass_utils import run_bass_kernel_spmd

F32 = mybir.dt.float32
BF16 = mybir.dt.bfloat16
I32 = mybir.dt.int32
U32 = mybir.dt.uint32
AF = mybir.ActivationFunctionType
ALU = mybir.AluOpType
AX = mybir.AxisListType

B, N, D = 4, 4096, 1024
HALF = N // 2
NT = HALF // 128
NF = D // 128
LN_EPS = 1e-5
DEBUG_L0 = False


ENGS = ("sync", "vector", "scalar", "tensor", "gpsimd")


class Op:
    __slots__ = ("idx", "eng", "fn", "kind", "deps", "need_inc", "val", "name")

    def __init__(self, idx, eng, fn, kind, deps, name):
        self.idx = idx
        self.eng = eng
        self.fn = fn
        self.kind = kind
        self.deps = deps
        self.need_inc = False
        self.val = None
        self.name = name


class Emitter:

    def __init__(self, nc):
        self.nc = nc
        self.ops = []
        self.last_writer = {}
        self.readers = {}
        self.out_dmas = []
        self.fence_deps = set()

    def fence(self):
        last = {}
        for op in self.ops:
            last[(op.eng, op.kind)] = op
        self.fence_deps = set(last.values())

    def add(self, eng, fn, reads=(), writes=(), kind="comp", name="", extra_deps=()):
        deps = set()
        for k in reads:
            w = self.last_writer.get(k)
            if w is not None:
                deps.add(w)
        for k in writes:
            w = self.last_writer.get(k)
            if w is not None:
                deps.add(w)
            for r in self.readers.get(k, ()):
                deps.add(r)
        for d in extra_deps:
            deps.add(d)
        deps |= self.fence_deps
        op = Op(len(self.ops), eng, fn, kind, deps, name)
        deps.discard(op)
        self.ops.append(op)
        for k in reads:
            self.readers.setdefault(k, []).append(op)
        for k in writes:
            self.last_writer[k] = op
            self.readers[k] = []
        return op

    def dma(self, eng, fn, reads=(), writes=(), name="", extra_deps=()):
        return self.add(eng, fn, reads, writes, kind="dma", name=name,
                        extra_deps=extra_deps)

    def cc(self, fn, reads=(), writes=(), name="", extra_deps=()):
        return self.add("gpsimd", fn, reads, writes, kind="cc", name=name,
                        extra_deps=extra_deps)

    def out_dma(self, eng, fn, reads=(), name=""):
        op = self.dma(eng, fn, reads=reads, writes=(), name=name)
        self.out_dmas.append(op)
        return op

    def _same_eng_skip(self, d, op):
        return (d.eng == op.eng and d.kind == "comp" and op.kind == "comp"
                and op.eng == "tensor")

    def prepare(self, ctx):
        nc = self.nc
        for op in self.ops:
            if op.kind != "comp":
                op.need_inc = True
            for d in op.deps:
                if self._same_eng_skip(d, op):
                    continue
                d.need_inc = True
        for op in self.out_dmas:
            op.need_inc = True
        sems = {}
        counts = {}
        for op in self.ops:
            key = (op.eng, op.kind)
            inc = 16 if op.kind == "dma" else 1
            if op.need_inc:
                counts[key] = counts.get(key, 0) + inc
                op.val = counts[key]
        for key in counts:
            sems[key] = ctx.enter_context(
                nc.semaphore(f"sem_{key[0]}_{key[1]}"))
        self.sems = sems
        self.counts = counts

    def emit(self, block):
        sems = self.sems
        per_eng = {e: [] for e in ENGS}
        for op in self.ops:
            per_eng[op.eng].append(op)
        prefix = {}
        for op in self.ops:
            if op.kind != "comp":
                key = (op.eng, op.kind)
                prefix.setdefault(key, []).append((op.idx, op.val))
        import bisect as _bis

        def async_val(key, before_idx):
            lst = prefix[key]
            i = _bis.bisect_left(lst, (before_idx, -1))
            return lst[i - 1][1]

        def emit_stream(eng_name, eng):
            waited = {}
            for op in per_eng[eng_name]:
                for d in sorted(op.deps, key=lambda o: o.idx):
                    if self._same_eng_skip(d, op):
                        continue
                    key = (d.eng, d.kind)
                    assert d.val is not None, (d.name, op.name)
                    v = d.val if d.kind == "comp" else async_val(key, op.idx)
                    if waited.get(key, 0) < v:
                        eng.wait_ge(sems[key], v)
                        waited[key] = v
                instr = op.fn(eng)
                if op.need_inc:
                    key = (op.eng, op.kind)
                    instr.then_inc(sems[key], 16 if op.kind == "dma" else 1)
            if eng_name == "sync":
                fin = {}
                for op in self.out_dmas:
                    key = (op.eng, op.kind)
                    fin[key] = max(fin.get(key, 0), op.val)
                for key, v in fin.items():
                    if waited.get(key, 0) < v:
                        eng.wait_ge(sems[key], v)

        @block.sync
        def _(sync):
            emit_stream("sync", sync)

        @block.vector
        def _(vector):
            emit_stream("vector", vector)

        @block.scalar
        def _(scalar):
            emit_stream("scalar", scalar)

        @block.tensor
        def _(tensor):
            emit_stream("tensor", tensor)

        @block.gpsimd
        def _(gpsimd):
            emit_stream("gpsimd", gpsimd)



class Ctx:
    pass


def alloc_common(cx, ctx):
    nc = cx.nc
    sb = lambda n, s, d: ctx.enter_context(nc.sbuf_tensor(n, s, d))
    cx.ident_i = sb("ident_i", [128, 128], I32)
    cx.ident = sb("ident", [128, 128], BF16)
    cx.psum = [ctx.enter_context(nc.psum_tensor(f"psA{i}", [128, 512], F32))
               for i in range(4)]
    cx.psT = [ctx.enter_context(nc.psum_tensor(f"psT{i}", [128, 1024], BF16))
              for i in range(2)]
    cx.psB = ctx.enter_context(nc.psum_tensor("psB", [128, 512], F32))
    cx.stat = sb("stat", [128, 6 * NT], F32)
    cx.sq_junk = sb("sq_junk", [128, 1024], BF16)
    cx.sgbuf = sb("sgbuf", [128, 1024], F32)


def make_identity(cx):
    em = cx.em
    em.add("gpsimd", lambda e: e.iota(cx.ident_i[:], [[1, 128]], base=0,
                                      channel_multiplier=-1),
           writes=("ident_i",), name="iota_ident")
    em.add("vector", lambda e: e.tensor_scalar(
        out=cx.ident[:], in0=cx.ident_i[:], scalar1=0.0, scalar2=None,
        op0=ALU.is_equal), reads=("ident_i",), writes=("ident",),
        name="ident_eq")


def layernorm_tm(cx, x_tile, xn_tile, n_tiles, x_key, xn_key, d=1024,
                 eps=LN_EPS, tag="ln"):
    em = cx.em
    st = cx.stat
    k = n_tiles
    mu = st[:, 0:k]
    sq = st[:, k:2 * k]
    var = st[:, 2 * k:3 * k]
    rstd = st[:, 3 * k:4 * k]
    tmp = st[:, 4 * k:5 * k]
    nmu = st[:, 5 * k:6 * k]
    for i in range(n_tiles):
        xt = x_tile(i)
        em.add("vector", lambda e, xt=xt, i=i: e.tensor_reduce(
            out=mu[:, i:i + 1], in_=xt, axis=AX.X, op=ALU.add),
            reads=(x_key,), writes=("stat",), name=f"{tag}_sum{i}")
        em.add("scalar", lambda e, xt=xt, i=i: e.activation(
            out=cx.sq_junk[:, 0:d], in_=xt, func=AF.Square,
            accum_out=sq[:, i:i + 1]),
            reads=(x_key,), writes=("stat", "sq_junk"), name=f"{tag}_sq{i}")
    s = ("stat",)
    em.add("vector", lambda e: e.tensor_scalar_mul(mu, mu, 1.0 / d),
           reads=s, writes=s, name=f"{tag}_mu")
    em.add("vector", lambda e: e.tensor_scalar_mul(sq, sq, 1.0 / d),
           reads=s, writes=s, name=f"{tag}_msq")
    em.add("vector", lambda e: e.scalar_tensor_tensor(
        out=var, in0=mu, scalar=-1.0, in1=mu, op0=ALU.mult, op1=ALU.mult),
        reads=s, writes=s, name=f"{tag}_mu2")
    em.add("vector", lambda e: e.tensor_add(var, var, sq),
           reads=s, writes=s, name=f"{tag}_var")
    em.add("vector", lambda e: e.tensor_scalar_add(var, var, eps),
           reads=s, writes=s, name=f"{tag}_vare")
    em.add("scalar", lambda e: e.activation(out=rstd, in_=var, func=AF.Sqrt),
           reads=s, writes=s, name=f"{tag}_sqrt")
    em.add("vector", lambda e: e.reciprocal(out=tmp, in_=rstd),
           reads=s, writes=s, name=f"{tag}_rc1")
    em.add("vector", lambda e: e.tensor_mul(tmp, tmp, var),
           reads=s, writes=s, name=f"{tag}_vs")
    em.add("vector", lambda e: e.tensor_add(rstd, rstd, tmp),
           reads=s, writes=s, name=f"{tag}_ad")
    em.add("vector", lambda e: e.tensor_scalar_mul(rstd, rstd, 0.5),
           reads=s, writes=s, name=f"{tag}_hf")
    em.add("vector", lambda e: e.reciprocal(out=rstd, in_=rstd),
           reads=s, writes=s, name=f"{tag}_rstd")
    em.add("vector", lambda e: e.tensor_scalar_mul(nmu, mu, -1.0),
           reads=s, writes=s, name=f"{tag}_nmu")
    for i in range(n_tiles):
        xt = x_tile(i)
        ot = xn_tile(i)
        em.add("vector", lambda e, xt=xt, ot=ot, i=i: e.tensor_scalar(
            out=ot, in0=xt, scalar1=nmu[:, i:i + 1], scalar2=rstd[:, i:i + 1],
            op0=ALU.add, op1=ALU.mult),
            reads=(x_key, "stat"), writes=(xn_key,), name=f"{tag}_nrm{i}")


def transpose_tm_to_fm(cx, tm_ap, fm_ap, tm_key, fm_key, n_tok_tiles,
                       n_feat_tiles, tag="tr", evac_eng="vector"):
    em = cx.em
    ntok = n_tok_tiles * 128
    dw = n_feat_tiles * 128
    for f in range(n_feat_tiles):
        for half in range((n_tok_tiles + 7) // 8):
            bank = cx.psT[half % 2]
            bkey = f"psT{half % 2}"
            lo = half * 8
            hi = min(lo + 8, n_tok_tiles)
            for i in range(lo, hi):
                src = tm_ap[:, dw * i + 128 * f: dw * i + 128 * (f + 1)]
                em.add("tensor",
                       lambda e, src=src, j=i - lo, bank=bank:
                       e.transpose(bank[:, 128 * j:128 * (j + 1)], src,
                                   cx.ident[:]),
                       reads=(tm_key, "ident"), writes=(bkey,),
                       name=f"{tag}_t{f}_{i}")
            dst = fm_ap[:, ntok * f + 128 * lo: ntok * f + 128 * hi]
            em.add(evac_eng,
                   lambda e, dst=dst, bank=bank, w=(hi - lo) * 128:
                   e.tensor_copy(out=dst, in_=bank[:, 0:w]),
                   reads=(bkey,), writes=(fm_key,), name=f"{tag}_e{f}_{half}")


def bcast_row(cx, dst_ap, src_row_ap, src_keys, dst_keys, width, name="bc"):
    em = cx.em
    nchunk = (width + 511) // 512
    for c in range(nchunk):
        w = min(512, width - 512 * c)
        ps = cx.psB
        em.add("tensor", lambda e, c=c, w=w: e.matmul(
            ps[:, 0:w], cx.ones1[:, :], src_row_ap[:, 512 * c:512 * c + w],
            start=True, stop=True),
            reads=tuple(src_keys) + ("ones1",), writes=("psB",),
            name=f"{name}_mm{c}")
        em.add("vector", lambda e, c=c, w=w: e.tensor_copy(
            out=dst_ap[:, 512 * c:512 * c + w], in_=ps[:, 0:w]),
            reads=("psB",), writes=tuple(dst_keys), name=f"{name}_ev{c}")


def make_ones1(cx, ctx):
    nc = cx.nc
    cx.ones1 = ctx.enter_context(nc.sbuf_tensor("ones1", [1, 128], BF16))
    cx.em.add("vector", lambda e: e.memset(cx.ones1[:], 1.0),
              writes=("ones1",), name="ones1")



def fmpack_f32(w):
    K, F = w.shape
    KT, FT = K // 128, F // 128
    return np.ascontiguousarray(
        w.reshape(KT, 128, FT, 128).transpose(2, 1, 0, 3).reshape(FT, 128, K))


def fmpack(w):
    import ml_dtypes
    return fmpack_f32(w).astype(ml_dtypes.bfloat16)


def slabpack_f32(w):
    K, F = w.shape
    return np.ascontiguousarray(w.reshape(K // 128, 128, F))


def slabpack(w):
    import ml_dtypes
    return slabpack_f32(w).astype(ml_dtypes.bfloat16)



def new_ctx(nc, ctx):
    cx = Ctx()
    cx.nc = nc
    cx.em = Emitter(nc)
    cx.uid = 0
    alloc_common(cx, ctx)
    make_ones1(cx, ctx)
    return cx


def t_ap(t, *idx):
    return t[idx] if idx else t[:]


def load_x_and_norm(cx, ctx, x_dram, with_fm=True):
    nc, em = cx.nc, cx.em
    sb = lambda n, s, d: ctx.enter_context(nc.sbuf_tensor(n, s, d))
    cx.x_tm = sb("x_tm", [128, NT * 1024], F32)
    cx.xn_tm = sb("xn_tm", [128, NT * 1024], BF16)
    if with_fm:
        cx.xn_fm = sb("xn_fm", [128, NF * HALF], BF16)
    xr = x_dram.rearrange("(i p) d -> i p d", p=128)
    for i in range(NT):
        em.dma("sync", lambda e, i=i: e.dma_start(
            out=cx.x_tm[:, 1024 * i:1024 * (i + 1)], in_=xr[i]),
            writes=("x_tm",), name=f"ldx{i}")
    layernorm_tm(cx, lambda i: cx.x_tm[:, 1024 * i:1024 * (i + 1)],
                 lambda i: cx.xn_tm[:, 1024 * i:1024 * (i + 1)],
                 NT, "x_tm", "xn_tm", tag="oln")
    if with_fm:
        transpose_tm_to_fm(cx, cx.xn_tm, cx.xn_fm, "xn_tm", "xn_fm", NT, NF,
                           tag="xtr")


def store_x_update(cx, x_dram, s_tm_ap, s_key, g_tm_ap, g_key):
    nc, em = cx.nc, cx.em
    xr = x_dram.rearrange("(i p) d -> i p d", p=128)
    for i in range(NT):
        sl = slice(1024 * i, 1024 * (i + 1))
        em.add("vector", lambda e, sl=sl: e.scalar_tensor_tensor(
            out=cx.xn_tm[:, sl], in0=s_tm_ap[:, sl], scalar=1.0,
            in1=g_tm_ap[:, sl], op0=ALU.mult, op1=ALU.mult),
            reads=(s_key, g_key), writes=("xn_tm",), name=f"sg{i}")
        em.add("vector", lambda e, sl=sl: e.tensor_add(
            cx.x_tm[:, sl], cx.x_tm[:, sl], cx.xn_tm[:, sl]),
            reads=("xn_tm", "x_tm"), writes=("x_tm",), name=f"xupd{i}")
        em.out_dma("sync", lambda e, i=i, sl=sl: e.dma_start(
            out=xr[i], in_=cx.x_tm[:, sl]),
            reads=("x_tm",), name=f"stx{i}")


def proj_fm(cx, w_dram_ft, src_fm_ap, src_key, dst_write, dst_key, wbuf,
            wbuf_key, n_ft, n_kt, ntok, tag, evac, kt_of_src=None,
            dtype_w=F32):
    em = cx.em
    nch = (ntok + 511) // 512
    for f in range(n_ft):
        wb = wbuf[f % 2]
        wk = wbuf_key[f % 2]
        em.dma("sync", lambda e, f=f, wb=wb: e.dma_start(
            out=wb, in_=w_dram_ft[f]), writes=(wk,), name=f"{tag}_w{f}")
        psl = []
        for c in range(nch):
            w = min(512, ntok - 512 * c)
            ps = cx.psum[c]
            psl.append((ps, w))
        for k in range(n_kt):
            lw = em.add("tensor", lambda e, wb=wb, k=k: e.ldweights(
                wb[:, 128 * k:128 * (k + 1)]),
                reads=(wk,), writes=(), name=f"{tag}_lw{f}_{k}")
            for c in range(nch):
                ps, w = psl[c]
                em.add("tensor", lambda e, wb=wb, k=k, c=c, ps=ps, w=w:
                       e.matmul(ps[:, 0:w], wb[:, 128 * k:128 * (k + 1)],
                                src_fm_ap[:, ntok * k + 512 * c:
                                          ntok * k + 512 * c + w],
                                start=(k == 0), stop=(k == n_kt - 1)),
                       reads=(wk, src_key), writes=(f"ps{c}",),
                       name=f"{tag}_mm{f}_{k}_{c}")
        evac(f, psl)


def transpose_fm_to_tm(cx, fm_ap, tm_ap, fm_key, tm_key, n_tok_tiles,
                       n_feat_tiles, tag="trb", evac_eng="vector"):
    em = cx.em
    ntok = n_tok_tiles * 128
    dw = n_feat_tiles * 128
    for i in range(n_tok_tiles):
        bank = cx.psT[i % 2]
        bkey = f"psT{i % 2}"
        for f in range(n_feat_tiles):
            src = fm_ap[:, ntok * f + 128 * i: ntok * f + 128 * (i + 1)]
            em.add("tensor", lambda e, src=src, f=f, bank=bank:
                   e.transpose(bank[:, 128 * f:128 * (f + 1)], src, cx.ident[:]),
                   reads=(fm_key, "ident"), writes=(bkey,), name=f"{tag}_t{i}_{f}")
        dst = tm_ap[:, dw * i: dw * (i + 1)]
        em.add(evac_eng, lambda e, dst=dst, bank=bank, w=dw:
               e.tensor_copy(out=dst, in_=bank[:, 0:w]),
               reads=(bkey,), writes=(tm_key,), name=f"{tag}_e{i}")


def outer_ln_streamed(cx, ctx, x_dram, tag="oln"):
    nc, em = cx.nc, cx.em
    sb = lambda n, s, d: ctx.enter_context(nc.sbuf_tensor(n, s, d))
    cx.xn_tm = sb(f"xn_tm_{tag}", [128, NT * 1024], BF16)
    cx.xn_fm = sb(f"xn_fm_{tag}", [128, NF * HALF], BF16)
    cx.xbuf = sb(f"xbuf_{tag}", [128, 2 * 1024], F32)
    xrk = getattr(cx, "x_rkey", None)
    xdep = (xrk,) if xrk else ()
    xr = x_dram.rearrange("(i p) d -> i p d", p=128)
    st = cx.stat
    k = NT
    mu, sq = st[:, 0:k], st[:, k:2 * k]
    var, rstd = st[:, 2 * k:3 * k], st[:, 3 * k:4 * k]
    tmp, nmu = st[:, 4 * k:5 * k], st[:, 5 * k:6 * k]
    for i in range(NT):
        xb = cx.xbuf[:, 1024 * (i % 2):1024 * (i % 2 + 1)]
        bk = f"xbuf{i % 2}"
        em.dma("sync", lambda e, i=i, xb=xb: e.dma_start(out=xb, in_=xr[i]),
               reads=xdep, writes=(bk,), name=f"{tag}_ld{i}")
        em.add("vector", lambda e, xb=xb, i=i: e.tensor_reduce(
            out=mu[:, i:i + 1], in_=xb, axis=AX.X, op=ALU.add),
            reads=(bk,), writes=("stat",), name=f"{tag}_sum{i}")
        em.add("scalar", lambda e, xb=xb, i=i: e.activation(
            out=cx.sq_junk[:, 0:1024], in_=xb, func=AF.Square,
            accum_out=sq[:, i:i + 1]),
            reads=(bk,), writes=("stat", "sq_junk"), name=f"{tag}_sq{i}")
    s = ("stat",)
    em.add("vector", lambda e: e.tensor_scalar_mul(mu, mu, 1.0 / D),
           reads=s, writes=s, name=f"{tag}_mu")
    em.add("vector", lambda e: e.tensor_scalar_mul(sq, sq, 1.0 / D),
           reads=s, writes=s, name=f"{tag}_msq")
    em.add("vector", lambda e: e.scalar_tensor_tensor(
        out=var, in0=mu, scalar=-1.0, in1=mu, op0=ALU.mult, op1=ALU.mult),
        reads=s, writes=s, name=f"{tag}_mu2")
    em.add("vector", lambda e: e.tensor_add(var, var, sq),
           reads=s, writes=s, name=f"{tag}_var")
    em.add("vector", lambda e: e.tensor_scalar_add(var, var, LN_EPS),
           reads=s, writes=s, name=f"{tag}_vare")
    em.add("scalar", lambda e: e.activation(out=rstd, in_=var, func=AF.Sqrt),
           reads=s, writes=s, name=f"{tag}_sqrt")
    em.add("vector", lambda e: e.reciprocal(out=tmp, in_=rstd),
           reads=s, writes=s, name=f"{tag}_rc1")
    em.add("vector", lambda e: e.tensor_mul(tmp, tmp, var),
           reads=s, writes=s, name=f"{tag}_vs")
    em.add("vector", lambda e: e.tensor_add(rstd, rstd, tmp),
           reads=s, writes=s, name=f"{tag}_ad")
    em.add("vector", lambda e: e.tensor_scalar_mul(rstd, rstd, 0.5),
           reads=s, writes=s, name=f"{tag}_hf")
    em.add("vector", lambda e: e.reciprocal(out=rstd, in_=rstd),
           reads=s, writes=s, name=f"{tag}_rstd")
    em.add("vector", lambda e: e.tensor_scalar_mul(nmu, mu, -1.0),
           reads=s, writes=s, name=f"{tag}_nmu")
    for i in range(NT):
        xb = cx.xbuf[:, 1024 * (i % 2):1024 * (i % 2 + 1)]
        bk = f"xbuf{i % 2}"
        em.dma("sync", lambda e, i=i, xb=xb: e.dma_start(out=xb, in_=xr[i]),
               reads=xdep, writes=(bk,), name=f"{tag}_ld2_{i}")
        em.add("vector", lambda e, xb=xb, i=i: e.tensor_scalar(
            out=cx.xn_tm[:, 1024 * i:1024 * (i + 1)], in0=xb,
            scalar1=nmu[:, i:i + 1], scalar2=rstd[:, i:i + 1],
            op0=ALU.add, op1=ALU.mult),
            reads=(bk, "stat"), writes=("xn_tm",), name=f"{tag}_nrm{i}")
    transpose_tm_to_fm(cx, cx.xn_tm, cx.xn_fm, "xn_tm", "xn_fm", NT, NF,
                       tag=f"{tag}_tr")


def final_update_streamed(cx, ctx, x_dram, xout_dram, s_tm, s_key, g_tm, g_key):
    nc, em = cx.nc, cx.em
    xrk = getattr(cx, "x_rkey", None)
    xdep = (xrk,) if xrk else ()
    xwk = getattr(cx, "x_wkey", None)
    xr = x_dram.rearrange("(i p) d -> i p d", p=128)
    xo = xout_dram.rearrange("(i p) d -> i p d", p=128)
    for i in range(NT):
        xb = cx.xbuf[:, 1024 * (i % 2):1024 * (i % 2 + 1)]
        bk = f"xbuf{i % 2}"
        sl = slice(1024 * i, 1024 * (i + 1))
        em.dma("sync", lambda e, i=i, xb=xb: e.dma_start(out=xb, in_=xr[i]),
               reads=xdep, writes=(bk,), name=f"fu_ld{i}")
        em.add("vector", lambda e, sl=sl: e.scalar_tensor_tensor(
            out=cx.sgbuf[:], in0=s_tm[:, sl], scalar=1.0, in1=g_tm[:, sl],
            op0=ALU.mult, op1=ALU.mult),
            reads=(s_key, g_key), writes=("sgbuf",), name=f"fu_sg{i}")
        em.add("vector", lambda e, xb=xb: e.tensor_add(xb, xb, cx.sgbuf[:]),
            reads=("sgbuf", bk), writes=(bk,), name=f"fu_add{i}")
        if xwk:
            em.dma("sync", lambda e, i=i, xb=xb: e.dma_start(
                out=xo[i], in_=xb), reads=(bk,), writes=(xwk,),
                name=f"fu_st{i}")
        else:
            em.out_dma("sync", lambda e, i=i, xb=xb: e.dma_start(
                out=xo[i], in_=xb), reads=(bk,), name=f"fu_st{i}")



def _flag8(nc):
    if not hasattr(nc, "_flag8_ap"):
        nc._flag8_ap = nc.dram_tensor("flag8", [1, 8], F32,
                                      kind="ExternalInput").ap()
    return nc._flag8_ap


def prep_l2_weights(inputs, h):
    w = {}
    ln_w = np.asarray(inputs["L2_attn_ln_w"])
    ln_b = np.asarray(inputs["L2_attn_ln_b"])
    w["l2_lnw"] = np.ascontiguousarray(ln_w.reshape(2, 8, 128))
    w["l2_lnb"] = np.ascontiguousarray(ln_b.reshape(2, 8, 128))
    for nm, key in (("wk", "L2_attn_Wk"), ("wv", "L2_attn_Wv"),
                    ("wr", "L2_attn_Wr"), ("wo", "L2_attn_Wo")):
        m = np.asarray(inputs[key])
        w[f"l2_{nm}"] = np.stack([fmpack(np.ascontiguousarray(m[i].T))
                                  for i in range(2)])
    w["l2f_wg"] = fmpack_f32(np.ascontiguousarray(np.asarray(inputs["L2_ffn_w_gate"]).T))
    w["l2f_wu"] = fmpack_f32(np.ascontiguousarray(np.asarray(inputs["L2_ffn_w_up"]).T))
    w["l2f_wd"] = slabpack_f32(np.ascontiguousarray(np.asarray(inputs["L2_ffn_w_down"]).T))
    w["flag8"] = np.full((1, 8), float(h), np.float32)
    return w


def declare_l2_dram(nc):
    d = {}
    d["l2_lnw"] = nc.dram_tensor("l2_lnw", [2, 8, 128], F32, kind="ExternalInput").ap()
    d["l2_lnb"] = nc.dram_tensor("l2_lnb", [2, 8, 128], F32, kind="ExternalInput").ap()
    for nm in ("wk", "wv", "wr", "wo"):
        d[f"l2_{nm}"] = nc.dram_tensor(f"l2_{nm}", [2, 8, 128, 1024], BF16,
                                       kind="ExternalInput").ap()
    d["l2f_wg"] = nc.dram_tensor("l2f_wg", [24, 128, 1024], I32, kind="ExternalInput").ap()
    d["l2f_wu"] = nc.dram_tensor("l2f_wu", [24, 128, 1024], I32, kind="ExternalInput").ap()
    d["l2f_wd"] = nc.dram_tensor("l2f_wd", [24, 128, 1024], I32, kind="ExternalInput").ap()
    d["flag8"] = _flag8(nc)
    d["l2_cc_in"] = nc.dram_tensor("l2_cc_in", [128, 4], F32).ap()
    d["l2_cc_out"] = nc.dram_tensor("l2_cc_out", [256, 4], F32).ap()
    for nm in ("wg16c", "wu16c", "wd16c"):
        d[nm] = nc.dram_tensor(nm, [24, 128, 1024], BF16).ap()
    return d


def build_l2(cx, ctx, dr):
    nc, em = cx.nc, cx.em
    sb = lambda n, s, d: ctx.enter_context(nc.sbuf_tensor(n, s, d))
    zfm = sb("l2_zfm", [128, NF * HALF], BF16)
    out1 = sb("l2_out1", [128, NF * HALF], BF16)
    g_tm = out1
    tmb = cx.xn_tm
    em.fence()
    psC = ctx.enter_context(nc.psum_tensor("psC", [128, 512], F32))
    FG = 4

    from contextlib import ExitStack as _ES
    with _ES() as scr:
        sc = lambda n, s, d: scr.enter_context(nc.sbuf_tensor(n, s, d))
        rd = sc("l2_rd", [128, FG * HALF], BF16)
        kf = sc("l2_kf", [128, HALF], F32)
        rf = sc("l2_rf", [128, HALF], BF16)
        kvf = sc("l2_kvf", [128, HALF], F32)
        stf = sc("l2_stf", [128, HALF], F32)
        dpow = sc("l2_dpow", [128, HALF], BF16)
        dec = cx.xbuf[:, 0:HALF]
        zero1 = sc("l2_zero1", [128, 1], F32)
        carry = sc("l2_carry", [128, 4], F32)
        carry_in = sc("l2_carry_in", [128, 4], F32)
        flag_s = sc("l2_flag", [1, 8], F32)
        lnv = sc("l2_lnv", [128, 32], F32)
        lnstage = sc("l2_lnstage", [128, 128], F32)
        ident32 = sc("ident32", [128, 128], F32)
        wkb = sc("l2_wkb", [128, 1024], BF16)
        wvb = sc("l2_wvb", [128, 1024], BF16)
        wrb = sc("l2_wrb", [128, 1024], BF16)
        wob = sc("l2_wob", [128, 1024], BF16)
        ones1f = sc("ones1f", [1, 128], F32)

        em.add("vector", lambda e: e.tensor_scalar(
            out=ident32[:], in0=cx.ident_i[:], scalar1=0.0, scalar2=None,
            op0=ALU.is_equal), reads=("ident_i",), writes=("ident32",),
            name="ident32")
        em.add("vector", lambda e: e.memset(ones1f[:], 1.0),
               writes=("ones1f",), name="ones1f")
        kfi = kf[:].bitcast(I32)
        em.add("gpsimd", lambda e: e.iota(kfi, [[1, HALF]], base=1,
                                          channel_multiplier=0),
               writes=("kf",), name="dpow_iota")
        em.add("scalar", lambda e: e.activation(
            out=dpow[:], in_=kfi, func=AF.Exp, scale=float(np.log(0.95))),
            reads=("kf",), writes=("dpow",), name="dpow_exp")
        em.add("vector", lambda e: e.memset(dec[:], 0.95), writes=("dec",),
               name="dec")
        em.add("vector", lambda e: e.memset(zero1[:], 0.0), writes=("zero1",),
               name="zero1")
        em.dma("sync", lambda e: e.dma_start(out=flag_s[:], in_=dr["flag8"][:]),
               writes=("flag_s",), name="flagld")
        em.add("tensor", lambda e: e.matmul(
            cx.psB[:, 0:8], ones1f[:, :], flag_s[:, 0:8], start=True, stop=True),
            reads=("ones1f", "flag_s"), writes=("psB",), name="fbc_mm")
        fbc = lnstage[:, 120:128]
        em.add("vector", lambda e: e.tensor_copy(out=fbc, in_=cx.psB[:, 0:8]),
               reads=("psB",), writes=("fbc",), name="fbc_ev")

        def load_ln_cols(sl):
            for (dram, col0) in ((dr["l2_lnw"], 0), (dr["l2_lnb"], 8)):
                em.add("vector", lambda e: e.memset(lnstage[:, 0:120], 0.0),
                       writes=("lnstage",), name="lnz")
                em.dma("sync", lambda e, dram=dram, sl=sl: e.dma_start(
                    out=lnstage[0:8, 0:128], in_=dram[sl]),
                    writes=("lnstage",), name="lnld")
                em.add("tensor", lambda e: e.transpose(
                    cx.psB[:, 0:128], lnstage[:, 0:128], ident32[:]),
                    reads=("lnstage", "ident32"), writes=("psB",), name="lntr")
                dst = lnv[:, 16 * sl + col0:16 * sl + col0 + 8]
                em.add("vector", lambda e, dst=dst: e.tensor_copy(
                    out=dst, in_=cx.psB[:, 0:8]),
                    reads=("psB",), writes=("lnv",), name="lnev")

        def sublayer(sl, src_fm, src_key, dst_fm, dst_keys):
            dst_key = dst_keys[0]
            load_ln_cols(sl)
            if sl == 0:
                layernorm_tm(cx, lambda i: tmb[:, 1024 * i:1024 * (i + 1)],
                             lambda i: tmb[:, 1024 * i:1024 * (i + 1)],
                             NT, "xn_tm", "ztm", tag=f"s{sl}ln")
            else:
                em.fence()
                transpose_fm_to_tm(cx, src_fm, tmb, src_key, "ztm", NT, NF,
                                   tag=f"s{sl}a")
                layernorm_tm(cx, lambda i: tmb[:, 1024 * i:1024 * (i + 1)],
                             lambda i: tmb[:, 1024 * i:1024 * (i + 1)],
                             NT, "ztm", "ztm", tag=f"s{sl}ln")
            transpose_tm_to_fm(cx, tmb, zfm, "ztm", "zfm", NT, NF,
                               tag=f"s{sl}b")
            for f in range(NF):
                wcol = lnv[:, 16 * sl + f:16 * sl + f + 1]
                bcol = lnv[:, 16 * sl + 8 + f:16 * sl + 8 + f + 1]
                em.add("vector", lambda e, f=f, wcol=wcol, bcol=bcol:
                       e.tensor_scalar(
                           out=zfm[:, HALF * f:HALF * (f + 1)],
                           in0=zfm[:, HALF * f:HALF * (f + 1)],
                           scalar1=wcol, scalar2=bcol, op0=ALU.mult,
                           op1=ALU.add),
                       reads=("zfm", "lnv"), writes=("zfm",), name=f"s{sl}af{f}")
            em.fence()
            y0 = tmb
            for grp in range(NF // FG):
                for j in range(FG):
                    f = FG * grp + j
                    for nm, buf, bkey in (("wk", wkb, "wkb"), ("wv", wvb, "wvb"),
                                          ("wr", wrb, "wrb")):
                        em.dma("sync",
                               lambda e, nm=nm, f=f, buf=buf:
                               e.dma_start(
                                   out=buf[:, 0:1024],
                                   in_=dr[f"l2_{nm}"][sl, f]),
                               writes=(bkey,), name=f"s{sl}w{nm}{f}")

                    def proj(buf, bkey, f=f):
                        for k in range(NF):
                            for c in range(4):
                                em.add("tensor",
                                       lambda e, buf=buf, k=k, c=c:
                                       e.matmul(
                                           cx.psum[c][:, 0:512],
                                           buf[:, 128 * k:128 * (k + 1)],
                                           zfm[:, HALF * k + 512 * c:
                                               HALF * k + 512 * (c + 1)],
                                           start=(k == 0), stop=(k == NF - 1)),
                                       reads=(bkey, "zfm"),
                                       writes=(f"ps{c}",),
                                       name=f"s{sl}mm{f}_{c}")

                    proj(wkb, "wkb")
                    for c in range(4):
                        em.add("vector", lambda e, c=c: e.tensor_copy(
                            out=kf[:, 512 * c:512 * (c + 1)],
                            in_=cx.psum[c][:, 0:512]),
                            reads=(f"ps{c}",), writes=("kf",),
                            name=f"s{sl}ke{f}_{c}")
                    proj(wvb, "wvb")
                    for c in range(4):
                        em.add("vector", lambda e, c=c: e.scalar_tensor_tensor(
                            out=kvf[:, 512 * c:512 * (c + 1)],
                            in0=cx.psum[c][:, 0:512], scalar=0.05,
                            in1=kf[:, 512 * c:512 * (c + 1)],
                            op0=ALU.mult, op1=ALU.mult),
                            reads=(f"ps{c}", "kf"), writes=("kvf",),
                            name=f"s{sl}kv{f}_{c}")
                    proj(wrb, "wrb")
                    for c in range(4):
                        em.add("scalar", lambda e, c=c: e.activation(
                            out=rf[:, 512 * c:512 * (c + 1)],
                            in_=cx.psum[c][:, 0:512], func=AF.Sigmoid),
                            reads=(f"ps{c}",), writes=("rf",),
                            name=f"s{sl}r{f}_{c}")
                    em.add("vector", lambda e: e.tensor_tensor_scan(
                        out=stf[:], data0=dec[:], data1=kvf[:],
                        initial=zero1[:], op0=ALU.mult, op1=ALU.add),
                        reads=("dec", "kvf", "zero1"), writes=("stf",),
                        name=f"s{sl}scan{f}")
                    em.add("vector", lambda e, j=j: e.tensor_copy(
                        out=carry[:, j:j + 1], in_=stf[:, HALF - 1:HALF]),
                        reads=("stf",), writes=("carry",), name=f"s{sl}cy{f}")
                    em.add("vector", lambda e, f=f: e.tensor_mul(
                        y0[:, HALF * f:HALF * (f + 1)], rf[:], stf[:]),
                        reads=("rf", "stf"), writes=("y0",), name=f"s{sl}y0{f}")
                    em.add("vector", lambda e, j=j: e.tensor_mul(
                        rd[:, HALF * j:HALF * (j + 1)], rf[:], dpow[:]),
                        reads=("rf", "dpow"), writes=("rd",), name=f"s{sl}rd{f}")
                em.dma("gpsimd", lambda e: e.dma_start(
                    out=dr["l2_cc_in"][:, :], in_=carry[:, :]),
                    reads=("carry",), writes=("cc_in",), name=f"s{sl}g{grp}ccst")
                em.cc(lambda e: e.collective_compute(
                    "AllGather", ALU.bypass,
                    replica_groups=[[0, 1], [2, 3], [4, 5], [6, 7]],
                    ins=[dr["l2_cc_in"].opt()],
                    outs=[dr["l2_cc_out"].opt()]),
                    reads=("cc_in",), writes=("cc_out",), name=f"s{sl}g{grp}cc")
                em.dma("gpsimd", lambda e: e.dma_start(
                    out=carry_in[:, :], in_=dr["l2_cc_out"][0:128, :]),
                    reads=("cc_out",), writes=("carry_in",),
                    name=f"s{sl}g{grp}ccld")
                em.add("vector", lambda e: e.tensor_mul(
                    carry_in[:, :], carry_in[:, :], fbc[:, 0:4]),
                    reads=("fbc", "carry_in"), writes=("carry_in",),
                    name=f"s{sl}g{grp}ce")
                for j in range(FG):
                    f = FG * grp + j
                    em.add("vector", lambda e, f=f, j=j: e.scalar_tensor_tensor(
                        out=y0[:, HALF * f:HALF * (f + 1)],
                        in0=rd[:, HALF * j:HALF * (j + 1)],
                        scalar=carry_in[:, j:j + 1],
                        in1=y0[:, HALF * f:HALF * (f + 1)],
                        op0=ALU.mult, op1=ALU.add),
                        reads=("rd", "carry_in", "y0"), writes=("y0",),
                        name=f"s{sl}fix{f}")
            for f in range(NF):
                em.dma("sync", lambda e, f=f: e.dma_start(
                    out=wob[:, 0:1024],
                    in_=dr["l2_wo"][sl, f]),
                    writes=("wob",), name=f"s{sl}wo{f}")
                for k in range(NF):
                    for c in range(4):
                        em.add("tensor", lambda e, f=f, k=k, c=c:
                               e.matmul(cx.psum[c][:, 0:512],
                                        wob[:, 128 * k:128 * (k + 1)],
                                        y0[:, HALF * k + 512 * c:
                                           HALF * k + 512 * (c + 1)],
                                        start=(k == 0), stop=(k == NF - 1)),
                               reads=("wob", "y0"), writes=(f"ps{c}",),
                               name=f"s{sl}womm{f}_{c}")
                for c in range(4):
                    em.add("vector", lambda e, f=f, c=c: e.scalar_tensor_tensor(
                        out=dst_fm[:, HALF * f + 512 * c:
                                   HALF * f + 512 * (c + 1)],
                        in0=cx.psum[c][:, 0:512], scalar=1.0,
                        in1=src_fm[:, HALF * f + 512 * c:
                                   HALF * f + 512 * (c + 1)],
                        op0=ALU.mult, op1=ALU.add),
                        reads=(f"ps{c}", src_key), writes=dst_keys,
                        name=f"s{sl}oev{f}_{c}")

        sublayer(0, cx.xn_fm, "xn_fm", out1, ("out1",))
        sublayer(1, out1, "out1", zfm, ("zfm_s", "zfm"))
        em.fence()
        transpose_fm_to_tm(cx, zfm, tmb, "zfm_s", "s_tm", NT, NF, tag="str")
        cx.s_tm, cx.s_key = tmb, "s_tm"
    em.fence()

    with _ES() as scr:
        sc = lambda n, s, d: scr.enter_context(nc.sbuf_tensor(n, s, d))
        wi32 = sc("l2_wi32", [128, 2 * 1024], I32)
        wgb = sc("l2_wgb", [128, 2 * 1024], BF16)
        wub = sc("l2_wub", [128, 2 * 1024], BF16)
        wdb = sc("l2_wdb", [128, 2 * 1024], BF16)
        hrnd = sc("l2_hrnd", [128, 24 * 256], BF16)
        sgb = sc("l2_sgb", [128, 256], BF16)
        for nm_src, nm_dst, wkey in (("l2f_wg", "wg16c", "wgq"),
                                     ("l2f_wu", "wu16c", "wuq"),
                                     ("l2f_wd", "wd16c", "wdq")):
            for f in range(24):
                half = f % 2
                em.dma("sync", lambda e, f=f, half=half, nm_src=nm_src:
                       e.dma_start(out=wi32[:, 1024 * half:1024 * (half + 1)],
                                   in_=dr[nm_src][f]),
                       writes=(f"wi32{half}",), name=f"dq_ld{nm_src}{f}")
                em.add("vector", lambda e, f=f, half=half: e.tensor_scalar_mul(
                    wgb[:, 1024 * half:1024 * (half + 1)],
                    wi32[:, 1024 * half:1024 * (half + 1)], 1.0 / 127.0),
                    reads=(f"wi32{half}",), writes=(f"wgb{half}",),
                    name=f"dq_ts{nm_src}{f}")
                em.dma("sync", lambda e, f=f, half=half, nm_dst=nm_dst:
                       e.dma_start(out=dr[nm_dst][f],
                                   in_=wgb[:, 1024 * half:1024 * (half + 1)]),
                       reads=(f"wgb{half}",), writes=(wkey,),
                       name=f"dq_st{nm_src}{f}")
        em.fence()
        for r in range(8):
            t0 = 256 * r
            for f in range(24):
                half = f % 2
                em.dma("sync", lambda e, f=f, half=half: e.dma_start(
                    out=wgb[:, 1024 * half:1024 * (half + 1)],
                    in_=dr["wg16c"][f]),
                    reads=("wgq",), writes=(f"wgb{half}",), name=f"ff_g{r}_{f}")
                em.dma("gpsimd", lambda e, f=f, half=half: e.dma_start(
                    out=wub[:, 1024 * half:1024 * (half + 1)],
                    in_=dr["wu16c"][f]),
                    reads=("wuq",), writes=(f"wub{half}",), name=f"ff_u{r}_{f}")
                em.dma("scalar", lambda e, f=f, half=half: e.dma_start(
                    out=wdb[:, 1024 * half:1024 * (half + 1)],
                    in_=dr["wd16c"][f]),
                    reads=("wdq",), writes=(f"wdb{half}",), name=f"ff_d{r}_{f}")
                for k in range(NF):
                    em.add("tensor", lambda e, f=f, k=k, half=half, t0=t0:
                           e.matmul(cx.psB[:, 0:256],
                                    wgb[:, 1024 * half + 128 * k:
                                        1024 * half + 128 * (k + 1)],
                                    cx.xn_fm[:, HALF * k + t0:
                                             HALF * k + t0 + 256],
                                    start=(k == 0), stop=(k == NF - 1)),
                           reads=(f"wgb{half}", "xn_fm"), writes=("psB",),
                           name=f"ff_gmm{r}_{f}_{k}")
                em.add("scalar", lambda e: e.activation(
                    out=sgb[:], in_=cx.psB[:, 0:256], func=AF.Silu),
                    reads=("psB",), writes=("sgb",), name=f"ff_sg{r}_{f}")
                for k in range(NF):
                    em.add("tensor", lambda e, f=f, k=k, half=half, t0=t0:
                           e.matmul(psC[:, 0:256],
                                    wub[:, 1024 * half + 128 * k:
                                        1024 * half + 128 * (k + 1)],
                                    cx.xn_fm[:, HALF * k + t0:
                                             HALF * k + t0 + 256],
                                    start=(k == 0), stop=(k == NF - 1)),
                           reads=(f"wub{half}", "xn_fm"), writes=("psC",),
                           name=f"ff_umm{r}_{f}_{k}")
                em.add("vector", lambda e, f=f: e.scalar_tensor_tensor(
                    out=hrnd[:, 256 * f:256 * (f + 1)], in0=psC[:, 0:256],
                    scalar=1.0, in1=sgb[:], op0=ALU.mult, op1=ALU.mult),
                    reads=("psC", "sgb"), writes=("hrnd",), name=f"ff_h{r}_{f}")
                for t2 in range(2):
                    for c in range(2):
                        em.add("tensor", lambda e, f=f, t2=t2, c=c, half=half:
                               e.matmul(cx.psum[2 * t2 + c][:, 0:512],
                                        hrnd[:, 256 * f + 128 * t2:
                                             256 * f + 128 * (t2 + 1)],
                                        wdb[:, 1024 * half + 512 * c:
                                            1024 * half + 512 * (c + 1)],
                                        start=(f == 0), stop=(f == 23)),
                               reads=("hrnd", f"wdb{half}"),
                               writes=(f"ps{2 * t2 + c}",),
                               name=f"ff_dmm{r}_{f}_{t2}_{c}")
            for t2 in range(2):
                for c in range(2):
                    dst = g_tm[:, 1024 * (2 * r + t2) + 512 * c:
                               1024 * (2 * r + t2) + 512 * (c + 1)]
                    em.add("scalar", lambda e, dst=dst, t2=t2, c=c:
                           e.activation(out=dst,
                                        in_=cx.psum[2 * t2 + c][:, 0:512],
                                        func=AF.Silu),
                           reads=(f"ps{2 * t2 + c}",), writes=("g_tm",),
                           name=f"ff_gout{r}_{t2}_{c}")
    em.fence()
    cx.g_tm, cx.g_key = g_tm, "g_tm"




def _pt_dst(pT, lo, hi, j):
    return pT[:, :].rearrange("p (t q) -> p t q", q=512)[:, lo:hi,
                                                        128 * j:128 * (j + 1)]


def emit_gm(cx, src, src_key, dst, dst_key, n_tiles, tag):
    em = cx.em
    st = cx.stat
    mx = st[:, 0:8]
    for i in range(n_tiles):
        sl = slice(64 * i, 64 * (i + 1))
        g3 = src[:, sl].rearrange("p (h e) -> p h e", e=8)
        em.add("vector", lambda e, g3=g3: e.tensor_reduce(
            out=mx, in_=g3, axis=AX.X, op=ALU.max),
            reads=(src_key,), writes=("stat",), name=f"{tag}mx{i}")
        em.add("vector", lambda e: e.tensor_copy(
            out=cx.gmsc[:, 0:64],
            in_=mx.unsqueeze(-1).to_broadcast([128, 8, 8])),
            reads=("stat",), writes=("gmsc",), name=f"{tag}bc{i}")
        em.add("vector", lambda e, sl=sl: e.tensor_tensor(
            out=cx.gmsc[:, 64:128], in0=src[:, sl], in1=cx.gmsc[:, 0:64],
            op=ALU.is_equal),
            reads=(src_key, "gmsc"), writes=("gmsc",), name=f"{tag}eq{i}")
        em.add("vector", lambda e: e.tensor_tensor_scan(
            out=cx.gmsc[:, 128:192], data0=cx.gmpat[:, 0:64],
            data1=cx.gmsc[:, 64:128], initial=cx.gmzero[:, 0:1],
            op0=ALU.mult, op1=ALU.add),
            reads=("gmsc", "gmpat", "gmzero"), writes=("gmsc",),
            name=f"{tag}cs{i}")
        em.add("vector", lambda e: e.tensor_scalar(
            out=cx.gmsc[:, 128:192], in0=cx.gmsc[:, 128:192], scalar1=1.0,
            scalar2=None, op0=ALU.is_equal),
            reads=("gmsc",), writes=("gmsc",), name=f"{tag}f1{i}")
        em.add("vector", lambda e: e.tensor_mul(
            cx.gmsc[:, 64:128], cx.gmsc[:, 64:128], cx.gmsc[:, 128:192]),
            reads=("gmsc",), writes=("gmsc",), name=f"{tag}f2{i}")
        em.add("vector", lambda e, sl=sl: e.tensor_mul(
            dst[:, sl], cx.gmsc[:, 64:128], src[:, sl]),
            reads=("gmsc", src_key), writes=(dst_key,), name=f"{tag}gm{i}")


def alloc_gm_consts(cx, ctx, full=True, tag=""):
    nc, em = cx.nc, cx.em
    if full:
        cx.gmsc = ctx.enter_context(
            nc.sbuf_tensor(f"gmsc{tag}", [128, 192], F32))
    gw = 64 if full else 8
    cx.gmpat = ctx.enter_context(
        nc.sbuf_tensor(f"gmpat{tag}", [128, gw], F32))
    cx.gmzero = ctx.enter_context(
        nc.sbuf_tensor(f"gmzero{tag}", [128, 1], F32))
    gp, gz = cx.gmpat, cx.gmzero
    gmi = gp[:, :].bitcast(I32)
    em.add("gpsimd", lambda e, gmi=gmi, gw=gw: e.iota(
        gmi, [[0, gw // 8], [1, 8]], base=0, channel_multiplier=0),
        writes=("gmpat",), name=f"gmpi{tag}")
    em.add("vector", lambda e, gp=gp, gmi=gmi: e.tensor_scalar(
        out=gp[:, :], in0=gmi, scalar1=0.0, scalar2=None,
        op0=ALU.is_gt), reads=("gmpat",), writes=("gmpat",),
        name=f"gmpg{tag}")
    em.add("vector", lambda e, gz=gz: e.memset(gz[:], 0.0),
           writes=("gmzero",), name=f"gmz{tag}")



RMS_EPS = float(np.finfo(np.float32).eps)


def prep_l3_weights(inputs, h):
    w = {}
    nw = np.asarray(inputs["L3_attn_norm_w"])
    w["l3a_nw"] = np.ascontiguousarray(nw.reshape(8, 128))
    w["l3a_wq"] = fmpack(np.asarray(inputs["L3_attn_Wq"]))
    w["l3a_wkv"] = fmpack(np.asarray(inputs["L3_attn_Wkv"]))
    w["l3a_wout"] = slabpack(np.asarray(inputs["L3_attn_Wout"]))
    wg = np.zeros((1024, 128), np.float32)
    wg[:, :16] = np.asarray(inputs["L3_attn_Wg"])
    w["l3a_wg"] = fmpack(wg)
    w["l3f_wq"] = fmpack(np.asarray(inputs["L3_ffn_w_q"]))
    w["l3f_wk"] = fmpack(np.asarray(inputs["L3_ffn_w_k"]))
    wgf = np.zeros((1024, 128), np.float32)
    wgf[:, :64] = np.asarray(inputs["L3_ffn_w_gate"])
    w["l3f_wg"] = fmpack(wgf)
    ve = np.asarray(inputs["L3_ffn_value_experts"])
    w["l3f_ve"] = np.stack([fmpack(np.ascontiguousarray(
        ve[e].transpose(1, 0, 2).reshape(1024, 512))) for e in range(8)])
    oe = np.asarray(inputs["L3_ffn_output_experts"])
    w["l3f_oe"] = np.stack([slabpack(np.ascontiguousarray(
        oe[e].reshape(512, 1024))) for e in range(8)])
    w["flag8"] = np.full((1, 8), float(h), np.float32)
    qidx = np.empty((128, NT), np.uint32)
    sgidx = np.empty((128, NT), np.uint32)
    p = np.arange(128, dtype=np.uint32)
    for i in range(NT):
        qidx[:, i] = 128 * (2 * i + h) + p
        g = 16 * h + i
        sgidx[:, i] = 2048 * (g % 2) + 128 * ((g - g % 2) // 2) + p
    w["l3_qidx"] = qidx
    w["l3_sgidx"] = sgidx
    return w


def declare_l3_dram(nc):
    d = {}
    d["l3a_nw"] = nc.dram_tensor("l3a_nw", [8, 128], F32, kind="ExternalInput").ap()
    d["l3a_wq"] = nc.dram_tensor("l3a_wq", [8, 128, 1024], BF16, kind="ExternalInput").ap()
    d["l3a_wkv"] = nc.dram_tensor("l3a_wkv", [16, 128, 1024], BF16, kind="ExternalInput").ap()
    d["l3a_wout"] = nc.dram_tensor("l3a_wout", [8, 128, 1024], BF16, kind="ExternalInput").ap()
    d["l3a_wg"] = nc.dram_tensor("l3a_wg", [1, 128, 1024], BF16, kind="ExternalInput").ap()
    d["l3f_wq"] = nc.dram_tensor("l3f_wq", [4, 128, 1024], BF16, kind="ExternalInput").ap()
    d["l3f_wk"] = nc.dram_tensor("l3f_wk", [4, 128, 1024], BF16, kind="ExternalInput").ap()
    d["l3f_wg"] = nc.dram_tensor("l3f_wg", [1, 128, 1024], BF16, kind="ExternalInput").ap()
    d["l3f_ve"] = nc.dram_tensor("l3f_ve", [8, 4, 128, 1024], BF16, kind="ExternalInput").ap()
    d["l3f_oe"] = nc.dram_tensor("l3f_oe", [8, 4, 128, 1024], BF16, kind="ExternalInput").ap()
    d["flag8"] = _flag8(nc)
    d["l3_xq_in"] = nc.dram_tensor("l3_xq_in", [2048, 1024], BF16).ap()
    d["l3_xq_out"] = nc.dram_tensor("l3_xq_out", [4096, 1024], BF16).ap()
    d["l3_qidx"] = nc.dram_tensor("l3_qidx", [128, NT], U32, kind="ExternalInput").ap()
    d["l3_sgidx"] = nc.dram_tensor("l3_sgidx", [128, NT], U32, kind="ExternalInput").ap()
    d["l3_kv_in"] = nc.dram_tensor("l3_kv_in", [2048, 2048], BF16).ap()
    d["l3_kv_out"] = nc.dram_tensor("l3_kv_out", [4096, 2048], BF16).ap()
    d["l3f_kv_in"] = nc.dram_tensor("l3f_kv_in", [1024, 2048], BF16).ap()
    d["l3f_kv_out"] = nc.dram_tensor("l3f_kv_out", [2048, 2048], BF16).ap()
    d["l3_sg_in"] = nc.dram_tensor("l3_sg_in", [2048, 1024], F32).ap()
    d["l3_sg_out"] = nc.dram_tensor("l3_sg_out", [4096, 1024], F32).ap()
    return d


def _cc_pair(em, kind, in_ap, out_ap, reads, writes, name):
    def f(e):
        from concourse.aot_env import allow_ici_all_to_all
        with allow_ici_all_to_all():
            return e.collective_compute(
                kind, ALU.bypass,
                replica_groups=[[0, 1], [2, 3], [4, 5], [6, 7]],
                ins=[in_ap.opt()], outs=[out_ap.opt()])
    em.cc(f, reads=reads, writes=writes, name=name)


def rms_stats(cx, src_tile, n_tiles, src_key, tag):
    em = cx.em
    st = cx.stat
    k = n_tiles
    sq = st[:, k:2 * k]
    var = st[:, 2 * k:3 * k]
    rstd = st[:, 3 * k:4 * k]
    tmp = st[:, 4 * k:5 * k]
    for i in range(n_tiles):
        em.add("scalar", lambda e, i=i: e.activation(
            out=cx.sq_junk[:, 0:1024], in_=src_tile(i), func=AF.Square,
            accum_out=sq[:, i:i + 1]),
            reads=(src_key,), writes=("stat", "sq_junk"), name=f"{tag}_sq{i}")
    s = ("stat",)
    em.add("vector", lambda e: e.tensor_scalar(
        out=var, in0=sq, scalar1=1.0 / D, scalar2=RMS_EPS,
        op0=ALU.mult, op1=ALU.add), reads=s, writes=s, name=f"{tag}_v")
    em.add("scalar", lambda e: e.activation(out=rstd, in_=var, func=AF.Sqrt),
           reads=s, writes=s, name=f"{tag}_sq")
    em.add("vector", lambda e: e.reciprocal(out=tmp, in_=rstd),
           reads=s, writes=s, name=f"{tag}_r1")
    em.add("vector", lambda e: e.tensor_mul(tmp, tmp, var),
           reads=s, writes=s, name=f"{tag}_vs")
    em.add("vector", lambda e: e.tensor_add(rstd, rstd, tmp),
           reads=s, writes=s, name=f"{tag}_ad")
    em.add("vector", lambda e: e.tensor_scalar_mul(rstd, rstd, 0.5),
           reads=s, writes=s, name=f"{tag}_hf")
    em.add("vector", lambda e: e.reciprocal(out=rstd, in_=rstd),
           reads=s, writes=s, name=f"{tag}_rs")
    return rstd


def proj_p2(cx, wdram, src_fm, src_key, wbuf, wkeys, n_ft, evac, tag,
            ntok=HALF, nkt=NF, chunks=4, wwidth=1024):
    em = cx.em
    for f in range(n_ft):
        wb = wbuf[f % 2]
        wk = wkeys[f % 2]
        em.dma("sync", lambda e, f=f, wb=wb: e.dma_start(
            out=wb[:, 0:wwidth], in_=wdram[f]), writes=(wk,), name=f"{tag}w{f}")
        for k in range(nkt):
            for c in range(chunks):
                em.add("tensor", lambda e, wb=wb, k=k, c=c: e.matmul(
                    cx.psum[c][:, 0:512], wb[:, 128 * k:128 * (k + 1)],
                    src_fm[:, ntok * k + 512 * c:ntok * k + 512 * (c + 1)],
                    start=(k == 0), stop=(k == nkt - 1)),
                    reads=(wk, src_key), writes=(f"ps{c}",),
                    name=f"{tag}mm{f}_{k}_{c}")
        evac(f)


def dump_fm(cx, name, buf, nf, key, width=HALF):
    nc, em = cx.nc, cx.em
    d = nc.dram_tensor(name, [nf, 128, width], F32, kind="ExternalOutput").ap()
    for f in range(nf):
        for c in range((width + 511) // 512):
            w = min(512, width - 512 * c)
            em.add("vector", lambda e, f=f, c=c, w=w: e.tensor_copy(
                out=cx.dmpbuf[:, 0:w], in_=buf[:, width * f + 512 * c:
                                               width * f + 512 * c + w]),
                reads=(key,), writes=("dmpbuf", "sgbuf"), name=f"dmp{name}{f}_{c}")
            em.out_dma("sync", lambda e, f=f, c=c, w=w: e.dma_start(
                out=d[f, :, 512 * c:512 * c + w], in_=cx.dmpbuf[:, 0:w]),
                reads=("dmpbuf",), name=f"dmps{name}{f}_{c}")


def build_l3(cx, ctx, dr):
    nc, em = cx.nc, cx.em
    sb = lambda n, s, d: ctx.enter_context(nc.sbuf_tensor(n, s, d))
    em.fence()
    SCALE = 0.125

    B3 = sb("l3_B3", [128, NF * HALF], BF16)
    B4 = sb("l3_B4", [128, NF * HALF], BF16)
    B5 = sb("l3_B5", [128, NF * HALF], BF16)
    q_all = B5
    qf_all = B4[:, 0:4 * HALF]
    ao = B3
    aof = B4[:, 4 * HALF:8 * HALF]
    gat_a = sb("l3_gata", [128, NT * 16], BF16)
    cx.l3idx = sb("l3_idx", [128, 2 * NT], U32)
    gm_q = sb("l3_gmq", [128, NT * 64], BF16)
    gmT_q = sb("l3_gmTq", [64, HALF], BF16)
    gmT_own = sb("l3_gmTo", [64, HALF], BF16)
    s_j = q_all
    stg = sb("l3_stg", [128, 2048], BF16)
    wb0 = sb("l3_wb0", [128, 1024], BF16)
    wb1 = sb("l3_wb1", [128, 1024], BF16)
    rloc = sb("l3_rloc", [128, 4], F32)
    colg = sb("l3_colg", [128, 512], I32)
    masks = sb("l3_masks", [128, 2 * 512], BF16)
    flag_s = sb("l3_flag", [1, 8], F32)
    ones1f2 = sb("l3_ones1f", [1, 128], F32)
    nwcol = sb("l3_nw", [128, 8], F32)
    lnstage = cx.xbuf[:, 0:128]
    ident32 = cx.xbuf[:, 128:256]

    tmb = cx.xn_tm
    if DEBUG_L3:
        cx.dmpbuf = cx.sgbuf[:, 0:512]

    em.add("vector", lambda e: e.tensor_scalar(
        out=ident32[:], in0=cx.ident_i[:], scalar1=0.0, scalar2=None,
        op0=ALU.is_equal), reads=("ident_i",), writes=("xbufc",),
        name="l3id32")
    em.add("vector", lambda e: e.memset(ones1f2[:], 1.0), writes=("l3ones",),
           name="l3ones")
    em.dma("sync", lambda e: e.dma_start(out=flag_s[:], in_=dr["flag8"][:]),
           writes=("l3flag",), name="l3flagld")
    em.add("tensor", lambda e: e.matmul(
        cx.psB[:, 0:8], ones1f2[:, :], flag_s[:, 0:8], start=True, stop=True),
        reads=("l3ones", "l3flag"), writes=("psB",), name="l3fb")
    em.add("vector", lambda e: e.tensor_scalar_mul(
        rloc[:, 3:4], cx.psB[:, 0:1], 128.0),
        reads=("psB",), writes=("l3rloc",), name="l3hb")
    iotacol = rloc[:, 2:3].bitcast(I32)
    em.add("gpsimd", lambda e: e.iota(iotacol, [[1, 1]], base=0,
                                      channel_multiplier=1),
           writes=("l3rloc",), name="l3iotap")
    em.add("vector", lambda e: e.scalar_tensor_tensor(
        out=rloc[:, 0:1], in0=iotacol, scalar=1.0, in1=rloc[:, 3:4],
        op0=ALU.mult, op1=ALU.add),
        reads=("l3rloc",), writes=("l3rloc",), name="l3rl0")
    em.add("vector", lambda e: e.tensor_scalar_add(
        rloc[:, 1:2], rloc[:, 0:1], 256.0),
        reads=("l3rloc",), writes=("l3rloc",), name="l3rl1")
    em.add("gpsimd", lambda e: e.iota(colg[:], [[1, 512]], base=0,
                                      channel_multiplier=0),
           writes=("l3colg",), name="l3colg")
    for par in range(2):
        em.add("vector", lambda e, par=par: e.tensor_scalar(
            out=masks[:, 512 * par:512 * (par + 1)], in0=colg[:],
            scalar1=rloc[:, par:par + 1], scalar2=None, op0=ALU.is_le),
            reads=("l3colg", "l3rloc"), writes=("l3masks",), name=f"l3m{par}")
    em.add("vector", lambda e: e.memset(lnstage[:], 0.0), writes=("xbufc",),
           name="l3lz")
    em.dma("sync", lambda e: e.dma_start(out=lnstage[0:8, 0:128],
                                         in_=dr["l3a_nw"][:]),
           writes=("xbufc",), name="l3lnld")
    em.add("tensor", lambda e: e.transpose(
        cx.psB[:, 0:128], lnstage[:], ident32[:]),
        reads=("xbufc",), writes=("psB",), name="l3lntr")
    em.add("vector", lambda e: e.tensor_copy(out=nwcol[:], in_=cx.psB[:, 0:8]),
           reads=("psB",), writes=("l3nw",), name="l3lnev")

    seli = cx.xbuf[:, :].bitcast(I32)[0:64, 0:2048]

    def build_sel(selb_ap, tag):
        for half in range(2):
            em.add("gpsimd", lambda e, half=half: e.iota(
                seli, [[1, 4], [16, 4], [8, 2], [0, 64]], base=4 * half,
                channel_multiplier=-1),
                writes=("xbufc",), name=f"{tag}i{half}")
            em.add("vector", lambda e, half=half, selb_ap=selb_ap:
                   e.tensor_scalar(
                       out=selb_ap[:, 2048 * half:2048 * (half + 1)],
                       in0=seli, scalar1=0.0, scalar2=None, op0=ALU.is_equal),
                   reads=("xbufc",), writes=("l3selb",), name=f"{tag}e{half}")
    alloc_gm_consts(cx, ctx, tag="l3")
    selb = sb("l3_selb", [64, 32 * 128], BF16)
    build_sel(selb, "l3sel")

    xa = dr["l3_xq_in"].rearrange("(i p) d -> i p d", p=128)
    for i in range(NT):
        em.dma("sync", lambda e, i=i: e.dma_start(
            out=xa[i], in_=tmb[:, 1024 * i:1024 * (i + 1)]),
            reads=("xn_tm",), writes=("l3xqin",), name=f"l3xqs{i}")
    _cc_pair(em, "AllGather", dr["l3_xq_in"], dr["l3_xq_out"],
             ("l3xqin",), ("l3xqout",), "l3xqcc")
    em.dma("sync", lambda e: e.dma_start(out=cx.l3idx[:, 0:NT],
                                         in_=dr["l3_qidx"][:]),
           writes=("l3idx",), name="l3qidxld")
    em.dma("sync", lambda e: e.dma_start(out=cx.l3idx[:, NT:2 * NT],
                                         in_=dr["l3_sgidx"][:]),
           writes=("l3idx",), name="l3sgidxld")

    rstd = rms_stats(cx, lambda i: tmb[:, 1024 * i:1024 * (i + 1)], NT,
                     "xn_tm", "l3rmso")
    for i in range(NT):
        em.add("vector", lambda e, i=i: e.tensor_scalar(
            out=tmb[:, 1024 * i:1024 * (i + 1)],
            in0=tmb[:, 1024 * i:1024 * (i + 1)],
            scalar1=rstd[:, i:i + 1], scalar2=None, op0=ALU.mult),
            reads=("stat",), writes=("x3a_tm", "xn_tm"), name=f"l3nrm{i}")
    with ExitStack() as scr:
        x3afm = B3
        transpose_tm_to_fm(cx, tmb, x3afm, "x3a_tm", "x3afm", NT, NF, tag="l3tr")
        for f in range(NF):
            em.add("vector", lambda e, f=f: e.tensor_scalar(
                out=x3afm[:, HALF * f:HALF * (f + 1)],
                in0=x3afm[:, HALF * f:HALF * (f + 1)],
                scalar1=nwcol[:, f:f + 1], scalar2=None, op0=ALU.mult),
                reads=("x3afm", "l3nw"), writes=("x3afm",), name=f"l3nwm{f}")
        kvr = dr["l3_kv_in"].rearrange("(f p) t -> f p t", p=128)

        def ev_kv(f):
            for c in range(4):
                em.add("vector", lambda e, f=f, c=c: e.tensor_copy(
                    out=stg[:, 512 * c:512 * (c + 1)],
                    in_=cx.psum[c][:, 0:512]),
                    reads=(f"ps{c}",), writes=("l3stg",), name=f"l3kve{f}_{c}")
            em.dma("sync", lambda e, f=f: e.dma_start(
                out=kvr[f], in_=stg[:]),
                reads=("l3stg",), writes=("l3kvin",), name=f"l3kvs{f}")

        proj_p2(cx, dr["l3a_wkv"], x3afm, "x3afm", (wb0, wb1),
                ("l3wb0", "l3wb1"), 16, ev_kv, "l3kv")
        _cc_pair(em, "AllGather", dr["l3_kv_in"], dr["l3_kv_out"],
                 ("l3kvin",), ("l3kvout",), "l3kvcc")

        gown = gm_q
        em.dma("sync", lambda e: e.dma_start(out=wb0[:], in_=dr["l3f_wg"][0]),
               writes=("l3wb0",), name="l3fgw")
        for i in range(NT):
            for k in range(NF):
                em.add("tensor", lambda e, i=i, k=k: e.matmul(
                    cx.psB[:, 0:128],
                    cx.xn_fm[:, HALF * k + 128 * i:HALF * k + 128 * (i + 1)],
                    wb0[:, 128 * k:128 * (k + 1)],
                    start=(k == 0), stop=(k == NF - 1)),
                    reads=("l3wb0", "xn_fm"), writes=("psB",),
                    name=f"l3fgo{i}_{k}")
            em.add("scalar", lambda e, i=i: e.activation(
                out=gown[:, 64 * i:64 * (i + 1)], in_=cx.psB[:, 0:64],
                func=AF.Sigmoid),
                reads=("psB",), writes=("l3gown",), name=f"l3fgoe{i}")
        gm_own = B4[:, 0:NT * 64]
        emit_gm(cx, gown, "l3gown", gm_own, "l3gmown", NT, tag="l3gmo")
        for i in range(NT):
            bank = cx.psT[i % 2]
            em.add("tensor", lambda e, i=i, bank=bank: e.transpose(
                bank[0:64, 0:128],
                gm_own[:, 64 * i:64 * (i + 1)], cx.ident[:]),
                reads=("l3gmown", "ident"), writes=(f"psT{i % 2}",),
                name=f"l3gmt{i}")
            em.add("vector", lambda e, i=i, bank=bank: e.tensor_copy(
                out=gmT_own[0:64, 128 * i:128 * (i + 1)], in_=bank[0:64, 0:128]),
                reads=(f"psT{i % 2}",), writes=("l3gmTo",), name=f"l3gmte{i}")
        vals = B4[:, 4 * HALF:8 * HALF]
        em.add("vector", lambda e: e.memset(vals[:], 0.0), writes=("l3vals",),
               name="l3vz")
        bcb = B4[:, 1024:1536]
        for e_i in range(8):
            vdr = dr["l3f_ve"][e_i]
            for ft in range(4):
                wb = (wb0, wb1)[ft % 2]
                wk = ("l3wb0", "l3wb1")[ft % 2]
                em.dma("gpsimd", lambda e, vdr=vdr, ft=ft, wb=wb: e.dma_start(
                    out=wb[:], in_=vdr[ft]), writes=(wk,), name=f"l3ve{e_i}_{ft}")
                for k in range(NF):
                    for c in range(4):
                        em.add("tensor", lambda e, wb=wb, k=k, c=c: e.matmul(
                            cx.psum[c][:, 0:512], wb[:, 128 * k:128 * (k + 1)],
                            cx.xn_fm[:, HALF * k + 512 * c:
                                     HALF * k + 512 * (c + 1)],
                            start=(k == 0), stop=(k == NF - 1)),
                            reads=(wk, "xn_fm"), writes=(f"ps{c}",),
                            name=f"l3xv{e_i}_{ft}_{k}_{c}")
                for c in range(4):
                    em.add("tensor", lambda e, e_i=e_i, ft=ft, c=c: e.matmul(
                        cx.psB[:, 0:512],
                        selb[:, 128 * (4 * e_i + ft):128 * (4 * e_i + ft + 1)],
                        gmT_own[0:64, 512 * c:512 * (c + 1)],
                        start=True, stop=True),
                        reads=("l3selb", "l3gmTo"), writes=("psB",),
                        name=f"l3bc{e_i}_{ft}_{c}")
                    em.add("vector", lambda e, c=c: e.tensor_copy(
                        out=bcb[:], in_=cx.psB[:, 0:512]),
                        reads=("psB",), writes=("l3bcb",), name=f"l3bce{e_i}_{ft}_{c}")
                    em.add("vector", lambda e, ft=ft, c=c: e.scalar_tensor_tensor(
                        out=stg[:, 0:512], in0=cx.psum[c][:, 0:512],
                        scalar=1.0, in1=bcb[:], op0=ALU.mult, op1=ALU.mult),
                        reads=(f"ps{c}", "l3bcb"), writes=("l3stg",),
                        name=f"l3msk{e_i}_{ft}_{c}")
                    em.add("vector", lambda e, ft=ft, c=c: e.tensor_add(
                        vals[:, HALF * ft + 512 * c:HALF * ft + 512 * (c + 1)],
                        vals[:, HALF * ft + 512 * c:HALF * ft + 512 * (c + 1)],
                        stg[:, 0:512]),
                        reads=("l3stg", "l3vals"), writes=("l3vals",),
                        name=f"l3va{e_i}_{ft}_{c}")
        if DEBUG_L3:
            dump_fm(cx, "dbg_vals", vals, 4, "l3vals")
            dump_fm(cx, "dbg_gmo", gm_own, 1, "l3gmown", width=NT * 64)
        kvf_r = dr["l3f_kv_in"].rearrange("(f p) t -> f p t", p=128)

        def ev_kf(f):
            for c in range(4):
                em.add("vector", lambda e, f=f, c=c: e.tensor_copy(
                    out=stg[:, 512 * c:512 * (c + 1)], in_=cx.psum[c][:, 0:512]),
                    reads=(f"ps{c}",), writes=("l3stg",), name=f"l3kfe{f}_{c}")
            em.dma("sync", lambda e, f=f: e.dma_start(
                out=kvf_r[f], in_=stg[:]),
                reads=("l3stg",), writes=("l3fkvin",), name=f"l3kfs{f}")

        proj_p2(cx, dr["l3f_wk"], cx.xn_fm, "xn_fm", (wb0, wb1),
                ("l3wb0", "l3wb1"), 4, ev_kf, "l3kf")
        for f in range(4):
            em.dma("sync", lambda e, f=f: e.dma_start(
                out=kvf_r[4 + f], in_=vals[:, HALF * f:HALF * (f + 1)]),
                reads=("l3vals",), writes=("l3fkvin",), name=f"l3vs{f}")
        _cc_pair(em, "AllGather", dr["l3f_kv_in"], dr["l3f_kv_out"],
                 ("l3fkvin",), ("l3fkvout",), "l3fkvcc")
    em.fence()

    from concourse.bass import IndirectOffsetOnAxis as _IOA
    for i in range(NT):
        em.dma("gpsimd", lambda e, i=i: e.indirect_dma_start(
            out=tmb[:, 1024 * i:1024 * (i + 1)], out_offset=None,
            in_=dr["l3_xq_out"][:],
            in_offset=_IOA(ap=cx.l3idx[:, i:i + 1], axis=0)),
            reads=("l3xqout", "l3idx"), writes=("xq_tm",), name=f"l3xql{i}")
    with ExitStack() as scr:
        xq_fm = B3
        x3aq = B4
        transpose_tm_to_fm(cx, tmb, xq_fm, "xq_tm", "xqfm", NT, NF, tag="l3qtr")
        rstdq = rms_stats(cx, lambda i: tmb[:, 1024 * i:1024 * (i + 1)], NT,
                          "xq_tm", "l3rmsq")
        x3aq_tm = tmb
        for i in range(NT):
            em.add("vector", lambda e, i=i: e.tensor_scalar(
                out=x3aq_tm[:, 1024 * i:1024 * (i + 1)],
                in0=x3aq_tm[:, 1024 * i:1024 * (i + 1)],
                scalar1=rstdq[:, i:i + 1], scalar2=None, op0=ALU.mult),
                reads=("stat",), writes=("x3aq_tm", "xq_tm"), name=f"l3qn{i}")
        transpose_tm_to_fm(cx, x3aq_tm, x3aq, "x3aq_tm", "x3aqfm", NT, NF,
                           tag="l3qtr2")
        for f in range(NF):
            em.add("vector", lambda e, f=f: e.tensor_scalar(
                out=x3aq[:, HALF * f:HALF * (f + 1)],
                in0=x3aq[:, HALF * f:HALF * (f + 1)],
                scalar1=nwcol[:, f:f + 1], scalar2=None, op0=ALU.mult),
                reads=("x3aqfm", "l3nw"), writes=("x3aqfm",), name=f"l3qnw{f}")

        def ev_q(f):
            for c in range(4):
                em.add("vector", lambda e, f=f, c=c: e.tensor_copy(
                    out=q_all[:, HALF * f + 512 * c:HALF * f + 512 * (c + 1)],
                    in_=cx.psum[c][:, 0:512]),
                    reads=(f"ps{c}",), writes=("l3qall",), name=f"l3qe{f}_{c}")

        proj_p2(cx, dr["l3a_wq"], x3aq, "x3aqfm", (wb0, wb1),
                ("l3wb0", "l3wb1"), NF, ev_q, "l3q")

        em.dma("sync", lambda e: e.dma_start(out=wb0[:], in_=dr["l3a_wg"][0]),
               writes=("l3wb0",), name="l3gw")
        for c in range(4):
            for k in range(NF):
                em.add("tensor", lambda e, k=k, c=c: e.matmul(
                    cx.psum[c][:, 0:512], wb0[:, 128 * k:128 * (k + 1)],
                    x3aq[:, HALF * k + 512 * c:HALF * k + 512 * (c + 1)],
                    start=(k == 0), stop=(k == NF - 1)),
                    reads=("l3wb0", "x3aqfm"), writes=(f"ps{c}",),
                    name=f"l3gmm{k}_{c}")
            em.add("scalar", lambda e, c=c: e.activation(
                out=stg[:, 512 * c:512 * (c + 1)], in_=cx.psum[c][:, 0:512],
                func=AF.Sigmoid),
                reads=(f"ps{c}",), writes=("l3stg",), name=f"l3gsig{c}")
        for i in range(NT):
            bank = cx.psT[i % 2]
            em.add("tensor", lambda e, i=i, bank=bank: e.transpose(
                bank[:, 0:128], stg[:, 128 * i:128 * (i + 1)], cx.ident[:]),
                reads=("l3stg", "ident"), writes=(f"psT{i % 2}",),
                name=f"l3gt{i}")
            em.add("vector", lambda e, i=i, bank=bank: e.tensor_copy(
                out=gat_a[:, 16 * i:16 * (i + 1)], in_=bank[:, 0:16]),
                reads=(f"psT{i % 2}",), writes=("l3gata",), name=f"l3gte{i}")

        em.fence()

        def ev_qf(f):
            for c in range(4):
                em.add("vector", lambda e, f=f, c=c: e.tensor_copy(
                    out=qf_all[:, HALF * f + 512 * c:HALF * f + 512 * (c + 1)],
                    in_=cx.psum[c][:, 0:512]),
                    reads=(f"ps{c}",), writes=("l3qfall",), name=f"l3qfe{f}_{c}")

        proj_p2(cx, dr["l3f_wq"], xq_fm, "xqfm", (wb0, wb1),
                ("l3wb0", "l3wb1"), 4, ev_qf, "l3qf")
        em.dma("sync", lambda e: e.dma_start(out=wb0[:], in_=dr["l3f_wg"][0]),
               writes=("l3wb0",), name="l3fgw2")
        for i in range(NT):
            for k in range(NF):
                em.add("tensor", lambda e, i=i, k=k: e.matmul(
                    cx.psB[:, 0:128],
                    xq_fm[:, HALF * k + 128 * i:HALF * k + 128 * (i + 1)],
                    wb0[:, 128 * k:128 * (k + 1)],
                    start=(k == 0), stop=(k == NF - 1)),
                    reads=("l3wb0", "xqfm"), writes=("psB",),
                    name=f"l3fgq{i}_{k}")
            em.add("scalar", lambda e, i=i: e.activation(
                out=stg[:, 0:64], in_=cx.psB[:, 0:64], func=AF.Sigmoid),
                reads=("psB",), writes=("l3stg",), name=f"l3fgqe{i}")
            em.add("vector", lambda e, i=i: e.tensor_copy(
                out=gm_q[:, 64 * i:64 * (i + 1)], in_=stg[:, 0:64]),
                reads=("l3stg",), writes=("l3gq",), name=f"l3fgqc{i}")
        emit_gm(cx, gm_q, "l3gq", gm_q, "l3gmq", NT, tag="l3gmq")
        if DEBUG_L3:
            dump_fm(cx, "dbg_qf", qf_all, 4, "l3qfall")
            dump_fm(cx, "dbg_gmq", gm_q, 1, "l3gmq", width=NT * 64)
        for i in range(NT):
            bank = cx.psT[i % 2]
            em.add("tensor", lambda e, i=i, bank=bank: e.transpose(
                bank[0:64, 0:128], gm_q[:, 64 * i:64 * (i + 1)], cx.ident[:]),
                reads=("l3gmq", "ident"), writes=(f"psT{i % 2}",),
                name=f"l3gqt{i}")
            em.add("vector", lambda e, i=i, bank=bank: e.tensor_copy(
                out=gmT_q[0:64, 128 * i:128 * (i + 1)], in_=bank[0:64, 0:128]),
                reads=(f"psT{i % 2}",), writes=("l3gmTq",), name=f"l3gqte{i}")
    em.fence()

    if True:
        khbuf = cx.xn_tm[:, 12288:16384]
        vh = cx.xbuf[:, :].bitcast(BF16)[0:64, 0:4096]
        vt = stg[:, :]
        simb = cx.xn_tm[:, :].bitcast(F32)[:, 0:4096]
        pbuf = cx.xn_tm[:, 8192:12288]
        pT = cx.xn_fm
        mcol = cx.stat[:, 64:80]
        scol = cx.stat[:, 80:96]
        kvo = dr["l3_kv_out"]
        fkvo = dr["l3f_kv_out"]

        def attn(nheads, src_q, src_q_key, k_row0, v_row0, rankoff, kvdram,
                 out_fm, out_key, gate_tm, tag):
            for h in range(nheads):
                p0 = 64 * (h % 2)
                kh = khbuf[p0:p0 + 64, :]
                em.dma("sync", lambda e, h=h, kh=kh: e.dma_start(
                    out=kh[:, 0:2048], in_=kvdram[k_row0 + 64 * h:
                                               k_row0 + 64 * (h + 1), :]),
                    writes=("l3kh",), name=f"{tag}k0_{h}")
                em.dma("sync", lambda e, h=h, kh=kh: e.dma_start(
                    out=kh[:, 2048:4096],
                    in_=kvdram[rankoff + k_row0 + 64 * h:
                               rankoff + k_row0 + 64 * (h + 1), :]),
                    writes=("l3kh",), name=f"{tag}k1_{h}")
                em.dma("gpsimd", lambda e, h=h: e.dma_start(
                    out=vh[:, 0:2048], in_=kvdram[v_row0 + 64 * h:
                                               v_row0 + 64 * (h + 1), :]),
                    writes=("l3vh", "xbufc"), name=f"{tag}v0_{h}")
                em.dma("gpsimd", lambda e, h=h: e.dma_start(
                    out=vh[:, 2048:4096],
                    in_=kvdram[rankoff + v_row0 + 64 * h:
                               rankoff + v_row0 + 64 * (h + 1), :]),
                    writes=("l3vh", "xbufc"), name=f"{tag}v1_{h}")
                for half2 in range(2):
                    bank = cx.psT[half2]
                    for t in range(16):
                        em.add("tensor", lambda e, half2=half2, t=t, bank=bank:
                               e.transpose(bank[:, 64 * t:64 * (t + 1)],
                                           vh[:, 2048 * half2 + 128 * t:
                                              2048 * half2 + 128 * (t + 1)],
                                           cx.ident[0:64, 0:64]),
                               reads=("l3vh", "ident"), writes=(f"psT{half2}",),
                               name=f"{tag}vt{h}_{half2}_{t}")
                    em.add("vector", lambda e, half2=half2, bank=bank:
                           e.tensor_copy(out=vt[:, 1024 * half2:
                                                1024 * (half2 + 1)],
                                         in_=bank[:, 0:1024]),
                           reads=(f"psT{half2}",), writes=("l3vt",),
                           name=f"{tag}vte{h}_{half2}")
                for g in range(4):
                    EC = 2 * g + 2
                    ET = 8 * g + 8
                    for j in range(4):
                        i = 4 * g + j
                        cd = i // 2
                        qs = src_q[64 * (h % 2):64 * (h % 2) + 64,
                                   HALF * (h // 2) + 128 * i:
                                   HALF * (h // 2) + 128 * (i + 1)]
                        for c in range(cd + 1):
                            ps = cx.psum[c % 2]
                            em.add("tensor", lambda e, qs=qs, c=c, ps=ps,
                                   kh=kh:
                                   e.matmul(ps[:, 0:512], qs,
                                            kh[:, 512 * c:512 * (c + 1)],
                                            start=True, stop=True),
                                   reads=("l3kh", src_q_key),
                                   writes=(f"ps{c % 2}",),
                                   name=f"{tag}sim{h}_{i}_{c}")
                            em.add("vector", lambda e, c=c, ps=ps:
                                   e.tensor_scalar(
                                       out=simb[:, 512 * c:512 * (c + 1)],
                                       in0=ps[:, 0:512], scalar1=-SCALE,
                                       scalar2=None, op0=ALU.mult,
                                       op1=ALU.min,
                                       accum_out=mcol[:, c:c + 1]),
                                   reads=(f"ps{c % 2}",),
                                   writes=("l3simb", "l3mcol", "xn_tm_alias"),
                                   name=f"{tag}sm{h}_{i}_{c}")
                        em.add("vector", lambda e, cd=cd: e.tensor_reduce(
                            out=mcol[:, 15:16], in_=mcol[:, 0:cd + 1],
                            axis=AX.X, op=ALU.min),
                            reads=("l3mcol",), writes=("l3mcol",),
                            name=f"{tag}mm{h}_{i}")
                        for c in range(cd + 1):
                            em.add("scalar", lambda e, c=c: e.activation(
                                out=pbuf[:, 512 * c:512 * (c + 1)],
                                in_=simb[:, 512 * c:512 * (c + 1)],
                                func=AF.Exp, scale=-1.0,
                                bias=mcol[:, 15:16],
                                accum_out=scol[:, c:c + 1]),
                                reads=("l3simb", "l3mcol"),
                                writes=("l3p", "l3scol"),
                                name=f"{tag}ex{h}_{i}_{c}")
                        em.add("vector", lambda e, cd=cd, i=i:
                               e.scalar_tensor_tensor(
                                   out=pbuf[:, 512 * cd:512 * (cd + 1)],
                                   in0=pbuf[:, 512 * cd:512 * (cd + 1)],
                                   scalar=1.0,
                                   in1=masks[:, 512 * (i % 2):
                                             512 * (i % 2 + 1)],
                                   op0=ALU.mult, op1=ALU.mult,
                                   accum_out=scol[:, cd:cd + 1]),
                               reads=("l3p", "l3masks"),
                               writes=("l3p", "l3scol"),
                               name=f"{tag}msk{h}_{i}")
                        if cd + 1 < EC:
                            em.add("vector", lambda e, cd=cd, EC=EC: e.memset(
                                pbuf[:, 512 * (cd + 1):512 * EC], 0.0),
                                writes=("l3p",), name=f"{tag}z{h}_{i}")
                        em.add("vector", lambda e, cd=cd: e.tensor_reduce(
                            out=scol[:, 15:16], in_=scol[:, 0:cd + 1],
                            axis=AX.X, op=ALU.add),
                            reads=("l3scol",), writes=("l3scol",),
                            name=f"{tag}ss{h}_{i}")
                        em.add("vector", lambda e: e.reciprocal(
                            out=scol[:, 14:15], in_=scol[:, 15:16]),
                            reads=("l3scol",), writes=("l3scol",),
                            name=f"{tag}rc{h}_{i}")
                        if gate_tm is not None:
                            em.add("vector", lambda e, i=i, h=h: e.tensor_mul(
                                scol[:, 14:15], scol[:, 14:15],
                                gate_tm[:, 16 * i + h:16 * i + h + 1]),
                                reads=("l3scol", "l3gata"),
                                writes=("l3scol",), name=f"{tag}gs{h}_{i}")
                        em.add("vector", lambda e, EC=EC: e.tensor_scalar_mul(
                            pbuf[:, 0:512 * EC], pbuf[:, 0:512 * EC],
                            scol[:, 14:15]),
                            reads=("l3p", "l3scol"), writes=("l3p",),
                            name=f"{tag}psc{h}_{i}")
                        for tb in range((ET + 7) // 8):
                            bank = cx.psT[tb % 2]
                            lo, hi = 8 * tb, min(8 * tb + 8, ET)
                            for t in range(lo, hi):
                                em.add("tensor",
                                       lambda e, t=t, bank=bank, lo=lo:
                                       e.transpose(
                                           bank[:, 128 * (t - lo):
                                                128 * (t - lo + 1)],
                                           pbuf[:, 128 * t:128 * (t + 1)],
                                           cx.ident[:]),
                                       reads=("l3p", "ident"),
                                       writes=(f"psT{tb % 2}",),
                                       name=f"{tag}pt{h}_{i}_{t}")
                            em.add("vector",
                                   lambda e, bank=bank, lo=lo, hi=hi, j=j:
                                   e.tensor_copy(
                                       out=_pt_dst(pT, lo, hi, j),
                                       in_=bank[:, 0:128 * (hi - lo)].rearrange(
                                           "p (t q) -> p t q", q=128)),
                                   reads=(f"psT{tb % 2}",), writes=("l3pT",),
                                   name=f"{tag}pte{h}_{i}_{tb}")
                    for t in range(ET):
                        em.add("tensor", lambda e, t=t, ET=ET: e.matmul(
                            cx.psum[2][0:64, 0:512], vt[:, 64 * t:64 * (t + 1)],
                            pT[:, 512 * t:512 * (t + 1)],
                            start=(t == 0), stop=(t == ET - 1)),
                            reads=("l3vt", "l3pT"), writes=("ps2",),
                            name=f"{tag}av{h}_{g}_{t}")
                    em.add("vector", lambda e, h=h, g=g: e.tensor_copy(
                        out=out_fm[64 * (h % 2):64 * (h % 2) + 64,
                                   HALF * (h // 2) + 512 * g:
                                   HALF * (h // 2) + 512 * (g + 1)],
                        in_=cx.psum[2][0:64, 0:512]),
                        reads=("ps2",), writes=(out_key,),
                        name=f"{tag}ave{h}_{g}")

        attn(16, q_all, "l3qall", 0, 1024, 2048, kvo, ao, "l3ao", gat_a, "lA")
        attn(8, qf_all, "l3qfall", 0, 512, 1024, fkvo, aof, "l3aof", None, "lF")
    em.fence()

    if DEBUG_L3:
        dump_fm(cx, "dbg_aof", aof, 4, "l3aof")
        dump_fm(cx, "dbg_ao", ao, 8, "l3ao")
    if True:
        wsl = cx.xn_tm[:, 0:NF * 1024]
        for k in range(NF):
            em.dma("sync", lambda e, k=k: e.dma_start(
                out=wsl[:, 1024 * k:1024 * (k + 1)], in_=dr["l3a_wout"][k]),
                writes=("l3wsl",), name=f"l3wo{k}")
        for i in range(NT):
            for k in range(NF):
                for c in range(2):
                    em.add("tensor", lambda e, k=k, i=i, c=c: e.matmul(
                        cx.psum[2 * (i % 2) + c][:, 0:512],
                        ao[:, HALF * k + 128 * i:HALF * k + 128 * (i + 1)],
                        wsl[:, 1024 * k + 512 * c:1024 * k + 512 * (c + 1)],
                        start=(k == 0), stop=(k == NF - 1)),
                        reads=("l3wsl", "l3ao"),
                        writes=(f"ps{2 * (i % 2) + c}",),
                        name=f"l3out{k}_{i}_{c}")
            for c in range(2):
                em.add("vector", lambda e, i=i, c=c: e.tensor_copy(
                    out=s_j[:, 1024 * i + 512 * c:1024 * i + 512 * (c + 1)],
                    in_=cx.psum[2 * (i % 2) + c][:, 0:512]),
                    reads=(f"ps{2 * (i % 2) + c}",), writes=("l3sj",),
                    name=f"l3oute{i}_{c}")
    em.fence()

    with ExitStack() as scr:
        mk = stg[:, 0:4 * 256]
        bcb2 = stg[:, 4 * 256:4 * 256 + 256]
        gf = ao
        oeb = [cx.xn_fm, cx.xn_tm]
        for e_i in range(8):
            for kt in range(4):
                idx = 4 * e_i + kt
                buf = oeb[idx // 16]
                off = 1024 * (idx % 16)
                em.dma("sync", lambda e, e_i=e_i, kt=kt, buf=buf, off=off:
                       e.dma_start(out=buf[:, off:off + 1024],
                                   in_=dr["l3f_oe"][e_i, kt]),
                       writes=("l3oeb",), name=f"l3oel{e_i}_{kt}")
        for qg in range(8):
            q0 = 256 * qg
            for e_i in range(8):
                for pt in range(4):
                    em.add("tensor", lambda e, e_i=e_i, pt=pt, q0=q0: e.matmul(
                        cx.psB[:, 0:256],
                        selb[:, 128 * (4 * e_i + pt):128 * (4 * e_i + pt + 1)],
                        gmT_q[0:64, q0:q0 + 256], start=True, stop=True),
                        reads=("l3selb", "l3gmTq"), writes=("psB",),
                        name=f"l3obc{qg}_{e_i}_{pt}")
                    em.add("vector", lambda e: e.tensor_copy(
                        out=bcb2[:], in_=cx.psB[:, 0:256]),
                        reads=("psB",), writes=("l3bcb2",),
                        name=f"l3obe{qg}_{e_i}_{pt}")
                    em.add("vector", lambda e, pt=pt, q0=q0: e.tensor_mul(
                        mk[:, 256 * pt:256 * (pt + 1)],
                        aof[:, HALF * pt + q0:HALF * pt + q0 + 256], bcb2[:]),
                        reads=("l3aof", "l3bcb2"), writes=("l3mk",),
                        name=f"l3omk{qg}_{e_i}_{pt}")
                for kt in range(4):
                    idx = 4 * e_i + kt
                    buf = oeb[idx // 16]
                    off = 1024 * (idx % 16)
                    for tq in range(2):
                        for c in range(2):
                            em.add("tensor",
                                   lambda e, kt=kt, tq=tq, c=c, e_i=e_i,
                                   buf=buf, off=off: e.matmul(
                                       cx.psum[2 * tq + c][:, 0:512],
                                       mk[:, 256 * kt + 128 * tq:
                                          256 * kt + 128 * (tq + 1)],
                                       buf[:, off + 512 * c:
                                           off + 512 * (c + 1)],
                                       start=(e_i == 0 and kt == 0),
                                       stop=(e_i == 7 and kt == 3)),
                                   reads=("l3mk", "l3oeb"),
                                   writes=(f"ps{2 * tq + c}",),
                                   name=f"l3omm{qg}_{e_i}_{kt}_{tq}_{c}")
            for tq in range(2):
                for c in range(2):
                    em.add("scalar", lambda e, qg=qg, tq=tq, c=c: e.activation(
                        out=gf[:, 1024 * (2 * qg + tq) + 512 * c:
                               1024 * (2 * qg + tq) + 512 * (c + 1)],
                        in_=cx.psum[2 * tq + c][:, 0:512], func=AF.Silu),
                        reads=(f"ps{2 * tq + c}",), writes=("l3gf",),
                        name=f"l3gout{qg}_{tq}_{c}")
    em.fence()

    sgr = dr["l3_sg_in"].rearrange("(j p) d -> j p d", p=128)
    for j in range(NT):
        em.add("vector", lambda e, j=j: e.scalar_tensor_tensor(
            out=cx.sgbuf[:], in0=s_j[:, 1024 * j:1024 * (j + 1)], scalar=1.0,
            in1=gf[:, 1024 * j:1024 * (j + 1)], op0=ALU.mult, op1=ALU.mult),
            reads=("l3sj", "l3gf"), writes=("sgbuf",), name=f"l3sg{j}")
        em.dma("sync", lambda e, j=j: e.dma_start(
            out=sgr[j], in_=cx.sgbuf[:]),
            reads=("sgbuf",), writes=("l3sgin",), name=f"l3sgs{j}")
    _cc_pair(em, "AllGather", dr["l3_sg_in"], dr["l3_sg_out"],
             ("l3sgin",), ("l3sgout",), "l3sgcc")
    cx.l3_sg_out = dr["l3_sg_out"]
    if DEBUG_L3:
        dbg_s = nc.dram_tensor("dbg_s", [2048, 1024], F32, kind="ExternalOutput").ap()
        dbg_g = nc.dram_tensor("dbg_g", [2048, 1024], F32, kind="ExternalOutput").ap()
        dsr = dbg_s.rearrange("(j p) d -> j p d", p=128)
        dgr = dbg_g.rearrange("(j p) d -> j p d", p=128)
        for j in range(NT):
            em.add("vector", lambda e, j=j: e.tensor_copy(
                out=cx.sgbuf[:], in_=s_j[:, 1024 * j:1024 * (j + 1)]),
                reads=("l3sj",), writes=("sgbuf",), name=f"dbgs{j}")
            em.out_dma("sync", lambda e, j=j: e.dma_start(
                out=dsr[j], in_=cx.sgbuf[:]), reads=("sgbuf",), name=f"dbgss{j}")
            em.add("vector", lambda e, j=j: e.tensor_copy(
                out=cx.sgbuf[:], in_=gf[:, 1024 * j:1024 * (j + 1)]),
                reads=("l3gf",), writes=("sgbuf",), name=f"dbgg{j}")
            em.out_dma("sync", lambda e, j=j: e.dma_start(
                out=dgr[j], in_=cx.sgbuf[:]), reads=("sgbuf",), name=f"dbggs{j}")
    em.fence()


def final_update_l3(cx, ctx, x_dram, xout_dram):
    nc, em = cx.nc, cx.em
    xrk = getattr(cx, "x_rkey", None)
    xdep = (xrk,) if xrk else ()
    xr = x_dram.rearrange("(i p) d -> i p d", p=128)
    xo = xout_dram.rearrange("(i p) d -> i p d", p=128)
    from concourse.bass import IndirectOffsetOnAxis as _IOA
    for i in range(NT):
        xb = cx.xbuf[:, 1024 * (i % 2):1024 * (i % 2 + 1)]
        bk = f"xbuf{i % 2}"
        em.dma("sync", lambda e, i=i, xb=xb: e.dma_start(out=xb, in_=xr[i]),
               reads=xdep, writes=(bk,), name=f"l3fu_ld{i}")
        em.dma("gpsimd", lambda e, i=i: e.indirect_dma_start(
            out=cx.sgbuf[:], out_offset=None, in_=cx.l3_sg_out[:],
            in_offset=_IOA(ap=cx.l3idx[:, NT + i:NT + i + 1], axis=0)),
            reads=("l3sgout", "l3idx"), writes=("sgbuf",), name=f"l3fu_sg{i}")
        em.add("vector", lambda e, xb=xb: e.tensor_add(xb, xb, cx.sgbuf[:]),
               reads=("sgbuf", bk), writes=(bk,), name=f"l3fu_add{i}")
        em.out_dma("sync", lambda e, i=i, xb=xb: e.dma_start(
            out=xo[i], in_=xb), reads=(bk,), name=f"l3fu_st{i}")





def prep_l1_weights(inputs, h):
    w = {}
    for nm, key in (("w1", "L1_attn_W1"), ("w2", "L1_attn_W2"),
                    ("w3", "L1_attn_W3")):
        w[f"l1_{nm}"] = fmpack(np.ascontiguousarray(np.asarray(inputs[key]).T))
    Wo = np.asarray(inputs["L1_attn_Wo"])
    perm = np.array([8 * (m % 128) + m // 128 for m in range(1024)])
    WoP = Wo[:, perm]
    w["l1_wo"] = fmpack(np.ascontiguousarray(WoP.T))
    qw = np.asarray(inputs["L1_ffn_q_w"])
    qsel = np.concatenate([qw[:, 64 * hh:64 * hh + 32] for hh in range(16)],
                          axis=1)
    w["l1f_qw"] = fmpack(np.ascontiguousarray(qsel))
    w["l1f_kw"] = fmpack(np.asarray(inputs["L1_ffn_k_w"]))
    w["l1f_vw"] = fmpack(np.asarray(inputs["L1_ffn_v_w"]))
    w["l1f_ow"] = fmpack(np.ascontiguousarray(
        np.asarray(inputs["L1_ffn_o_w"])[:512]))
    rw = np.zeros((1024, 128), np.float32)
    rw[:, :8] = np.asarray(inputs["L1_ffn_router_w"])
    w["l1f_rw"] = fmpack(rw)
    w["l1f_w1"] = np.stack([fmpack(np.asarray(inputs["L1_ffn_w1"])[e])
                            for e in range(8)])
    w["l1f_w2"] = np.stack([slabpack(np.asarray(inputs["L1_ffn_w2"])[e])
                            for e in range(8)])
    qbs = np.asarray(inputs["L1_ffn_q_b"])
    qb = np.concatenate([qbs[64 * hh:64 * hh + 32] for hh in range(16)])
    w["l1f_qb"] = np.ascontiguousarray(qb.reshape(4, 128))
    w["l1f_kb"] = np.ascontiguousarray(
        np.asarray(inputs["L1_ffn_k_b"]).reshape(4, 128))
    w["l1f_vb"] = np.ascontiguousarray(
        np.asarray(inputs["L1_ffn_v_b"]).reshape(4, 128))
    w["l1f_ob"] = np.ascontiguousarray(
        np.asarray(inputs["L1_ffn_o_b"]).reshape(8, 128))
    rb = np.zeros(128, np.float32)
    rb[:8] = np.asarray(inputs["L1_ffn_router_b"])
    w["l1f_rb"] = np.ascontiguousarray(rb.reshape(1, 128))
    w["l1f_b1"] = np.ascontiguousarray(
        np.asarray(inputs["L1_ffn_b1"]).reshape(8, 16, 128))
    w["l1f_b2"] = np.ascontiguousarray(
        np.asarray(inputs["L1_ffn_b2"]).reshape(8, 8, 128))
    oidx = np.empty((128, 32), np.uint32)
    p = np.arange(128, dtype=np.uint32)
    for t in range(16):
        oidx[:, t] = 2048 * h + 128 * t + p
        oidx[:, 16 + t] = 4096 + 2048 * h + 128 * t + p
    w["l1_oidx"] = oidx
    return w


def declare_l1_dram(nc):
    d = {}
    for nm, sh in (("l1_w1", [8, 128, 1024]), ("l1_w2", [8, 128, 1024]),
                   ("l1_w3", [8, 128, 1024]), ("l1_wo", [8, 128, 1024]),
                   ("l1f_qw", [4, 128, 1024]), ("l1f_kw", [4, 128, 1024]),
                   ("l1f_vw", [4, 128, 1024]), ("l1f_ow", [8, 128, 512]),
                   ("l1f_rw", [1, 128, 1024]),
                   ("l1f_w1", [8, 16, 128, 1024]),
                   ("l1f_w2", [8, 16, 128, 1024])):
        d[nm] = nc.dram_tensor(nm, sh, BF16, kind="ExternalInput").ap()
    for nm, sh in (("l1f_qb", [4, 128]), ("l1f_kb", [4, 128]),
                   ("l1f_vb", [4, 128]), ("l1f_ob", [8, 128]),
                   ("l1f_rb", [1, 128]), ("l1f_b1", [8, 16, 128]),
                   ("l1f_b2", [8, 8, 128])):
        d[nm] = nc.dram_tensor(nm, sh, F32, kind="ExternalInput").ap()
    d["l1_oidx"] = nc.dram_tensor("l1_oidx", [128, 32], U32,
                                  kind="ExternalInput").ap()
    d["l1_ao_in"] = nc.dram_tensor("l1_ao_in", [4096, 512], BF16).ap()
    d["l1_ao_out"] = nc.dram_tensor("l1_ao_out", [8192, 512], BF16).ap()
    d["l1f_kv_in"] = nc.dram_tensor("l1f_kv_in", [1024, 2048], BF16).ap()
    d["l1f_kv_out"] = nc.dram_tensor("l1f_kv_out", [2048, 2048], BF16).ap()
    d["l1f_w1c"] = nc.dram_tensor("l1f_w1c", [8, 16, 128, 1024], BF16).ap()
    d["l1f_w2c"] = nc.dram_tensor("l1f_w2c", [8, 16, 128, 1024], BF16).ap()
    return d


def build_l1(cx, ctx, dr):
    nc, em = cx.nc, cx.em
    sb = lambda n, s, d: ctx.enter_context(nc.sbuf_tensor(n, s, d))
    em.fence()
    C1 = sb("l1_C1", [128, NF * HALF], BF16)
    C2 = sb("l1_C2", [128, NF * HALF], BF16)
    C3 = sb("l1_C3", [128, NF * HALF], BF16)
    C4 = sb("l1_C4", [128, NF * HALF], BF16)
    costab = cx.xn_tm[:, 0:4096]
    sintab = cx.xn_tm[:, 4096:8192]
    alloc_gm_consts(cx, ctx, full=False, tag="l1")
    ident32x = cx.xbuf[:, 128:256]
    em.add("vector", lambda e: e.tensor_scalar(
        out=ident32x, in0=cx.ident_i[:], scalar1=0.0, scalar2=None,
        op0=ALU.is_equal), reads=("ident_i",), writes=("xbufc",),
        name="l1id32")
    wb0 = cx.xbuf[:, 1280:1792].bitcast(BF16)
    wb1 = cx.sq_junk[:, 0:1024]
    stg = cx.xbuf[:, 256:1280].bitcast(BF16)
    bcols = sb("l1_bcols", [128, 48], F32)
    combs = cx.xbuf[:, 1792:1920]
    cx.l1idx = sb("l1_idx", [128, 32], U32)
    em.dma("sync", lambda e: e.dma_start(out=cx.l1idx[:], in_=dr["l1_oidx"][:]),
           writes=("l1idx",), name="l1idxld")

    invf = bcols[:, 44:45]
    pmod = bcols[:, 45:46]
    pidx = bcols[:, 46:47].bitcast(I32)
    em.add("gpsimd", lambda e: e.iota(pidx, [[1, 1]], base=0,
                                      channel_multiplier=1),
           writes=("l1bc",), name="l1pi")
    em.add("vector", lambda e: e.tensor_scalar(
        out=pmod.bitcast(I32), in0=pidx, scalar1=1, scalar2=None,
        op0=ALU.bitwise_and), reads=("l1bc",), writes=("l1bc",), name="l1pa")
    em.add("vector", lambda e: e.scalar_tensor_tensor(
        out=invf, in0=pmod.bitcast(I32), scalar=-1.0, in1=pidx,
        op0=ALU.mult, op1=ALU.add),
        reads=("l1bc",), writes=("l1bc",), name="l1ps")
    em.add("scalar", lambda e: e.activation(
        out=invf, in_=invf, func=AF.Exp,
        scale=float(-np.log(10000.0) / 128.0)),
        reads=("l1bc",), writes=("l1bc",), name="l1if")
    em.add("vector", lambda e: e.tensor_scalar(
        out=bcols[:, 43:44], in0=pmod.bitcast(I32), scalar1=2.0, scalar2=-1.0,
        op0=ALU.mult, op1=ALU.add),
        reads=("l1bc",), writes=("l1bc",), name="l1sg")
    angf = C4[:, :].bitcast(F32)[:, 0:1024]
    angiI = C4[:, :].bitcast(F32)[:, 1024:2048].bitcast(I32)

    def build_tab(shift, outtab, tag):
        for hf in range(4):
            em.add("gpsimd", lambda e, hf=hf: e.iota(
                angiI, [[1, 2], [8, 512]], base=2 * hf,
                channel_multiplier=0),
                writes=("l1ang",), name=f"{tag}i{hf}")
            em.add("vector", lambda e: e.tensor_scalar(
                out=angf, in0=angiI, scalar1=invf, scalar2=None,
                op0=ALU.mult), reads=("l1ang", "l1bc"), writes=("l1ang",),
                name=f"{tag}m{hf}")
            if shift:
                em.add("vector", lambda e: e.tensor_scalar_add(
                    angf, angf, float(np.pi / 2)),
                    reads=("l1ang",), writes=("l1ang",), name=f"{tag}s{hf}")
            em.add("vector", lambda e: e.tensor_scalar_mul(
                cx.sgbuf[:], angf, float(1.0 / (2 * np.pi))),
                reads=("l1ang",), writes=("sgbuf",), name=f"{tag}d{hf}")
            em.add("vector", lambda e: e.tensor_copy(
                out=cx.sgbuf[:].bitcast(I32), in_=cx.sgbuf[:]),
                reads=("sgbuf",), writes=("sgbuf",), name=f"{tag}r{hf}")
            em.add("vector", lambda e: e.scalar_tensor_tensor(
                out=angf, in0=cx.sgbuf[:].bitcast(I32),
                scalar=float(-2 * np.pi), in1=angf,
                op0=ALU.mult, op1=ALU.add),
                reads=("sgbuf", "l1ang"), writes=("l1ang",),
                name=f"{tag}f{hf}")
            em.add("scalar", lambda e, hf=hf, outtab=outtab: e.activation(
                out=outtab[:, 1024 * hf:1024 * (hf + 1)], in_=angf,
                func=AF.Sin),
                reads=("l1ang",), writes=("l1tab",), name=f"{tag}sn{hf}")

    if DEBUG_L1:
        cx.dmpbuf = cx.sgbuf[:, 0:512]
    build_tab(True, costab, "l1co")
    build_tab(False, sintab, "l1si")
    em.add("vector", lambda e: e.tensor_scalar(
        out=sintab[:], in0=sintab[:], scalar1=bcols[:, 43:44], scalar2=None,
        op0=ALU.mult), reads=("l1tab", "l1bc"), writes=("l1tab",),
        name="l1ss")
    if DEBUG_L1:
        dump_fm(cx, "dbg_cos", costab, 1, "l1tab", width=4096)
        dump_fm(cx, "dbg_sin", sintab, 1, "l1tab", width=4096)
    em.fence()

    def ev_to(dstbuf, dkey, tag):
        def ev(f):
            for c in range(4):
                em.add("vector", lambda e, f=f, c=c: e.tensor_copy(
                    out=dstbuf[:, HALF * f + 512 * c:HALF * f + 512 * (c + 1)],
                    in_=cx.psum[c][:, 0:512]),
                    reads=(f"ps{c}",), writes=(dkey,), name=f"{tag}{f}_{c}")
        return ev

    proj_p2(cx, dr["l1_w1"], cx.xn_fm, "xn_fm", (wb0, wb1),
            ("l1wb0", "l1wb1"), NF, ev_to(C1, "l1q", "l1qp"), "l1w1")
    proj_p2(cx, dr["l1_w2"], cx.xn_fm, "xn_fm", (wb0, wb1),
            ("l1wb0", "l1wb1"), NF, ev_to(C2, "l1k", "l1kp"), "l1w2")
    proj_p2(cx, dr["l1_w3"], cx.xn_fm, "xn_fm", (wb0, wb1),
            ("l1wb0", "l1wb1"), NF, ev_to(C3, "l1vf", "l1vp"), "l1w3")
    transpose_fm_to_tm(cx, C3, C4, "l1vf", "l1v", NT, NF, tag="l1vt")
    vproj_tm = C4
    em.fence()
    if DEBUG_L1:
        dump_fm(cx, "dbg_q", C1, NF, "l1q")
        dump_fm(cx, "dbg_k", C2, NF, "l1k")

    shmask = []
    for i in range(0, 32, 2):
        shmask += [i + 1, i]
    for l in range(4):
        for (srcb, skey) in ((C1, "l1q"), (C2, "l1k")):
            for c in range(NF):
                src = srcb[:, HALF * c + 512 * l:HALF * c + 512 * (l + 1)]
                tabsl = slice(512 * c, 512 * (c + 1))
                em.add("vector", lambda e, src=src: e.stream_shuffle(
                    out=stg[:, 0:512], in_=src, mask=shmask),
                    reads=(skey,), writes=("l1stg",), name=f"l1sh{l}{skey}{c}")
                em.add("vector", lambda e, tabsl=tabsl: e.tensor_tensor(
                    out=stg[:, 512:1024], in0=stg[:, 0:512],
                    in1=sintab[:, tabsl], op=ALU.mult),
                    reads=("l1stg", "l1tab"), writes=("l1stg",),
                    name=f"l1rm{l}{skey}{c}")
                em.add("vector", lambda e, src=src, tabsl=tabsl:
                       e.tensor_tensor(out=src, in0=src,
                                       in1=costab[:, tabsl], op=ALU.mult),
                       reads=(skey, "l1tab"), writes=(skey,),
                       name=f"l1rc{l}{skey}{c}")
                em.add("vector", lambda e, src=src: e.tensor_add(
                    src, src, stg[:, 512:1024]),
                    reads=("l1stg", skey), writes=(skey,),
                    name=f"l1ra{l}{skey}{c}")
    em.fence()

    simb = cx.xn_tm[:, :].bitcast(F32)[:, 0:4096]
    pbuf = cx.xn_tm[:, 8192:12288]
    outh = cx.xn_tm[:, 12288:16384]
    pT = C3
    mcol = cx.stat[:, 64:80]
    scol = cx.stat[:, 80:96]
    aor = dr["l1_ao_in"]
    for l in range(4):
        for g in range(8):
            for j in range(4):
                qt = 4 * g + j
                for c2 in range(8):
                    ps = cx.psum[c2 % 2]
                    cq, jq = qt // 4, qt % 4
                    qsl = (HALF * cq + 512 * l + 128 * jq)
                    ck, jk = c2 // 1, 0
                    em.add("tensor", lambda e, qsl=qsl, c2=c2, ps=ps, l=l:
                           e.matmul(ps[:, 0:512],
                                    C1[:, qsl:qsl + 128],
                                    C2[:, HALF * (c2 // 1 % 8) + 512 * l:
                                       HALF * (c2 % 8) + 512 * (l + 1)][
                                       :, 0:512] if False else
                                    C2[:, HALF * c2 + 512 * l:
                                       HALF * c2 + 512 * l + 512],
                                    start=True, stop=True),
                           reads=("l1q", "l1k"), writes=(f"ps{c2 % 2}",),
                           name=f"l1sim{l}_{qt}_{c2}")
                    em.add("vector", lambda e, c2=c2, ps=ps: e.tensor_scalar(
                        out=simb[:, 512 * c2:512 * (c2 + 1)], in0=ps[:, 0:512],
                        scalar1=float(-1.0 / np.sqrt(8.0)), scalar2=None,
                        op0=ALU.mult, op1=ALU.min,
                        accum_out=mcol[:, c2:c2 + 1]),
                        reads=(f"ps{c2 % 2}",), writes=("l1simb", "l1mcol"),
                        name=f"l1sm{l}_{qt}_{c2}")
                em.add("vector", lambda e: e.tensor_reduce(
                    out=mcol[:, 15:16], in_=mcol[:, 0:8], axis=AX.X,
                    op=ALU.min), reads=("l1mcol",), writes=("l1mcol",),
                    name=f"l1mm{l}_{qt}")
                for c2 in range(8):
                    em.add("scalar", lambda e, c2=c2: e.activation(
                        out=pbuf[:, 512 * c2:512 * (c2 + 1)],
                        in_=simb[:, 512 * c2:512 * (c2 + 1)], func=AF.Exp,
                        scale=-1.0, bias=mcol[:, 15:16],
                        accum_out=scol[:, c2:c2 + 1]),
                        reads=("l1simb", "l1mcol"), writes=("l1p", "l1scol"),
                        name=f"l1ex{l}_{qt}_{c2}")
                em.add("vector", lambda e: e.tensor_reduce(
                    out=scol[:, 15:16], in_=scol[:, 0:8], axis=AX.X,
                    op=ALU.add), reads=("l1scol",), writes=("l1scol",),
                    name=f"l1sr{l}_{qt}")
                em.add("vector", lambda e: e.reciprocal(
                    out=scol[:, 14:15], in_=scol[:, 15:16]),
                    reads=("l1scol",), writes=("l1scol",),
                    name=f"l1rp{l}_{qt}")
                em.add("vector", lambda e: e.tensor_scalar_mul(
                    pbuf[:, 0:4096], pbuf[:, 0:4096], scol[:, 14:15]),
                    reads=("l1p", "l1scol"), writes=("l1p",),
                    name=f"l1pc{l}_{qt}")
                for tb in range(4):
                    bank = cx.psT[tb % 2]
                    for t in range(8 * tb, 8 * tb + 8):
                        em.add("tensor", lambda e, t=t, tb=tb, bank=bank:
                               e.transpose(bank[:, 128 * (t - 8 * tb):
                                                128 * (t - 8 * tb + 1)],
                                           pbuf[:, 128 * t:128 * (t + 1)],
                                           cx.ident[:]),
                               reads=("l1p", "ident"),
                               writes=(f"psT{tb % 2}",),
                               name=f"l1pt{l}_{qt}_{t}")
                    em.add("vector", lambda e, tb=tb, bank=bank, j=j:
                           e.tensor_copy(
                               out=_pt_dst(pT, 8 * tb, 8 * tb + 8, j),
                               in_=bank[:, 0:1024].rearrange(
                                   "p (t q) -> p t q", q=128)),
                           reads=(f"psT{tb % 2}",), writes=("l1pT",),
                           name=f"l1pe{l}_{qt}_{tb}")
            for t in range(32):
                voff = 1024 * (4 * l + t % 4) + 128 * (t // 4)
                em.add("tensor", lambda e, t=t, voff=voff: e.matmul(
                    cx.psum[2][:, 0:512], vproj_tm[:, voff:voff + 128],
                    pT[:, 512 * t:512 * (t + 1)],
                    start=(t == 0), stop=(t == 31)),
                    reads=("l1v", "l1pT"), writes=("ps2",),
                    name=f"l1av{l}_{g}_{t}")
            em.add("vector", lambda e: e.tensor_copy(
                out=stg[:, 0:512], in_=cx.psum[2][:, 0:512]),
                reads=("ps2",), writes=("l1stg",), name=f"l1ae{l}_{g}")
            for t4 in range(4):
                qt = 4 * g + t4
                bank = cx.psT[t4 % 2]
                em.add("tensor", lambda e, t4=t4, bank=bank: e.transpose(
                    bank[:, 0:128], stg[:, 128 * t4:128 * (t4 + 1)],
                    cx.ident[:]),
                    reads=("l1stg", "ident"), writes=(f"psT{t4 % 2}",),
                    name=f"l1ot{l}_{g}_{t4}")
                em.add("vector", lambda e, qt=qt, bank=bank: e.tensor_copy(
                    out=outh[:, 128 * qt:128 * (qt + 1)], in_=bank[:, 0:128]),
                    reads=(f"psT{t4 % 2}",), writes=("l1outh",),
                    name=f"l1oe{l}_{g}_{t4}")
        for m in range(32):
            c, jb = m // 4, m % 4
            em.dma("sync", lambda e, m=m, c=c, jb=jb, l=l: e.dma_start(
                out=aor[1024 * jb + 128 * c:1024 * jb + 128 * (c + 1),
                        128 * l:128 * (l + 1)],
                in_=outh[:, 128 * m:128 * (m + 1)]),
                reads=("l1outh",), writes=("l1aoin",), name=f"l1shp{l}_{m}")
    _cc_pair(em, "AllGather", dr["l1_ao_in"], dr["l1_ao_out"],
             ("l1aoin",), ("l1aoout",), "l1aocc")
    em.fence()
    outfull = C1
    sg16 = cx.sgbuf[:, :].bitcast(BF16)
    l1fl = ctx.enter_context(nc.sbuf_tensor("l1_fl", [1, 8], BF16))
    l1fc = ctx.enter_context(nc.sbuf_tensor("l1_fc", [128, 2], F32))
    em.dma("gpsimd", lambda e: e.dma_start(out=l1fl[:], in_=dr["flag8"][:]),
           writes=("l1fl",), name="l1flld")
    em.add("tensor", lambda e: e.matmul(
        cx.psB[:, 0:8], cx.ones1[:, :], l1fl[:, 0:8], start=True, stop=True),
        reads=("l1fl", "ones1"), writes=("psB",), name="l1flb")
    em.add("vector", lambda e: e.tensor_copy(
        out=l1fc[:, 0:1], in_=cx.psB[:, 0:1]),
        reads=("psB",), writes=("l1fc",), name="l1fce")
    em.add("vector", lambda e: e.tensor_scalar(
        out=l1fc[:, 1:2], in0=l1fc[:, 0:1], scalar1=-1.0, scalar2=1.0,
        op0=ALU.mult, op1=ALU.add),
        reads=("l1fc",), writes=("l1fc",), name="l1fci")
    for t in range(16):
        for half in range(2):
            p2 = (2 * t + half) % 2
            sA = sg16[:, 1024 * p2:1024 * p2 + 512]
            sB = sg16[:, 1024 * p2 + 512:1024 * (p2 + 1)]
            ka, kb = f"l1gA{p2}", f"l1gB{p2}"
            base = 4096 * half + 128 * t
            em.dma("sync", lambda e, sA=sA, base=base: e.dma_start(
                out=sA, in_=dr["l1_ao_out"][base:base + 128, :]),
                reads=("l1aoout",), writes=(ka,), name=f"l1gA{t}_{half}")
            em.dma("gpsimd", lambda e, sB=sB, base=base: e.dma_start(
                out=sB, in_=dr["l1_ao_out"][base + 2048:base + 2176, :]),
                reads=("l1aoout",), writes=(kb,), name=f"l1gB{t}_{half}")
            em.add("vector", lambda e, sA=sA: e.tensor_scalar(
                out=sA, in0=sA, scalar1=l1fc[:, 1:2], scalar2=None,
                op0=ALU.mult),
                reads=(ka, "l1fc"), writes=(ka,), name=f"l1gs{t}_{half}")
            em.add("vector", lambda e, sA=sA, sB=sB, t=t, half=half:
                   e.scalar_tensor_tensor(
                out=outfull[:, 1024 * t + 512 * half:
                            1024 * t + 512 * (half + 1)],
                in0=sB, scalar=l1fc[:, 0:1], in1=sA, op0=ALU.mult,
                op1=ALU.add),
                reads=(ka, kb, "l1fc"), writes=("l1of",),
                name=f"l1gb{t}_{half}")
    s_tm = C2
    for u2 in range(2):
        for w8 in range(NF):
            em.dma("sync", lambda e, w8=w8: e.dma_start(
                out=wb0[:], in_=dr["l1_wo"][w8]),
                writes=("l1wb0",), name=f"l1wol{u2}_{w8}")
            for kt in range(NF):
                for c in range(2):
                    em.add("tensor", lambda e, u2=u2, w8=w8, kt=kt, c=c:
                           e.matmul(
                               cx.psum[2 * (w8 % 2) + c][:, 0:512],
                               wb0[:, 128 * kt:128 * (kt + 1)],
                               outfull[:, 1024 * (8 * u2 + kt) + 512 * c:
                                        1024 * (8 * u2 + kt) + 512 * (c + 1)],
                               start=(kt == 0), stop=(kt == NF - 1)),
                           reads=("l1wb0", "l1of"),
                           writes=(f"ps{2 * (w8 % 2) + c}",),
                           name=f"l1wo{u2}_{w8}_{kt}_{c}")
            for c in range(2):
                em.add("vector", lambda e, u2=u2, w8=w8, c=c: e.tensor_copy(
                    out=s_tm[:, 1024 * (8 * u2 + w8) + 512 * c:
                             1024 * (8 * u2 + w8) + 512 * (c + 1)],
                    in_=cx.psum[2 * (w8 % 2) + c][:, 0:512]),
                    reads=(f"ps{2 * (w8 % 2) + c}",), writes=("l1s",),
                    name=f"l1se{u2}_{w8}_{c}")
    cx.s_tm, cx.s_key = s_tm, "l1s"
    if DEBUG_L1:
        dump_fm(cx, "dbg_of", outfull, NT, "l1of", width=1024)
        dump_fm(cx, "dbg_s", s_tm, NT, "l1s", width=1024)

    em.fence()

    aof = C1[:, 0:4 * HALF]
    em.add("vector", lambda e: e.memset(C4[:], 0.0), writes=("l1moe",),
           name="l1moez")
    qf = C1[:, 4 * HALF:8 * HALF]
    mla = C3
    moe = C4
    kvship = cx.xn_tm[:, 4 * HALF:4 * HALF + 2048]
    kst = C3[0:32, 0:4096]
    vst = C3[0:32, 4096:8192]
    vt2 = C3[:, 8192:9216]
    qst = C3[0:32, 9216:11264]
    est = C3[0:32, 11264:11776]
    b2all = cx.sq_junk[0:8, 0:1024]
    lnst2 = cx.xbuf[:, 0:128]

    def load_bias(dram, n, col0, tag):
        em.add("vector", lambda e: e.memset(lnst2, 0.0), writes=("xbufc",),
               name=f"{tag}z")
        em.dma("sync", lambda e, dram=dram, n=n: e.dma_start(
            out=cx.xbuf[0:n, 0:128], in_=dram), writes=("xbufc",),
            name=f"{tag}l")
        em.add("tensor", lambda e: e.transpose(
            cx.psB[:, 0:128], lnst2, ident32x),
            reads=("xbufc",), writes=("psB",), name=f"{tag}t")
        em.add("vector", lambda e, n=n, col0=col0: e.tensor_copy(
            out=bcols[:, col0:col0 + n], in_=cx.psB[:, 0:n]),
            reads=("psB",), writes=("l1bc",), name=f"{tag}e")

    load_bias(dr["l1f_qb"][:], 4, 0, "l1bq")
    load_bias(dr["l1f_kb"][:], 4, 4, "l1bk")
    load_bias(dr["l1f_vb"][:], 4, 8, "l1bv")
    load_bias(dr["l1f_ob"][:], 8, 12, "l1bo")
    em.dma("gpsimd", lambda e: e.dma_start(out=stg[0:1, 0:128],
                                           in_=dr["l1f_rb"][:]),
           writes=("l1stg",), name="l1rbl")
    em.add("tensor", lambda e: e.matmul(
        cx.psB[:, 0:8], cx.ones1[:, :], stg[0:1, 0:8],
        start=True, stop=True),
        reads=("l1stg", "ones1"), writes=("psB",), name="l1rbb")
    em.add("vector", lambda e: e.tensor_copy(
        out=bcols[:, 24:32], in_=cx.psB[:, 0:8]),
        reads=("psB",), writes=("l1bc",), name="l1rbe")

    def ev_bias(dstbuf, dkey, bcol0, tag):
        def ev(f):
            for c in range(4):
                em.add("vector", lambda e, f=f, c=c, dstbuf=dstbuf:
                       e.tensor_scalar(
                           out=dstbuf[:, HALF * f + 512 * c:
                                      HALF * f + 512 * (c + 1)],
                           in0=cx.psum[c][:, 0:512],
                           scalar1=bcols[:, bcol0 + f:bcol0 + f + 1],
                           scalar2=None, op0=ALU.add),
                       reads=(f"ps{c}", "l1bc"), writes=(dkey,),
                       name=f"{tag}{f}_{c}")
        return ev

    proj_p2(cx, dr["l1f_qw"], cx.xn_fm, "xn_fm", (wb0, wb1),
            ("l1wb0", "l1wb1"), 4, ev_bias(qf, "l1fq", 0, "l1fqp"), "l1fq")
    kvr = dr["l1f_kv_in"].rearrange("(f p) t -> f p t", p=128)

    def ev_ship(bcol0, rowoff, tag):
        def ev(f):
            for c in range(4):
                em.add("vector", lambda e, f=f, c=c, bcol0=bcol0:
                       e.tensor_scalar(
                           out=kvship[:, 512 * c:512 * (c + 1)],
                           in0=cx.psum[c][:, 0:512],
                           scalar1=bcols[:, bcol0 + f:bcol0 + f + 1],
                           scalar2=None, op0=ALU.add),
                       reads=(f"ps{c}", "l1bc"), writes=("l1kvs",),
                       name=f"{tag}{f}_{c}")
            em.dma("sync", lambda e, f=f, rowoff=rowoff: e.dma_start(
                out=kvr[rowoff + f], in_=kvship[:]),
                reads=("l1kvs",), writes=("l1fkvin",), name=f"{tag}s{f}")
        return ev

    proj_p2(cx, dr["l1f_kw"], cx.xn_fm, "xn_fm", (wb0, wb1),
            ("l1wb0", "l1wb1"), 4, ev_ship(4, 0, "l1fkp"), "l1fk")
    proj_p2(cx, dr["l1f_vw"], cx.xn_fm, "xn_fm", (wb0, wb1),
            ("l1wb0", "l1wb1"), 4, ev_ship(8, 4, "l1fvp"), "l1fv")
    _cc_pair(em, "AllGather", dr["l1f_kv_in"], dr["l1f_kv_out"],
             ("l1fkvin",), ("l1fkvout",), "l1fkvcc")
    em.fence()

    simb2 = cx.xn_tm[:, :].bitcast(F32)[:, 0:4096]
    pbuf2 = cx.xn_tm[:, 8192:12288]
    pT2 = cx.xn_tm[:, 12288:16384]
    mcol = cx.stat[:, 64:80]
    scol = cx.stat[:, 80:96]
    fkv = dr["l1f_kv_out"]
    for h in range(16):
        em.dma("gpsimd", lambda e, h=h: e.dma_start(
            out=qst[:, :], in_=qf[32 * (h % 4):32 * (h % 4) + 32,
                                  HALF * (h // 4):HALF * (h // 4 + 1)]),
            reads=("l1fq",), writes=("l1qst",), name=f"f1q{h}")
        em.dma("sync", lambda e, h=h: e.dma_start(
            out=kst[:, 0:2048], in_=fkv[32 * h:32 * (h + 1), :]),
            writes=("l1kst",), name=f"f1k0{h}")
        em.dma("sync", lambda e, h=h: e.dma_start(
            out=kst[:, 2048:4096],
            in_=fkv[1024 + 32 * h:1024 + 32 * (h + 1), :]),
            writes=("l1kst",), name=f"f1k1{h}")
        em.dma("gpsimd", lambda e, h=h: e.dma_start(
            out=vst[:, 0:2048], in_=fkv[512 + 32 * h:512 + 32 * (h + 1), :]),
            writes=("l1vst",), name=f"f1v0{h}")
        em.dma("gpsimd", lambda e, h=h: e.dma_start(
            out=vst[:, 2048:4096],
            in_=fkv[1536 + 32 * h:1536 + 32 * (h + 1), :]),
            writes=("l1vst",), name=f"f1v1{h}")
        for half2 in range(2):
            bank = cx.psT[half2]
            for t in range(16):
                em.add("tensor", lambda e, half2=half2, t=t, bank=bank:
                       e.transpose(bank[:, 32 * t:32 * (t + 1)],
                                   vst[:, 2048 * half2 + 128 * t:
                                       2048 * half2 + 128 * (t + 1)],
                                   cx.ident[0:32, 0:32]),
                       reads=("l1vst", "ident"), writes=(f"psT{half2}",),
                       name=f"f1vt{h}_{half2}_{t}")
            em.add("vector", lambda e, half2=half2, bank=bank: e.tensor_copy(
                out=vt2[:, 512 * half2:512 * (half2 + 1)], in_=bank[:, 0:512]),
                reads=(f"psT{half2}",), writes=("l1vt2",),
                name=f"f1ve{h}_{half2}")
        for qt in range(16):
            for c2 in range(8):
                ps = cx.psum[c2 % 2]
                em.add("tensor", lambda e, qt=qt, c2=c2, ps=ps: e.matmul(
                    ps[:, 0:512], qst[:, 128 * qt:128 * (qt + 1)],
                    kst[:, 512 * c2:512 * (c2 + 1)], start=True, stop=True),
                    reads=("l1qst", "l1kst"), writes=(f"ps{c2 % 2}",),
                    name=f"f1sim{h}_{qt}_{c2}")
                em.add("vector", lambda e, c2=c2, ps=ps: e.tensor_scalar(
                    out=simb2[:, 512 * c2:512 * (c2 + 1)], in0=ps[:, 0:512],
                    scalar1=float(-1.0 / np.sqrt(32.0)), scalar2=None,
                    op0=ALU.mult, op1=ALU.min, accum_out=mcol[:, c2:c2 + 1]),
                    reads=(f"ps{c2 % 2}",), writes=("f1simb2", "f1mcol"),
                    name=f"f1sm{h}_{qt}_{c2}")
            em.add("vector", lambda e: e.tensor_reduce(
                out=mcol[:, 15:16], in_=mcol[:, 0:8], axis=AX.X, op=ALU.min),
                reads=("f1mcol",), writes=("f1mcol",), name=f"f1mm{h}_{qt}")
            for c2 in range(8):
                em.add("scalar", lambda e, c2=c2: e.activation(
                    out=pbuf2[:, 512 * c2:512 * (c2 + 1)],
                    in_=simb2[:, 512 * c2:512 * (c2 + 1)], func=AF.Exp,
                    scale=-1.0, bias=mcol[:, 15:16],
                    accum_out=scol[:, c2:c2 + 1]),
                    reads=("f1simb2", "f1mcol"), writes=("f1p", "f1scol"),
                    name=f"f1ex{h}_{qt}_{c2}")
            em.add("vector", lambda e: e.tensor_reduce(
                out=scol[:, 15:16], in_=scol[:, 0:8], axis=AX.X, op=ALU.add),
                reads=("f1scol",), writes=("f1scol",), name=f"f1sr{h}_{qt}")
            em.add("vector", lambda e: e.reciprocal(
                out=scol[:, 14:15], in_=scol[:, 15:16]),
                reads=("f1scol",), writes=("f1scol",), name=f"f1rp{h}_{qt}")
            em.add("vector", lambda e: e.tensor_scalar_mul(
                pbuf2[:, 0:4096], pbuf2[:, 0:4096], scol[:, 14:15]),
                reads=("f1p", "f1scol"), writes=("f1p",), name=f"f1pc{h}_{qt}")
            for tb in range(4):
                bank = cx.psT[tb % 2]
                for t in range(8 * tb, 8 * tb + 8):
                    em.add("tensor", lambda e, t=t, tb=tb, bank=bank:
                           e.transpose(bank[:, 128 * (t - 8 * tb):
                                            128 * (t - 8 * tb + 1)],
                                       pbuf2[:, 128 * t:128 * (t + 1)],
                                       cx.ident[:]),
                           reads=("f1p", "ident"), writes=(f"psT{tb % 2}",),
                           name=f"f1pt{h}_{qt}_{t}")
                em.add("vector", lambda e, tb=tb, bank=bank: e.tensor_copy(
                    out=pT2[:, 1024 * tb:1024 * (tb + 1)], in_=bank[:, 0:1024]),
                    reads=(f"psT{tb % 2}",), writes=("f1pT2",),
                    name=f"f1pe{h}_{qt}_{tb}")
            for t in range(32):
                em.add("tensor", lambda e, t=t: e.matmul(
                    cx.psum[2][0:32, 0:128], vt2[:, 32 * t:32 * (t + 1)],
                    pT2[:, 128 * t:128 * (t + 1)],
                    start=(t == 0), stop=(t == 31)),
                    reads=("l1vt2", "f1pT2"), writes=("ps2",),
                    name=f"f1av{h}_{qt}_{t}")
            em.add("vector", lambda e: e.tensor_copy(
                out=est[:, 0:128], in_=cx.psum[2][0:32, 0:128]),
                reads=("ps2",), writes=("l1est",), name=f"f1ae{h}_{qt}")
            em.dma("gpsimd", lambda e, h=h, qt=qt: e.dma_start(
                out=aof[32 * (h % 4):32 * (h % 4) + 32,
                        HALF * (h // 4) + 128 * qt:
                        HALF * (h // 4) + 128 * (qt + 1)],
                in_=est[:, 0:128]),
                reads=("l1est",), writes=("f1aof",), name=f"f1as{h}_{qt}")
    em.fence()

    def ev_mla(f):
        for c in range(4):
            em.add("vector", lambda e, f=f, c=c: e.tensor_scalar(
                out=mla[:, HALF * f + 512 * c:HALF * f + 512 * (c + 1)],
                in0=cx.psum[c][:, 0:512],
                scalar1=bcols[:, 12 + f:12 + f + 1], scalar2=None,
                op0=ALU.add),
                reads=(f"ps{c}", "l1bc"), writes=("l1mla",),
                name=f"l1mla{f}_{c}")

    proj_p2(cx, dr["l1f_ow"], aof, "f1aof", (wb0, wb1),
            ("l1wb0", "l1wb1"), NF, ev_mla, "l1ow", nkt=4, wwidth=512)

    if DEBUG_L1:
        dump_fm(cx, "dbg_mla", mla, NF, "l1mla")
        dump_fm(cx, "dbg_aof", aof, 4, "f1aof")
    em.dma("sync", lambda e: e.dma_start(out=wb0[:], in_=dr["l1f_rw"][0]),
           writes=("l1wb0",), name="l1rw")
    pr = cx.stat[:, 56:96]
    for i in range(NT):
        for k in range(NF):
            em.add("tensor", lambda e, i=i, k=k: e.matmul(
                cx.psB[:, 0:128],
                mla[:, HALF * k + 128 * i:HALF * k + 128 * (i + 1)],
                wb0[:, 128 * k:128 * (k + 1)],
                start=(k == 0), stop=(k == NF - 1)),
                reads=("l1wb0", "l1mla"), writes=("psB",), name=f"l1ro{i}_{k}")
        P0, EQ, CU = pr[:, 0:8], pr[:, 8:16], pr[:, 16:24]
        F1, F2 = pr[:, 24:32], pr[:, 32:40]
        sc_ = lambda j: pr[:, 40 - 41 + 41 + j - 1:40 + j] if False else None
        m1 = cx.stat[:, 48:49]; m2 = cx.stat[:, 49:50]
        mx = cx.stat[:, 50:51]; nmx = cx.stat[:, 51:52]
        sm = cx.stat[:, 52:53]; rs = cx.stat[:, 53:54]
        dn = cx.stat[:, 54:55]
        em.add("vector", lambda e: e.tensor_tensor(
            out=P0, in0=cx.psB[:, 0:8], in1=bcols[:, 24:32], op=ALU.add),
            reads=("psB", "l1bc"), writes=("l1pr",), name=f"l1rl{i}")
        em.add("vector", lambda e: e.tensor_reduce(
            out=mx, in_=P0, axis=AX.X, op=ALU.max),
            reads=("l1pr",), writes=("l1pr",), name=f"l1rx{i}")
        em.add("vector", lambda e: e.tensor_scalar_mul(nmx, mx, -1.0),
               reads=("l1pr",), writes=("l1pr",), name=f"l1rn{i}")
        em.add("scalar", lambda e: e.activation(
            out=P0, in_=P0, func=AF.Exp, scale=1.0, bias=nmx,
            accum_out=sm), reads=("l1pr",), writes=("l1pr",), name=f"l1rE{i}")
        em.add("vector", lambda e: e.reciprocal(out=rs, in_=sm),
               reads=("l1pr",), writes=("l1pr",), name=f"l1rr{i}")
        em.add("vector", lambda e: e.tensor_scalar_mul(P0, P0, rs),
               reads=("l1pr",), writes=("l1pr",), name=f"l1rp{i}")
        em.add("vector", lambda e: e.tensor_reduce(
            out=m1, in_=P0, axis=AX.X, op=ALU.max),
            reads=("l1pr",), writes=("l1pr",), name=f"l1t1{i}")
        em.add("vector", lambda e: e.tensor_scalar(
            out=EQ, in0=P0, scalar1=m1, scalar2=None, op0=ALU.is_equal),
            reads=("l1pr",), writes=("l1pr",), name=f"l1e1{i}")
        em.add("vector", lambda e: e.tensor_tensor_scan(
            out=CU, data0=cx.gmpat[:, 0:8], data1=EQ,
            initial=cx.gmzero[:, 0:1], op0=ALU.mult, op1=ALU.add),
            reads=("l1pr", "gmpat", "gmzero"), writes=("l1pr",),
            name=f"l1c1{i}")
        em.add("vector", lambda e: e.tensor_scalar(
            out=CU, in0=CU, scalar1=1.0, scalar2=None, op0=ALU.is_equal),
            reads=("l1pr",), writes=("l1pr",), name=f"l1f1a{i}")
        em.add("vector", lambda e: e.tensor_mul(F1, EQ, CU),
               reads=("l1pr",), writes=("l1pr",), name=f"l1f1{i}")
        em.add("vector", lambda e: e.scalar_tensor_tensor(
            out=EQ, in0=F1, scalar=-2.0, in1=P0, op0=ALU.mult, op1=ALU.add),
            reads=("l1pr",), writes=("l1pr",), name=f"l1x{i}")
        em.add("vector", lambda e: e.tensor_reduce(
            out=m2, in_=EQ, axis=AX.X, op=ALU.max),
            reads=("l1pr",), writes=("l1pr",), name=f"l1t2{i}")
        em.add("vector", lambda e: e.tensor_scalar(
            out=F2, in0=EQ, scalar1=m2, scalar2=None, op0=ALU.is_equal),
            reads=("l1pr",), writes=("l1pr",), name=f"l1e2{i}")
        em.add("vector", lambda e: e.tensor_tensor_scan(
            out=CU, data0=cx.gmpat[:, 0:8], data1=F2,
            initial=cx.gmzero[:, 0:1], op0=ALU.mult, op1=ALU.add),
            reads=("l1pr", "gmpat", "gmzero"), writes=("l1pr",),
            name=f"l1c2{i}")
        em.add("vector", lambda e: e.tensor_scalar(
            out=CU, in0=CU, scalar1=1.0, scalar2=None, op0=ALU.is_equal),
            reads=("l1pr",), writes=("l1pr",), name=f"l1f2a{i}")
        em.add("vector", lambda e: e.tensor_mul(F2, F2, CU),
               reads=("l1pr",), writes=("l1pr",), name=f"l1f2{i}")
        em.add("vector", lambda e: e.tensor_add(dn, m1, m2),
               reads=("l1pr",), writes=("l1pr",), name=f"l1dn{i}")
        em.add("vector", lambda e: e.tensor_scalar_add(dn, dn, 1e-8),
               reads=("l1pr",), writes=("l1pr",), name=f"l1dne{i}")
        em.add("vector", lambda e: e.reciprocal(out=rs, in_=dn),
               reads=("l1pr",), writes=("l1pr",), name=f"l1dr{i}")
        em.add("vector", lambda e: e.tensor_scalar(
            out=F1, in0=F1, scalar1=m1, scalar2=rs, op0=ALU.mult,
            op1=ALU.mult), reads=("l1pr",), writes=("l1pr",),
            name=f"l1w1{i}")
        em.add("vector", lambda e: e.tensor_scalar(
            out=F2, in0=F2, scalar1=m2, scalar2=rs, op0=ALU.mult,
            op1=ALU.mult), reads=("l1pr",), writes=("l1pr",),
            name=f"l1w2{i}")
        em.add("vector", lambda e, i=i: e.tensor_tensor(
            out=combs[:, 8 * i:8 * (i + 1)], in0=F1, in1=F2, op=ALU.add),
            reads=("l1pr",), writes=("l1combs",), name=f"l1cb{i}")
    em.fence()

    em.fence()
    h1r = qf
    w2res = cx.xn_tm
    b1c = bcols[:, 32:48]
    for e_i in range(8):
        em.add("vector", lambda e: e.memset(lnst2, 0.0), writes=("xbufc",),
               name=f"l1b1z{e_i}")
        em.dma("sync", lambda e, e_i=e_i: e.dma_start(
            out=cx.xbuf[0:16, 0:128], in_=dr["l1f_b1"][e_i]),
            writes=("xbufc",), name=f"l1b1l{e_i}")
        em.add("tensor", lambda e: e.transpose(
            cx.psB[:, 0:128], lnst2, ident32x),
            reads=("xbufc",), writes=("psB",), name=f"l1b1t{e_i}")
        em.add("vector", lambda e: e.tensor_copy(
            out=b1c, in_=cx.psB[:, 0:16]),
            reads=("psB",), writes=("l1bc",), name=f"l1b1e{e_i}")
        for kt in range(16):
            em.dma("scalar", lambda e, e_i=e_i, kt=kt: e.dma_start(
                out=w2res[:, 1024 * kt:1024 * (kt + 1)],
                in_=dr["l1f_w2"][e_i, kt]),
                reads=("l1f_w2",), writes=("l1w2r",), name=f"l1w2l{e_i}_{kt}")
        for r in range(4):
            t0 = 512 * r
            for f in range(16):
                wb = (wb0, wb1)[f % 2]
                wk = f"l1wb{f % 2}"
                em.dma("sync", lambda e, e_i=e_i, f=f, wb=wb: e.dma_start(
                    out=wb[:], in_=dr["l1f_w1"][e_i, f]),
                    reads=("l1f_w1",), writes=(wk,),
                    name=f"l1h1w{e_i}_{r}_{f}")
                ps = cx.psum[f % 2]
                for k in range(NF):
                    em.add("tensor", lambda e, f=f, k=k, wb=wb, t0=t0, ps=ps:
                           e.matmul(ps[:, 0:512],
                                    wb[:, 128 * k:128 * (k + 1)],
                                    mla[:, HALF * k + t0:HALF * k + t0 + 512],
                                    start=(k == 0), stop=(k == NF - 1)),
                           reads=(wk, "l1mla"), writes=(f"ps{f % 2}",),
                           name=f"l1h1m{e_i}_{r}_{f}_{k}")
                em.add("scalar", lambda e, f=f, ps=ps: e.activation(
                    out=h1r[:, 512 * f:512 * (f + 1)], in_=ps[:, 0:512],
                    func=AF.Gelu, bias=b1c[:, f:f + 1], scale=1.0),
                    reads=(f"ps{f % 2}", "l1bc"), writes=("l1h1",),
                    name=f"l1h1g{e_i}_{r}_{f}")
            for tt in range(4):
                gtile = 4 * r + tt
                pd0 = cx.psum[2 * (tt % 2)]
                pd1 = cx.psum[2 * (tt % 2) + 1]
                for kt in range(16):
                    for c, pd in ((0, pd0), (1, pd1)):
                        em.add("tensor",
                               lambda e, kt=kt, tt=tt, c=c, pd=pd: e.matmul(
                                   pd[:, 0:512],
                                   h1r[:, 512 * kt + 128 * tt:
                                       512 * kt + 128 * (tt + 1)],
                                   w2res[:, 1024 * kt + 512 * c:
                                         1024 * kt + 512 * (c + 1)],
                                   start=(kt == 0), stop=(kt == 15)),
                               reads=("l1w2r", "l1h1"),
                               writes=(f"ps{2 * (tt % 2) + c}",),
                               name=f"l1h2m{e_i}_{r}_{kt}_{tt}_{c}")
                for c, pd in ((0, pd0), (1, pd1)):
                    em.add("vector",
                           lambda e, gtile=gtile, c=c, pd=pd, e_i=e_i, tt=tt:
                           e.scalar_tensor_tensor(
                               out=moe[:, 1024 * gtile + 512 * c:
                                       1024 * gtile + 512 * (c + 1)],
                               in0=pd[:, 0:512],
                               scalar=combs[:, 8 * gtile + e_i:
                                            8 * gtile + e_i + 1],
                               in1=moe[:, 1024 * gtile + 512 * c:
                                       1024 * gtile + 512 * (c + 1)],
                               op0=ALU.mult, op1=ALU.add),
                           reads=(f"ps{2 * (tt % 2) + c}", "l1combs",
                                  "l1moe"),
                           writes=("l1moe",),
                           name=f"l1me{e_i}_{r}_{tt}_{c}")
    em.fence()
    em.dma("gpsimd", lambda e: e.dma_start(
        out=b2all[:, :], in_=dr["l1f_b2"].rearrange("e h p -> e (h p)")),
        writes=("l1b2a",), name="l1b2l")
    for i in range(NT):
        bank = cx.psT[i % 2]
        em.add("vector", lambda e, i=i: e.tensor_copy(
            out=stg[:, 0:8], in_=combs[:, 8 * i:8 * (i + 1)]),
            reads=("l1combs",), writes=("l1stg",), name=f"l1cbc{i}")
        em.add("tensor", lambda e, i=i, bank=bank: e.transpose(
            bank[0:8, 0:128], stg[:, 0:8], cx.ident[:]),
            reads=("l1stg", "ident"), writes=(f"psT{i % 2}",),
            name=f"l1cbt{i}")
        em.add("vector", lambda e, i=i, bank=bank: e.tensor_copy(
            out=est[0:8, 0:128], in_=bank[0:8, 0:128]),
            reads=(f"psT{i % 2}",), writes=("l1est",), name=f"l1cbe{i}")
        for c in range(2):
            em.add("tensor", lambda e, i=i, c=c: e.matmul(
                cx.psB[:, 0:512], est[0:8, 0:128],
                b2all[:, 512 * c:512 * (c + 1)], start=True, stop=True),
                reads=("l1est", "l1b2a"), writes=("psB",),
                name=f"l1b2m{i}_{c}")
            em.add("vector", lambda e, i=i, c=c: e.tensor_add(
                moe[:, 1024 * i + 512 * c:1024 * i + 512 * (c + 1)],
                moe[:, 1024 * i + 512 * c:1024 * i + 512 * (c + 1)],
                cx.psB[:, 0:512]),
                reads=("psB", "l1moe"), writes=("l1moe",),
                name=f"l1b2a{i}_{c}")
    em.fence()
    if DEBUG_L1:
        dump_fm(cx, "dbg_moe", moe, NT, "l1moe", width=1024)
        dump_fm(cx, "dbg_combs", combs, 1, "l1combs", width=128)
    xn_tm2 = cx.xn_tm
    transpose_fm_to_tm(cx, cx.xn_fm, xn_tm2, "xn_fm", "l1xnt", NT, NF,
                       tag="l1xr")
    g_tm = mla
    for i in range(NT):
        sl = slice(1024 * i, 1024 * (i + 1))
        em.add("vector", lambda e, sl=sl: e.scalar_tensor_tensor(
            out=stg[:, 0:1024], in0=moe[:, sl], scalar=0.3,
            in1=xn_tm2[:, sl], op0=ALU.mult, op1=ALU.add),
            reads=("l1moe", "l1xnt"), writes=("l1stg",), name=f"l1gm{i}")
        em.add("scalar", lambda e, sl=sl: e.activation(
            out=g_tm[:, sl], in_=stg[:, 0:1024], func=AF.Silu),
            reads=("l1stg",), writes=("l1g",), name=f"l1gs{i}")
    cx.g_tm, cx.g_key = g_tm, "l1g"
    em.fence()



DEBUG_L3 = False
DEBUG_L1 = False



def prep_l0_weights(inputs, h):
    w = {}
    w["l0_gamma"] = np.ascontiguousarray(
        np.asarray(inputs["L0_attn_gamma_h"]).reshape(8, 128))
    w["l0_lng"] = np.ascontiguousarray(
        np.asarray(inputs["L0_attn_ln_g"]).reshape(8, 128))
    w["l0_lnb"] = np.ascontiguousarray(
        np.asarray(inputs["L0_attn_ln_b"]).reshape(8, 128))
    w["l0_wqkv"] = fmpack(np.asarray(inputs["L0_attn_wqkv_l"]))
    w["l0_wol"] = slabpack(np.asarray(inputs["L0_attn_wo_l"]))
    w["l0_wqh"] = fmpack(np.asarray(inputs["L0_attn_wq_h"]))
    w["l0_wkvh"] = fmpack(np.asarray(inputs["L0_attn_wkv_h"]))
    w["l0_woh"] = slabpack(np.asarray(inputs["L0_attn_wo_h"]))
    w["l0_rt"] = np.stack([np.asarray(inputs["L0_attn_rt_q"]).reshape(8, 128),
                           np.asarray(inputs["L0_attn_rt_kv"]).reshape(8, 128)])
    w["l0f_wg"] = slabpack_f32(np.asarray(inputs["L0_ffn_w_gate"]))
    w["l0f_wu"] = slabpack_f32(np.asarray(inputs["L0_ffn_w_up"]))
    w["l0f_wd"] = slabpack_f32(np.asarray(inputs["L0_ffn_w_down"]))
    w["l0_flag8"] = np.full((1, 8), float(h), np.float32)
    hi = np.zeros(128, np.uint32)
    base = 2048 * h
    for p in range(64):
        hi[p] = (base - 64 + p) % 4096
        hi[64 + p] = (base + 2048 + p) % 4096
    w["l0_hidx"] = np.ascontiguousarray(hi.reshape(128, 1))
    return w


def declare_l0_dram(nc):
    d = {}
    for nm, sh, dt in (("l0_gamma", [8, 128], F32), ("l0_lng", [8, 128], F32),
                       ("l0_lnb", [8, 128], F32),
                       ("l0_wqkv", [12, 128, 1024], BF16),
                       ("l0_wol", [4, 128, 1024], BF16),
                       ("l0_wqh", [4, 128, 1024], BF16),
                       ("l0_wkvh", [8, 128, 1024], BF16),
                       ("l0_woh", [4, 128, 1024], BF16),
                       ("l0_rt", [2, 8, 128], F32),
                       ("l0f_wg", [22, 128, 1024], F32),
                       ("l0f_wu", [22, 128, 1024], F32),
                       ("l0f_wd", [8, 128, 2816], F32),
                       ("l0_flag8", [1, 8], F32),
                       ("l0_hidx", [128, 1], U32)):
        d[nm] = nc.dram_tensor(nm, sh, dt, kind="ExternalInput").ap()
    d["l0_xn_in"] = nc.dram_tensor("l0_xn_in", [2048, 1024], BF16).ap()
    d["l0_xn_out"] = nc.dram_tensor("l0_xn_out", [4096, 1024], BF16).ap()
    d["l0_sc_in"] = nc.dram_tensor("l0_sc_in", [128, 32], F32).ap()
    d["l0_sc_out"] = nc.dram_tensor("l0_sc_out", [256, 32], F32).ap()
    d["l0_listq"] = nc.dram_tensor("l0_listq", [4096, 1], U32).ap()
    d["l0_listkv"] = nc.dram_tensor("l0_listkv", [4096, 1], U32).ap()
    d["l0_hqT"] = nc.dram_tensor("l0_hqT", [22, 128, 2048], BF16).ap()
    d["l0_ho_in"] = nc.dram_tensor("l0_ho_in", [512, 1024], BF16).ap()
    d["l0_ho_out"] = nc.dram_tensor("l0_ho_out", [1024, 1024], BF16).ap()
    return d


def emit_fp4_quant(cx, src_ap, dst_ap, rkeys, wkeys, width, scr, tag,
                   eng="vector"):
    em = cx.em
    nb = width // 16
    K = (scr["key"],)
    mx, sc, rsc = scr["mx"], scr["sc"], scr["rsc"]
    v, q, b, t = scr["v"], scr["q"], scr["b"], scr["t"]
    src3 = src_ap.rearrange("p (n s) -> p n s", s=16)
    E = eng
    em.add(E, lambda e: e.tensor_reduce(
        out=mx[:, 0:nb], in_=src3, axis=AX.X, op=ALU.max,
        apply_absolute_value=True),
        reads=rkeys, writes=K, name=f"{tag}mx")
    em.add(E, lambda e: e.tensor_scalar(
        out=sc[:, 0:nb], in0=mx[:, 0:nb], scalar1=1.0 / 6.0, scalar2=1e-12,
        op0=ALU.mult, op1=ALU.max),
        reads=K, writes=K, name=f"{tag}sc")
    em.add("vector", lambda e: e.reciprocal(out=rsc[:, 0:nb], in_=sc[:, 0:nb]),
           reads=K, writes=K, name=f"{tag}rs")
    em.add(E, lambda e: e.tensor_tensor(
        out=v[:, 0:width].rearrange("p (n s) -> p n s", s=16), in0=src3,
        in1=rsc[:, 0:nb].unsqueeze(-1).to_broadcast([128, nb, 16]),
        op=ALU.mult),
        reads=rkeys + K, writes=K, name=f"{tag}v")
    em.add(E, lambda e: e.tensor_scalar_mul(q[:, 0:width], v[:, 0:width], 2.0),
           reads=K, writes=K, name=f"{tag}a1")
    em.add(E, lambda e: e.tensor_copy(out=q[:, 0:width].bitcast(I32),
                                      in_=q[:, 0:width]),
           reads=K, writes=K, name=f"{tag}a2")
    em.add(E, lambda e: e.tensor_scalar(
        out=q[:, 0:width], in0=q[:, 0:width].bitcast(I32), scalar1=0.5,
        scalar2=None, op0=ALU.mult),
        reads=K, writes=K, name=f"{tag}a3")
    em.add(E, lambda e: e.tensor_copy(out=b[:, 0:width].bitcast(I32),
                                      in_=v[:, 0:width]),
           reads=K, writes=K, name=f"{tag}b1")
    em.add(E, lambda e: e.tensor_scalar(
        out=b[:, 0:width], in0=b[:, 0:width].bitcast(I32), scalar1=-4.0,
        scalar2=4.0, op0=ALU.max, op1=ALU.min),
        reads=K, writes=K, name=f"{tag}b2")
    em.add(E, lambda e: e.tensor_scalar(
        out=t[:, 0:width], in0=v[:, 0:width], scalar1=5.0, scalar2=2.0,
        op0=ALU.is_gt, op1=ALU.mult),
        reads=K, writes=K, name=f"{tag}b3")
    em.add(E, lambda e: e.tensor_add(b[:, 0:width], b[:, 0:width],
                                     t[:, 0:width]),
           reads=K, writes=K, name=f"{tag}b4")
    em.add(E, lambda e: e.tensor_scalar(
        out=t[:, 0:width], in0=v[:, 0:width], scalar1=-5.0, scalar2=-2.0,
        op0=ALU.is_lt, op1=ALU.mult),
        reads=K, writes=K, name=f"{tag}b5")
    em.add(E, lambda e: e.tensor_add(b[:, 0:width], b[:, 0:width],
                                     t[:, 0:width]),
           reads=K, writes=K, name=f"{tag}b6")
    em.add(E, lambda e: e.scalar_tensor_tensor(
        out=t[:, 0:width], in0=v[:, 0:width], scalar=-1.0, in1=v[:, 0:width],
        op0=ALU.mult, op1=ALU.max),
        reads=K, writes=K, name=f"{tag}s1")
    em.add(E, lambda e: e.tensor_scalar(
        out=t[:, 0:width], in0=t[:, 0:width], scalar1=2.0, scalar2=None,
        op0=ALU.is_le),
        reads=K, writes=K, name=f"{tag}s2")
    em.add(E, lambda e: e.tensor_sub(q[:, 0:width], q[:, 0:width],
                                     b[:, 0:width]),
           reads=K, writes=K, name=f"{tag}s3")
    em.add(E, lambda e: e.tensor_mul(q[:, 0:width], q[:, 0:width],
                                     t[:, 0:width]),
           reads=K, writes=K, name=f"{tag}s4")
    em.add(E, lambda e: e.tensor_add(q[:, 0:width], q[:, 0:width],
                                     b[:, 0:width]),
           reads=K, writes=K, name=f"{tag}s5")
    em.add(E, lambda e: e.tensor_tensor(
        out=dst_ap.rearrange("p (n s) -> p n s", s=16),
        in0=q[:, 0:width].rearrange("p (n s) -> p n s", s=16),
        in1=sc[:, 0:nb].unsqueeze(-1).to_broadcast([128, nb, 16]),
        op=ALU.mult),
        reads=K, writes=wkeys, name=f"{tag}o")


def emit_fp4_quant_chunked(cx, src_ap, dst_ap, rkeys, wkeys, width, scr, tag,
                           chunk=512, eng="vector"):
    c0 = 0
    j = 0
    while c0 < width:
        w = min(chunk, width - c0)
        emit_fp4_quant(cx, src_ap[:, c0:c0 + w], dst_ap[:, c0:c0 + w],
                       rkeys, wkeys, w, scr, f"{tag}_{j}", eng=eng)
        c0 += w
        j += 1


def build_l0(cx, ctx, dr):
    from concourse.bass import IndirectOffsetOnAxis
    nc, em = cx.nc, cx.em
    sb = lambda n, s, d: ctx.enter_context(nc.sbuf_tensor(n, s, d))
    SC = 0.125
    KW = 2304

    s_tm = sb("l0_s", [128, NT * 1024], BF16)
    xqt = sb("l0_xqt", [128, NT * 1024], BF16)
    B1 = sb("l0_B1", [128, NF * HALF], BF16)
    B2 = sb("l0_B2", [128, NF * HALF], BF16)
    bcols = sb("l0_bcols", [128, 48], F32)
    hidx_s = sb("l0_hidxs", [128, 1], U32)
    flag_s = sb("l0_flag", [1, 8], BF16)
    A1 = cx.xn_tm
    A2 = cx.xn_fm
    xb = cx.xbuf

    scfull = xb[:, 0:64]
    vcat = xb[:, 128:160]
    gidxf = xb[:, 192:224]
    selA = xb[:, 256:288]
    self_ = xb[:, 320:352]
    tie = xb[:, 384:416]
    posf = xb[:, 448:480]
    posu = xb[:, 512:544].bitcast(I32)
    work = xb[:, 576:608]
    gidu = xb[:, 640:672].bitcast(I32)
    SCL = xb[:, 704:768]
    identf = xb[:, 768:896]
    kvlu = xb[:, 896:904].bitcast(I32)
    qlu = xb[:, 904:912].bitcast(I32)
    qlf = xb[:, 912:920]
    qgf = xb[:, 920:924]
    qgl = xb[:, 924:928]
    qgu = xb[:, 928:932].bitcast(I32)
    qgl8 = xb[:, 932:940]
    crow = xb[:, 960:1344]
    tcol1 = xb[:, 1364:1365]
    tcol2 = xb[:, 1365:1366]
    pcol = xb[:, 1366:1367]
    mc_s = [xb[:, 1344 + j:1345 + j] for j in range(4)]
    sc_s = [xb[:, 1348 + j:1349 + j] for j in range(4)]
    rc_s = [xb[:, 1352 + j:1353 + j] for j in range(4)]
    hmc = [xb[:, 1356:1359], xb[:, 1368:1371]]
    hsc = [xb[:, 1359:1363], xb[:, 1371:1375]]
    ptb = [xb[:, 1400:1592].bitcast(BF16), xb[:, 1592:1784].bitcast(BF16),
           xb[:, 1784:1976].bitcast(BF16), None]
    selT = xb[0:32, 960:1088]
    Wsc = xb[0:32, 1088:1216]
    scr3 = xb[0:32, 1216:1344]
    scr4 = xb[0:32, 1472:1600]
    iotap = xb[0:32, 1600:1728]
    zcol = xb[0:32, 1728:1729]
    coloff = xb[0:32, 1729:1730]
    c128c = xb[0:32, 1730:1731]
    pcc = xb[0:32, 1731:1732]
    LTf = xb[0:32, 1760:1792]
    onesrow = xb[0:1, 1912:2040]
    onescol = xb[:, 2040:2041]
    f2048 = bcols[:, 45:46]
    fcol = bcols[:, 47:48]
    finv = bcols[:, 46:47]

    def scol(i):
        return SCL[:, i:i + 1]

    def dump_sb(name, ap, rkeys, shape=None):
        if not DEBUG_L0:
            return
        dt_ = ap.tensor.dtype if hasattr(ap, "tensor") else BF16
        dd = nc.dram_tensor(name, [128, ap.shape[-1]], dt_,
                            kind="ExternalOutput").ap()
        em.out_dma("sync", lambda e: e.dma_start(out=dd, in_=ap),
                   reads=tuple(rkeys), name=f"dmp_{name}")
    LO, HI, MID, CNT, GG, GI, T1, T2, NHI, KEEP, JLO, JHI, CNTP, TOT = range(14)
    SM = ("l0sm",)

    em.dma("gpsimd", lambda e: e.dma_start(out=flag_s[:], in_=dr["l0_flag8"][:]),
           writes=("l0flag",), name="l0flagld")
    em.add("tensor", lambda e: e.matmul(
        cx.psB[:, 0:8], cx.ones1[:, :], flag_s[:, 0:8],
        start=True, stop=True),
        reads=("l0flag", "ones1"), writes=("psB",), name="l0fb")
    em.add("vector", lambda e: e.tensor_copy(out=fcol, in_=cx.psB[:, 0:1]),
           reads=("psB",), writes=("l0bc",), name="l0fc")
    em.add("vector", lambda e: e.tensor_scalar(
        out=finv, in0=fcol, scalar1=-1.0, scalar2=1.0, op0=ALU.mult,
        op1=ALU.add), reads=("l0bc",), writes=("l0bc",), name="l0fi")
    em.add("vector", lambda e: e.tensor_scalar(
        out=f2048, in0=fcol, scalar1=2048.0, scalar2=None, op0=ALU.mult),
        reads=("l0bc",), writes=("l0bc",), name="l0f2k")
    em.add("vector", lambda e: e.tensor_scalar(
        out=identf, in0=cx.ident_i[:], scalar1=0.0, scalar2=None,
        op0=ALU.is_equal), reads=("ident_i",), writes=SM, name="l0idf0")

    lnst2 = xb[:, 0:128]

    def load_cols(dram, n, col0, tag):
        em.add("vector", lambda e: e.memset(lnst2, 0.0), reads=SM, writes=SM,
               name=f"{tag}z")
        em.dma("sync", lambda e, dram=dram, n=n: e.dma_start(
            out=lnst2[0:n, 0:128], in_=dram), reads=SM, writes=SM,
            name=f"{tag}l")
        em.add("tensor", lambda e: e.transpose(
            cx.psB[:, 0:128], lnst2, identf),
            reads=SM, writes=("psB",), name=f"{tag}t")
        em.add("vector", lambda e, n=n, col0=col0: e.tensor_copy(
            out=bcols[:, col0:col0 + n], in_=cx.psB[:, 0:n]),
            reads=("psB",), writes=("l0bc",), name=f"{tag}e")

    load_cols(dr["l0_lng"][:], 8, 0, "l0lg")
    load_cols(dr["l0_lnb"][:], 8, 8, "l0lb")
    load_cols(dr["l0_gamma"][:], 8, 16, "l0gm")
    load_cols(dr["l0_rt"][0], 8, 24, "l0rq")
    load_cols(dr["l0_rt"][1], 8, 32, "l0rk")
    rtb = B2[:, 14464:14480]
    em.add("vector", lambda e: e.tensor_copy(out=rtb, in_=bcols[:, 24:40]),
           reads=("l0bc",), writes=("l0rtb",), name="l0rtb")

    xnr = dr["l0_xn_in"].rearrange("(i p) d -> i p d", p=128)
    for i in range(NT):
        em.dma("sync", lambda e, i=i: e.dma_start(
            out=xnr[i], in_=cx.xn_tm[:, 1024 * i:1024 * (i + 1)]),
            reads=("xn_tm",), writes=("l0xnin",), name=f"l0xs{i}")
    _cc_pair(em, "AllGather", dr["l0_xn_in"], dr["l0_xn_out"],
             ("l0xnin",), ("l0xnout",), "l0xncc")
    cc1_op = em.ops[-1]
    sc_own = B2[:, 15120:15184].bitcast(F32)
    for rt in range(2):
        for i in range(NT):
            for k in range(NF):
                em.add("tensor", lambda e, rt=rt, i=i, k=k: e.matmul(
                    cx.psB[:, 0:1],
                    cx.xn_fm[:, HALF * k + 128 * i:HALF * k + 128 * (i + 1)],
                    rtb[:, 8 * rt + k:8 * rt + k + 1],
                    start=(k == 0), stop=(k == NF - 1)),
                    reads=("xn_fm", "l0rtb"), writes=("psB",),
                    name=f"l0scm{rt}_{i}_{k}")
            em.add("vector", lambda e, rt=rt, i=i: e.tensor_copy(
                out=sc_own[:, 16 * rt + i:16 * rt + i + 1],
                in_=cx.psB[:, 0:1]),
                reads=("psB",), writes=("l0sco",), name=f"l0sce{rt}_{i}")
    em.dma("sync", lambda e: e.dma_start(out=dr["l0_sc_in"][:], in_=sc_own),
           reads=("l0sco",), writes=("l0scin",), name="l0scs")
    _cc_pair(em, "AllGather", dr["l0_sc_in"], dr["l0_sc_out"],
             ("l0scin",), ("l0scout",), "l0sccc")
    em.ops[-1].deps.add(cc1_op)

    st = cx.stat
    k16 = NT
    nmu = st[:, 5 * k16:6 * k16]
    rstd = st[:, 3 * k16:4 * k16]
    xr0 = cx.cur_x.rearrange("(i p) d -> i p d", p=128)
    xdep0 = (cx.x_rkey,) if getattr(cx, "x_rkey", None) else ()
    qsc = {
        "key": "l0xqS",
        "mx": cx.sgbuf[:, 0:64], "sc": cx.sgbuf[:, 64:128],
        "rsc": cx.sgbuf[:, 128:192],
        "v": s_tm[:, 0:2048].bitcast(F32),
        "q": s_tm[:, 2048:4096].bitcast(F32),
        "b": s_tm[:, 4096:6144].bitcast(F32),
        "t": s_tm[:, 6144:8192].bitcast(F32),
    }
    xnf32 = s_tm[:, 8192:10240].bitcast(F32)
    for i in range(NT):
        xbb = xb[:, 1024 * (i % 2):1024 * (i % 2 + 1)]
        bk = f"xbuf{i % 2}"
        em.dma("sync", lambda e, i=i, xbb=xbb: e.dma_start(
            out=xbb, in_=xr0[i]), reads=xdep0, writes=(bk,),
            name=f"l0q_ld{i}")
        em.add("vector", lambda e, xbb=xbb, i=i: e.tensor_scalar(
            out=xnf32, in0=xbb, scalar1=nmu[:, i:i + 1],
            scalar2=rstd[:, i:i + 1], op0=ALU.add, op1=ALU.mult),
            reads=(bk, "stat"), writes=("l0xqS",), name=f"l0q_n{i}")
        emit_fp4_quant_chunked(
            cx, xnf32, xqt[:, 1024 * i:1024 * (i + 1)],
            ("l0xqS",), ("l0xq",), 1024, qsc, f"l0xq{i}", chunk=1024)
    em.fence()

    layernorm_tm(cx, lambda i: cx.xn_tm[:, 1024 * i:1024 * (i + 1)],
                 lambda i: cx.xn_tm[:, 1024 * i:1024 * (i + 1)],
                 NT, "xn_tm", "xn_tm", tag="l0iln")
    em.fence()
    xl_fm = A2
    transpose_tm_to_fm(cx, cx.xn_tm, xl_fm, "xn_tm", "l0xlf", NT, NF,
                       tag="l0xtr")
    for f in range(NF):
        em.add("vector", lambda e, f=f: e.tensor_scalar(
            out=xl_fm[:, HALF * f:HALF * (f + 1)],
            in0=xl_fm[:, HALF * f:HALF * (f + 1)],
            scalar1=bcols[:, f:f + 1], scalar2=bcols[:, 8 + f:8 + f + 1],
            op0=ALU.mult, op1=ALU.add),
            reads=("l0xlf", "l0bc"), writes=("l0xlf",), name=f"l0af{f}")
    halo_xn = B2[:, 9216:10240]
    em.dma("sync", lambda e: e.dma_start(out=hidx_s[:], in_=dr["l0_hidx"]),
           writes=("l0hidx",), name="l0hixl")
    em.dma("gpsimd", lambda e: e.indirect_dma_start(
        out=halo_xn, out_offset=None, in_=dr["l0_xn_out"],
        in_offset=IndirectOffsetOnAxis(ap=hidx_s[:], axis=0)),
        reads=("l0xnout", "l0hidx"), writes=("l0halo",), name="l0hld")
    layernorm_tm(cx, lambda i: halo_xn, lambda i: halo_xn, 1,
                 "l0halo", "l0halo", tag="l0hln")
    halo_fm = B2[:, 10240:11264]
    for f in range(NF):
        bank = cx.psT[f % 2]
        em.add("tensor", lambda e, f=f, bank=bank: e.transpose(
            bank[:, 0:128], halo_xn[:, 128 * f:128 * (f + 1)], cx.ident[:]),
            reads=("l0halo", "ident"), writes=(f"psT{f % 2}",),
            name=f"l0ht{f}")
        em.add("vector", lambda e, f=f, bank=bank: e.tensor_scalar(
            out=halo_fm[:, 128 * f:128 * (f + 1)], in0=bank[:, 0:128],
            scalar1=bcols[:, f:f + 1], scalar2=bcols[:, 8 + f:8 + f + 1],
            op0=ALU.mult, op1=ALU.add),
            reads=(f"psT{f % 2}", "l0bc"), writes=("l0hfm",), name=f"l0he{f}")
    em.fence()

    q_l = B1[:, 0:4 * HALF]
    ao_fm = B1[:, 4 * HALF:8 * HALF]
    k_ext = B2[:, 0:9216]
    v_ext = A1
    wba = B2[:, 11264:12288]
    wbb = B2[:, 12288:13312]
    vhstage = B2[:, 14480:14608]
    vstg = B2[:, 14608:15120]
    for r in range(4):
        em.add("vector", lambda e, r=r: e.memset(
            k_ext[:, KW * r:KW * r + 64], 0.0), writes=("l0ke",),
            name=f"l0kz{r}")
        em.add("vector", lambda e, r=r: e.memset(
            k_ext[:, KW * r + 2240:KW * r + 2304], 0.0), writes=("l0ke",),
            name=f"l0kz2{r}")
    em.add("vector", lambda e: e.memset(v_ext[:, 0:512], 0.0),
           writes=("l0ve",), name="l0vz0")
    em.add("vector", lambda e: e.memset(v_ext[:, 512 * 17:512 * 18], 0.0),
           writes=("l0ve",), name="l0vz17")

    for f in range(12):
        wb = wba if f % 2 == 0 else wbb
        wk = "l0wbA" if f % 2 == 0 else "l0wbB"
        em.dma("sync", lambda e, f=f, wb=wb: e.dma_start(
            out=wb, in_=dr["l0_wqkv"][f]), writes=(wk,), name=f"l0w{f}")
        for c in range(4):
            for k in range(NF):
                em.add("tensor", lambda e, wb=wb, wk=wk, k=k, c=c: e.matmul(
                    cx.psum[c][:, 0:512], wb[:, 128 * k:128 * (k + 1)],
                    xl_fm[:, HALF * k + 512 * c:HALF * k + 512 * (c + 1)],
                    start=(k == 0), stop=(k == NF - 1)),
                    reads=(wk, "l0xlf"), writes=(f"ps{c}",),
                    name=f"l0pm{f}_{c}_{k}")
        if f >= 4:
            for k in range(NF):
                em.add("tensor", lambda e, wb=wb, wk=wk, k=k: e.matmul(
                    cx.psB[:, 0:128], wb[:, 128 * k:128 * (k + 1)],
                    halo_fm[:, 128 * k:128 * (k + 1)],
                    start=(k == 0), stop=(k == NF - 1)),
                    reads=(wk, "l0hfm"), writes=("psB",),
                    name=f"l0ph{f}_{k}")
        if f < 4:
            for c in range(4):
                em.add("vector", lambda e, f=f, c=c: e.tensor_copy(
                    out=q_l[:, HALF * f + 512 * c:HALF * f + 512 * (c + 1)],
                    in_=cx.psum[c][:, 0:512]),
                    reads=(f"ps{c}",), writes=("l0ql",), name=f"l0qe{f}_{c}")
        elif f < 8:
            r = f - 4
            for c in range(4):
                em.add("vector", lambda e, r=r, c=c: e.tensor_copy(
                    out=k_ext[:, KW * r + 128 + 512 * c:
                              KW * r + 128 + 512 * (c + 1)],
                    in_=cx.psum[c][:, 0:512]),
                    reads=(f"ps{c}",), writes=("l0ke",), name=f"l0kev{r}_{c}")
            em.add("vector", lambda e, r=r: e.tensor_copy(
                out=k_ext[:, KW * r + 64:KW * r + 128],
                in_=cx.psB[:, 0:64]),
                reads=("psB",), writes=("l0ke",), name=f"l0khl{r}")
            em.add("vector", lambda e, r=r: e.tensor_copy(
                out=k_ext[:, KW * r + 2176:KW * r + 2240],
                in_=cx.psB[:, 64:128]),
                reads=("psB",), writes=("l0ke",), name=f"l0khr{r}")
        else:
            r = f - 8
            for c in range(4):
                em.add("vector", lambda e, c=c: e.tensor_copy(
                    out=vstg, in_=cx.psum[c][:, 0:512]),
                    reads=(f"ps{c}",), writes=("l0vstg",), name=f"l0vs{r}_{c}")
                for u in range(4):
                    em.add("tensor", lambda e, u=u: e.transpose(
                        cx.psT[0][:, 128 * u:128 * (u + 1)],
                        vstg[:, 128 * u:128 * (u + 1)], cx.ident[:]),
                        reads=("l0vstg", "ident"), writes=("psT0",),
                        name=f"l0vt{r}_{c}_{u}")
                for u in range(4):
                    em.add("vector", lambda e, r=r, c=c, u=u: e.tensor_copy(
                        out=v_ext[:, 512 * (1 + 4 * c + u) + 128 * r:
                                  512 * (1 + 4 * c + u) + 128 * (r + 1)],
                        in_=cx.psT[0][:, 128 * u:128 * (u + 1)]),
                        reads=("psT0",), writes=("l0ve",),
                        name=f"l0vev{r}_{c}_{u}")
            em.add("vector", lambda e: e.tensor_copy(
                out=vhstage, in_=cx.psB[:, 0:128]),
                reads=("psB",), writes=("l0vhs",), name=f"l0vhs{r}")
            em.add("tensor", lambda e: e.transpose(
                cx.psT[1][:, 0:128], vhstage, cx.ident[:]),
                reads=("l0vhs", "ident"), writes=("psT1",), name=f"l0vht{r}")
            em.add("vector", lambda e, r=r: e.tensor_copy(
                out=v_ext[64:128, 128 * r:128 * (r + 1)],
                in_=cx.psT[1][0:64, 0:128]),
                reads=("psT1",), writes=("l0ve",), name=f"l0vhl{r}")
            em.add("vector", lambda e, r=r: e.tensor_copy(
                out=v_ext[0:64, 512 * 17 + 128 * r:512 * 17 + 128 * (r + 1)],
                in_=cx.psT[1][64:128, 0:128]),
                reads=("psT1",), writes=("l0ve",), name=f"l0vhr{r}")
    em.fence()

    masks = B2[:, 13312:14464]

    def maskap(m):
        return masks[:, 384 * m:384 * (m + 1)]
    em.add("gpsimd", lambda e: e.iota(
        crow.bitcast(I32), [[1, 384]], base=0, channel_multiplier=0),
        writes=("l0crw",), name="l0crw")
    em.add("gpsimd", lambda e: e.iota(
        pcol.bitcast(I32), [[0, 1]], base=0, channel_multiplier=1),
        writes=("l0tc",), name="l0pc")
    em.add("vector", lambda e: e.tensor_scalar(
        out=tcol1, in0=pcol.bitcast(I32), scalar1=64.0, scalar2=64.0,
        op0=ALU.is_ge, op1=ALU.mult),
        reads=("l0tc",), writes=("l0tc",), name="l0t1a")
    em.add("vector", lambda e: e.tensor_scalar(
        out=tcol1, in0=tcol1, scalar1=64.0, scalar2=None, op0=ALU.add),
        reads=("l0tc",), writes=("l0tc",), name="l0t1b")
    em.add("vector", lambda e: e.tensor_scalar(
        out=tcol2, in0=tcol1, scalar1=192.0, scalar2=None, op0=ALU.add),
        reads=("l0tc",), writes=("l0tc",), name="l0t2")
    em.add("vector", lambda e: e.tensor_scalar(
        out=maskap(0), in0=crow.bitcast(I32), scalar1=tcol1, scalar2=None,
        op0=ALU.is_ge), reads=("l0crw", "l0tc"), writes=("l0msk",),
        name="l0mA1")
    em.add("vector", lambda e: e.tensor_scalar(
        out=maskap(1), in0=crow.bitcast(I32), scalar1=tcol2, scalar2=None,
        op0=ALU.is_lt), reads=("l0crw", "l0tc"), writes=("l0msk",),
        name="l0mA2")
    em.add("vector", lambda e: e.tensor_mul(maskap(0), maskap(0), maskap(1)),
           reads=("l0msk",), writes=("l0msk",), name="l0mA3")
    em.add("vector", lambda e: e.tensor_copy(out=maskap(1), in_=maskap(0)),
           reads=("l0msk",), writes=("l0msk",), name="l0m0c")
    em.add("vector", lambda e: e.tensor_scalar(
        out=maskap(1)[0:64, 64:128], in0=maskap(1)[0:64, 64:128],
        scalar1=fcol[0:64], scalar2=None, op0=ALU.mult),
        reads=("l0msk", "l0bc"), writes=("l0msk",), name="l0m0f")
    em.add("vector", lambda e: e.tensor_copy(out=maskap(2), in_=maskap(0)),
           reads=("l0msk",), writes=("l0msk",), name="l0m15c")
    em.add("vector", lambda e: e.tensor_scalar(
        out=maskap(2)[64:128, 256:320], in0=maskap(2)[64:128, 256:320],
        scalar1=finv[64:128], scalar2=None, op0=ALU.mult),
        reads=("l0msk", "l0bc"), writes=("l0msk",), name="l0m15f")

    sgb16 = cx.sgbuf[:, :].bitcast(BF16)
    pstage = [cx.sq_junk[:, 0:384], cx.sq_junk[:, 384:768],
              sgb16[:, 0:384], sgb16[:, 384:768]]
    ptb[3] = B2[:, 15248:15632]
    for i in range(NT):
        mi = 1 if i == 0 else (2 if i == NT - 1 else 0)
        for hh in range(8):
            sidx = (i * 8 + hh) % 4
            ps = cx.psum[sidx]
            pk = f"ps{sidx}"
            stg = pstage[sidx]
            sk = f"l0pg{sidx}"
            ptbuf = ptb[sidx]
            ptk = f"l0pt{sidx}"
            rb = 64 * (hh % 2)
            pt4 = hh // 2
            mcol = mc_s[sidx]
            scc = sc_s[sidx]
            rcc = rc_s[sidx]
            mk = f"l0mc{sidx}"
            em.add("tensor", lambda e, ps=ps, rb=rb, pt4=pt4, i=i: e.matmul(
                ps[:, 0:384],
                q_l[rb:rb + 64, HALF * pt4 + 128 * i:HALF * pt4 + 128 * (i + 1)],
                k_ext[rb:rb + 64, KW * pt4 + 128 * i:KW * pt4 + 128 * i + 384],
                start=True, stop=True),
                reads=("l0ql", "l0ke"), writes=(pk,),
                name=f"l0sim{i}_{hh}")
            em.add("vector", lambda e, ps=ps, stg=stg, mcol=mcol:
                   e.tensor_scalar(
                out=stg, in0=ps[:, 0:384], scalar1=-SC, scalar2=3.0e38,
                op0=ALU.mult, op1=ALU.min, accum_out=mcol),
                reads=(pk,), writes=(sk, mk), name=f"l0ng{i}_{hh}")
            em.add("scalar", lambda e, stg=stg, mcol=mcol: e.activation(
                out=stg, in_=stg, func=AF.Exp, bias=mcol, scale=-1.0),
                reads=(sk, mk), writes=(sk,), name=f"l0ex{i}_{hh}")
            em.add("vector", lambda e, stg=stg, mi=mi: e.tensor_mul(
                stg, stg, maskap(mi)),
                reads=(sk, "l0msk"), writes=(sk,), name=f"l0pmm{i}_{hh}")
            em.add("vector", lambda e, stg=stg, scc=scc: e.tensor_reduce(
                out=scc, in_=stg, axis=AX.X, op=ALU.add),
                reads=(sk,), writes=(mk,), name=f"l0psm{i}_{hh}")
            em.add("vector", lambda e, scc=scc, rcc=rcc: e.reciprocal(
                out=rcc, in_=scc),
                reads=(mk,), writes=(mk,), name=f"l0prc{i}_{hh}")
            em.add("vector", lambda e, stg=stg, rcc=rcc: e.tensor_scalar(
                out=stg, in0=stg, scalar1=rcc, scalar2=None, op0=ALU.mult),
                reads=(sk, mk), writes=(sk,), name=f"l0pnm{i}_{hh}")
            bankk = f"psT{sidx % 2}"
            bank = cx.psT[sidx % 2]
            for j in range(3):
                em.add("tensor", lambda e, stg=stg, j=j, bank=bank:
                       e.transpose(
                    bank[:, 128 * j:128 * (j + 1)],
                    stg[:, 128 * j:128 * (j + 1)], cx.ident[:]),
                    reads=(sk, "ident"), writes=(bankk,),
                    name=f"l0ptT{i}_{hh}_{j}")
            em.add("vector", lambda e, bank=bank, ptbuf=ptbuf: e.tensor_copy(
                out=ptbuf, in_=bank[:, 0:384]),
                reads=(bankk,), writes=(ptk,), name=f"l0pev{i}_{hh}")
            avb = ps
            avk = pk
            for j in range(3):
                em.add("tensor", lambda e, j=j, i=i, hh=hh, ptbuf=ptbuf,
                       avb=avb, ptk=ptk: e.matmul(
                    avb[0:64, 384:512],
                    v_ext[:, 512 * (i + j) + 64 * hh:
                          512 * (i + j) + 64 * (hh + 1)],
                    ptbuf[:, 128 * j:128 * (j + 1)],
                    start=(j == 0), stop=(j == 2)),
                    reads=("l0ve", ptk), writes=(avk,),
                    name=f"l0av{i}_{hh}_{j}")
            em.add("vector", lambda e, avb=avb, rb=rb, pt4=pt4, i=i:
                   e.tensor_copy(
                out=ao_fm[rb:rb + 64, HALF * pt4 + 128 * i:
                          HALF * pt4 + 128 * (i + 1)],
                in_=avb[0:64, 384:512]),
                reads=(avk,), writes=("l0ao",), name=f"l0ae{i}_{hh}")
    wolb = B2[:, 9216:13312]
    for kt in range(4):
        em.dma("sync", lambda e, kt=kt: e.dma_start(
            out=wolb[:, 1024 * kt:1024 * (kt + 1)], in_=dr["l0_wol"][kt]),
            writes=("l0wol", "l0halo", "l0hfm", "l0wbA", "l0wbB"),
            name=f"l0wol{kt}")
    for i in range(NT):
        for half in range(2):
            ps = cx.psum[half]
            pk = f"ps{half}"
            for kt in range(4):
                em.add("tensor", lambda e, i=i, half=half, kt=kt, ps=ps:
                       e.matmul(
                    ps[:, 0:512],
                    ao_fm[:, HALF * kt + 128 * i:HALF * kt + 128 * (i + 1)],
                    wolb[:, 1024 * kt + 512 * half:
                         1024 * kt + 512 * (half + 1)],
                    start=(kt == 0), stop=(kt == 3)),
                    reads=("l0ao", "l0wol"), writes=(pk,),
                    name=f"l0wo{i}_{half}_{kt}")
            em.add("vector", lambda e, i=i, half=half, ps=ps: e.tensor_copy(
                out=s_tm[:, 1024 * i + 512 * half:1024 * i + 512 * (half + 1)],
                in_=ps[:, 0:512]),
                reads=(pk,), writes=("l0s",), name=f"l0se{i}_{half}")
    em.fence()
    dump_sb("dbg_slight", s_tm[:, 0:16384], ("l0s",))
    dump_sb("dbg_ao", ao_fm, ("l0ao",))
    dump_sb("dbg_ql", q_l, ("l0ql",))
    dump_sb("dbg_ke", k_ext, ("l0ke",))
    dump_sb("dbg_ve", v_ext[:, 0:9216], ("l0ve",))
    dump_sb("dbg_xlf", xl_fm[:, 0:16384], ("l0xlf",))

    em.dma("sync", lambda e: e.dma_start(
        out=scfull[:, 0:32], in_=dr["l0_sc_out"][0:128, :]),
        reads=("l0scout",), writes=SM, name="l0sf0")
    em.dma("sync", lambda e: e.dma_start(
        out=scfull[:, 32:64], in_=dr["l0_sc_out"][128:256, :]),
        reads=("l0scout",), writes=SM, name="l0sf1")
    em.add("vector", lambda e: e.tensor_scalar(
        out=identf, in0=cx.ident_i[:], scalar1=0.0, scalar2=None,
        op0=ALU.is_equal), reads=("ident_i",), writes=SM, name="l0idf")
    em.add("vector", lambda e: e.tensor_scalar(
        out=LTf, in0=cx.ident_i[0:32, 0:32], scalar1=0.0, scalar2=None,
        op0=ALU.is_gt), reads=("ident_i",), writes=SM, name="l0lt")
    em.add("vector", lambda e: e.memset(onesrow, 1.0), reads=SM, writes=SM,
           name="l0or")
    em.add("vector", lambda e: e.memset(onescol, 1.0), reads=SM, writes=SM,
           name="l0oc")
    em.add("gpsimd", lambda e: e.iota(
        gidxf[:, 0:16].bitcast(I32), [[128, 16]], base=0,
        channel_multiplier=1), reads=SM, writes=SM, name="l0gi0")
    em.add("gpsimd", lambda e: e.iota(
        gidxf[:, 16:32].bitcast(I32), [[128, 16]], base=2048,
        channel_multiplier=1), reads=SM, writes=SM, name="l0gi1")
    em.add("vector", lambda e: e.tensor_copy(
        out=gidxf, in_=gidxf.bitcast(I32)),
        reads=SM, writes=SM, name="l0gif")
    em.add("gpsimd", lambda e: e.iota(
        iotap.bitcast(I32), [[1, 128]], base=0, channel_multiplier=0),
        reads=SM, writes=SM, name="l0ip")
    em.add("vector", lambda e: e.tensor_copy(
        out=iotap, in_=iotap.bitcast(I32)),
        reads=SM, writes=SM, name="l0ipf")
    em.add("gpsimd", lambda e: e.iota(
        c128c.bitcast(I32), [[0, 1]], base=0, channel_multiplier=128),
        reads=SM, writes=SM, name="l0c128")
    em.add("vector", lambda e: e.tensor_copy(
        out=c128c, in_=c128c.bitcast(I32)),
        reads=SM, writes=SM, name="l0c128f")
    em.add("vector", lambda e: e.memset(zcol, 0.0), reads=SM, writes=SM,
           name="l0zc")
    jrow = A2.bitcast(F32)[:, 2048:3072]
    em.add("gpsimd", lambda e: e.iota(
        jrow.bitcast(I32), [[1, 1024]], base=0, channel_multiplier=0),
        writes=("l0jr",), name="l0jri")
    em.add("vector", lambda e: e.tensor_copy(
        out=jrow, in_=jrow.bitcast(I32)),
        reads=("l0jr",), writes=("l0jr",), name="l0jrf")

    def vop(fn, name):
        em.add("vector", fn, reads=SM, writes=SM, name=name)

    def pop(fn, name):
        em.add("vector", fn, reads=SM + ("psB",), writes=SM, name=name)

    def count_into(src_ap, dst_idx, tag):
        em.add("tensor", lambda e: e.matmul(
            cx.psB[0:1, 0:1], src_ap, onescol, start=True, stop=True),
            reads=SM, writes=("psB",), name=f"{tag}r")
        pop(lambda e: e.tensor_copy(
            out=scol(TOT)[0:1], in_=cx.psB[0:1, 0:1]), f"{tag}c")
        em.add("tensor", lambda e: e.matmul(
            cx.psB[:, 1:2], onesrow, scol(TOT)[0:1], start=True, stop=True),
            reads=SM, writes=("psB",), name=f"{tag}b")
        pop(lambda e: e.tensor_copy(
            out=scol(dst_idx), in_=cx.psB[:, 1:2]), f"{tag}e")

    def bisect_route(r, listu, tag):
        vop(lambda e: e.tensor_copy(
            out=vcat[:, 0:16], in_=scfull[:, 16 * r:16 * r + 16]),
            f"{tag}v0")
        vop(lambda e: e.tensor_copy(
            out=vcat[:, 16:32], in_=scfull[:, 32 + 16 * r:48 + 16 * r]),
            f"{tag}v1")
        vop(lambda e: e.memset(scol(LO), -16384.0), f"{tag}lo")
        vop(lambda e: e.memset(scol(HI), 16384.0), f"{tag}hi")
        for it in range(42):
            vop(lambda e: e.tensor_add(scol(MID), scol(LO), scol(HI)),
                f"{tag}m{it}")
            vop(lambda e: e.tensor_scalar_mul(scol(MID), scol(MID), 0.5),
                f"{tag}mh{it}")
            vop(lambda e: e.tensor_scalar(
                out=work, in0=vcat, scalar1=scol(MID), scalar2=0.0,
                op0=ALU.is_gt, op1=ALU.add, accum_out=scol(CNTP)),
                f"{tag}ct{it}")
            count_into(scol(CNTP), CNT, f"{tag}cc{it}")
            vop(lambda e: e.tensor_scalar(
                out=scol(GG), in0=scol(CNT), scalar1=1024.0, scalar2=None,
                op0=ALU.is_ge), f"{tag}g{it}")
            vop(lambda e: e.tensor_scalar(
                out=scol(GI), in0=scol(GG), scalar1=-1.0, scalar2=1.0,
                op0=ALU.mult, op1=ALU.add), f"{tag}gi{it}")
            vop(lambda e: e.tensor_sub(scol(T1), scol(MID), scol(LO)),
                f"{tag}t1{it}")
            vop(lambda e: e.tensor_mul(scol(T1), scol(T1), scol(GG)),
                f"{tag}t1m{it}")
            vop(lambda e: e.tensor_add(scol(LO), scol(LO), scol(T1)),
                f"{tag}lou{it}")
            vop(lambda e: e.tensor_sub(scol(T2), scol(MID), scol(HI)),
                f"{tag}t2{it}")
            vop(lambda e: e.tensor_mul(scol(T2), scol(T2), scol(GI)),
                f"{tag}t2m{it}")
            vop(lambda e: e.tensor_add(scol(HI), scol(HI), scol(T2)),
                f"{tag}hiu{it}")
        vop(lambda e: e.tensor_scalar(
            out=selA, in0=vcat, scalar1=scol(HI), scalar2=0.0,
            op0=ALU.is_gt, op1=ALU.add, accum_out=scol(CNTP)), f"{tag}sa")
        count_into(scol(CNTP), NHI, f"{tag}nh")
        vop(lambda e: e.tensor_scalar(
            out=scol(KEEP), in0=scol(NHI), scalar1=-1.0, scalar2=1024.0,
            op0=ALU.mult, op1=ALU.add), f"{tag}kp")
        vop(lambda e: e.tensor_scalar(
            out=self_, in0=vcat, scalar1=scol(LO), scalar2=None,
            op0=ALU.is_gt), f"{tag}sb")
        vop(lambda e: e.tensor_sub(tie, self_, selA), f"{tag}tie")
        vop(lambda e: e.memset(scol(JLO), 0.0), f"{tag}jl")
        vop(lambda e: e.memset(scol(JHI), 4097.0), f"{tag}jh")
        for it in range(14):
            vop(lambda e: e.tensor_add(scol(MID), scol(JLO), scol(JHI)),
                f"{tag}jm{it}")
            vop(lambda e: e.tensor_scalar_mul(scol(MID), scol(MID), 0.5),
                f"{tag}jmh{it}")
            vop(lambda e: e.tensor_scalar(
                out=work, in0=gidxf, scalar1=scol(MID), scalar2=None,
                op0=ALU.is_ge), f"{tag}jw{it}")
            vop(lambda e: e.tensor_mul(work, work, tie), f"{tag}jt{it}")
            vop(lambda e: e.tensor_reduce(
                out=scol(CNTP), in_=work, axis=AX.X, op=ALU.add),
                f"{tag}jr{it}")
            count_into(scol(CNTP), CNT, f"{tag}jc{it}")
            vop(lambda e: e.tensor_scalar(
                out=scol(GG), in0=scol(CNT), scalar1=scol(KEEP), scalar2=None,
                op0=ALU.is_ge), f"{tag}jg{it}")
            vop(lambda e: e.tensor_scalar(
                out=scol(GI), in0=scol(GG), scalar1=-1.0, scalar2=1.0,
                op0=ALU.mult, op1=ALU.add), f"{tag}jgi{it}")
            vop(lambda e: e.tensor_sub(scol(T1), scol(MID), scol(JLO)),
                f"{tag}jt1{it}")
            vop(lambda e: e.tensor_mul(scol(T1), scol(T1), scol(GG)),
                f"{tag}jt1m{it}")
            vop(lambda e: e.tensor_add(scol(JLO), scol(JLO), scol(T1)),
                f"{tag}jlou{it}")
            vop(lambda e: e.tensor_sub(scol(T2), scol(MID), scol(JHI)),
                f"{tag}jt2{it}")
            vop(lambda e: e.tensor_mul(scol(T2), scol(T2), scol(GI)),
                f"{tag}jt2m{it}")
            vop(lambda e: e.tensor_add(scol(JHI), scol(JHI), scol(T2)),
                f"{tag}jhiu{it}")
        vop(lambda e: e.tensor_scalar(
            out=work, in0=gidxf, scalar1=scol(JLO), scalar2=None,
            op0=ALU.is_ge), f"{tag}fw")
        vop(lambda e: e.tensor_mul(work, work, tie), f"{tag}fm")
        vop(lambda e: e.tensor_add(self_, selA, work), f"{tag}fs")
        em.add("tensor", lambda e: e.transpose(
            cx.psB[0:32, 0:128], self_, identf),
            reads=SM, writes=("psB",), name=f"{tag}tT")
        pop(lambda e: e.tensor_copy(out=selT, in_=cx.psB[0:32, 0:128]),
            f"{tag}tTe")
        vop(lambda e: e.memset(scr3, 0.0), f"{tag}z3")
        vop(lambda e: e.tensor_tensor_scan(
            out=Wsc, data0=selT, data1=scr3, initial=zcol, op0=ALU.add,
            op1=ALU.add), f"{tag}scan")
        em.add("tensor", lambda e: e.matmul(
            cx.psB[0:32, 2:3], LTf, Wsc[:, 127:128], start=True, stop=True),
            reads=SM, writes=("psB",), name=f"{tag}co")
        pop(lambda e: e.tensor_copy(out=coloff, in_=cx.psB[0:32, 2:3]),
            f"{tag}coe")
        vop(lambda e: e.tensor_scalar(
            out=pcc, in0=coloff, scalar1=-1.0, scalar2=1024.0, op0=ALU.mult,
            op1=ALU.add), f"{tag}pc1")
        vop(lambda e: e.tensor_add(pcc, pcc, c128c), f"{tag}pc2")
        vop(lambda e: e.tensor_sub(scr3, Wsc, selT), f"{tag}p1")
        vop(lambda e: e.tensor_scalar(
            out=scr3, in0=scr3, scalar1=coloff, scalar2=None, op0=ALU.add),
            f"{tag}p2")
        vop(lambda e: e.tensor_sub(scr4, iotap, Wsc), f"{tag}k1")
        vop(lambda e: e.tensor_add(scr4, scr4, selT), f"{tag}k2")
        vop(lambda e: e.tensor_scalar(
            out=scr4, in0=scr4, scalar1=pcc, scalar2=None, op0=ALU.add),
            f"{tag}k3")
        vop(lambda e: e.tensor_sub(scr3, scr3, scr4), f"{tag}c1")
        vop(lambda e: e.tensor_mul(scr3, scr3, selT), f"{tag}c2")
        vop(lambda e: e.tensor_add(scr3, scr3, scr4), f"{tag}c3")
        em.add("tensor", lambda e: e.transpose(
            cx.psB[:, 128:160], scr3, identf[0:32, 0:32]),
            reads=SM, writes=("psB",), name=f"{tag}pT")
        pop(lambda e: e.tensor_copy(out=posf, in_=cx.psB[:, 128:160]),
            f"{tag}pTe")
        oh = [A2.bitcast(F32)[:, 0:128], A2.bitcast(F32)[:, 128:256]]
        ohk = ("l0ohA", "l0ohB")
        bank = cx.psum[3]
        for jc in range(8):
            for g in range(32):
                em.add("vector", lambda e, g=g, jc=jc: e.tensor_scalar(
                    out=oh[g % 2], in0=jrow[:, 128 * jc:128 * (jc + 1)],
                    scalar1=posf[:, g:g + 1],
                    scalar2=None, op0=ALU.is_equal),
                    reads=SM + ("l0jr",), writes=(ohk[g % 2],),
                    name=f"{tag}oh{jc}_{g}")
                em.add("tensor", lambda e, g=g, jc=jc: e.matmul(
                    bank[:, jc:jc + 1], oh[g % 2],
                    gidxf[:, g:g + 1], start=(g == 0), stop=(g == 31)),
                    reads=(ohk[g % 2],) + SM, writes=("ps3",),
                    name=f"{tag}lm{jc}_{g}")
        em.add("vector", lambda e, listu=listu: e.tensor_copy(
            out=listu, in_=bank[:, 0:8]),
            reads=("ps3",) + SM, writes=SM, name=f"{tag}le")


    bisect_route(0, qlu, "l0bq")
    bisect_route(1, kvlu, "l0bk")
    em.add("vector", lambda e: e.tensor_copy(out=qlf, in_=qlu.bitcast(I32)),
           reads=SM, writes=SM, name="l0qlf")
    em.add("vector", lambda e: e.tensor_scalar(
        out=qgf, in0=qlf[:, 0:4], scalar1=finv, scalar2=None, op0=ALU.mult),
        reads=SM + ("l0bc",), writes=SM, name="l0qg1")
    em.add("vector", lambda e: e.tensor_scalar(
        out=qgl, in0=qlf[:, 4:8], scalar1=fcol, scalar2=None, op0=ALU.mult),
        reads=SM + ("l0bc",), writes=SM, name="l0qg2")
    em.add("vector", lambda e: e.tensor_add(qgf, qgf, qgl),
           reads=SM, writes=SM, name="l0qg3")
    em.add("vector", lambda e: e.tensor_copy(out=qgu, in_=qgf),
           reads=SM, writes=SM, name="l0qgu")
    em.add("vector", lambda e: e.tensor_scalar(
        out=qgl8, in0=qlf, scalar1=f2048, scalar2=None, op0=ALU.subtract),
        reads=SM + ("l0bc",), writes=SM, name="l0qgl8")
    em.fence()
    dump_sb("dbg_sm", xb[:, 0:2048], SM)

    rkv_tm = B1[:, 0:8192]
    rkv_fm = B1[:, 8192:16384]
    rq_tm = A1[:, 0:4096]
    rq_fm = A1[:, 4096:8192]
    iota_own = A1[:, 9216:13312].bitcast(I32)
    wbh = [A1[:, 13312:14336], A1[:, 14336:15360]]
    wosl = A1[:, 15360:16384]
    onehot = A2[:, 0:16384]
    hq_fm = A2[:, 8192:10240]
    hstages = [A2[:, 10240:11264], A2[:, 14336:15360]]
    hpTs = [A2[:, 11264:12288], A2[:, 15360:16384]]
    ao_h = A2[:, 12288:14336]
    hk_fm = B2[:, 0:4096]
    hv_fm = B2[:, 4096:8192]
    hv_tm = B2[:, 8192:12288]
    ho_tm = B2[:, 12288:16384]

    for c in range(8):
        em.dma("gpsimd",
               lambda e, c=c: e.indirect_dma_start(
            out=rkv_tm[:, 1024 * c:1024 * (c + 1)], out_offset=None,
            in_=dr["l0_xn_out"],
            in_offset=IndirectOffsetOnAxis(
                ap=kvlu.bitcast(U32)[:, c:c + 1], axis=0)),
            reads=SM + ("l0xnout",), writes=("l0rkv",), name=f"l0gkv{c}")
    for c in range(4):
        em.dma("gpsimd",
               lambda e, c=c: e.indirect_dma_start(
            out=rq_tm[:, 1024 * c:1024 * (c + 1)], out_offset=None,
            in_=dr["l0_xn_out"],
            in_offset=IndirectOffsetOnAxis(
                ap=qgu.bitcast(U32)[:, c:c + 1], axis=0)),
            reads=SM + ("l0xnout",), writes=("l0rq",), name=f"l0gq{c}")
    rstd8 = rms_stats(cx, lambda i: rkv_tm[:, 1024 * i:1024 * (i + 1)], 8,
                      "l0rkv", "l0rmk")
    for i in range(8):
        em.add("vector", lambda e, i=i: e.tensor_scalar(
            out=rkv_tm[:, 1024 * i:1024 * (i + 1)],
            in0=rkv_tm[:, 1024 * i:1024 * (i + 1)],
            scalar1=rstd8[:, i:i + 1], scalar2=None, op0=ALU.mult),
            reads=("l0rkv", "stat"), writes=("l0rkv",), name=f"l0rka{i}")
    transpose_tm_to_fm(cx, rkv_tm, rkv_fm, "l0rkv", "l0rkf", 8, 8,
                       tag="l0rkt")
    for f in range(NF):
        em.add("vector", lambda e, f=f: e.tensor_scalar(
            out=rkv_fm[:, 1024 * f:1024 * (f + 1)],
            in0=rkv_fm[:, 1024 * f:1024 * (f + 1)],
            scalar1=bcols[:, 16 + f:17 + f], scalar2=None, op0=ALU.mult),
            reads=("l0rkf", "l0bc"), writes=("l0rkf",), name=f"l0rkg{f}")
    rstd4 = rms_stats(cx, lambda i: rq_tm[:, 1024 * i:1024 * (i + 1)], 4,
                      "l0rq", "l0rmq")
    for i in range(4):
        em.add("vector", lambda e, i=i: e.tensor_scalar(
            out=rq_tm[:, 1024 * i:1024 * (i + 1)],
            in0=rq_tm[:, 1024 * i:1024 * (i + 1)],
            scalar1=rstd4[:, i:i + 1], scalar2=None, op0=ALU.mult),
            reads=("l0rq", "stat"), writes=("l0rq",), name=f"l0rqa{i}")
    transpose_tm_to_fm(cx, rq_tm, rq_fm, "l0rq", "l0rqf", 4, 8, tag="l0rqt")
    for f in range(NF):
        em.add("vector", lambda e, f=f: e.tensor_scalar(
            out=rq_fm[:, 512 * f:512 * (f + 1)],
            in0=rq_fm[:, 512 * f:512 * (f + 1)],
            scalar1=bcols[:, 16 + f:17 + f], scalar2=None, op0=ALU.mult),
            reads=("l0rqf", "l0bc"), writes=("l0rqf",), name=f"l0rqg{f}")

    def ev_hq(f):
        em.add("vector", lambda e, f=f: e.tensor_copy(
            out=hq_fm[:, 512 * f:512 * (f + 1)], in_=cx.psum[0][:, 0:512]),
            reads=("ps0",), writes=("l0hq",), name=f"l0hqe{f}")

    proj_p2(cx, dr["l0_wqh"], rq_fm, "l0rqf", wbh, ("l0whA", "l0whB"),
            4, ev_hq, "l0hqp", ntok=512, nkt=8, chunks=1)

    def ev_hkv(f):
        dst = hk_fm if f < 4 else hv_fm
        dk = "l0hk" if f < 4 else "l0hv"
        fo = f if f < 4 else f - 4
        for c in range(2):
            em.add("vector", lambda e, dst=dst, fo=fo, c=c, dk=dk:
                   e.tensor_copy(
                out=dst[:, 1024 * fo + 512 * c:1024 * fo + 512 * (c + 1)],
                in_=cx.psum[c][:, 0:512]),
                reads=(f"ps{c}",), writes=(dk,), name=f"l0kve{f}_{c}")

    proj_p2(cx, dr["l0_wkvh"], rkv_fm, "l0rkf", wbh, ("l0whA", "l0whB"),
            8, ev_hkv, "l0kvp", ntok=1024, nkt=8, chunks=2)
    for g in range(8):
        for fp in range(4):
            em.add("tensor", lambda e, g=g, fp=fp: e.transpose(
                cx.psT[0][:, 128 * fp:128 * (fp + 1)],
                hv_fm[:, 1024 * fp + 128 * g:1024 * fp + 128 * (g + 1)],
                cx.ident[:]),
                reads=("l0hv", "ident"), writes=("psT0",),
                name=f"l0vtt{g}_{fp}")
        em.add("vector", lambda e, g=g: e.tensor_copy(
            out=hv_tm[:, 512 * g:512 * (g + 1)], in_=cx.psT[0][:, 0:512]),
            reads=("psT0",), writes=("l0hvt",), name=f"l0vte{g}")
    for hh in range(8):
        rb = 64 * (hh % 2)
        pt4 = hh // 2
        for qt in range(4):
            st = (hh * 4 + qt) % 2
            hst = hstages[st]
            hstk = f"l0hst{st}"
            hpT = hpTs[st]
            hpk = f"l0hpT{st}"
            hm = hmc[st]
            hs2 = hsc[st]
            hmk = f"l0hmc{st}"
            for c in range(2):
                ps = cx.psum[2 * st + c]
                psk = f"ps{2 * st + c}"
                em.add("tensor", lambda e, ps=ps, rb=rb, pt4=pt4, qt=qt, c=c:
                       e.matmul(
                    ps[:, 0:512],
                    hq_fm[rb:rb + 64, 512 * pt4 + 128 * qt:
                          512 * pt4 + 128 * (qt + 1)],
                    hk_fm[rb:rb + 64, 1024 * pt4 + 512 * c:
                          1024 * pt4 + 512 * (c + 1)],
                    start=True, stop=True),
                    reads=("l0hq", "l0hk"), writes=(psk,),
                    name=f"l0hsim{hh}_{qt}_{c}")
                em.add("vector", lambda e, ps=ps, c=c, hst=hst, hm=hm:
                       e.tensor_scalar(
                    out=hst[:, 512 * c:512 * (c + 1)], in0=ps[:, 0:512],
                    scalar1=-SC, scalar2=3.0e38, op0=ALU.mult, op1=ALU.min,
                    accum_out=hm[:, c:c + 1]),
                    reads=(psk,), writes=(hstk, hmk),
                    name=f"l0hng{hh}_{qt}_{c}")
            em.add("vector", lambda e, hm=hm: e.tensor_reduce(
                out=hm[:, 2:3], in_=hm[:, 0:2], axis=AX.X, op=ALU.min),
                reads=(hmk,), writes=(hmk,), name=f"l0hmr{hh}_{qt}")
            for c in range(2):
                em.add("scalar", lambda e, c=c, hst=hst, hm=hm, hs2=hs2:
                       e.activation(
                    out=hst[:, 512 * c:512 * (c + 1)],
                    in_=hst[:, 512 * c:512 * (c + 1)], func=AF.Exp,
                    bias=hm[:, 2:3], scale=-1.0,
                    accum_out=hs2[:, c:c + 1]),
                    reads=(hstk, hmk), writes=(hstk, hmk),
                    name=f"l0hex{hh}_{qt}_{c}")
            em.add("vector", lambda e, hs2=hs2: e.tensor_add(
                hs2[:, 2:3], hs2[:, 0:1], hs2[:, 1:2]),
                reads=(hmk,), writes=(hmk,), name=f"l0hsm{hh}_{qt}")
            em.add("vector", lambda e, hs2=hs2: e.reciprocal(
                out=hs2[:, 3:4], in_=hs2[:, 2:3]),
                reads=(hmk,), writes=(hmk,), name=f"l0hrc{hh}_{qt}")
            em.add("vector", lambda e, hst=hst, hs2=hs2: e.tensor_scalar(
                out=hst, in0=hst, scalar1=hs2[:, 3:4], scalar2=None,
                op0=ALU.mult),
                reads=(hstk, hmk), writes=(hstk,),
                name=f"l0hnm{hh}_{qt}")
            for j in range(8):
                em.add("tensor", lambda e, j=j, hst=hst, st=st: e.transpose(
                    cx.psT[st][:, 128 * j:128 * (j + 1)],
                    hst[:, 128 * j:128 * (j + 1)], cx.ident[:]),
                    reads=(hstk, "ident"), writes=(f"psT{st}",),
                    name=f"l0hpt{hh}_{qt}_{j}")
            em.add("vector", lambda e, hpT=hpT, st=st: e.tensor_copy(
                out=hpT, in_=cx.psT[st][:, 0:1024]),
                reads=(f"psT{st}",), writes=(hpk,), name=f"l0hpe{hh}_{qt}")
            for g in range(8):
                em.add("tensor", lambda e, g=g, hh=hh, hpT=hpT, st=st:
                       e.matmul(
                    cx.psB[0:64, 128 * st:128 * (st + 1)],
                    hv_tm[:, 512 * g + 64 * hh:512 * g + 64 * (hh + 1)],
                    hpT[:, 128 * g:128 * (g + 1)],
                    start=(g == 0), stop=(g == 7)),
                    reads=("l0hvt", hpk), writes=("psB",),
                    name=f"l0hav{hh}_{qt}_{g}")
            em.add("vector", lambda e, rb=rb, pt4=pt4, qt=qt, st=st:
                   e.tensor_copy(
                out=ao_h[rb:rb + 64, 512 * pt4 + 128 * qt:
                         512 * pt4 + 128 * (qt + 1)],
                in_=cx.psB[0:64, 128 * st:128 * (st + 1)]),
                reads=("psB",), writes=("l0aoh",), name=f"l0hae{hh}_{qt}")
    for qt in range(4):
        for kt in range(4):
            em.dma("sync", lambda e, kt=kt, qt=qt: e.dma_start(
                out=wosl, in_=dr["l0_woh"][kt]),
                writes=("l0wos",), name=f"l0whl{qt}_{kt}")
            for half in range(2):
                em.add("tensor", lambda e, qt=qt, kt=kt, half=half:
                       e.matmul(
                    cx.psum[half][:, 0:512],
                    ao_h[:, 512 * kt + 128 * qt:512 * kt + 128 * (qt + 1)],
                    wosl[:, 512 * half:512 * (half + 1)],
                    start=(kt == 0), stop=(kt == 3)),
                    reads=("l0aoh", "l0wos"), writes=(f"ps{half}",),
                    name=f"l0woh{qt}_{half}_{kt}")
        for half in range(2):
            em.add("vector", lambda e, qt=qt, half=half: e.tensor_copy(
                out=ho_tm[:, 1024 * qt + 512 * half:
                          1024 * qt + 512 * (half + 1)],
                in_=cx.psum[half][:, 0:512]),
                reads=(f"ps{half}",), writes=("l0ho",),
                name=f"l0whe{qt}_{half}")
    em.fence()
    dump_sb("dbg_ho", ho_tm, ("l0ho",))
    dump_sb("dbg_aoh", ao_h, ("l0aoh",))
    dump_sb("dbg_hq", hq_fm, ("l0hq",))
    dump_sb("dbg_hk", hk_fm, ("l0hk",))
    dump_sb("dbg_rkv", rkv_tm, ("l0rkv",))
    hor = dr["l0_ho_in"].rearrange("(i p) d -> i p d", p=128)
    for i in range(4):
        em.dma("sync", lambda e, i=i: e.dma_start(
            out=hor[i], in_=ho_tm[:, 1024 * i:1024 * (i + 1)]),
            reads=("l0ho",), writes=("l0hoin",), name=f"l0hos{i}")
    _cc_pair(em, "AllGather", dr["l0_ho_in"], dr["l0_ho_out"],
             ("l0hoin",), ("l0hoout",), "l0hocc")
    ho_agg = B2[:, 8192:16384]
    em.dma("sync", lambda e: e.dma_start(
        out=ho_agg.rearrange("p (c d) -> p c d", d=1024),
        in_=dr["l0_ho_out"].rearrange("(c p) d -> p c d", p=128)),
        reads=("l0hoout",), writes=("l0hoA", "l0hvt", "l0ho"),
        name="l0hol")
    em.add("gpsimd", lambda e: e.iota(
        iota_own[:, 0:2048], [[1, 2048]], base=0, channel_multiplier=0),
        writes=("l0io",), name="l0io")
    for jt in range(8):
        em.add("vector", lambda e, jt=jt: e.tensor_scalar(
            out=onehot[:, 2048 * jt:2048 * (jt + 1)],
            in0=iota_own[:, 0:2048], scalar1=qgl8[:, jt:jt + 1],
            scalar2=None, op0=ALU.is_equal),
            reads=("l0io",) + SM, writes=("l0oh",), name=f"l0oh{jt}")
    for i in range(NT):
        for half in range(2):
            ps = cx.psum[half]
            for jt in range(8):
                em.add("tensor", lambda e, ps=ps, i=i, half=half, jt=jt:
                       e.matmul(
                    ps[:, 0:512],
                    onehot[:, 2048 * jt + 128 * i:2048 * jt + 128 * (i + 1)],
                    ho_agg[:, 1024 * jt + 512 * half:
                           1024 * jt + 512 * (half + 1)],
                    start=(jt == 0), stop=(jt == 7)),
                    reads=("l0oh", "l0hoA"), writes=(f"ps{half}",),
                    name=f"l0sct{i}_{half}_{jt}")
            em.add("vector", lambda e, ps=ps, i=i, half=half:
                   e.scalar_tensor_tensor(
                out=s_tm[:, 1024 * i + 512 * half:1024 * i + 512 * (half + 1)],
                in0=ps[:, 0:512], scalar=1.0,
                in1=s_tm[:, 1024 * i + 512 * half:1024 * i + 512 * (half + 1)],
                op0=ALU.mult, op1=ALU.add),
                reads=(f"ps{half}", "l0s"), writes=("l0s",),
                name=f"l0sadd{i}_{half}")
    em.fence()
    dump_sb("dbg_s", s_tm[:, 0:16384], ("l0s",))

    xq_fm = A1
    transpose_tm_to_fm(cx, xqt, xq_fm, "l0xq", "l0xqf", NT, NF, tag="l0qtr")
    dump_sb("dbg_xqf", xq_fm[:, 0:16384], ("l0xqf",))
    em.fence()

    def gslab(si):
        if si < 5:
            return A2[:, 2816 * si:2816 * (si + 1)], "l0wgA2"
        if si < 10:
            return B1[:, 2816 * (si - 5):2816 * (si - 4)], "l0wgB1"
        if si < 15:
            return B2[:, 2816 * (si - 10):2816 * (si - 9)], "l0wgB2"
        return xqt[:, 0:2816], "l0wgXQ"

    wldf = B2[:, 14080:16128].bitcast(F32)
    wnat = xqt[:, 2816:3840]
    hqs = xqt[:, 3840:6656]
    h_tm = xqt[:, 6656:9472]
    gsil = xqt[:, 9472:10496].bitcast(F32)
    qv = xqt[:, 10496:11520].bitcast(F32)
    qq = xqt[:, 11520:12544].bitcast(F32)
    qb = xqt[:, 12544:13568].bitcast(F32)
    qt2 = xqt[:, 13568:14592].bitcast(F32)
    qmx = B2[:, 16128:16192].bitcast(F32)
    qsc2 = B2[:, 16192:16256].bitcast(F32)
    qrs2 = B2[:, 16256:16320].bitcast(F32)
    fsc = {"key": "l0fscr", "mx": qmx, "sc": qsc2, "rsc": qrs2, "v": qv,
           "q": qq, "b": qb, "t": qt2}
    for wi, wdram in enumerate((dr["l0f_wg"], dr["l0f_wu"])):
        for s5 in range(22):
            em.dma("sync", lambda e, wdram=wdram, s5=s5: e.dma_start(
                out=wldf, in_=wdram[s5]), reads=("l0fwS",),
                writes=("l0fwS",), name=f"l0fwl{wi}_{s5}")
            emit_fp4_quant_chunked(
                cx, wldf, wnat, ("l0fwS",), ("l0fwS",), 1024,
                fsc, f"l0fq{wi}_{s5}", chunk=512)
            for kt in range(8):
                em.add("tensor", lambda e, kt=kt: e.transpose(
                    cx.psT[0][:, 128 * kt:128 * (kt + 1)],
                    wnat[:, 128 * kt:128 * (kt + 1)], cx.ident[:]),
                    reads=("l0fwS", "ident"), writes=("psT0",),
                    name=f"l0ftt{wi}_{s5}_{kt}")
            for kt in range(8):
                sl, slk = gslab(wi * 8 + kt)
                em.add("vector", lambda e, kt=kt, sl=sl, s5=s5: e.tensor_copy(
                    out=sl[:, 128 * s5:128 * (s5 + 1)],
                    in_=cx.psT[0][:, 128 * kt:128 * (kt + 1)]),
                    reads=("psT0",), writes=(slk,),
                    name=f"l0fte{wi}_{s5}_{kt}")
    hq_dram = dr["l0_hqT"]
    for i in range(NT):
        for c6 in range(6):
            wd = 512 if c6 < 5 else 256
            for which in range(2):
                ps = cx.psum[which]
                for kt in range(8):
                    sl, slk = gslab(which * 8 + kt)
                    em.add("tensor", lambda e, ps=ps, kt=kt, sl=sl, slk=slk,
                           i=i, c6=c6, wd=wd: e.matmul(
                        ps[:, 0:wd],
                        xq_fm[:, HALF * kt + 128 * i:HALF * kt + 128 * (i + 1)],
                        sl[:, 512 * c6:512 * c6 + wd],
                        start=(kt == 0), stop=(kt == 7)),
                        reads=("l0xqf", slk), writes=(f"ps{which}",),
                        name=f"l0gm{i}_{c6}_{which}_{kt}")
            em.add("scalar", lambda e, wd=wd: e.activation(
                out=gsil[:, 0:wd], in_=cx.psum[0][:, 0:wd], func=AF.Silu),
                reads=("ps0",), writes=("l0gs",), name=f"l0sl{i}_{c6}")
            em.add("vector", lambda e, i=i, c6=c6, wd=wd:
                   e.scalar_tensor_tensor(
                out=h_tm[:, 512 * c6:512 * c6 + wd], in0=cx.psum[1][:, 0:wd],
                scalar=1.0, in1=gsil[:, 0:wd], op0=ALU.mult, op1=ALU.mult),
                reads=("ps1", "l0gs"), writes=("l0fwS",),
                name=f"l0hh{i}_{c6}")
        emit_fp4_quant_chunked(
            cx, h_tm, h_tm, ("l0fwS",), ("l0fwS",), 2816, fsc,
            f"l0hq{i}", chunk=512)
        for b4 in range(3):
            lo = b4 * 8
            hi2 = min(lo + 8, 22)
            for kt in range(lo, hi2):
                em.add("tensor", lambda e, kt=kt, j=kt - lo: e.transpose(
                    cx.psT[0][:, 128 * j:128 * (j + 1)],
                    h_tm[:, 128 * kt:128 * (kt + 1)], cx.ident[:]),
                    reads=("l0fwS", "ident"), writes=("psT0",),
                    name=f"l0htt{i}_{kt}")
            em.add("vector", lambda e, lo=lo, hi2=hi2: e.tensor_copy(
                out=hqs[:, 128 * lo:128 * hi2],
                in_=cx.psT[0][:, 0:128 * (hi2 - lo)]),
                reads=("psT0",), writes=("l0hqs",), name=f"l0hte{i}_{b4}")
        em.dma("sync", lambda e, i=i: e.dma_start(
            out=hq_dram[:, :, 128 * i:128 * (i + 1)].rearrange(
                "k p t -> p k t"),
            in_=hqs.rearrange("p (k t) -> p k t", t=128)),
            reads=("l0hqs",), writes=("l0hqd",), name=f"l0hqw{i}")
    em.fence()
    g_tm = A1
    cx.s_tm, cx.s_key = s_tm, "l0s"
    cx.g_tm, cx.g_key = g_tm, "l0g"

    def dslab(kt):
        if kt < 16:
            return A2[:, 1024 * kt:1024 * (kt + 1)], "l0wdA2"
        return B1[:, 1024 * (kt - 16):1024 * (kt - 15)], "l0wdB1"

    wdf = B2[:, 0:5632].bitcast(F32)
    wdnat = B2[:, 5632:8448]
    hqin = [B2[:, 8448:11264], B2[:, 11264:14080]]
    dv = B2[:, 14080:15104].bitcast(F32)
    dq = B2[:, 15104:16128].bitcast(F32)
    dmx = B2[:, 16128:16192].bitcast(F32)
    dsc = B2[:, 16192:16256].bitcast(F32)
    drs = B2[:, 16256:16320].bitcast(F32)
    dsc2 = {"key": "l0fscr", "mx": dmx, "sc": dsc, "rsc": drs, "v": dv,
            "q": dq, "b": qb, "t": qt2}
    for s5 in range(8):
        em.dma("sync", lambda e, s5=s5: e.dma_start(
            out=wdf, in_=dr["l0f_wd"][s5]), reads=("l0fwS",),
            writes=("l0fwS",), name=f"l0dwl{s5}")
        emit_fp4_quant_chunked(
            cx, wdf, wdnat, ("l0fwS",), ("l0fwS",), 2816, dsc2,
            f"l0dq{s5}", chunk=512)
        for b4 in range(3):
            lo = b4 * 8
            hi2 = min(lo + 8, 22)
            for kt in range(lo, hi2):
                em.add("tensor", lambda e, kt=kt, j=kt - lo: e.transpose(
                    cx.psT[0][:, 128 * j:128 * (j + 1)],
                    wdnat[:, 128 * kt:128 * (kt + 1)], cx.ident[:]),
                    reads=("l0fwS", "ident"), writes=("psT0",),
                    name=f"l0dtt{s5}_{kt}")
            for kt in range(lo, hi2):
                sl, slk = dslab(kt)
                em.add("vector", lambda e, kt=kt, j=kt - lo, sl=sl, s5=s5:
                       e.tensor_copy(
                    out=sl[:, 128 * s5:128 * (s5 + 1)],
                    in_=cx.psT[0][:, 128 * j:128 * (j + 1)]),
                    reads=("psT0",), writes=(slk,), name=f"l0dte{s5}_{kt}")
    for i in range(NT):
        hb = hqin[i % 2]
        hk2 = f"l0hqi{i % 2}"
        em.dma("sync", lambda e, i=i, hb=hb: e.dma_start(
            out=hb.rearrange("p (k t) -> p k t", t=128),
            in_=hq_dram[:, :, 128 * i:128 * (i + 1)].rearrange(
                "k p t -> p k t")),
            reads=("l0hqd",), writes=(hk2,), name=f"l0hqr{i}")
        for half in range(2):
            ps = cx.psum[half]
            for kt in range(22):
                sl, slk = dslab(kt)
                em.add("tensor", lambda e, ps=ps, kt=kt, sl=sl, slk=slk,
                       hb=hb, hk2=hk2, half=half: e.matmul(
                    ps[:, 0:512], hb[:, 128 * kt:128 * (kt + 1)],
                    sl[:, 512 * half:512 * (half + 1)],
                    start=(kt == 0), stop=(kt == 21)),
                    reads=(hk2, slk), writes=(f"ps{half}",),
                    name=f"l0dm{i}_{half}_{kt}")
            em.add("scalar", lambda e, ps=ps, i=i, half=half: e.activation(
                out=g_tm[:, 1024 * i + 512 * half:
                         1024 * i + 512 * (half + 1)],
                in_=ps[:, 0:512], func=AF.Silu),
                reads=(f"ps{half}",), writes=("l0g",),
                name=f"l0ge{i}_{half}")
    em.fence()
    dump_sb("dbg_g", g_tm[:, 0:16384], ("l0g",))


def build_program(layers, trunc=None):
    nc = bass.Bass("TRN2", target_bir_lowering=False)
    x_in = nc.dram_tensor("x_in", [HALF, D], F32, kind="ExternalInput").ap()
    x_out = nc.dram_tensor("x_out", [HALF, D], F32, kind="ExternalOutput").ap()
    xmids = [nc.dram_tensor(f"x_mid{j}", [HALF, D], F32).ap()
             for j in range(max(0, len(layers) - 1))]
    decl = {0: declare_l0_dram, 1: declare_l1_dram, 2: declare_l2_dram, 3: declare_l3_dram}
    build = {0: build_l0, 1: build_l1, 2: build_l2, 3: build_l3}
    dr = {}
    for li in layers:
        dr.update(decl[li](nc))
    with ExitStack() as ctx:
        cx = new_ctx(nc, ctx)
        make_identity(cx)
        cur_in = x_in
        for j, li in enumerate(layers):
            last = j == len(layers) - 1
            cur_out = x_out if last else xmids[j]
            cx.cur_x = cur_in
            cx.x_rkey = None if j == 0 else f"xm{j - 1}"
            cx.x_wkey = None if last else f"xm{j}"
            with ExitStack() as lctx:
                outer_ln_streamed(cx, lctx, cur_in, tag=f"oln{li}")
                build[li](cx, lctx, dr)
                if li == 3:
                    final_update_l3(cx, lctx, cur_in, cur_out)
                else:
                    final_update_streamed(cx, lctx, cur_in, cur_out, cx.s_tm,
                                          cx.s_key, cx.g_tm, cx.g_key)
            cx.em.fence()
            cur_in = cur_out
        if trunc is not None:
            cx.em.ops = cx.em.ops[:trunc]
            keep = set(id(o) for o in cx.em.ops)
            cx.em.out_dmas = [o for o in cx.em.out_dmas if id(o) in keep]
        cx.em.prepare(ctx)
        block = ctx.enter_context(nc.Block())
        cx.em.emit(block)
    return nc


PREP = {0: prep_l0_weights, 1: prep_l1_weights, 2: prep_l2_weights, 3: prep_l3_weights}


def run_layers(layers, x, inputs, trunc=None):
    nc = build_program(layers, trunc=trunc)
    in_maps = []
    for c in range(8):
        b, h = c // 2, c % 2
        m = {"x_in": np.ascontiguousarray(x[b, HALF * h:HALF * (h + 1)])}
        for li in layers:
            m.update(PREP[li](inputs, h))
        in_maps.append(m)
    res = run_bass_kernel_spmd(nc, in_maps, core_ids=list(range(8)))
    out = np.empty_like(x)
    for c in range(8):
        b, h = c // 2, c % 2
        out[b, HALF * h:HALF * (h + 1)] = res.results[c]["x_out"]
    if "dbg_s" in res.results[0]:
        out = (out, [dict(r) for r in res.results])
    return out


def kernel(**inputs):
    x = np.asarray(inputs["L0_attn_x"], np.float32)
    return run_layers([0, 1, 2, 3], x, inputs)
